# version 04
# baseline (speedup 1.0000x reference)
import numpy as np
from contextlib import ExitStack

import concourse.bass as bass
import concourse.mybir as mybir
from concourse.bass_utils import run_bass_kernel_spmd

F32 = mybir.dt.float32
BF16 = mybir.dt.bfloat16
AF = mybir.ActivationFunctionType
ALU = mybir.AluOpType
AX = mybir.AxisListType

NCORE = 8
TC = 1024
TEXT = 2048
D = 2048
NH = 32
NKV = 8
DH = 64
WIN = 1024
DFF = 1408
NE = 8
NT = TC // 128
NS = TEXT // 128
ND = D // 128

SLOPES = [float(2.0 ** (-0.25 * h)) for h in range(NH)]

ENGINES = ("pe", "act", "dve", "pool", "sync")


class DmaSlot:
    __slots__ = ("name", "sem", "count", "groups")

    def __init__(self, name):
        self.name = name
        self.sem = None
        self.count = 0
        self.groups = {}


class Op:
    __slots__ = ("idx", "engine", "emit", "reads", "writes", "deps", "slot",
                 "slot_count", "slot_group", "need_inc", "milestone", "tag")

    def __init__(self, idx, engine, emit, reads, writes, slot, tag):
        self.idx = idx
        self.engine = engine
        self.emit = emit
        self.reads = reads
        self.writes = writes
        self.deps = []
        self.slot = slot
        self.slot_count = None
        self.slot_group = None
        self.need_inc = False
        self.milestone = None
        self.tag = tag


class Prog:
    def __init__(self, nc, self_sync=True):
        self.nc = nc
        self.ops = []
        self.track = {}
        self.slots = {}
        self.self_sync = self_sync
        self.pending_free = []
        self.newdeps = {}
        self.psum_names = set()
        self.n_waits = 0
        self.n_incs = 0

    def slot(self, name):
        s = self.slots.get(name)
        if s is None:
            s = self.slots[name] = DmaSlot(name)
        return s

    def new_tensor(self, name):
        if self.pending_free:
            self.newdeps[name] = self._prune(self.pending_free)

    def note_free(self, name):
        t = self.track.get(name)
        if t is None:
            return
        ops = list(t["w"].values())
        for lst in t["r"].values():
            ops.extend(lst)
        self.pending_free.extend(ops)
        self.pending_free = self._prune(self.pending_free)

    @staticmethod
    def _prune(deps):
        best = {}
        for d in deps:
            key = ("s", d.slot.name) if d.slot is not None else ("e", d.engine)
            cur = best.get(key)
            if cur is None or d.idx > cur.idx:
                best[key] = d
        return list(best.values())

    @staticmethod
    def _regions(lst):
        out = []
        for r in lst or ():
            if isinstance(r, tuple):
                out.append(r)
            else:
                out.append((r, None))
        return out

    def _dep_read(self, name, key):
        t = self.track.get(name)
        if t is None:
            return []
        w = t["w"]
        if key is None:
            return list(w.values())
        deps = []
        if key in w:
            deps.append(w[key])
        if None in w:
            deps.append(w[None])
        return deps

    def _dep_write(self, name, key):
        deps = self._dep_read(name, key)
        t = self.track.get(name)
        if t is not None:
            r = t["r"]
            if key is None:
                for lst in r.values():
                    deps.extend(lst)
            else:
                deps.extend(r.get(key, ()))
                deps.extend(r.get(None, ()))
        return deps

    def op(self, engine, emit, reads=(), writes=(), slot=None, tag="",
           gid=None):
        wr = self._regions(writes)
        rd = self._regions(reads)
        if engine == "pe":
            wr = [(n, None) if n in self.psum_names else (n, k)
                  for (n, k) in wr]
        else:
            promote = [(n, None) for (n, k) in rd if n in self.psum_names]
            if promote:
                rd = [(n, k) for (n, k) in rd if n not in self.psum_names]
                wr = wr + promote
        o = Op(len(self.ops), engine, emit, rd, wr, slot, tag)
        deps = []
        for (n, k) in o.reads:
            deps.extend(self._dep_read(n, k))
            if n in self.newdeps:
                deps.extend(self.newdeps.pop(n))
        for (n, k) in o.writes:
            deps.extend(self._dep_write(n, k))
            if n in self.newdeps:
                deps.extend(self.newdeps.pop(n))
        pruned = {}
        for d in deps:
            if d is o:
                continue
            if d.engine == o.engine and d.slot is None:
                if o.engine == "pe" or not self.self_sync:
                    continue
            key = ("s", d.slot.name) if d.slot is not None else ("e", d.engine)
            cur = pruned.get(key)
            if cur is None or d.idx > cur.idx:
                pruned[key] = d
        o.deps = list(pruned.values())
        if slot is not None:
            slot.count += 1
            o.slot_count = slot.count
            if gid is None:
                gid = ("u", slot.count)
            o.slot_group = gid
            g = slot.groups.get(gid, 0)
            slot.groups[gid] = max(g, slot.count)
        for (n, k) in o.reads:
            self.track.setdefault(n, {"w": {}, "r": {}})["r"].setdefault(
                k, []).append(o)
        for (n, k) in o.writes:
            t = self.track.setdefault(n, {"w": {}, "r": {}})
            if k is None:
                t["w"] = {None: o}
                t["r"] = {}
            else:
                t["w"][k] = o
                t["r"].pop(k, None)
        self.ops.append(o)
        return o

    def dma(self, queue, emit, reads=(), writes=(), slot=None, tag="",
            gid=None):
        assert slot is not None
        return self.op(queue, emit, reads=reads, writes=writes, slot=slot,
                       tag=tag, gid=gid)

    def emit_all(self, block, engine_sems):
        for o in self.ops:
            for d in o.deps:
                if d.slot is None:
                    d.need_inc = True
        counters = {e: 0 for e in ENGINES}
        for o in self.ops:
            if o.need_inc and o.slot is None:
                counters[o.engine] += 1
                o.milestone = counters[o.engine]
        per_engine = {e: [] for e in ENGINES}
        for o in self.ops:
            per_engine[o.engine].append(o)
        prog = self

        def runner(eng_name):
            def body(eng):
                waited = {}
                for o in per_engine[eng_name]:
                    for d in o.deps:
                        if d.slot is not None:
                            sem = d.slot.sem
                            val = 16 * d.slot.groups[d.slot_group]
                        else:
                            sem = engine_sems[d.engine]
                            val = d.milestone
                        key = sem.name
                        if waited.get(key, -1) >= val:
                            continue
                        eng.wait_ge(sem, val)
                        waited[key] = val
                        prog.n_waits += 1
                    inst = o.emit(eng)
                    if o.slot is not None:
                        inst.then_inc(o.slot.sem, 16)
                        prog.n_incs += 1
                    elif o.need_inc:
                        inst.then_inc(engine_sems[o.engine], 1)
                        prog.n_incs += 1
            return body

        section = {"pe": block.tensor, "act": block.scalar,
                   "dve": block.vector, "pool": block.gpsimd,
                   "sync": block.sync}
        for e in ENGINES:
            if per_engine[e]:
                section[e](runner(e))


ARENA_BYTES = 207872


class TMgr:

    def __init__(self, nc, prog, off):
        self.nc = nc
        self.p = prog
        self.off = off
        self.cms = {}
        self.logical = {}
        self.acm = nc.sbuf_tensor("arena", [128, ARENA_BYTES], mybir.dt.uint8)
        self.ah = self.acm.__enter__()
        self.abase = nc.lookup_mloc(self.ah).addr
        self.spans = {}

    def sbuf(self, name, shape, dtype):
        import functools, operator
        off = self.off[name]
        per_part = functools.reduce(operator.mul, shape[1:], 1) *             mybir.dt.size(dtype)
        assert off + per_part <= ARENA_BYTES, (name, off, per_part)
        assert off % 64 == 0, (name, off)
        h = self.nc.alloc_sbuf_tensor_at(name, shape, dtype,
                                         offset=self.abase + off)
        self.spans[name] = (off, off + per_part)
        self.logical[h.name] = name
        self.p.new_tensor(name)
        return h

    def psum(self, name, shape, dtype=F32):
        cm = self.nc.psum_tensor(name, shape, dtype)
        h = cm.__enter__()
        self.cms[name] = cm
        self.logical[h.name] = name
        self.p.psum_names.add(name)
        self.p.new_tensor(name)
        return h

    def free(self, *handles):
        for h in handles:
            name = self.logical.get(h.name, h.name)
            self.p.note_free(name)
            cm = self.cms.pop(name, None)
            if cm is not None:
                cm.__exit__(None, None, None)


def build_nc(taps=(), stop_after=None):
    nc = bass.Bass("TRN2", target_bir_lowering=False)
    taps = set(taps)

    din = lambda n, s: nc.dram_tensor(n, s, F32, kind="ExternalInput").ap()
    XE = din("xe", [TEXT, D])
    WQ = din("wq", [D, D])
    WK = din("wk", [512, D])
    WV = din("wv", [512, D])
    WO = din("wo", [D, D])
    QNW = din("qnw", [1, D])
    KNW = din("knw", [1, 512])
    GW = din("gw", [NE, D])
    WGATE = din("wgate", [NE, D, DFF])
    WUP = din("wup", [NE, D, DFF])
    WDOWN = din("wdown", [NE, DFF, D])
    SWG = din("swg", [D, DFF])
    SWU = din("swu", [D, DFF])
    SWD = din("swd", [DFF, D])
    MM = din("mmask", [128, 1152])
    POI = din("poison", [128, NS])
    IDF = din("identf", [128, 128])
    Y = nc.dram_tensor("y", [TC, D], F32, kind="ExternalOutput").ap()
    tap_outs = {}
    tap_shapes = {"KT": [512, TEXT], "QT": [D, TC], "MALL": [NH, TC],
                  "AO": [D, TC], "Y1": [TC, D], "GATES": [TC, NE],
                  "ACC2": [TC, D]}
    for t in taps:
        tap_outs[t] = nc.dram_tensor("tap_" + t, tap_shapes[t], F32,
                                     kind="ExternalOutput").ap()

    RA, RB, RC, RD, RE = 10240, 75776, 109568, 142336, 175104
    OFF = {
        "idb": 0, "idf": 256, "ones1": 768, "onesf": 1024, "small": 1536,
        "bns": 1792, "aggr": 1920, "knacc": 1984, "knbneg": 2048,
        "knscr": 2176, "epsc": 9920, "gates8": 3520, "grow8": 3776, "garow0": 5824,
        "garow1": 7872,
        "wkT": RA, "wvT": RA + 16384, "wstg0": RA + 32768, "wstg1": RA + 36864,
        **{f"wqT{i}": RA + 4096 * i for i in range(16)},
        **{f"Fr{i}": RA + 2048 * i for i in range(10)},
        "woq0": RA, "woq1": RA + 16384, "wosg0": RA + 32768,
        "wosg1": RA + 36864, "xrs0": RA + 40960, "xrs1": RA + 49152,
        "ybf": RA + 57344, "gwbf": RA + 61440, "gwT": RA + 57344,
        **{f"wgb{i}": RA + 8192 * i for i in range(2)},
        **{f"wub{i}": RA + 16384 + 8192 * i for i in range(2)},
        **{f"wdb{i}": RA + 32768 + 12288 * i for i in range(2)},
        **{f"KT{i}": RB + 4096 * i for i in range(4)},
        **{f"VA{i}": RB + 16384 + 2112 * i for i in range(8)},
        **{f"ACC{i}": (RB + 8192 * i if i < 4 else RC + 8192 * (i - 4))
           for i in range(8)},
        "m_all": RC, "mrow0": RC + 2048, "mrow1": RC + 4096,
        "rrecb": RC + 6144,
        "Qsb": RC + 8192, "sqs": RC + 16384, "wstgq0": RC + 24576,
        "wstgq1": RC + 28672,
        "Mm": RC + 8192, "poi": RC + 12800, "Hb0": RC + 12864,
        "Hb1": RC + 15168, "Fb0": RC + 17472, "Fb1": RC + 19520,
        "rsb": RC + 21568, "rsb2": RC + 25664, "rbsb": RC + 29760,
        "xstg0": RD, "xstg1": RD + 4096, "xTc0": RD + 8192,
        "xTc1": RD + 12288, "ksqs": RD + 16384, "row1": RD + 18432,
        "kwb": RD + 22528, "qwb": RD + 24576,
        **{f"AO{i}": RD + 2048 * i for i in range(16)},
        **{f"hT{i}": RD + 2048 * i for i in range(11)},
        "gbsb": RD + 22528, "gsb": RD + 26624, "usb": RD + 28672,
        "ttmp_KT": RA + 32768, "ttmp_QT": RA + 32768, "matmp": RA + 40960,
        "dummyA": RE + 24576, "dummyB": RA + 32768,
        "gtmp": RA + 32768, "ttmp_AO": RA + 32768,
        **{f"QT{i}": RE + 2048 * i for i in range(16)},
        **{f"y1T{i}": RE + 2048 * i for i in range(16)},
    }
    p = Prog(nc, self_sync=True)
    tm = TMgr(nc, p, OFF)
    st = ExitStack()

    idb = tm.sbuf("idb", [128, 128], BF16)
    idf = tm.sbuf("idf", [128, 128], F32)
    ones1 = tm.sbuf("ones1", [1, 128], BF16)
    onesf = tm.sbuf("onesf", [1, 128], F32)
    epsc = tm.sbuf("epsc", [128, 2], F32)
    knscr = tm.sbuf("knscr", [8, 336], F32)
    m_all = tm.sbuf("m_all", [NH, TC], BF16)
    mrows = [tm.sbuf(f"mrow{i}", [1, TC], BF16) for i in range(2)]
    knacc = tm.sbuf("knacc", [128, NKV], F32)
    knbneg = tm.sbuf("knbneg", [128, NH], F32)
    gates8 = tm.sbuf("gates8", [128, NT * NE], F32)
    grow8 = tm.sbuf("grow8", [NE, TC], BF16)
    garow = [tm.sbuf(f"garow{i}", [1, TC], BF16) for i in range(2)]
    small = tm.sbuf("small", [128, 64], F32)
    bns = tm.sbuf("bns", [128, 24], F32)
    aggr = tm.sbuf("aggr", [128, 2], F32)

    sems = {}
    s_misc = p.slot("misc")
    s_miscp = p.slot("miscp")
    s_out = p.slot("out")

    def sync_dma(dst, src, reads, writes, slot, gid=None, **kw):
        return p.dma("sync", lambda e: e.dma_start(dst, src, **kw),
                     reads=reads, writes=writes, slot=slot, gid=gid)

    def cast_dma(dst, src, reads, writes, slot, gid=None):
        return p.dma("pool", lambda e: e.dma_start(dst, src),
                     reads=reads, writes=writes, slot=slot, gid=gid)

    def finish_dummy(which):
        dmy = tm.sbuf(which, [128, D], F32)
        p.op("dve", lambda e: e.memset(dmy[:], 0.5), writes=[which])
        for t in range(NT):
            sync_dma(Y[128 * t:128 * (t + 1), :], dmy[:], [which],
                     [("OUT", t)], s_out, gid="yg")
        p.op("pool", lambda e: e.nop(), reads=[("OUT", None)], tag="final")
        with nc.Block() as block:
            for e in ENGINES:
                sems[e] = st.enter_context(nc.semaphore(f"sem_{e}"))
            for sl_ in p.slots.values():
                sl_.sem = st.enter_context(nc.semaphore(f"dma_{sl_.name}"))
            p.emit_all(block, sems)
        st.close()
        nc._dbg_prog = p
        return nc

    def finish_acc(ACC):
        for t in range(NT):
            sync_dma(Y[128 * t:128 * (t + 1), :], ACC[t][:],
                     [(f"ACC{t}", "lo"), (f"ACC{t}", "hi")],
                     [("OUT", t)], s_out, gid="yg")
        p.op("pool", lambda e: e.nop(), reads=[("OUT", None)], tag="final")
        with nc.Block() as block:
            for e in ENGINES:
                sems[e] = st.enter_context(nc.semaphore(f"sem_{e}"))
            for sl_ in p.slots.values():
                sl_.sem = st.enter_context(nc.semaphore(f"dma_{sl_.name}"))
            p.emit_all(block, sems)
        st.close()
        nc._dbg_prog = p
        return nc

    sems_holder = sems

    sync_dma(idf[:], IDF, [], ["idf"], s_misc)
    cast_dma(idb[:], IDF, [], ["idb"], s_miscp)
    p.op("dve", lambda e: e.memset(ones1[:], 1.0), writes=["ones1"])
    p.op("dve", lambda e: e.memset(onesf[:], 1.0), writes=["onesf"])
    p.op("dve", lambda e: e.memset(epsc[:, 0:1], 1e-6), writes=[("epsc", 0)])
    p.op("dve", lambda e: e.memset(epsc[:, 1:2], 1e-5), writes=[("epsc", 1)])

    pt0 = tm.psum("pt0", [128, 1024], BF16)
    pt1 = tm.psum("pt1", [128, 1024], BF16)
    pts = [pt0, pt1]
    pK = tm.psum("pK", [128, 512])
    pV = tm.psum("pV", [128, 512])
    pB = tm.psum("pB", [128, 512])
    pQ = tm.psum("pQ", [128, 512])

    wkT = tm.sbuf("wkT", [128, ND * 512], BF16)
    wvT = tm.sbuf("wvT", [128, ND * 512], BF16)
    wstg = [tm.sbuf(f"wstg{i}", [128, D], BF16) for i in range(2)]
    s_wstg = [p.slot(f"wstg{i}") for i in range(2)]
    KT = [tm.sbuf(f"KT{i}", [128, TEXT], BF16) for i in range(4)]
    VA = [tm.sbuf(f"VA{i}", [128, NS * 65], BF16) for i in range(NKV)]
    kwb = tm.sbuf("kwb", [128, 512], F32)
    row1 = tm.sbuf("row1", [1, 2048], BF16)

    def wT_build(W, nrow_tiles, dstT, dname):
        for r in range(nrow_tiles):
            b = r % 2
            cast_dma(wstg[b][:], W[128 * r:128 * (r + 1), :],
                     [], [f"wstg{b}"], s_wstg[b])
            for dc in range(ND):
                pb = pts[dc // 8]
                p.op("pe", lambda e, b=b, dc=dc, pb=pb: e.transpose(
                    pb[:, 128 * (dc % 8):128 * (dc % 8 + 1)],
                    wstg[b][:, 128 * dc:128 * (dc + 1)], idb[:]),
                    reads=[f"wstg{b}", "idb"], writes=[(pb.name, dc % 8)])
            for half in range(2):
                pb = pts[half]
                dst3 = dstT[:].rearrange("p (dc o) -> p dc o", o=512)[
                    :, 8 * half:8 * (half + 1), 128 * r:128 * (r + 1)]
                src3 = pb[:].rearrange("p (dc o) -> p dc o", o=128)
                p.op("dve", lambda e, dst3=dst3, src3=src3: e.tensor_copy(
                    dst3, src3),
                    reads=[pb.name, (pb.name, 0), (pb.name, 1), (pb.name, 2),
                           (pb.name, 3), (pb.name, 4), (pb.name, 5),
                           (pb.name, 6), (pb.name, 7)],
                    writes=[(dname, (r, half))])

    wT_build(WK, 4, wkT, "wkT")
    wT_build(WV, 4, wvT, "wvT")

    cast_dma(row1[0:1, 0:512], KNW, [], [("row1", "k")], p.slot("knw"))
    p.op("pe", lambda e: e.matmul(pB[:], ones1[:], row1[0:1, 0:512],
                                  start=True, stop=True),
         reads=[("row1", "k"), "ones1"], writes=["pB"])
    p.op("dve", lambda e: e.tensor_copy(kwb[:], pB[:]), reads=["pB"],
         writes=["kwb"])

    xstg = [tm.sbuf(f"xstg{i}", [128, D], BF16) for i in range(2)]
    s_xstg = [p.slot(f"xstg{i}") for i in range(2)]
    xTc = [tm.sbuf(f"xTc{i}", [128, D], BF16) for i in range(2)]
    ksqs = tm.sbuf("ksqs", [128, 512], F32)

    p.op("dve", lambda e: e.memset(knacc[:], 0.0), writes=["knacc"])

    def transpose_tile(srcname, src, key=None):
        rkey = srcname if key is None else (srcname, key)
        for dc in range(ND):
            pb = pts[dc // 8]
            p.op("pe", lambda e, dc=dc, pb=pb, src=src: e.transpose(
                pb[:, 128 * (dc % 8):128 * (dc % 8 + 1)],
                src[:, 128 * dc:128 * (dc + 1)], idb[:]),
                reads=[rkey, "idb"], writes=[(pb.name, dc % 8)])

    def evac_xTc(b):
        for half in range(2):
            pb = pts[half]
            p.op("dve", lambda e, b=b, half=half, pb=pb: e.tensor_copy(
                xTc[b][:, 1024 * half:1024 * (half + 1)], pb[:]),
                reads=[pb.name] + [(pb.name, i) for i in range(8)],
                writes=[(f"xTc{b}", half)])

    for j in range(NS):
        b = j % 2
        cast_dma(xstg[b][:], XE[128 * j:128 * (j + 1), :], [], [f"xstg{b}"],
                 s_xstg[b])
        transpose_tile(f"xstg{b}", xstg[b][:])
        evac_xTc(b)
        for dc in range(ND):
            p.op("pe", lambda e, b=b, dc=dc: e.matmul(
                pK[:], xTc[b][:, 128 * dc:128 * (dc + 1)],
                wkT[:, 512 * dc:512 * (dc + 1)],
                start=(dc == 0), stop=(dc == ND - 1)),
                reads=[(f"xTc{b}", dc // 8), "wkT"], writes=["pK"])
        p.op("act", lambda e: e.activation(ksqs[:], pK[:], AF.Square),
             reads=["pK"], writes=["ksqs"])
        p.op("dve", lambda e: e.reduce_sum(small[:, 0:1], ksqs[:], AX.X),
             reads=["ksqs"], writes=[("small", 0)])
        p.op("act", lambda e: e.activation(small[:, 1:2], small[:, 0:1],
                                           AF.Sqrt, bias=epsc[:, 0:1],
                                           scale=1.0 / 512),
             reads=[("small", 0), ("epsc", 0)], writes=[("small", 1)])
        p.op("dve", lambda e: e.reciprocal(small[:, 2:3], small[:, 1:2]),
             reads=[("small", 1)], writes=[("small", 2)])
        p.op("dve", lambda e, b=b: e.scalar_tensor_tensor(
            xstg[b][:, 0:512], pK[:], small[:, 2:3], kwb[:],
            ALU.mult, ALU.mult),
            reads=["pK", ("small", 2), "kwb"], writes=[(f"xstg{b}", "ks")])
        p.op("act", lambda e, b=b: e.activation(ksqs[:], xstg[b][:, 0:512],
                                                AF.Square),
             reads=[(f"xstg{b}", "ks")], writes=["ksqs"])
        p.op("dve", lambda e: e.reduce_sum(
            small[:, 8:16], ksqs[:].rearrange("p (g x) -> p g x", g=8), AX.X),
            reads=["ksqs"], writes=[("small", 8)])
        p.op("dve", lambda e: e.tensor_tensor(knacc[:], knacc[:],
                                              small[:, 8:16], ALU.max),
             reads=[("small", 8), "knacc"], writes=["knacc"])
        for c in range(4):
            p.op("pe", lambda e, b=b, c=c: e.transpose(
                pt0[:, 128 * c:128 * (c + 1)],
                xstg[b][:, 128 * c:128 * (c + 1)], idb[:]),
                reads=[(f"xstg{b}", "ks"), "idb"], writes=[("pt0", c)])
        for c in range(4):
            p.op("dve", lambda e, c=c, j=j: e.tensor_copy(
                KT[c][:, 128 * j:128 * (j + 1)], pt0[:, 128 * c:128 * (c + 1)]),
                reads=[("pt0", c)], writes=[(f"KT{c}", j)])
        for dc in range(ND):
            p.op("pe", lambda e, b=b, dc=dc: e.matmul(
                pV[:], xTc[b][:, 128 * dc:128 * (dc + 1)],
                wvT[:, 512 * dc:512 * (dc + 1)],
                start=(dc == 0), stop=(dc == ND - 1)),
                reads=[(f"xTc{b}", dc // 8), "wvT"], writes=["pV"])
        for kv in range(NKV):
            p.op("dve", lambda e, kv=kv, j=j: e.tensor_copy(
                VA[kv][:, 65 * j:65 * j + 64], pV[:, 64 * kv:64 * (kv + 1)]),
                reads=["pV"], writes=[(f"VA{kv}", j)])
        for kv in range(NKV):
            p.op("dve", lambda e, kv=kv, j=j: e.memset(
                VA[kv][:, 65 * j + 64:65 * j + 65], 1.0),
                writes=[(f"VA{kv}", j)])

    if stop_after == "A1":
        return finish_dummy("dummyA")

    p.op("pe", lambda e: e.transpose(pB[0:8, 0:128], knacc[:], idf[:]),
         reads=["knacc", "idf"], writes=["pB"])
    p.op("dve", lambda e: e.tensor_copy(knscr[0:8, 0:128], pB[0:8, 0:128]),
         reads=["pB"], writes=["knscr"])
    p.op("dve", lambda e: e.reduce_max(small[0:8, 3:4], knscr[0:8, 0:128],
                                       AX.X),
         reads=["knscr"], writes=[("small", 3)])
    p.op("act", lambda e: e.activation(small[0:8, 4:5], small[0:8, 3:4],
                                       AF.Sqrt), reads=[("small", 3)],
         writes=[("small", 4)])
    p.op("dve", lambda e: e.tensor_scalar(small[0:8, 5:6], small[0:8, 4:5],
                                          -1.0, None, ALU.mult),
         reads=[("small", 4)], writes=[("small", 5)])
    p.op("pe", lambda e: e.transpose(pB[0:1, 128:136], small[0:8, 5:6],
                                     idf[0:8, 0:8]),
         reads=[("small", 5), "idf"], writes=["pB"])
    p.op("dve", lambda e: e.tensor_copy(knscr[0:1, 128:136],
                                        pB[0:1, 128:136]),
         reads=["pB"], writes=["knscr"])
    s_kn = p.slot("knr")
    for m_ in range(2):
        for r_ in range(4):
            sync_dma(knscr[0:1, 160 + 2 * r_ + m_:192:8],
                     knscr[0:1, 128 + m_:136:2],
                     ["knscr"], ["knscr"], s_kn,
                     allow_slow_non_contiguous=True)
    p.op("pe", lambda e: e.matmul(pB[:, 0:32], onesf[:], knscr[0:1, 160:192],
                                  start=True, stop=True),
         reads=["knscr", "onesf"], writes=["pB"])
    p.op("dve", lambda e: e.tensor_copy(knbneg[:], pB[:, 0:32]),
         reads=["pB"], writes=["knbneg"])

    tm.free(wstg[1], wstg[0], wkT, wvT)

    if stop_after == "KN":
        return finish_dummy("dummyA")

    wqT = [tm.sbuf(f"wqT{i}", [128, D], BF16) for i in range(ND)]
    wstgq = [tm.sbuf(f"wstgq{i}", [128, D], BF16) for i in range(2)]
    s_wstgq = [p.slot(f"wstgq{i}") for i in range(2)]
    for r in range(ND):
        b = r % 2
        cast_dma(wstgq[b][:], WQ[128 * r:128 * (r + 1), :], [], [f"wstgq{b}"],
                 s_wstgq[b])
        transpose_tile(f"wstgq{b}", wstgq[b][:])
        for dc in range(ND):
            pb = pts[dc // 8]
            p.op("dve", lambda e, r=r, dc=dc, pb=pb: e.tensor_copy(
                wqT[dc][:, 128 * r:128 * (r + 1)],
                pb[:, 128 * (dc % 8):128 * (dc % 8 + 1)]),
                reads=[(pb.name, dc % 8)], writes=[(f"wqT{dc}", r)])

    if stop_after == "WQT":
        return finish_dummy("dummyA")

    qwb = tm.sbuf("qwb", [128, D], F32)
    cast_dma(row1[0:1, 0:2048], QNW, [], [("row1", "q")], p.slot("qnw"))
    for oc in range(4):
        p.op("pe", lambda e, oc=oc: e.matmul(
            pB[:], ones1[:], row1[0:1, 512 * oc:512 * (oc + 1)],
            start=True, stop=True),
            reads=[("row1", "q"), "ones1"], writes=["pB"])
        p.op("act", lambda e, oc=oc: e.activation(
            qwb[:, 512 * oc:512 * (oc + 1)], pB[:], AF.Copy, scale=0.125),
            reads=["pB"], writes=[("qwb", oc)])

    if stop_after == "QWB":
        return finish_dummy("dummyA")

    QT = [tm.sbuf(f"QT{i}", [128, TC], BF16) for i in range(ND)]
    Qsb = tm.sbuf("Qsb", [128, D], F32)
    sqs = tm.sbuf("sqs", [128, D], F32)

    _nt = NT
    _ql = 99
    if stop_after and stop_after.startswith("Q") and len(stop_after) == 2:
        _nt = int(stop_after[1]) + 1
    if stop_after and stop_after.startswith("QL"):
        _nt, _ql = 1, int(stop_after[2:])
    for t in range(_nt):
        b = t % 2
        cast_dma(xstg[b][:], XE[TC + 128 * t:TC + 128 * (t + 1), :], [],
                 [f"xstg{b}"], s_xstg[b])
        transpose_tile(f"xstg{b}", xstg[b][:])
        evac_xTc(b)
        if _ql < 2:
            continue
        for oc in range(4):
            for dc in range(ND):
                p.op("pe", lambda e, b=b, dc=dc, oc=oc: e.matmul(
                    pQ[:], xTc[b][:, 128 * dc:128 * (dc + 1)],
                    wqT[dc][:, 512 * oc:512 * (oc + 1)],
                    start=(dc == 0), stop=(dc == ND - 1)),
                    reads=[(f"xTc{b}", dc // 8), f"wqT{dc}"], writes=["pQ"])
            p.op("dve", lambda e, oc=oc: e.tensor_copy(
                Qsb[:, 512 * oc:512 * (oc + 1)], pQ[:]),
                reads=["pQ"], writes=[("Qsb", oc)])
            if _ql < 3:
                continue
            p.op("act", lambda e, oc=oc: e.activation(
                sqs[:, 512 * oc:512 * (oc + 1)], pQ[:], AF.Square),
                reads=["pQ"], writes=[("sqs", oc)])
        if _ql < 4:
            continue
        p.op("dve", lambda e: e.reduce_sum(
            small[:, 20:21], sqs[:], AX.X),
            reads=["sqs", ("sqs", 0), ("sqs", 1), ("sqs", 2), ("sqs", 3)],
            writes=[("small", 20)])
        p.op("act", lambda e: e.activation(small[:, 21:22], small[:, 20:21],
                                           AF.Sqrt, bias=epsc[:, 0:1],
                                           scale=1.0 / D),
             reads=[("small", 20), ("epsc", 0)], writes=[("small", 21)])
        p.op("dve", lambda e: e.reciprocal(small[:, 22:23], small[:, 21:22]),
             reads=[("small", 21)], writes=[("small", 22)])
        if _ql < 5:
            continue
        p.op("dve", lambda e, b=b: e.scalar_tensor_tensor(
            xstg[b][:], Qsb[:], small[:, 22:23], qwb[:], ALU.mult, ALU.mult),
            reads=[("Qsb", 0), ("Qsb", 1), ("Qsb", 2), ("Qsb", 3),
                   ("small", 22), "qwb"],
            writes=[(f"xstg{b}", "qh")])
        if _ql < 6:
            continue
        p.op("act", lambda e, b=b: e.activation(sqs[:], xstg[b][:], AF.Square),
             reads=[(f"xstg{b}", "qh")],
             writes=["sqs", ("sqs", 0), ("sqs", 1), ("sqs", 2), ("sqs", 3)])
        p.op("dve", lambda e: e.reduce_sum(
            small[:, 32:64], sqs[:].rearrange("p (g x) -> p g x", g=32), AX.X),
            reads=["sqs"], writes=[("small", 32)])
        p.op("act", lambda e: e.activation(small[:, 32:64], small[:, 32:64],
                                           AF.Sqrt),
             reads=[("small", 32)], writes=[("small", 32)])
        if _ql < 7:
            continue
        p.op("dve", lambda e: e.tensor_tensor(small[:, 32:64], small[:, 32:64],
                                              knbneg[:], ALU.mult),
             reads=[("small", 32), "knbneg"], writes=[("small", 32)])
        p.op("pe", lambda e: e.transpose(pB[0:32, 0:128], small[:, 32:64],
                                         idf[:]),
             reads=[("small", 32), "idf"], writes=["pB"])
        p.op("dve", lambda e, t=t: e.tensor_copy(
            m_all[:, 128 * t:128 * (t + 1)], pB[0:32, 0:128]),
            reads=["pB"], writes=[("m_all", t)])
        if _ql < 8:
            continue
        transpose_tile(f"xstg{b}", xstg[b][:], key="qh")
        for dc in range(ND):
            pb = pts[dc // 8]
            p.op("dve", lambda e, t=t, dc=dc, pb=pb: e.tensor_copy(
                QT[dc][:, 128 * t:128 * (t + 1)],
                pb[:, 128 * (dc % 8):128 * (dc % 8 + 1)]),
                reads=[(pb.name, dc % 8)], writes=[(f"QT{dc}", t)])

    if stop_after and stop_after.startswith("Q"):
        return finish_dummy("dummyA")

    tm.free(Qsb, sqs, qwb, kwb, ksqs, row1, *wqT, *xstg, *xTc, *wstgq)
    tm.free(pQ, pB, pV, pK, pt1, pt0)

    def tap_dump(name, tiles, nrow, regions_of, width=TC):
        if name not in tap_outs:
            return
        ttmp = tm.sbuf("ttmp_" + name, [nrow, width], F32)
        s_tap = p.slot("tap_" + name)
        for c, tile in enumerate(tiles):
            p.op("act", lambda e, tile=tile: e.activation(ttmp[:], tile[:],
                                                          AF.Copy),
                 reads=regions_of(c), writes=["ttmp_" + name])
            sync_dma(tap_outs[name][nrow * c:nrow * (c + 1), :], ttmp[:],
                     ["ttmp_" + name], [("OUT", f"{name}{c}")], s_tap)
        tm.free(ttmp)

    if stop_after == "A":
        return finish_dummy("dummyA")

    tap_dump("KT", KT, 128, lambda c: [f"KT{c}"] + [(f"KT{c}", j) for j in range(NS)], width=TEXT)
    tap_dump("QT", QT, 128, lambda c: [f"QT{c}"] + [(f"QT{c}", t) for t in range(NT)])
    if "MALL" in tap_outs:
        matmp = tm.sbuf("matmp", [NH, TC], F32)
        p.op("act", lambda e: e.activation(matmp[:], m_all[:], AF.Copy),
             reads=["m_all"] + [("m_all", t) for t in range(NT)],
             writes=["matmp"])
        sync_dma(tap_outs["MALL"], matmp[:], ["matmp"], [("OUT", "matap")],
                 p.slot("tap_MALL"))
        tm.free(matmp)

    Mm = tm.sbuf("Mm", [128, 1152], F32)
    sync_dma(Mm[:], MM, [], ["Mm"], p.slot("mm"))
    poi = tm.sbuf("poi", [128, NS], F32)
    sync_dma(poi[:], POI, [], ["poi"], p.slot("poi"))
    Hb = [tm.sbuf(f"Hb{i}", [128, 1152], BF16) for i in range(2)]
    Fring = [tm.sbuf(f"Fr{i}", [128, TC], BF16) for i in range(10)]
    rsb = tm.sbuf("rsb", [32, TC], F32)
    rsb2 = tm.sbuf("rsb2", [32, TC], F32)
    
    rrecb = tm.sbuf("rrecb", [1, TC], BF16)
    rbsb = tm.sbuf("rbsb", [64, TC], BF16)
    AOsb = [tm.sbuf(f"AO{i}", [128, TC], BF16) for i in range(ND)]

    p.op("dve", lambda e: e.memset(rsb[:], 0.0), writes=[("rsb", None)])
    p.op("dve", lambda e: e.memset(rsb2[:], 0.0), writes=["rsb2"])
    Sps = [tm.psum(f"Sps{i}", [128, 1024]) for i in range(2)]
    AOps = tm.psum("AOps", [65, 1024])
    rbps = tm.psum("rbps", [64, 1024])

    s_mrow = [p.slot("mrowd0"), p.slot("mrowd1")]
    mall_regions = ["m_all"] + [("m_all", t) for t in range(NT)]

    for sslot in range(NH):
        jt = sslot // 2
        bb = sslot % 2
        h = 8 * (jt // 4) + (jt % 4) + 4 * bb
        hb = sslot % 2
        kv = h // 4
        ktile = KT[kv // 2]
        kbase = 64 * (kv % 2)
        qtile = QT[jt]
        qbase = 64 * bb
        p.op("dve", lambda e, hb=hb, h=h: e.tensor_scalar(
            Hb[hb][:], Mm[:], SLOPES[h], None, ALU.mult),
            reads=["Mm"], writes=[f"Hb{hb}"])
        sync_dma(mrows[hb][:], m_all[sslot:sslot + 1, :], mall_regions,
                 [f"mrow{hb}"], s_mrow[hb])
        for j in range(NS):
            sb = Sps[j % 2]
            t_lo = 128 * max(0, j - 8)
            t_hi = 128 * (min(j, 7) + 1)
            span = t_hi - t_lo
            i_lo = t_lo + 1024 - 128 * j
            for off in range(0, span, 512):
                w = min(512, span - off)
                p.op("pe", lambda e, sb=sb, off=off, w=w, hb=hb, i_lo=i_lo:
                     e.matmul(sb[:, off:off + w], idb[:],
                              Hb[hb][:, i_lo + off:i_lo + off + w],
                              start=True, stop=False),
                     reads=[f"Hb{hb}", "idb"], writes=[sb.name])
                p.op("pe", lambda e, sb=sb, off=off, w=w, j=j, t_lo=t_lo,
                     ktile=ktile, kbase=kbase, qtile=qtile, qbase=qbase:
                     e.matmul(sb[:, off:off + w],
                              ktile[kbase:kbase + 64, 128 * j:128 * (j + 1)],
                              qtile[qbase:qbase + 64,
                                    t_lo + off:t_lo + off + w],
                              start=False, stop=False),
                     reads=[(ktile.name, j), qtile.name], writes=[sb.name])
                p.op("pe", lambda e, sb=sb, off=off, w=w, hb=hb, t_lo=t_lo:
                     e.matmul(sb[:, off:off + w], ones1[:],
                              mrows[hb][0:1, t_lo + off:t_lo + off + w],
                              start=False, stop=True),
                     reads=[f"mrow{hb}", "ones1"], writes=[sb.name])
            fr = Fring[j % 10]
            p.op("act", lambda e, sb=sb, span=span, j=j, fr=fr: e.activation(
                fr[:, 0:span], sb[:, 0:span], AF.Exp,
                bias=poi[:, j:j + 1]),
                reads=[sb.name, "poi"], writes=[fr.name])
            if j >= 8:
                tt = j - 8
                for jj in range(tt, tt + 9):
                    tlo_jj = 128 * max(0, jj - 8)
                    frj = Fring[jj % 10]
                    p.op("pe", lambda e, tt=tt, jj=jj, kv=kv, frj=frj,
                         tlo_jj=tlo_jj:
                         e.matmul(AOps[:, 128 * tt:128 * (tt + 1)],
                                  VA[kv][:, 65 * jj:65 * (jj + 1)],
                                  frj[:, 128 * tt - tlo_jj:
                                      128 * (tt + 1) - tlo_jj],
                                  start=(jj == tt), stop=(jj == tt + 8)),
                         reads=[(f"VA{kv}", jj), frj.name],
                         writes=[("AOps", tt)])
        p.op("dve", lambda e: e.tensor_copy(rsb[0:1, :], AOps[64:65, :]),
             reads=[("AOps", None)] + [("AOps", tt) for tt in range(8)],
             writes=[("rsb", None)])
        p.op("dve", lambda e: e.transpose(rsb2[:], rsb[:]),
             reads=[("rsb", None)], writes=["rsb2"])
        rview = rsb2[:].rearrange("p (b c) -> p b c", c=32)[:, :, 0:1]
        p.op("dve", lambda e: e.reciprocal(rview, rview),
             reads=["rsb2"], writes=["rsb2"])
        p.op("dve", lambda e: e.transpose(rsb[:], rsb2[:]),
             reads=["rsb2"], writes=[("rsb", None)])
        p.op("dve", lambda e: e.tensor_copy(rrecb[:], rsb[0:1, :]),
             reads=[("rsb", None)], writes=["rrecb"])
        for half in range(2):
            p.op("pe", lambda e, half=half: e.matmul(
                rbps[:, 512 * half:512 * (half + 1)], ones1[0:1, 0:64],
                rrecb[0:1, 512 * half:512 * (half + 1)],
                start=True, stop=True),
                reads=["rrecb", "ones1"], writes=["rbps"])
        p.op("dve", lambda e: e.tensor_copy(rbsb[:], rbps[:]),
             reads=["rbps"], writes=["rbsb"])
        p.op("dve", lambda e, jt=jt, bb=bb: e.tensor_tensor(
            AOsb[jt][64 * bb:64 * bb + 64, :], AOps[0:64, :],
            rbsb[:], ALU.mult),
            reads=[("AOps", None)] + [("AOps", tt) for tt in range(8)] +
                  ["rbsb"],
            writes=[(f"AO{jt}", bb)])

    tm.free(*QT, *KT, *VA, Mm, *Hb, *Fring, poi, rsb, rsb2, rrecb, rbsb,
            m_all, mrows[0], mrows[1])
    tm.free(rbps, AOps, Sps[1], Sps[0])

    if stop_after == "B":
        return finish_dummy("dummyB")

    tap_dump("AO", AOsb, 128, lambda c: [f"AO{c}", (f"AO{c}", 0), (f"AO{c}", 1)])

    ptc0 = tm.psum("ptc0", [128, 1024], BF16)
    ptc1 = tm.psum("ptc1", [128, 1024], BF16)
    ptcs = [ptc0, ptc1]
    pc = [tm.psum(f"pc{i}", [128, 1024]) for i in range(2)]

    ACC = [tm.sbuf(f"ACC{i}", [128, D], F32) for i in range(NT)]
    y1T = [tm.sbuf(f"y1T{i}", [128, TC], BF16) for i in range(ND)]
    woq = [tm.sbuf(f"woq{i}", [128, 4 * D], BF16) for i in range(2)]
    wosg = [tm.sbuf(f"wosg{i}", [128, 2048], BF16) for i in range(2)]
    s_wos = [p.slot(f"wos{i}") for i in range(2)]
    xrs = [tm.sbuf(f"xrs{i}", [128, D], F32) for i in range(2)]
    s_xrs = [p.slot(f"xrs{i}") for i in range(2)]
    ybf = tm.sbuf("ybf", [128, D], BF16)

    WOr = WO.rearrange("(a p) o -> p a o", p=128)
    for oq in range(4):
        qb = oq % 2
        for g in range(4):
            b2 = g % 2
            cast_dma(
                wosg[b2][:].rearrange("p (a o) -> p a o", o=512)[:, 0:4, :],
                WOr[:, 4 * g:4 * (g + 1), 512 * oq:512 * (oq + 1)],
                [], [f"wosg{b2}"], s_wos[b2])
            for ai in range(4):
                a = 4 * g + ai
                for c in range(4):
                    pb = ptcs[c % 2]
                    p.op("pe", lambda e, b2=b2, ai=ai, a=a, c=c, pb=pb:
                         e.transpose(
                             pb[:, 512 * (a % 2) + 128 * (c // 2):
                                512 * (a % 2) + 128 * (c // 2) + 128],
                             wosg[b2][:, 512 * ai + 128 * c:
                                     512 * ai + 128 * (c + 1)],
                             idb[:]),
                         reads=[f"wosg{b2}", "idb"],
                         writes=[(pb.name, (a % 2, c // 2))])
                    p.op("dve", lambda e, qb=qb, a=a, c=c, pb=pb:
                         e.tensor_copy(
                             woq[qb][:, D * c + 128 * a:D * c + 128 * (a + 1)],
                             pb[:, 512 * (a % 2) + 128 * (c // 2):
                                512 * (a % 2) + 128 * (c // 2) + 128]),
                         reads=[(pb.name, (a % 2, c // 2))],
                         writes=[(f"woq{qb}", (a, c))])
        for t in range(NT):
            tb = t % 2
            if oq == 0:
                sync_dma(xrs[tb][:], XE[TC + 128 * t:TC + 128 * (t + 1), :],
                         [], [f"xrs{tb}"], s_xrs[tb])
            for dhalf in range(2):
                for c in range(4):
                    oc = 4 * oq + c
                    for dd in range(2):
                        p.op("pe", lambda e, qb=qb, c=c, t=t, oc=oc,
                             dhalf=dhalf, dd=dd: e.matmul(
                                 pc[t % 2][:, 512 * dd:512 * (dd + 1)],
                                 AOsb[oc][:, 128 * t:128 * (t + 1)],
                                 woq[qb][:, D * c + 1024 * dhalf + 512 * dd:
                                         D * c + 1024 * dhalf + 512 * (dd + 1)],
                                 start=(c == 0), stop=(c == 3)),
                             reads=[f"AO{oc}", (f"AO{oc}", 0), (f"AO{oc}", 1),
                                    f"woq{qb}"],
                             writes=[pc[t % 2].name])
                hk = "lo" if dhalf == 0 else "hi"
                if oq == 0:
                    p.op("dve", lambda e, t=t, tb=tb, dhalf=dhalf: e.tensor_tensor(
                        ACC[t][:, 1024 * dhalf:1024 * (dhalf + 1)],
                        xrs[tb][:, 1024 * dhalf:1024 * (dhalf + 1)],
                        pc[t % 2][:], ALU.add),
                        reads=[f"xrs{tb}", pc[t % 2].name],
                        writes=[(f"ACC{t}", hk)])
                else:
                    p.op("dve", lambda e, t=t, dhalf=dhalf: e.tensor_tensor(
                        ACC[t][:, 1024 * dhalf:1024 * (dhalf + 1)],
                        ACC[t][:, 1024 * dhalf:1024 * (dhalf + 1)],
                        pc[t % 2][:], ALU.add),
                        reads=[pc[t % 2].name, (f"ACC{t}", hk)],
                        writes=[(f"ACC{t}", hk)])

    def layernorm_tile(t):
        acc_h = ACC[t]
        tname = f"ACC{t}"
        for c in range(4):
            p.op("dve", lambda e, c=c: e.bn_stats(
                bns[:, 6 * c:6 * (c + 1)], acc_h[:, 512 * c:512 * (c + 1)]),
                reads=[(tname, "lo"), (tname, "hi")], writes=[("bns", c)])
        p.op("dve", lambda e: e.bn_aggr(aggr[:], bns[:]),
             reads=[("bns", 0), ("bns", 1), ("bns", 2), ("bns", 3)],
             writes=["aggr"])
        p.op("act", lambda e: e.activation(small[:, 24:25], aggr[:, 1:2],
                                           AF.Sqrt, bias=epsc[:, 1:2]),
             reads=["aggr", ("epsc", 1)], writes=[("small", 24)])
        p.op("dve", lambda e: e.reciprocal(small[:, 25:26], small[:, 24:25]),
             reads=[("small", 24)], writes=[("small", 25)])
        p.op("dve", lambda e: e.tensor_scalar(
            acc_h[:], acc_h[:], aggr[:, 0:1], small[:, 25:26],
            ALU.subtract, ALU.mult),
            reads=[(tname, "lo"), (tname, "hi"), "aggr", ("small", 25)],
            writes=[(tname, "lo"), (tname, "hi")])

    for t in range(NT):
        layernorm_tile(t)
        p.op("act", lambda e, t=t: e.activation(ybf[:], ACC[t][:], AF.Copy),
             reads=[(f"ACC{t}", "lo"), (f"ACC{t}", "hi")], writes=["ybf"])
        for dc in range(ND):
            pb = ptcs[dc // 8]
            p.op("pe", lambda e, dc=dc, pb=pb: e.transpose(
                pb[:, 128 * (dc % 8):128 * (dc % 8 + 1)],
                ybf[:, 128 * dc:128 * (dc + 1)], idb[:]),
                reads=["ybf", "idb"], writes=[(pb.name, dc % 8)])
        for dc in range(ND):
            pb = ptcs[dc // 8]
            p.op("dve", lambda e, t=t, dc=dc, pb=pb: e.tensor_copy(
                y1T[dc][:, 128 * t:128 * (t + 1)],
                pb[:, 128 * (dc % 8):128 * (dc % 8 + 1)]),
                reads=[(pb.name, dc % 8)], writes=[(f"y1T{dc}", t)])

    tm.free(*AOsb, *woq, *wosg, *xrs, ybf)

    if "Y1" in tap_outs:
        for t in range(NT):
            sync_dma(tap_outs["Y1"][128 * t:128 * (t + 1), :], ACC[t][:],
                     [(f"ACC{t}", "lo"), (f"ACC{t}", "hi")],
                     [("OUT", f"y1tap{t}")], p.slot("tap_Y1"), gid="y1g")

    if stop_after == "C":
        return finish_acc(ACC)

    gwbf = tm.sbuf("gwbf", [NE, D], BF16)
    gwT = tm.sbuf("gwT", [128, 128], BF16)
    cast_dma(gwbf[:], GW, [], ["gwbf"], p.slot("gw"))
    pL = tm.psum("pL", [128, 512])
    for dc in range(ND):
        p.op("pe", lambda e, dc=dc: e.transpose(
            ptc0[:, 8 * dc:8 * (dc + 1)],
            gwbf[:, 128 * dc:128 * (dc + 1)], idb[0:8, 0:8]),
            reads=["gwbf", "idb"], writes=[("ptc0", ("g", dc))])
        p.op("dve", lambda e, dc=dc: e.tensor_copy(
            gwT[:, 8 * dc:8 * (dc + 1)], ptc0[:, 8 * dc:8 * (dc + 1)]),
            reads=[("ptc0", ("g", dc))], writes=[("gwT", dc)])

    for t in range(NT):
        for dc in range(ND):
            p.op("pe", lambda e, t=t, dc=dc: e.matmul(
                pL[:, 0:8], y1T[dc][:, 128 * t:128 * (t + 1)],
                gwT[:, 8 * dc:8 * (dc + 1)],
                start=(dc == 0), stop=(dc == ND - 1)),
                reads=[(f"y1T{dc}", t), ("gwT", dc)], writes=["pL"])
        g0 = 40
        gsl = gates8[:, 8 * t:8 * (t + 1)]
        p.op("dve", lambda e: e.reduce_max(small[:, g0:g0 + 1], pL[:, 0:8],
                                           AX.X, negate=True),
             reads=["pL"], writes=[("small", g0)])
        p.op("act", lambda e, gsl=gsl: e.activation(
            gsl, pL[:, 0:8], AF.Exp, bias=small[:, g0:g0 + 1]),
            reads=["pL", ("small", g0)], writes=[("gates8", t)])
        p.op("dve", lambda e, gsl=gsl: e.reduce_sum(
            small[:, g0 + 1:g0 + 2], gsl, AX.X),
            reads=[("gates8", t)], writes=[("small", g0 + 1)])
        p.op("dve", lambda e: e.reciprocal(small[:, g0 + 2:g0 + 3],
                                           small[:, g0 + 1:g0 + 2]),
             reads=[("small", g0 + 1)], writes=[("small", g0 + 2)])
        p.op("dve", lambda e, gsl=gsl: e.tensor_scalar(
            gsl, gsl, small[:, g0 + 2:g0 + 3], None, ALU.mult),
            reads=[("gates8", t), ("small", g0 + 2)], writes=[("gates8", t)])
        p.op("dve", lambda e, gsl=gsl: e.reduce_max(
            small[:, g0 + 3:g0 + 4], gsl, AX.X),
            reads=[("gates8", t)], writes=[("small", g0 + 3)])
        p.op("dve", lambda e, gsl=gsl: e.tensor_scalar(
            small[:, 48:56], gsl, small[:, g0 + 3:g0 + 4], 1e30,
            ALU.is_ge, ALU.mult),
            reads=[("gates8", t), ("small", g0 + 3)], writes=[("small", 48)])
        p.op("dve", lambda e, gsl=gsl: e.tensor_tensor(
            small[:, 48:56], gsl, small[:, 48:56], ALU.subtract),
            reads=[("gates8", t), ("small", 48)], writes=[("small", 48)])
        p.op("dve", lambda e: e.reduce_max(small[:, g0 + 4:g0 + 5],
                                           small[:, 48:56], AX.X),
             reads=[("small", 48)], writes=[("small", g0 + 4)])
        p.op("dve", lambda e: e.tensor_tensor(small[:, g0 + 5:g0 + 6],
                                              small[:, g0 + 3:g0 + 4],
                                              small[:, g0 + 4:g0 + 5],
                                              ALU.add),
             reads=[("small", g0 + 3), ("small", g0 + 4)],
             writes=[("small", g0 + 5)])
        p.op("dve", lambda e: e.reciprocal(small[:, g0 + 6:g0 + 7],
                                           small[:, g0 + 5:g0 + 6]),
             reads=[("small", g0 + 5)], writes=[("small", g0 + 6)])
        p.op("dve", lambda e, gsl=gsl: e.tensor_scalar(
            small[:, 48:56], gsl, small[:, g0 + 4:g0 + 5], None, ALU.is_ge),
            reads=[("gates8", t), ("small", g0 + 4)], writes=[("small", 48)])
        p.op("dve", lambda e, gsl=gsl: e.tensor_tensor(
            gsl, gsl, small[:, 48:56], ALU.mult),
            reads=[("gates8", t), ("small", 48)], writes=[("gates8", t)])
        p.op("dve", lambda e, gsl=gsl: e.tensor_scalar(
            gsl, gsl, small[:, g0 + 6:g0 + 7], None, ALU.mult),
            reads=[("gates8", t), ("small", g0 + 6)], writes=[("gates8", t)])
        p.op("pe", lambda e, gsl=gsl: e.transpose(
            pL[0:8, 128:256], gsl, idf[:]),
            reads=[("gates8", t), "idf"], writes=["pL"])
        p.op("dve", lambda e, t=t: e.tensor_copy(
            grow8[:, 128 * t:128 * (t + 1)], pL[0:8, 128:256]),
            reads=["pL"], writes=[("grow8", t)])

    if "GATES" in tap_outs:
        gtmp = tm.sbuf("gtmp", [128, NT * NE], F32)
        p.op("dve", lambda e: e.tensor_copy(gtmp[:], gates8[:]),
             reads=["gates8"] + [("gates8", t) for t in range(NT)],
             writes=["gtmp"])
        sync_dma(tap_outs["GATES"].rearrange("(t p) e -> p t e", p=128),
                 gtmp[:].rearrange("p (t e) -> p t e", e=NE),
                 ["gtmp"], [("OUT", "gtap")], p.slot("tap_GATES"),
                 allow_slow_non_contiguous=True)
        tm.free(gtmp)

    tm.free(gwbf, gwT)
    tm.free(pL, pc[1], pc[0], ptc1, ptc0)

    if stop_after == "D":
        return finish_acc(ACC)

    FGRP = [(0, 256), (256, 256), (512, 256), (768, 256), (1024, 256),
            (1280, 128)]
    wgb = [tm.sbuf(f"wgb{i}", [128, ND * 256], BF16) for i in range(2)]
    wub = [tm.sbuf(f"wub{i}", [128, ND * 256], BF16) for i in range(2)]
    s_wgb = [p.slot(f"wgb{i}") for i in range(2)]
    s_wub = [p.slot(f"wub{i}") for i in range(2)]
    wdb = [tm.sbuf(f"wdb{i}", [128, 11 * 512], BF16) for i in range(2)]
    s_wdb = [p.slot(f"wdb{i}") for i in range(2)]
    hT = [tm.sbuf(f"hT{i}", [128, TC], BF16) for i in range(11)]
    gbsb = tm.sbuf("gbsb", [128, TC], F32)
    gsb = tm.sbuf("gsb", [128, TC], BF16)
    usb = tm.sbuf("usb", [128, TC], BF16)
    pG = tm.psum("pG", [128, 1024])
    pU = tm.psum("pU", [128, 1024])
    pd = [tm.psum(f"pd{i}", [128, 512]) for i in range(2)]

    WGr = [WGATE[e].rearrange("(a p) f -> p a f", p=128) for e in range(NE)]
    WUr = [WUP[e].rearrange("(a p) f -> p a f", p=128) for e in range(NE)]
    WDr = [WDOWN[e].rearrange("(c p) d -> p c d", p=128) for e in range(NE)]
    WGr.append(SWG.rearrange("(a p) f -> p a f", p=128))
    WUr.append(SWU.rearrange("(a p) f -> p a f", p=128))
    WDr.append(SWD.rearrange("(c p) d -> p c d", p=128))
    grow_regions = ["grow8"] + [("grow8", t) for t in range(NT)]

    gi = 0
    di = 0
    for e_i in range(NE + 1):
        routed = e_i < NE
        for g, (f0, fw) in enumerate(FGRP):
            b = gi % 2
            gi += 1
            for blk in range(0, ND, 4):
                cast_dma(
                    wgb[b][:].rearrange("p (a f) -> p a f", f=256)
                    [:, blk:blk + 4, 0:fw],
                    WGr[e_i][:, blk:blk + 4, f0:f0 + fw],
                    [], [(f"wgb{b}", blk)], s_wgb[b], gid=("g", e_i, g))
                cast_dma(
                    wub[b][:].rearrange("p (a f) -> p a f", f=256)
                    [:, blk:blk + 4, 0:fw],
                    WUr[e_i][:, blk:blk + 4, f0:f0 + fw],
                    [], [(f"wub{b}", blk)], s_wub[b], gid=("g", e_i, g))
            for cc in range(fw // 128):
                fc = f0 // 128 + cc
                for th in range(2):
                    for a in range(ND):
                        p.op("pe", lambda e, b=b, a=a, cc=cc, th=th: e.matmul(
                            pG[:, 512 * th:512 * (th + 1)],
                            wgb[b][:, 256 * a + 128 * cc:
                                   256 * a + 128 * (cc + 1)],
                            y1T[a][:, 512 * th:512 * (th + 1)],
                            start=(a == 0), stop=(a == ND - 1)),
                            reads=[f"wgb{b}", f"y1T{a}"] +
                                  [(f"y1T{a}", t) for t in range(NT)],
                            writes=["pG"])
                    for a in range(ND):
                        p.op("pe", lambda e, b=b, a=a, cc=cc, th=th: e.matmul(
                            pU[:, 512 * th:512 * (th + 1)],
                            wub[b][:, 256 * a + 128 * cc:
                                   256 * a + 128 * (cc + 1)],
                            y1T[a][:, 512 * th:512 * (th + 1)],
                            start=(a == 0), stop=(a == ND - 1)),
                            reads=[f"wub{b}", f"y1T{a}"] +
                                  [(f"y1T{a}", t) for t in range(NT)],
                            writes=["pU"])
                p.op("act", lambda e: e.activation(gsb[:], pG[:], AF.Silu),
                     reads=["pG"], writes=["gsb"])
                p.op("dve", lambda e: e.tensor_copy(usb[:], pU[:]),
                     reads=["pU"], writes=["usb"])
                p.op("pool", lambda e, fc=fc: e.tensor_tensor(
                    hT[fc][:], gsb[:], usb[:], ALU.mult),
                    reads=["gsb", "usb"], writes=[f"hT{fc}"])
        for q in range(4):
            b = di % 2
            di += 1
            for blk, cnt in ((0, 4), (4, 4), (8, 3)):
                cast_dma(
                    wdb[b][:].rearrange("p (c d) -> p c d", d=512)
                    [:, blk:blk + cnt, :],
                    WDr[e_i][:, blk:blk + cnt, 512 * q:512 * (q + 1)],
                    [], [(f"wdb{b}", blk)], s_wdb[b], gid=("d", e_i, q))
            for t in range(NT):
                pdb = pd[t % 2]
                hk = "lo" if q < 2 else "hi"
                for fc in range(11):
                    p.op("pe", lambda e, b=b, fc=fc, t=t, pdb=pdb: e.matmul(
                        pdb[:], hT[fc][:, 128 * t:128 * (t + 1)],
                        wdb[b][:, 512 * fc:512 * (fc + 1)],
                        start=(fc == 0), stop=(fc == 10)),
                        reads=[f"hT{fc}", f"wdb{b}"],
                        writes=[pdb.name])
                if routed:
                    p.op("dve", lambda e, t=t, q=q, e_i=e_i, pdb=pdb:
                         e.scalar_tensor_tensor(
                             ACC[t][:, 512 * q:512 * (q + 1)], pdb[:],
                             gates8[:, 8 * t + e_i:8 * t + e_i + 1],
                             ACC[t][:, 512 * q:512 * (q + 1)],
                             ALU.mult, ALU.add),
                         reads=[pdb.name, ("gates8", t), (f"ACC{t}", hk)],
                         writes=[(f"ACC{t}", hk)])
                else:
                    p.op("dve", lambda e, t=t, q=q, pdb=pdb: e.tensor_tensor(
                        ACC[t][:, 512 * q:512 * (q + 1)],
                        ACC[t][:, 512 * q:512 * (q + 1)], pdb[:], ALU.add),
                        reads=[pdb.name, (f"ACC{t}", hk)],
                        writes=[(f"ACC{t}", hk)])

    if "ACC2" in tap_outs:
        for t in range(NT):
            sync_dma(tap_outs["ACC2"][128 * t:128 * (t + 1), :], ACC[t][:],
                     [(f"ACC{t}", "lo"), (f"ACC{t}", "hi")],
                     [("OUT", f"acc2tap{t}")], p.slot("tap_ACC2"),
                     gid="a2g")

    for t in range(NT):
        layernorm_tile(t)
        sync_dma(Y[128 * t:128 * (t + 1), :], ACC[t][:],
                 [(f"ACC{t}", "lo"), (f"ACC{t}", "hi")],
                 [("OUT", t)], s_out)

    p.op("pool", lambda e: e.nop(), reads=[("OUT", None)], tag="final")

    with nc.Block() as block:
        for e in ENGINES:
            sems[e] = st.enter_context(nc.semaphore(f"sem_{e}"))
        for s in p.slots.values():
            s.sem = st.enter_context(nc.semaphore(f"dma_{s.name}"))
        p.emit_all(block, sems)
    st.close()
    nc._dbg_prog = p
    return nc


def host_consts(core):
    M = np.zeros((128, 1152), dtype=np.float32)
    s = np.arange(128)[:, None]
    i = np.arange(1152)[None, :]
    d = i - s
    M[:] = np.where((d >= 0) & (d <= WIN), (s - i).astype(np.float32),
                    np.float32(-1e9)).astype(np.float32)
    poi = np.zeros((128, NS), dtype=np.float32)
    if core == 0:
        poi[:, 0:8] = -1e5
    idf = np.eye(128, dtype=np.float32)
    return M, poi, idf


_nc_cache = {}


def _get_nc(taps=()):
    key = tuple(sorted(taps))
    if key not in _nc_cache:
        _nc_cache[key] = build_nc(key)
    return _nc_cache[key]


def kernel(L0_attn_x, L0_attn_wq, L0_attn_wk, L0_attn_wv, L0_attn_wo,
           L0_attn_q_norm_w, L0_attn_k_norm_w, L0_ffn_gate_w, L0_ffn_w_gate,
           L0_ffn_w_up, L0_ffn_w_down, L0_ffn_sw_gate, L0_ffn_sw_up,
           L0_ffn_sw_down, _taps=(), _collect=None):
    x = np.ascontiguousarray(np.asarray(L0_attn_x, dtype=np.float32)
                             .reshape(NCORE * TC, D))
    asf = lambda a: np.ascontiguousarray(np.asarray(a, dtype=np.float32))
    wq, wk, wv, wo = (asf(L0_attn_wq), asf(L0_attn_wk), asf(L0_attn_wv),
                      asf(L0_attn_wo))
    operm = np.concatenate(
        [np.arange(64 * h, 64 * h + 64)
         for j in range(16) for h in (8 * (j // 4) + (j % 4),
                                      8 * (j // 4) + (j % 4) + 4)])
    wq = np.ascontiguousarray(wq[operm, :])
    wo = np.ascontiguousarray(wo[:, operm])
    qnw = asf(L0_attn_q_norm_w).reshape(1, D)
    qnw = np.ascontiguousarray(qnw[:, operm])
    knw = asf(L0_attn_k_norm_w).reshape(1, 512)
    gw = asf(L0_ffn_gate_w)
    wgate, wup, wdown = asf(L0_ffn_w_gate), asf(L0_ffn_w_up), asf(L0_ffn_w_down)
    swg, swu, swd = asf(L0_ffn_sw_gate), asf(L0_ffn_sw_up), asf(L0_ffn_sw_down)

    nc = _get_nc(tuple(_taps))
    in_maps = []
    for c in range(NCORE):
        if c == 0:
            prev = np.zeros((TC, D), dtype=np.float32)
        else:
            prev = x[TC * (c - 1):TC * c]
        xe = np.ascontiguousarray(
            np.concatenate([prev, x[TC * c:TC * (c + 1)]], axis=0))
        M, poi, idf = host_consts(c)
        in_maps.append(dict(
            xe=xe, wq=wq, wk=wk, wv=wv, wo=wo, qnw=qnw, knw=knw, gw=gw,
            wgate=wgate, wup=wup, wdown=wdown, swg=swg, swu=swu, swd=swd,
            mmask=M, poison=poi, identf=idf))
    res = run_bass_kernel_spmd(nc, in_maps, core_ids=list(range(NCORE)))
    if _collect is not None:
        _collect.extend(res.results)
    y = np.concatenate([res.results[c]["y"] for c in range(NCORE)], axis=0)
    return y.reshape(1, NCORE * TC, D)


# version 05
# speedup vs baseline: 1.1827x; 1.1827x over previous
import numpy as np
from contextlib import ExitStack

import concourse.bass as bass
import concourse.mybir as mybir
from concourse.bass_utils import run_bass_kernel_spmd

F32 = mybir.dt.float32
BF16 = mybir.dt.bfloat16
AF = mybir.ActivationFunctionType
ALU = mybir.AluOpType
AX = mybir.AxisListType

NCORE = 8
TC = 1024
TEXT = 2048
D = 2048
NH = 32
NKV = 8
DH = 64
WIN = 1024
DFF = 1408
NE = 8
NT = TC // 128
NS = TEXT // 128
ND = D // 128

SLOPES = [float(2.0 ** (-0.25 * h)) for h in range(NH)]

ENGINES = ("pe", "act", "dve", "pool", "sync")


class DmaSlot:
    __slots__ = ("name", "sem", "count", "groups")

    def __init__(self, name):
        self.name = name
        self.sem = None
        self.count = 0
        self.groups = {}


class Op:
    __slots__ = ("idx", "engine", "emit", "reads", "writes", "deps", "slot",
                 "slot_count", "slot_group", "need_inc", "milestone", "tag")

    def __init__(self, idx, engine, emit, reads, writes, slot, tag):
        self.idx = idx
        self.engine = engine
        self.emit = emit
        self.reads = reads
        self.writes = writes
        self.deps = []
        self.slot = slot
        self.slot_count = None
        self.slot_group = None
        self.need_inc = False
        self.milestone = None
        self.tag = tag


class Prog:
    def __init__(self, nc, self_sync=True):
        self.nc = nc
        self.ops = []
        self.track = {}
        self.slots = {}
        self.self_sync = self_sync
        self.pending_free = []
        self.newdeps = {}
        self.psum_names = set()
        self.n_waits = 0
        self.n_incs = 0

    def slot(self, name):
        s = self.slots.get(name)
        if s is None:
            s = self.slots[name] = DmaSlot(name)
        return s

    def new_tensor(self, name):
        if self.pending_free:
            self.newdeps[name] = self._prune(self.pending_free)

    def note_free(self, name):
        t = self.track.get(name)
        if t is None:
            return
        ops = list(t["w"].values())
        for lst in t["r"].values():
            ops.extend(lst)
        self.pending_free.extend(ops)
        self.pending_free = self._prune(self.pending_free)

    @staticmethod
    def _prune(deps):
        best = {}
        for d in deps:
            key = ("s", d.slot.name) if d.slot is not None else ("e", d.engine)
            cur = best.get(key)
            if cur is None or d.idx > cur.idx:
                best[key] = d
        return list(best.values())

    @staticmethod
    def _regions(lst):
        out = []
        for r in lst or ():
            if isinstance(r, tuple):
                out.append(r)
            else:
                out.append((r, None))
        return out

    def _dep_read(self, name, key):
        t = self.track.get(name)
        if t is None:
            return []
        w = t["w"]
        if key is None:
            return list(w.values())
        deps = []
        if key in w:
            deps.append(w[key])
        if None in w:
            deps.append(w[None])
        return deps

    def _dep_write(self, name, key):
        deps = self._dep_read(name, key)
        t = self.track.get(name)
        if t is not None:
            r = t["r"]
            if key is None:
                for lst in r.values():
                    deps.extend(lst)
            else:
                deps.extend(r.get(key, ()))
                deps.extend(r.get(None, ()))
        return deps

    def op(self, engine, emit, reads=(), writes=(), slot=None, tag="",
           gid=None):
        wr = self._regions(writes)
        rd = self._regions(reads)
        if engine == "pe":
            wr = [(n, None) if n in self.psum_names else (n, k)
                  for (n, k) in wr]
        else:
            promote = [(n, None) for (n, k) in rd if n in self.psum_names]
            if promote:
                rd = [(n, k) for (n, k) in rd if n not in self.psum_names]
                wr = wr + promote
        o = Op(len(self.ops), engine, emit, rd, wr, slot, tag)
        deps = []
        for (n, k) in o.reads:
            deps.extend(self._dep_read(n, k))
            if n in self.newdeps:
                deps.extend(self.newdeps.pop(n))
        for (n, k) in o.writes:
            deps.extend(self._dep_write(n, k))
            if n in self.newdeps:
                deps.extend(self.newdeps.pop(n))
        pruned = {}
        for d in deps:
            if d is o:
                continue
            if d.engine == o.engine and d.slot is None:
                if o.engine == "pe" or not self.self_sync:
                    continue
            key = ("s", d.slot.name) if d.slot is not None else ("e", d.engine)
            cur = pruned.get(key)
            if cur is None or d.idx > cur.idx:
                pruned[key] = d
        o.deps = list(pruned.values())
        if slot is not None:
            slot.count += 1
            o.slot_count = slot.count
            if gid is None:
                gid = ("u", slot.count)
            o.slot_group = gid
            g = slot.groups.get(gid, 0)
            slot.groups[gid] = max(g, slot.count)
        for (n, k) in o.reads:
            self.track.setdefault(n, {"w": {}, "r": {}})["r"].setdefault(
                k, []).append(o)
        for (n, k) in o.writes:
            t = self.track.setdefault(n, {"w": {}, "r": {}})
            if k is None:
                t["w"] = {None: o}
                t["r"] = {}
            else:
                t["w"][k] = o
                t["r"].pop(k, None)
        self.ops.append(o)
        return o

    def dma(self, queue, emit, reads=(), writes=(), slot=None, tag="",
            gid=None):
        assert slot is not None
        return self.op(queue, emit, reads=reads, writes=writes, slot=slot,
                       tag=tag, gid=gid)

    def emit_all(self, block, engine_sems):
        for o in self.ops:
            for d in o.deps:
                if d.slot is None:
                    d.need_inc = True
        counters = {e: 0 for e in ENGINES}
        for o in self.ops:
            if o.need_inc and o.slot is None:
                counters[o.engine] += 1
                o.milestone = counters[o.engine]
        per_engine = {e: [] for e in ENGINES}
        for o in self.ops:
            per_engine[o.engine].append(o)
        prog = self

        def runner(eng_name):
            def body(eng):
                waited = {}
                for o in per_engine[eng_name]:
                    for d in o.deps:
                        if d.slot is not None:
                            sem = d.slot.sem
                            val = 16 * d.slot.groups[d.slot_group]
                        else:
                            sem = engine_sems[d.engine]
                            val = d.milestone
                        key = sem.name
                        if waited.get(key, -1) >= val:
                            continue
                        eng.wait_ge(sem, val)
                        waited[key] = val
                        prog.n_waits += 1
                    inst = o.emit(eng)
                    if o.slot is not None:
                        inst.then_inc(o.slot.sem, 16)
                        prog.n_incs += 1
                    elif o.need_inc:
                        inst.then_inc(engine_sems[o.engine], 1)
                        prog.n_incs += 1
            return body

        section = {"pe": block.tensor, "act": block.scalar,
                   "dve": block.vector, "pool": block.gpsimd,
                   "sync": block.sync}
        for e in ENGINES:
            if per_engine[e]:
                section[e](runner(e))


ARENA_BYTES = 211968


class TMgr:

    def __init__(self, nc, prog, off):
        self.nc = nc
        self.p = prog
        self.off = off
        self.cms = {}
        self.logical = {}
        self.acm = nc.sbuf_tensor("arena", [128, ARENA_BYTES], mybir.dt.uint8)
        self.ah = self.acm.__enter__()
        self.abase = nc.lookup_mloc(self.ah).addr
        self.spans = {}

    def sbuf(self, name, shape, dtype):
        import functools, operator
        off = self.off[name]
        per_part = functools.reduce(operator.mul, shape[1:], 1) *             mybir.dt.size(dtype)
        assert off + per_part <= ARENA_BYTES, (name, off, per_part)
        assert off % 64 == 0, (name, off)
        h = self.nc.alloc_sbuf_tensor_at(name, shape, dtype,
                                         offset=self.abase + off)
        self.spans[name] = (off, off + per_part)
        self.logical[h.name] = name
        self.p.new_tensor(name)
        return h

    def psum(self, name, shape, dtype=F32):
        cm = self.nc.psum_tensor(name, shape, dtype)
        h = cm.__enter__()
        self.cms[name] = cm
        self.logical[h.name] = name
        self.p.psum_names.add(name)
        self.p.new_tensor(name)
        return h

    def free(self, *handles):
        for h in handles:
            name = self.logical.get(h.name, h.name)
            self.p.note_free(name)
            cm = self.cms.pop(name, None)
            if cm is not None:
                cm.__exit__(None, None, None)


def build_nc(taps=(), stop_after=None):
    nc = bass.Bass("TRN2", target_bir_lowering=False)
    taps = set(taps)

    din = lambda n, s: nc.dram_tensor(n, s, F32, kind="ExternalInput").ap()
    XE = din("xe", [TEXT, D])
    WQ = din("wq", [D, D])
    WK = din("wk", [512, D])
    WV = din("wv", [512, D])
    WO = din("wo", [D, D])
    QNW = din("qnw", [1, D])
    KNW = din("knw", [1, 512])
    GW = din("gw", [NE, D])
    WGATE = din("wgate", [NE, D, DFF])
    WUP = din("wup", [NE, D, DFF])
    WDOWN = din("wdown", [NE, DFF, D])
    SWG = din("swg", [D, DFF])
    SWU = din("swu", [D, DFF])
    SWD = din("swd", [DFF, D])
    MM = din("mmask", [128, 1152])
    POI = din("poison", [128, NS])
    IDF = din("identf", [128, 128])
    Y = nc.dram_tensor("y", [TC, D], F32, kind="ExternalOutput").ap()
    tap_outs = {}
    tap_shapes = {"KT": [512, TEXT], "QT": [D, TC], "MALL": [NH, TC],
                  "AO": [D, TC], "Y1": [TC, D], "GATES": [TC, NE],
                  "ACC2": [TC, D]}
    for t in taps:
        tap_outs[t] = nc.dram_tensor("tap_" + t, tap_shapes[t], F32,
                                     kind="ExternalOutput").ap()

    RA, RB, RC, RD, RE = 10240, 79872, 113664, 146432, 179200
    OFF = {
        "idb": 0, "idf": 256, "ones1": 768, "onesf": 1024, "small": 1536,
        "bns": 1792, "aggr": 1920, "knacc": 1984, "knbneg": 2048,
        "knscr": 2176, "epsc": 9920, "gates8": 3520, "grow8": 3776, "garow0": 5824,
        "garow1": 7872,
        "wkT": RA, "wvT": RA + 16384, "wstg0": RA + 32768, "wstg1": RA + 36864,
        **{f"wqT{i}": RA + 4096 * i for i in range(16)},
        **{f"Fr{i}": RA + 2048 * i for i in range(10)},
        "woq0": RA, "woq1": RA + 16384, "wosg0": RA + 32768,
        "wosg1": RA + 36864, "xrs0": RA + 40960, "xrs1": RA + 49152,
        "ybf": RA + 57344, "gwbf": RA + 61440, "gwT": RA + 57344,
        **{f"wgb{i}": RA + 8192 * i for i in range(3)},
        **{f"wub{i}": RA + 24576 + 8192 * i for i in range(3)},
        **{f"wdb{i}": RA + 49152 + 5696 * i for i in range(3)},
        **{f"KT{i}": RB + 4096 * i for i in range(4)},
        **{f"VA{i}": RB + 16384 + 2112 * i for i in range(8)},
        **{f"ACC{i}": (RB + 8192 * i if i < 4 else RC + 8192 * (i - 4))
           for i in range(8)},
        "m_all": RC, "mrow0": RC + 2048, "mrow1": RC + 4096,
        "rrecb": RC + 6144,
        "Qsb": RC + 8192, "sqs": RC + 16384, "wstgq0": RC + 24576,
        "wstgq1": RC + 28672,
        "Mm": RC + 8192, "poi": RC + 12800, "Hb0": RC + 12864,
        "Hb1": RC + 15168, "Fb0": RC + 17472, "Fb1": RC + 19520,
        "rsb": RC + 21568, "rsb2": RC + 25664, "rbsb": RC + 29760,
        "xstg0": RD, "xstg1": RD + 4096, "xTc0": RD + 8192,
        "xTc1": RD + 12288, "ksqs": RD + 16384, "row1": RD + 18432,
        "kwb": RD + 22528, "qwb": RD + 24576,
        **{f"AO{i}": RD + 2048 * i for i in range(16)},
        **{f"hT{i}": RD + 2048 * i for i in range(11)},
        "gbsb": RD + 22528, "gsb": RD + 26624, "usb": RD + 28672,
        "ttmp_KT": RA + 32768, "ttmp_QT": RA + 32768, "matmp": RA + 40960,
        "dummyA": RE + 24576, "dummyB": RA + 32768,
        "gtmp": RA + 32768, "ttmp_AO": RA + 32768,
        **{f"QT{i}": RE + 2048 * i for i in range(16)},
        **{f"y1T{i}": RE + 2048 * i for i in range(16)},
    }
    p = Prog(nc, self_sync=True)
    tm = TMgr(nc, p, OFF)
    st = ExitStack()

    idb = tm.sbuf("idb", [128, 128], BF16)
    idf = tm.sbuf("idf", [128, 128], F32)
    ones1 = tm.sbuf("ones1", [1, 128], BF16)
    onesf = tm.sbuf("onesf", [1, 128], F32)
    epsc = tm.sbuf("epsc", [128, 2], F32)
    knscr = tm.sbuf("knscr", [8, 336], F32)
    m_all = tm.sbuf("m_all", [NH, TC], BF16)
    mrows = [tm.sbuf(f"mrow{i}", [1, TC], BF16) for i in range(2)]
    knacc = tm.sbuf("knacc", [128, NKV], F32)
    knbneg = tm.sbuf("knbneg", [128, NH], F32)
    gates8 = tm.sbuf("gates8", [128, NT * NE], F32)
    grow8 = tm.sbuf("grow8", [NE, TC], BF16)
    garow = [tm.sbuf(f"garow{i}", [1, TC], BF16) for i in range(2)]
    small = tm.sbuf("small", [128, 64], F32)
    bns = tm.sbuf("bns", [128, 24], F32)
    aggr = tm.sbuf("aggr", [128, 2], F32)

    sems = {}
    s_misc = p.slot("misc")
    s_miscp = p.slot("miscp")
    s_out = p.slot("out")

    def sync_dma(dst, src, reads, writes, slot, gid=None, **kw):
        return p.dma("sync", lambda e: e.dma_start(dst, src, **kw),
                     reads=reads, writes=writes, slot=slot, gid=gid)

    def cast_dma(dst, src, reads, writes, slot, gid=None):
        return p.dma("pool", lambda e: e.dma_start(dst, src),
                     reads=reads, writes=writes, slot=slot, gid=gid)

    def finish_dummy(which):
        dmy = tm.sbuf(which, [128, D], F32)
        p.op("dve", lambda e: e.memset(dmy[:], 0.5), writes=[which])
        for t in range(NT):
            sync_dma(Y[128 * t:128 * (t + 1), :], dmy[:], [which],
                     [("OUT", t)], s_out, gid="yg")
        p.op("pool", lambda e: e.nop(), reads=[("OUT", None)], tag="final")
        with nc.Block() as block:
            for e in ENGINES:
                sems[e] = st.enter_context(nc.semaphore(f"sem_{e}"))
            for sl_ in p.slots.values():
                sl_.sem = st.enter_context(nc.semaphore(f"dma_{sl_.name}"))
            p.emit_all(block, sems)
        st.close()
        nc._dbg_prog = p
        return nc

    def finish_acc(ACC):
        for t in range(NT):
            sync_dma(Y[128 * t:128 * (t + 1), :], ACC[t][:],
                     [(f"ACC{t}", "lo"), (f"ACC{t}", "hi")],
                     [("OUT", t)], s_out, gid="yg")
        p.op("pool", lambda e: e.nop(), reads=[("OUT", None)], tag="final")
        with nc.Block() as block:
            for e in ENGINES:
                sems[e] = st.enter_context(nc.semaphore(f"sem_{e}"))
            for sl_ in p.slots.values():
                sl_.sem = st.enter_context(nc.semaphore(f"dma_{sl_.name}"))
            p.emit_all(block, sems)
        st.close()
        nc._dbg_prog = p
        return nc

    sems_holder = sems

    sync_dma(idf[:], IDF, [], ["idf"], s_misc)
    cast_dma(idb[:], IDF, [], ["idb"], s_miscp)
    p.op("dve", lambda e: e.memset(ones1[:], 1.0), writes=["ones1"])
    p.op("dve", lambda e: e.memset(onesf[:], 1.0), writes=["onesf"])
    p.op("dve", lambda e: e.memset(epsc[:, 0:1], 1e-6), writes=[("epsc", 0)])
    p.op("dve", lambda e: e.memset(epsc[:, 1:2], 1e-5), writes=[("epsc", 1)])

    pt0 = tm.psum("pt0", [128, 1024], BF16)
    pt1 = tm.psum("pt1", [128, 1024], BF16)
    pts = [pt0, pt1]
    pK = tm.psum("pK", [128, 512])
    pV = tm.psum("pV", [128, 512])
    pB = tm.psum("pB", [128, 512])
    pQ = tm.psum("pQ", [128, 512])

    wkT = tm.sbuf("wkT", [128, ND * 512], BF16)
    wvT = tm.sbuf("wvT", [128, ND * 512], BF16)
    wstg = [tm.sbuf(f"wstg{i}", [128, D], BF16) for i in range(2)]
    s_wstg = [p.slot(f"wstg{i}") for i in range(2)]
    KT = [tm.sbuf(f"KT{i}", [128, TEXT], BF16) for i in range(4)]
    VA = [tm.sbuf(f"VA{i}", [128, NS * 65], BF16) for i in range(NKV)]
    kwb = tm.sbuf("kwb", [128, 512], F32)
    row1 = tm.sbuf("row1", [1, 2048], BF16)

    def wT_build(W, nrow_tiles, dstT, dname):
        for r in range(nrow_tiles):
            b = r % 2
            cast_dma(wstg[b][:], W[128 * r:128 * (r + 1), :],
                     [], [f"wstg{b}"], s_wstg[b])
            for dc in range(ND):
                pb = pts[dc // 8]
                p.op("pe", lambda e, b=b, dc=dc, pb=pb: e.transpose(
                    pb[:, 128 * (dc % 8):128 * (dc % 8 + 1)],
                    wstg[b][:, 128 * dc:128 * (dc + 1)], idb[:]),
                    reads=[f"wstg{b}", "idb"], writes=[(pb.name, dc % 8)])
            for half in range(2):
                pb = pts[half]
                dst3 = dstT[:].rearrange("p (dc o) -> p dc o", o=512)[
                    :, 8 * half:8 * (half + 1), 128 * r:128 * (r + 1)]
                src3 = pb[:].rearrange("p (dc o) -> p dc o", o=128)
                p.op("dve", lambda e, dst3=dst3, src3=src3: e.tensor_copy(
                    dst3, src3),
                    reads=[pb.name, (pb.name, 0), (pb.name, 1), (pb.name, 2),
                           (pb.name, 3), (pb.name, 4), (pb.name, 5),
                           (pb.name, 6), (pb.name, 7)],
                    writes=[(dname, (r, half))])

    wT_build(WK, 4, wkT, "wkT")
    wT_build(WV, 4, wvT, "wvT")

    cast_dma(row1[0:1, 0:512], KNW, [], [("row1", "k")], p.slot("knw"))
    p.op("pe", lambda e: e.matmul(pB[:], ones1[:], row1[0:1, 0:512],
                                  start=True, stop=True),
         reads=[("row1", "k"), "ones1"], writes=["pB"])
    p.op("dve", lambda e: e.tensor_copy(kwb[:], pB[:]), reads=["pB"],
         writes=["kwb"])

    xstg = [tm.sbuf(f"xstg{i}", [128, D], BF16) for i in range(2)]
    s_xstg = [p.slot(f"xstg{i}") for i in range(2)]
    xTc = [tm.sbuf(f"xTc{i}", [128, D], BF16) for i in range(2)]
    ksqs = tm.sbuf("ksqs", [128, 512], F32)

    p.op("dve", lambda e: e.memset(knacc[:], 0.0), writes=["knacc"])

    def transpose_tile(srcname, src, key=None):
        rkey = srcname if key is None else (srcname, key)
        for dc in range(ND):
            pb = pts[dc // 8]
            p.op("pe", lambda e, dc=dc, pb=pb, src=src: e.transpose(
                pb[:, 128 * (dc % 8):128 * (dc % 8 + 1)],
                src[:, 128 * dc:128 * (dc + 1)], idb[:]),
                reads=[rkey, "idb"], writes=[(pb.name, dc % 8)])

    def evac_xTc(b):
        for half in range(2):
            pb = pts[half]
            p.op("dve", lambda e, b=b, half=half, pb=pb: e.tensor_copy(
                xTc[b][:, 1024 * half:1024 * (half + 1)], pb[:]),
                reads=[pb.name] + [(pb.name, i) for i in range(8)],
                writes=[(f"xTc{b}", half)])

    for j in range(NS):
        b = j % 2
        cast_dma(xstg[b][:], XE[128 * j:128 * (j + 1), :], [], [f"xstg{b}"],
                 s_xstg[b])
        transpose_tile(f"xstg{b}", xstg[b][:])
        evac_xTc(b)
        for dc in range(ND):
            p.op("pe", lambda e, b=b, dc=dc: e.matmul(
                pK[:], xTc[b][:, 128 * dc:128 * (dc + 1)],
                wkT[:, 512 * dc:512 * (dc + 1)],
                start=(dc == 0), stop=(dc == ND - 1)),
                reads=[(f"xTc{b}", dc // 8), "wkT"], writes=["pK"])
        p.op("act", lambda e: e.activation(ksqs[:], pK[:], AF.Square),
             reads=["pK"], writes=["ksqs"])
        p.op("dve", lambda e: e.reduce_sum(small[:, 0:1], ksqs[:], AX.X),
             reads=["ksqs"], writes=[("small", 0)])
        p.op("act", lambda e: e.activation(small[:, 1:2], small[:, 0:1],
                                           AF.Sqrt, bias=epsc[:, 0:1],
                                           scale=1.0 / 512),
             reads=[("small", 0), ("epsc", 0)], writes=[("small", 1)])
        p.op("dve", lambda e: e.reciprocal(small[:, 2:3], small[:, 1:2]),
             reads=[("small", 1)], writes=[("small", 2)])
        p.op("dve", lambda e, b=b: e.scalar_tensor_tensor(
            xstg[b][:, 0:512], pK[:], small[:, 2:3], kwb[:],
            ALU.mult, ALU.mult),
            reads=["pK", ("small", 2), "kwb"], writes=[(f"xstg{b}", "ks")])
        p.op("act", lambda e, b=b: e.activation(ksqs[:], xstg[b][:, 0:512],
                                                AF.Square),
             reads=[(f"xstg{b}", "ks")], writes=["ksqs"])
        p.op("dve", lambda e: e.reduce_sum(
            small[:, 8:16], ksqs[:].rearrange("p (g x) -> p g x", g=8), AX.X),
            reads=["ksqs"], writes=[("small", 8)])
        p.op("dve", lambda e: e.tensor_tensor(knacc[:], knacc[:],
                                              small[:, 8:16], ALU.max),
             reads=[("small", 8), "knacc"], writes=["knacc"])
        for c in range(4):
            p.op("pe", lambda e, b=b, c=c: e.transpose(
                pt0[:, 128 * c:128 * (c + 1)],
                xstg[b][:, 128 * c:128 * (c + 1)], idb[:]),
                reads=[(f"xstg{b}", "ks"), "idb"], writes=[("pt0", c)])
        for c in range(4):
            p.op("dve", lambda e, c=c, j=j: e.tensor_copy(
                KT[c][:, 128 * j:128 * (j + 1)], pt0[:, 128 * c:128 * (c + 1)]),
                reads=[("pt0", c)], writes=[(f"KT{c}", j)])
        for dc in range(ND):
            p.op("pe", lambda e, b=b, dc=dc: e.matmul(
                pV[:], xTc[b][:, 128 * dc:128 * (dc + 1)],
                wvT[:, 512 * dc:512 * (dc + 1)],
                start=(dc == 0), stop=(dc == ND - 1)),
                reads=[(f"xTc{b}", dc // 8), "wvT"], writes=["pV"])
        for kv in range(NKV):
            p.op("dve", lambda e, kv=kv, j=j: e.tensor_copy(
                VA[kv][:, 65 * j:65 * j + 64], pV[:, 64 * kv:64 * (kv + 1)]),
                reads=["pV"], writes=[(f"VA{kv}", j)])
        for kv in range(NKV):
            p.op("dve", lambda e, kv=kv, j=j: e.memset(
                VA[kv][:, 65 * j + 64:65 * j + 65], 1.0),
                writes=[(f"VA{kv}", j)])

    if stop_after == "A1":
        return finish_dummy("dummyA")

    p.op("pe", lambda e: e.transpose(pB[0:8, 0:128], knacc[:], idf[:]),
         reads=["knacc", "idf"], writes=["pB"])
    p.op("dve", lambda e: e.tensor_copy(knscr[0:8, 0:128], pB[0:8, 0:128]),
         reads=["pB"], writes=["knscr"])
    p.op("dve", lambda e: e.reduce_max(small[0:8, 3:4], knscr[0:8, 0:128],
                                       AX.X),
         reads=["knscr"], writes=[("small", 3)])
    p.op("act", lambda e: e.activation(small[0:8, 4:5], small[0:8, 3:4],
                                       AF.Sqrt), reads=[("small", 3)],
         writes=[("small", 4)])
    p.op("dve", lambda e: e.tensor_scalar(small[0:8, 5:6], small[0:8, 4:5],
                                          -1.0, None, ALU.mult),
         reads=[("small", 4)], writes=[("small", 5)])
    p.op("pe", lambda e: e.transpose(pB[0:1, 128:136], small[0:8, 5:6],
                                     idf[0:8, 0:8]),
         reads=[("small", 5), "idf"], writes=["pB"])
    p.op("dve", lambda e: e.tensor_copy(knscr[0:1, 128:136],
                                        pB[0:1, 128:136]),
         reads=["pB"], writes=["knscr"])
    s_kn = p.slot("knr")
    for m_ in range(2):
        for r_ in range(4):
            sync_dma(knscr[0:1, 160 + 2 * r_ + m_:192:8],
                     knscr[0:1, 128 + m_:136:2],
                     ["knscr"], ["knscr"], s_kn,
                     allow_slow_non_contiguous=True)
    p.op("pe", lambda e: e.matmul(pB[:, 0:32], onesf[:], knscr[0:1, 160:192],
                                  start=True, stop=True),
         reads=["knscr", "onesf"], writes=["pB"])
    p.op("dve", lambda e: e.tensor_copy(knbneg[:], pB[:, 0:32]),
         reads=["pB"], writes=["knbneg"])

    tm.free(wstg[1], wstg[0], wkT, wvT)

    if stop_after == "KN":
        return finish_dummy("dummyA")

    wqT = [tm.sbuf(f"wqT{i}", [128, D], BF16) for i in range(ND)]
    wstgq = [tm.sbuf(f"wstgq{i}", [128, D], BF16) for i in range(2)]
    s_wstgq = [p.slot(f"wstgq{i}") for i in range(2)]
    for r in range(ND):
        b = r % 2
        cast_dma(wstgq[b][:], WQ[128 * r:128 * (r + 1), :], [], [f"wstgq{b}"],
                 s_wstgq[b])
        transpose_tile(f"wstgq{b}", wstgq[b][:])
        for dc in range(ND):
            pb = pts[dc // 8]
            p.op("dve", lambda e, r=r, dc=dc, pb=pb: e.tensor_copy(
                wqT[dc][:, 128 * r:128 * (r + 1)],
                pb[:, 128 * (dc % 8):128 * (dc % 8 + 1)]),
                reads=[(pb.name, dc % 8)], writes=[(f"wqT{dc}", r)])

    if stop_after == "WQT":
        return finish_dummy("dummyA")

    qwb = tm.sbuf("qwb", [128, D], F32)
    cast_dma(row1[0:1, 0:2048], QNW, [], [("row1", "q")], p.slot("qnw"))
    for oc in range(4):
        p.op("pe", lambda e, oc=oc: e.matmul(
            pB[:], ones1[:], row1[0:1, 512 * oc:512 * (oc + 1)],
            start=True, stop=True),
            reads=[("row1", "q"), "ones1"], writes=["pB"])
        p.op("act", lambda e, oc=oc: e.activation(
            qwb[:, 512 * oc:512 * (oc + 1)], pB[:], AF.Copy, scale=0.125),
            reads=["pB"], writes=[("qwb", oc)])

    if stop_after == "QWB":
        return finish_dummy("dummyA")

    QT = [tm.sbuf(f"QT{i}", [128, TC], BF16) for i in range(ND)]
    Qsb = tm.sbuf("Qsb", [128, D], F32)
    sqs = tm.sbuf("sqs", [128, D], F32)

    _nt = NT
    _ql = 99
    if stop_after and stop_after.startswith("Q") and len(stop_after) == 2:
        _nt = int(stop_after[1]) + 1
    if stop_after and stop_after.startswith("QL"):
        _nt, _ql = 1, int(stop_after[2:])
    for t in range(_nt):
        b = t % 2
        cast_dma(xstg[b][:], XE[TC + 128 * t:TC + 128 * (t + 1), :], [],
                 [f"xstg{b}"], s_xstg[b])
        transpose_tile(f"xstg{b}", xstg[b][:])
        evac_xTc(b)
        if _ql < 2:
            continue
        for oc in range(4):
            for dc in range(ND):
                p.op("pe", lambda e, b=b, dc=dc, oc=oc: e.matmul(
                    pQ[:], xTc[b][:, 128 * dc:128 * (dc + 1)],
                    wqT[dc][:, 512 * oc:512 * (oc + 1)],
                    start=(dc == 0), stop=(dc == ND - 1)),
                    reads=[(f"xTc{b}", dc // 8), f"wqT{dc}"], writes=["pQ"])
            p.op("dve", lambda e, oc=oc: e.tensor_copy(
                Qsb[:, 512 * oc:512 * (oc + 1)], pQ[:]),
                reads=["pQ"], writes=[("Qsb", oc)])
            if _ql < 3:
                continue
            p.op("act", lambda e, oc=oc: e.activation(
                sqs[:, 512 * oc:512 * (oc + 1)], pQ[:], AF.Square),
                reads=["pQ"], writes=[("sqs", oc)])
        if _ql < 4:
            continue
        p.op("dve", lambda e: e.reduce_sum(
            small[:, 20:21], sqs[:], AX.X),
            reads=["sqs", ("sqs", 0), ("sqs", 1), ("sqs", 2), ("sqs", 3)],
            writes=[("small", 20)])
        p.op("act", lambda e: e.activation(small[:, 21:22], small[:, 20:21],
                                           AF.Sqrt, bias=epsc[:, 0:1],
                                           scale=1.0 / D),
             reads=[("small", 20), ("epsc", 0)], writes=[("small", 21)])
        p.op("dve", lambda e: e.reciprocal(small[:, 22:23], small[:, 21:22]),
             reads=[("small", 21)], writes=[("small", 22)])
        if _ql < 5:
            continue
        p.op("dve", lambda e, b=b: e.scalar_tensor_tensor(
            xstg[b][:], Qsb[:], small[:, 22:23], qwb[:], ALU.mult, ALU.mult),
            reads=[("Qsb", 0), ("Qsb", 1), ("Qsb", 2), ("Qsb", 3),
                   ("small", 22), "qwb"],
            writes=[(f"xstg{b}", "qh")])
        if _ql < 6:
            continue
        p.op("act", lambda e, b=b: e.activation(sqs[:], xstg[b][:], AF.Square),
             reads=[(f"xstg{b}", "qh")],
             writes=["sqs", ("sqs", 0), ("sqs", 1), ("sqs", 2), ("sqs", 3)])
        p.op("dve", lambda e: e.reduce_sum(
            small[:, 32:64], sqs[:].rearrange("p (g x) -> p g x", g=32), AX.X),
            reads=["sqs"], writes=[("small", 32)])
        p.op("act", lambda e: e.activation(small[:, 32:64], small[:, 32:64],
                                           AF.Sqrt),
             reads=[("small", 32)], writes=[("small", 32)])
        if _ql < 7:
            continue
        p.op("dve", lambda e: e.tensor_tensor(small[:, 32:64], small[:, 32:64],
                                              knbneg[:], ALU.mult),
             reads=[("small", 32), "knbneg"], writes=[("small", 32)])
        p.op("pe", lambda e: e.transpose(pB[0:32, 0:128], small[:, 32:64],
                                         idf[:]),
             reads=[("small", 32), "idf"], writes=["pB"])
        p.op("dve", lambda e, t=t: e.tensor_copy(
            m_all[:, 128 * t:128 * (t + 1)], pB[0:32, 0:128]),
            reads=["pB"], writes=[("m_all", t)])
        if _ql < 8:
            continue
        transpose_tile(f"xstg{b}", xstg[b][:], key="qh")
        for dc in range(ND):
            pb = pts[dc // 8]
            p.op("dve", lambda e, t=t, dc=dc, pb=pb: e.tensor_copy(
                QT[dc][:, 128 * t:128 * (t + 1)],
                pb[:, 128 * (dc % 8):128 * (dc % 8 + 1)]),
                reads=[(pb.name, dc % 8)], writes=[(f"QT{dc}", t)])

    if stop_after and stop_after.startswith("Q"):
        return finish_dummy("dummyA")

    tm.free(Qsb, sqs, qwb, kwb, ksqs, row1, *wqT, *xstg, *xTc, *wstgq)
    tm.free(pQ, pB, pV, pK, pt1, pt0)

    def tap_dump(name, tiles, nrow, regions_of, width=TC):
        if name not in tap_outs:
            return
        ttmp = tm.sbuf("ttmp_" + name, [nrow, width], F32)
        s_tap = p.slot("tap_" + name)
        for c, tile in enumerate(tiles):
            p.op("act", lambda e, tile=tile: e.activation(ttmp[:], tile[:],
                                                          AF.Copy),
                 reads=regions_of(c), writes=["ttmp_" + name])
            sync_dma(tap_outs[name][nrow * c:nrow * (c + 1), :], ttmp[:],
                     ["ttmp_" + name], [("OUT", f"{name}{c}")], s_tap)
        tm.free(ttmp)

    if stop_after == "A":
        return finish_dummy("dummyA")

    tap_dump("KT", KT, 128, lambda c: [f"KT{c}"] + [(f"KT{c}", j) for j in range(NS)], width=TEXT)
    tap_dump("QT", QT, 128, lambda c: [f"QT{c}"] + [(f"QT{c}", t) for t in range(NT)])
    if "MALL" in tap_outs:
        matmp = tm.sbuf("matmp", [NH, TC], F32)
        p.op("act", lambda e: e.activation(matmp[:], m_all[:], AF.Copy),
             reads=["m_all"] + [("m_all", t) for t in range(NT)],
             writes=["matmp"])
        sync_dma(tap_outs["MALL"], matmp[:], ["matmp"], [("OUT", "matap")],
                 p.slot("tap_MALL"))
        tm.free(matmp)

    Mm = tm.sbuf("Mm", [128, 1152], F32)
    sync_dma(Mm[:], MM, [], ["Mm"], p.slot("mm"))
    poi = tm.sbuf("poi", [128, NS], F32)
    sync_dma(poi[:], POI, [], ["poi"], p.slot("poi"))
    Hb = [tm.sbuf(f"Hb{i}", [128, 1152], BF16) for i in range(2)]
    Fring = [tm.sbuf(f"Fr{i}", [128, TC], BF16) for i in range(10)]
    rsb = tm.sbuf("rsb", [32, TC], F32)
    rsb2 = tm.sbuf("rsb2", [32, TC], F32)
    
    rrecb = tm.sbuf("rrecb", [1, TC], BF16)
    rbsb = tm.sbuf("rbsb", [64, TC], BF16)
    AOsb = [tm.sbuf(f"AO{i}", [128, TC], BF16) for i in range(ND)]

    p.op("dve", lambda e: e.memset(rsb[:], 0.0), writes=[("rsb", None)])
    p.op("dve", lambda e: e.memset(rsb2[:], 0.0), writes=["rsb2"])
    Sps = [tm.psum(f"Sps{i}", [128, 1024]) for i in range(2)]
    AOps = tm.psum("AOps", [65, 1024])
    rbps = tm.psum("rbps", [64, 1024])

    s_mrow = [p.slot("mrowd0"), p.slot("mrowd1")]
    mall_regions = ["m_all"] + [("m_all", t) for t in range(NT)]

    for sslot in range(NH):
        jt = sslot // 2
        bb = sslot % 2
        h = 8 * (jt // 4) + (jt % 4) + 4 * bb
        hb = sslot % 2
        kv = h // 4
        ktile = KT[kv // 2]
        kbase = 64 * (kv % 2)
        qtile = QT[jt]
        qbase = 64 * bb
        p.op("dve", lambda e, hb=hb, h=h: e.tensor_scalar(
            Hb[hb][:], Mm[:], SLOPES[h], None, ALU.mult),
            reads=["Mm"], writes=[f"Hb{hb}"])
        sync_dma(mrows[hb][:], m_all[sslot:sslot + 1, :], mall_regions,
                 [f"mrow{hb}"], s_mrow[hb])
        for j in range(NS):
            sb = Sps[j % 2]
            t_lo = 128 * max(0, j - 8)
            t_hi = 128 * (min(j, 7) + 1)
            span = t_hi - t_lo
            i_lo = t_lo + 1024 - 128 * j
            for off in range(0, span, 512):
                w = min(512, span - off)
                p.op("pe", lambda e, sb=sb, off=off, w=w, hb=hb, i_lo=i_lo:
                     e.matmul(sb[:, off:off + w], idb[:],
                              Hb[hb][:, i_lo + off:i_lo + off + w],
                              start=True, stop=False),
                     reads=[f"Hb{hb}", "idb"], writes=[sb.name])
                p.op("pe", lambda e, sb=sb, off=off, w=w, j=j, t_lo=t_lo,
                     ktile=ktile, kbase=kbase, qtile=qtile, qbase=qbase:
                     e.matmul(sb[:, off:off + w],
                              ktile[kbase:kbase + 64, 128 * j:128 * (j + 1)],
                              qtile[qbase:qbase + 64,
                                    t_lo + off:t_lo + off + w],
                              start=False, stop=False),
                     reads=[(ktile.name, j), qtile.name], writes=[sb.name])
                p.op("pe", lambda e, sb=sb, off=off, w=w, hb=hb, t_lo=t_lo:
                     e.matmul(sb[:, off:off + w], ones1[:],
                              mrows[hb][0:1, t_lo + off:t_lo + off + w],
                              start=False, stop=True),
                     reads=[f"mrow{hb}", "ones1"], writes=[sb.name])
            fr = Fring[j % 10]
            p.op("act", lambda e, sb=sb, span=span, j=j, fr=fr: e.activation(
                fr[:, 0:span], sb[:, 0:span], AF.Exp,
                bias=poi[:, j:j + 1]),
                reads=[sb.name, "poi"], writes=[fr.name])
            if j >= 8:
                tt = j - 8
                for jj in range(tt, tt + 9):
                    tlo_jj = 128 * max(0, jj - 8)
                    frj = Fring[jj % 10]
                    p.op("pe", lambda e, tt=tt, jj=jj, kv=kv, frj=frj,
                         tlo_jj=tlo_jj:
                         e.matmul(AOps[:, 128 * tt:128 * (tt + 1)],
                                  VA[kv][:, 65 * jj:65 * (jj + 1)],
                                  frj[:, 128 * tt - tlo_jj:
                                      128 * (tt + 1) - tlo_jj],
                                  start=(jj == tt), stop=(jj == tt + 8)),
                         reads=[(f"VA{kv}", jj), frj.name],
                         writes=[("AOps", tt)])
        p.op("dve", lambda e: e.tensor_copy(rsb[0:1, :], AOps[64:65, :]),
             reads=[("AOps", None)] + [("AOps", tt) for tt in range(8)],
             writes=[("rsb", None)])
        p.op("dve", lambda e: e.transpose(rsb2[:], rsb[:]),
             reads=[("rsb", None)], writes=["rsb2"])
        rview = rsb2[:].rearrange("p (b c) -> p b c", c=32)[:, :, 0:1]
        p.op("dve", lambda e: e.reciprocal(rview, rview),
             reads=["rsb2"], writes=["rsb2"])
        p.op("dve", lambda e: e.transpose(rsb[:], rsb2[:]),
             reads=["rsb2"], writes=[("rsb", None)])
        p.op("dve", lambda e: e.tensor_copy(rrecb[:], rsb[0:1, :]),
             reads=[("rsb", None)], writes=["rrecb"])
        for half in range(2):
            p.op("pe", lambda e, half=half: e.matmul(
                rbps[:, 512 * half:512 * (half + 1)], ones1[0:1, 0:64],
                rrecb[0:1, 512 * half:512 * (half + 1)],
                start=True, stop=True),
                reads=["rrecb", "ones1"], writes=["rbps"])
        p.op("dve", lambda e: e.tensor_copy(rbsb[:], rbps[:]),
             reads=["rbps"], writes=["rbsb"])
        p.op("dve", lambda e, jt=jt, bb=bb: e.tensor_tensor(
            AOsb[jt][64 * bb:64 * bb + 64, :], AOps[0:64, :],
            rbsb[:], ALU.mult),
            reads=[("AOps", None)] + [("AOps", tt) for tt in range(8)] +
                  ["rbsb"],
            writes=[(f"AO{jt}", bb)])

    tm.free(*QT, *KT, *VA, Mm, *Hb, *Fring, poi, rsb, rsb2, rrecb, rbsb,
            m_all, mrows[0], mrows[1])
    tm.free(rbps, AOps, Sps[1], Sps[0])

    if stop_after == "B":
        return finish_dummy("dummyB")

    tap_dump("AO", AOsb, 128, lambda c: [f"AO{c}", (f"AO{c}", 0), (f"AO{c}", 1)])

    ptc0 = tm.psum("ptc0", [128, 1024], BF16)
    ptc1 = tm.psum("ptc1", [128, 1024], BF16)
    ptcs = [ptc0, ptc1]
    pc = [tm.psum(f"pc{i}", [128, 1024]) for i in range(2)]

    ACC = [tm.sbuf(f"ACC{i}", [128, D], F32) for i in range(NT)]
    y1T = [tm.sbuf(f"y1T{i}", [128, TC], BF16) for i in range(ND)]
    woq = [tm.sbuf(f"woq{i}", [128, 4 * D], BF16) for i in range(2)]
    wosg = [tm.sbuf(f"wosg{i}", [128, 2048], BF16) for i in range(2)]
    s_wos = [p.slot(f"wos{i}") for i in range(2)]
    xrs = [tm.sbuf(f"xrs{i}", [128, D], F32) for i in range(2)]
    s_xrs = [p.slot(f"xrs{i}") for i in range(2)]
    ybf = tm.sbuf("ybf", [128, D], BF16)

    WOr = WO.rearrange("(a p) o -> p a o", p=128)
    for oq in range(4):
        qb = oq % 2
        for g in range(4):
            b2 = g % 2
            cast_dma(
                wosg[b2][:].rearrange("p (a o) -> p a o", o=512)[:, 0:4, :],
                WOr[:, 4 * g:4 * (g + 1), 512 * oq:512 * (oq + 1)],
                [], [f"wosg{b2}"], s_wos[b2])
            for ai in range(4):
                a = 4 * g + ai
                for c in range(4):
                    pb = ptcs[c % 2]
                    p.op("pe", lambda e, b2=b2, ai=ai, a=a, c=c, pb=pb:
                         e.transpose(
                             pb[:, 512 * (a % 2) + 128 * (c // 2):
                                512 * (a % 2) + 128 * (c // 2) + 128],
                             wosg[b2][:, 512 * ai + 128 * c:
                                     512 * ai + 128 * (c + 1)],
                             idb[:]),
                         reads=[f"wosg{b2}", "idb"],
                         writes=[(pb.name, (a % 2, c // 2))])
                    p.op("dve", lambda e, qb=qb, a=a, c=c, pb=pb:
                         e.tensor_copy(
                             woq[qb][:, D * c + 128 * a:D * c + 128 * (a + 1)],
                             pb[:, 512 * (a % 2) + 128 * (c // 2):
                                512 * (a % 2) + 128 * (c // 2) + 128]),
                         reads=[(pb.name, (a % 2, c // 2))],
                         writes=[(f"woq{qb}", (a, c))])
        for t in range(NT):
            tb = t % 2
            if oq == 0:
                sync_dma(xrs[tb][:], XE[TC + 128 * t:TC + 128 * (t + 1), :],
                         [], [f"xrs{tb}"], s_xrs[tb])
            for dhalf in range(2):
                for c in range(4):
                    oc = 4 * oq + c
                    for dd in range(2):
                        p.op("pe", lambda e, qb=qb, c=c, t=t, oc=oc,
                             dhalf=dhalf, dd=dd: e.matmul(
                                 pc[t % 2][:, 512 * dd:512 * (dd + 1)],
                                 AOsb[oc][:, 128 * t:128 * (t + 1)],
                                 woq[qb][:, D * c + 1024 * dhalf + 512 * dd:
                                         D * c + 1024 * dhalf + 512 * (dd + 1)],
                                 start=(c == 0), stop=(c == 3)),
                             reads=[f"AO{oc}", (f"AO{oc}", 0), (f"AO{oc}", 1),
                                    f"woq{qb}"],
                             writes=[pc[t % 2].name])
                hk = "lo" if dhalf == 0 else "hi"
                if oq == 0:
                    p.op("dve", lambda e, t=t, tb=tb, dhalf=dhalf: e.tensor_tensor(
                        ACC[t][:, 1024 * dhalf:1024 * (dhalf + 1)],
                        xrs[tb][:, 1024 * dhalf:1024 * (dhalf + 1)],
                        pc[t % 2][:], ALU.add),
                        reads=[f"xrs{tb}", pc[t % 2].name],
                        writes=[(f"ACC{t}", hk)])
                else:
                    p.op("dve", lambda e, t=t, dhalf=dhalf: e.tensor_tensor(
                        ACC[t][:, 1024 * dhalf:1024 * (dhalf + 1)],
                        ACC[t][:, 1024 * dhalf:1024 * (dhalf + 1)],
                        pc[t % 2][:], ALU.add),
                        reads=[pc[t % 2].name, (f"ACC{t}", hk)],
                        writes=[(f"ACC{t}", hk)])

    def layernorm_tile(t):
        acc_h = ACC[t]
        tname = f"ACC{t}"
        for c in range(4):
            p.op("dve", lambda e, c=c: e.bn_stats(
                bns[:, 6 * c:6 * (c + 1)], acc_h[:, 512 * c:512 * (c + 1)]),
                reads=[(tname, "lo"), (tname, "hi")], writes=[("bns", c)])
        p.op("dve", lambda e: e.bn_aggr(aggr[:], bns[:]),
             reads=[("bns", 0), ("bns", 1), ("bns", 2), ("bns", 3)],
             writes=["aggr"])
        p.op("act", lambda e: e.activation(small[:, 24:25], aggr[:, 1:2],
                                           AF.Sqrt, bias=epsc[:, 1:2]),
             reads=["aggr", ("epsc", 1)], writes=[("small", 24)])
        p.op("dve", lambda e: e.reciprocal(small[:, 25:26], small[:, 24:25]),
             reads=[("small", 24)], writes=[("small", 25)])
        p.op("dve", lambda e: e.tensor_scalar(
            acc_h[:], acc_h[:], aggr[:, 0:1], small[:, 25:26],
            ALU.subtract, ALU.mult),
            reads=[(tname, "lo"), (tname, "hi"), "aggr", ("small", 25)],
            writes=[(tname, "lo"), (tname, "hi")])

    for t in range(NT):
        layernorm_tile(t)
        p.op("act", lambda e, t=t: e.activation(ybf[:], ACC[t][:], AF.Copy),
             reads=[(f"ACC{t}", "lo"), (f"ACC{t}", "hi")], writes=["ybf"])
        for dc in range(ND):
            pb = ptcs[dc // 8]
            p.op("pe", lambda e, dc=dc, pb=pb: e.transpose(
                pb[:, 128 * (dc % 8):128 * (dc % 8 + 1)],
                ybf[:, 128 * dc:128 * (dc + 1)], idb[:]),
                reads=["ybf", "idb"], writes=[(pb.name, dc % 8)])
        for dc in range(ND):
            pb = ptcs[dc // 8]
            p.op("dve", lambda e, t=t, dc=dc, pb=pb: e.tensor_copy(
                y1T[dc][:, 128 * t:128 * (t + 1)],
                pb[:, 128 * (dc % 8):128 * (dc % 8 + 1)]),
                reads=[(pb.name, dc % 8)], writes=[(f"y1T{dc}", t)])

    tm.free(*AOsb, *woq, *wosg, *xrs, ybf)

    if "Y1" in tap_outs:
        for t in range(NT):
            sync_dma(tap_outs["Y1"][128 * t:128 * (t + 1), :], ACC[t][:],
                     [(f"ACC{t}", "lo"), (f"ACC{t}", "hi")],
                     [("OUT", f"y1tap{t}")], p.slot("tap_Y1"), gid="y1g")

    if stop_after == "C":
        return finish_acc(ACC)

    gwbf = tm.sbuf("gwbf", [NE, D], BF16)
    gwT = tm.sbuf("gwT", [128, 128], BF16)
    cast_dma(gwbf[:], GW, [], ["gwbf"], p.slot("gw"))
    pL = tm.psum("pL", [128, 512])
    for dc in range(ND):
        p.op("pe", lambda e, dc=dc: e.transpose(
            ptc0[:, 8 * dc:8 * (dc + 1)],
            gwbf[:, 128 * dc:128 * (dc + 1)], idb[0:8, 0:8]),
            reads=["gwbf", "idb"], writes=[("ptc0", ("g", dc))])
        p.op("dve", lambda e, dc=dc: e.tensor_copy(
            gwT[:, 8 * dc:8 * (dc + 1)], ptc0[:, 8 * dc:8 * (dc + 1)]),
            reads=[("ptc0", ("g", dc))], writes=[("gwT", dc)])

    for t in range(NT):
        for dc in range(ND):
            p.op("pe", lambda e, t=t, dc=dc: e.matmul(
                pL[:, 0:8], y1T[dc][:, 128 * t:128 * (t + 1)],
                gwT[:, 8 * dc:8 * (dc + 1)],
                start=(dc == 0), stop=(dc == ND - 1)),
                reads=[(f"y1T{dc}", t), ("gwT", dc)], writes=["pL"])
        g0 = 40
        gsl = gates8[:, 8 * t:8 * (t + 1)]
        p.op("dve", lambda e: e.reduce_max(small[:, g0:g0 + 1], pL[:, 0:8],
                                           AX.X, negate=True),
             reads=["pL"], writes=[("small", g0)])
        p.op("act", lambda e, gsl=gsl: e.activation(
            gsl, pL[:, 0:8], AF.Exp, bias=small[:, g0:g0 + 1]),
            reads=["pL", ("small", g0)], writes=[("gates8", t)])
        p.op("dve", lambda e, gsl=gsl: e.reduce_sum(
            small[:, g0 + 1:g0 + 2], gsl, AX.X),
            reads=[("gates8", t)], writes=[("small", g0 + 1)])
        p.op("dve", lambda e: e.reciprocal(small[:, g0 + 2:g0 + 3],
                                           small[:, g0 + 1:g0 + 2]),
             reads=[("small", g0 + 1)], writes=[("small", g0 + 2)])
        p.op("dve", lambda e, gsl=gsl: e.tensor_scalar(
            gsl, gsl, small[:, g0 + 2:g0 + 3], None, ALU.mult),
            reads=[("gates8", t), ("small", g0 + 2)], writes=[("gates8", t)])
        p.op("dve", lambda e, gsl=gsl: e.reduce_max(
            small[:, g0 + 3:g0 + 4], gsl, AX.X),
            reads=[("gates8", t)], writes=[("small", g0 + 3)])
        p.op("dve", lambda e, gsl=gsl: e.tensor_scalar(
            small[:, 48:56], gsl, small[:, g0 + 3:g0 + 4], 1e30,
            ALU.is_ge, ALU.mult),
            reads=[("gates8", t), ("small", g0 + 3)], writes=[("small", 48)])
        p.op("dve", lambda e, gsl=gsl: e.tensor_tensor(
            small[:, 48:56], gsl, small[:, 48:56], ALU.subtract),
            reads=[("gates8", t), ("small", 48)], writes=[("small", 48)])
        p.op("dve", lambda e: e.reduce_max(small[:, g0 + 4:g0 + 5],
                                           small[:, 48:56], AX.X),
             reads=[("small", 48)], writes=[("small", g0 + 4)])
        p.op("dve", lambda e: e.tensor_tensor(small[:, g0 + 5:g0 + 6],
                                              small[:, g0 + 3:g0 + 4],
                                              small[:, g0 + 4:g0 + 5],
                                              ALU.add),
             reads=[("small", g0 + 3), ("small", g0 + 4)],
             writes=[("small", g0 + 5)])
        p.op("dve", lambda e: e.reciprocal(small[:, g0 + 6:g0 + 7],
                                           small[:, g0 + 5:g0 + 6]),
             reads=[("small", g0 + 5)], writes=[("small", g0 + 6)])
        p.op("dve", lambda e, gsl=gsl: e.tensor_scalar(
            small[:, 48:56], gsl, small[:, g0 + 4:g0 + 5], None, ALU.is_ge),
            reads=[("gates8", t), ("small", g0 + 4)], writes=[("small", 48)])
        p.op("dve", lambda e, gsl=gsl: e.tensor_tensor(
            gsl, gsl, small[:, 48:56], ALU.mult),
            reads=[("gates8", t), ("small", 48)], writes=[("gates8", t)])
        p.op("dve", lambda e, gsl=gsl: e.tensor_scalar(
            gsl, gsl, small[:, g0 + 6:g0 + 7], None, ALU.mult),
            reads=[("gates8", t), ("small", g0 + 6)], writes=[("gates8", t)])
        p.op("pe", lambda e, gsl=gsl: e.transpose(
            pL[0:8, 128:256], gsl, idf[:]),
            reads=[("gates8", t), "idf"], writes=["pL"])
        p.op("dve", lambda e, t=t: e.tensor_copy(
            grow8[:, 128 * t:128 * (t + 1)], pL[0:8, 128:256]),
            reads=["pL"], writes=[("grow8", t)])

    if "GATES" in tap_outs:
        gtmp = tm.sbuf("gtmp", [128, NT * NE], F32)
        p.op("dve", lambda e: e.tensor_copy(gtmp[:], gates8[:]),
             reads=["gates8"] + [("gates8", t) for t in range(NT)],
             writes=["gtmp"])
        sync_dma(tap_outs["GATES"].rearrange("(t p) e -> p t e", p=128),
                 gtmp[:].rearrange("p (t e) -> p t e", e=NE),
                 ["gtmp"], [("OUT", "gtap")], p.slot("tap_GATES"),
                 allow_slow_non_contiguous=True)
        tm.free(gtmp)

    tm.free(gwbf, gwT)
    tm.free(pL, pc[1], pc[0], ptc1, ptc0)

    if stop_after == "D":
        return finish_acc(ACC)

    FGRP = [(0, 256), (256, 256), (512, 256), (768, 256), (1024, 256),
            (1280, 128)]
    wgb = [tm.sbuf(f"wgb{i}", [128, ND * 256], BF16) for i in range(3)]
    wub = [tm.sbuf(f"wub{i}", [128, ND * 256], BF16) for i in range(3)]
    s_wgb = [p.slot(f"wgb{i}") for i in range(3)]
    s_wub = [p.slot(f"wub{i}") for i in range(3)]
    wdb = [tm.sbuf(f"wdb{i}", [128, 11 * 256], BF16) for i in range(3)]
    s_wdb = [p.slot(f"wdb{i}") for i in range(3)]
    hT = [tm.sbuf(f"hT{i}", [128, TC], BF16) for i in range(11)]
    gbsb = tm.sbuf("gbsb", [128, TC], F32)
    gsb = tm.sbuf("gsb", [128, TC], BF16)
    usb = tm.sbuf("usb", [128, TC], BF16)
    pG = tm.psum("pG", [128, 1024])
    pU = tm.psum("pU", [128, 1024])
    pd = [tm.psum(f"pd{i}", [128, 512]) for i in range(2)]

    WGr = [WGATE[e].rearrange("(a p) f -> p a f", p=128) for e in range(NE)]
    WUr = [WUP[e].rearrange("(a p) f -> p a f", p=128) for e in range(NE)]
    WDr = [WDOWN[e].rearrange("(c p) d -> p c d", p=128) for e in range(NE)]
    WGr.append(SWG.rearrange("(a p) f -> p a f", p=128))
    WUr.append(SWU.rearrange("(a p) f -> p a f", p=128))
    WDr.append(SWD.rearrange("(c p) d -> p c d", p=128))
    grow_regions = ["grow8"] + [("grow8", t) for t in range(NT)]

    gi = 0
    di = 0
    for e_i in range(NE + 1):
        routed = e_i < NE
        for g, (f0, fw) in enumerate(FGRP):
            b = gi % 3
            gi += 1
            for blk in range(0, ND, 4):
                cast_dma(
                    wgb[b][:].rearrange("p (a f) -> p a f", f=256)
                    [:, blk:blk + 4, 0:fw],
                    WGr[e_i][:, blk:blk + 4, f0:f0 + fw],
                    [], [(f"wgb{b}", blk)], s_wgb[b], gid=("g", e_i, g))
                cast_dma(
                    wub[b][:].rearrange("p (a f) -> p a f", f=256)
                    [:, blk:blk + 4, 0:fw],
                    WUr[e_i][:, blk:blk + 4, f0:f0 + fw],
                    [], [(f"wub{b}", blk)], s_wub[b], gid=("g", e_i, g))
            for cc in range(fw // 128):
                fc = f0 // 128 + cc
                for th in range(2):
                    for a in range(ND):
                        p.op("pe", lambda e, b=b, a=a, cc=cc, th=th: e.matmul(
                            pG[:, 512 * th:512 * (th + 1)],
                            wgb[b][:, 256 * a + 128 * cc:
                                   256 * a + 128 * (cc + 1)],
                            y1T[a][:, 512 * th:512 * (th + 1)],
                            start=(a == 0), stop=(a == ND - 1)),
                            reads=[f"wgb{b}", f"y1T{a}"] +
                                  [(f"y1T{a}", t) for t in range(NT)],
                            writes=["pG"])
                    for a in range(ND):
                        p.op("pe", lambda e, b=b, a=a, cc=cc, th=th: e.matmul(
                            pU[:, 512 * th:512 * (th + 1)],
                            wub[b][:, 256 * a + 128 * cc:
                                   256 * a + 128 * (cc + 1)],
                            y1T[a][:, 512 * th:512 * (th + 1)],
                            start=(a == 0), stop=(a == ND - 1)),
                            reads=[f"wub{b}", f"y1T{a}"] +
                                  [(f"y1T{a}", t) for t in range(NT)],
                            writes=["pU"])
                p.op("act", lambda e: e.activation(gsb[:], pG[:], AF.Silu),
                     reads=["pG"], writes=["gsb"])
                p.op("dve", lambda e: e.tensor_copy(usb[:], pU[:]),
                     reads=["pU"], writes=["usb"])
                p.op("dve", lambda e, fc=fc: e.tensor_tensor(
                    hT[fc][:], gsb[:], usb[:], ALU.mult),
                    reads=["gsb", "usb"], writes=[f"hT{fc}"])
        for q in range(8):
            b = di % 3
            di += 1
            for blk, cnt in ((0, 4), (4, 4), (8, 3)):
                cast_dma(
                    wdb[b][:].rearrange("p (c d) -> p c d", d=256)
                    [:, blk:blk + cnt, :],
                    WDr[e_i][:, blk:blk + cnt, 256 * q:256 * (q + 1)],
                    [], [(f"wdb{b}", blk)], s_wdb[b], gid=("d", e_i, q))
            for t in range(NT):
                pdb = pd[t % 2]
                hk = "lo" if q < 4 else "hi"
                for fc in range(11):
                    p.op("pe", lambda e, b=b, fc=fc, t=t, pdb=pdb, q=q:
                         e.matmul(
                             pdb[:, 0:256], hT[fc][:, 128 * t:128 * (t + 1)],
                             wdb[b][:, 256 * fc:256 * (fc + 1)],
                             start=(fc == 0), stop=(fc == 10)),
                         reads=[f"hT{fc}", f"wdb{b}"],
                         writes=[pdb.name])
                if routed:
                    p.op("dve", lambda e, t=t, q=q, e_i=e_i, pdb=pdb:
                         e.scalar_tensor_tensor(
                             ACC[t][:, 256 * q:256 * (q + 1)], pdb[:, 0:256],
                             gates8[:, 8 * t + e_i:8 * t + e_i + 1],
                             ACC[t][:, 256 * q:256 * (q + 1)],
                             ALU.mult, ALU.add),
                         reads=[pdb.name, ("gates8", t), (f"ACC{t}", hk)],
                         writes=[(f"ACC{t}", hk)])
                else:
                    p.op("dve", lambda e, t=t, q=q, pdb=pdb: e.tensor_tensor(
                        ACC[t][:, 256 * q:256 * (q + 1)],
                        ACC[t][:, 256 * q:256 * (q + 1)], pdb[:, 0:256],
                        ALU.add),
                        reads=[pdb.name, (f"ACC{t}", hk)],
                        writes=[(f"ACC{t}", hk)])

    if "ACC2" in tap_outs:
        for t in range(NT):
            sync_dma(tap_outs["ACC2"][128 * t:128 * (t + 1), :], ACC[t][:],
                     [(f"ACC{t}", "lo"), (f"ACC{t}", "hi")],
                     [("OUT", f"acc2tap{t}")], p.slot("tap_ACC2"),
                     gid="a2g")

    for t in range(NT):
        layernorm_tile(t)
        sync_dma(Y[128 * t:128 * (t + 1), :], ACC[t][:],
                 [(f"ACC{t}", "lo"), (f"ACC{t}", "hi")],
                 [("OUT", t)], s_out)

    p.op("pool", lambda e: e.nop(), reads=[("OUT", None)], tag="final")

    with nc.Block() as block:
        for e in ENGINES:
            sems[e] = st.enter_context(nc.semaphore(f"sem_{e}"))
        for s in p.slots.values():
            s.sem = st.enter_context(nc.semaphore(f"dma_{s.name}"))
        p.emit_all(block, sems)
    st.close()
    nc._dbg_prog = p
    return nc


def host_consts(core):
    M = np.zeros((128, 1152), dtype=np.float32)
    s = np.arange(128)[:, None]
    i = np.arange(1152)[None, :]
    d = i - s
    M[:] = np.where((d >= 0) & (d <= WIN), (s - i).astype(np.float32),
                    np.float32(-1e9)).astype(np.float32)
    poi = np.zeros((128, NS), dtype=np.float32)
    if core == 0:
        poi[:, 0:8] = -1e5
    idf = np.eye(128, dtype=np.float32)
    return M, poi, idf


_nc_cache = {}


def _get_nc(taps=()):
    key = tuple(sorted(taps))
    if key not in _nc_cache:
        _nc_cache[key] = build_nc(key)
    return _nc_cache[key]


def kernel(L0_attn_x, L0_attn_wq, L0_attn_wk, L0_attn_wv, L0_attn_wo,
           L0_attn_q_norm_w, L0_attn_k_norm_w, L0_ffn_gate_w, L0_ffn_w_gate,
           L0_ffn_w_up, L0_ffn_w_down, L0_ffn_sw_gate, L0_ffn_sw_up,
           L0_ffn_sw_down, _taps=(), _collect=None):
    x = np.ascontiguousarray(np.asarray(L0_attn_x, dtype=np.float32)
                             .reshape(NCORE * TC, D))
    asf = lambda a: np.ascontiguousarray(np.asarray(a, dtype=np.float32))
    wq, wk, wv, wo = (asf(L0_attn_wq), asf(L0_attn_wk), asf(L0_attn_wv),
                      asf(L0_attn_wo))
    operm = np.concatenate(
        [np.arange(64 * h, 64 * h + 64)
         for j in range(16) for h in (8 * (j // 4) + (j % 4),
                                      8 * (j // 4) + (j % 4) + 4)])
    wq = np.ascontiguousarray(wq[operm, :])
    wo = np.ascontiguousarray(wo[:, operm])
    qnw = asf(L0_attn_q_norm_w).reshape(1, D)
    qnw = np.ascontiguousarray(qnw[:, operm])
    knw = asf(L0_attn_k_norm_w).reshape(1, 512)
    gw = asf(L0_ffn_gate_w)
    wgate, wup, wdown = asf(L0_ffn_w_gate), asf(L0_ffn_w_up), asf(L0_ffn_w_down)
    swg, swu, swd = asf(L0_ffn_sw_gate), asf(L0_ffn_sw_up), asf(L0_ffn_sw_down)

    nc = _get_nc(tuple(_taps))
    in_maps = []
    for c in range(NCORE):
        if c == 0:
            prev = np.zeros((TC, D), dtype=np.float32)
        else:
            prev = x[TC * (c - 1):TC * c]
        xe = np.ascontiguousarray(
            np.concatenate([prev, x[TC * c:TC * (c + 1)]], axis=0))
        M, poi, idf = host_consts(c)
        in_maps.append(dict(
            xe=xe, wq=wq, wk=wk, wv=wv, wo=wo, qnw=qnw, knw=knw, gw=gw,
            wgate=wgate, wup=wup, wdown=wdown, swg=swg, swu=swu, swd=swd,
            mmask=M, poison=poi, identf=idf))
    res = run_bass_kernel_spmd(nc, in_maps, core_ids=list(range(NCORE)))
    if _collect is not None:
        _collect.extend(res.results)
    y = np.concatenate([res.results[c]["y"] for c in range(NCORE)], axis=0)
    return y.reshape(1, NCORE * TC, D)


# version 06
# speedup vs baseline: 1.2162x; 1.0284x over previous
import numpy as np
from contextlib import ExitStack

import concourse.bass as bass
import concourse.mybir as mybir
from concourse.bass_utils import run_bass_kernel_spmd

F32 = mybir.dt.float32
BF16 = mybir.dt.bfloat16
AF = mybir.ActivationFunctionType
ALU = mybir.AluOpType
AX = mybir.AxisListType

NCORE = 8
TC = 1024
TEXT = 2048
D = 2048
NH = 32
NKV = 8
DH = 64
WIN = 1024
DFF = 1408
NE = 8
NT = TC // 128
NS = TEXT // 128
ND = D // 128

SLOPES = [float(2.0 ** (-0.25 * h)) for h in range(NH)]

ENGINES = ("pe", "act", "dve", "pool", "sync")


class DmaSlot:
    __slots__ = ("name", "sem", "count", "groups")

    def __init__(self, name):
        self.name = name
        self.sem = None
        self.count = 0
        self.groups = {}


class Op:
    __slots__ = ("idx", "engine", "emit", "reads", "writes", "deps", "slot",
                 "slot_count", "slot_group", "need_inc", "milestone", "tag")

    def __init__(self, idx, engine, emit, reads, writes, slot, tag):
        self.idx = idx
        self.engine = engine
        self.emit = emit
        self.reads = reads
        self.writes = writes
        self.deps = []
        self.slot = slot
        self.slot_count = None
        self.slot_group = None
        self.need_inc = False
        self.milestone = None
        self.tag = tag


class Prog:
    def __init__(self, nc, self_sync=True):
        self.nc = nc
        self.ops = []
        self.track = {}
        self.slots = {}
        self.self_sync = self_sync
        self.pending_free = []
        self.newdeps = {}
        self.psum_names = set()
        self.n_waits = 0
        self.n_incs = 0

    def slot(self, name):
        s = self.slots.get(name)
        if s is None:
            s = self.slots[name] = DmaSlot(name)
        return s

    def new_tensor(self, name):
        if self.pending_free:
            self.newdeps[name] = self._prune(self.pending_free)

    def note_free(self, name):
        t = self.track.get(name)
        if t is None:
            return
        ops = list(t["w"].values())
        for lst in t["r"].values():
            ops.extend(lst)
        self.pending_free.extend(ops)
        self.pending_free = self._prune(self.pending_free)

    @staticmethod
    def _prune(deps):
        best = {}
        for d in deps:
            key = ("s", d.slot.name) if d.slot is not None else ("e", d.engine)
            cur = best.get(key)
            if cur is None or d.idx > cur.idx:
                best[key] = d
        return list(best.values())

    @staticmethod
    def _regions(lst):
        out = []
        for r in lst or ():
            if isinstance(r, tuple):
                out.append(r)
            else:
                out.append((r, None))
        return out

    def _dep_read(self, name, key):
        t = self.track.get(name)
        if t is None:
            return []
        w = t["w"]
        if key is None:
            return list(w.values())
        deps = []
        if key in w:
            deps.append(w[key])
        if None in w:
            deps.append(w[None])
        return deps

    def _dep_write(self, name, key):
        deps = self._dep_read(name, key)
        t = self.track.get(name)
        if t is not None:
            r = t["r"]
            if key is None:
                for lst in r.values():
                    deps.extend(lst)
            else:
                deps.extend(r.get(key, ()))
                deps.extend(r.get(None, ()))
        return deps

    def op(self, engine, emit, reads=(), writes=(), slot=None, tag="",
           gid=None):
        wr = self._regions(writes)
        rd = self._regions(reads)
        if engine == "pe":
            wr = [(n, None) if n in self.psum_names else (n, k)
                  for (n, k) in wr]
        else:
            promote = [(n, None) for (n, k) in rd if n in self.psum_names]
            if promote:
                rd = [(n, k) for (n, k) in rd if n not in self.psum_names]
                wr = wr + promote
        o = Op(len(self.ops), engine, emit, rd, wr, slot, tag)
        deps = []
        for (n, k) in o.reads:
            deps.extend(self._dep_read(n, k))
            if n in self.newdeps:
                deps.extend(self.newdeps.pop(n))
        for (n, k) in o.writes:
            deps.extend(self._dep_write(n, k))
            if n in self.newdeps:
                deps.extend(self.newdeps.pop(n))
        pruned = {}
        for d in deps:
            if d is o:
                continue
            if d.engine == o.engine and d.slot is None:
                if o.engine == "pe" or not self.self_sync:
                    continue
            key = ("s", d.slot.name) if d.slot is not None else ("e", d.engine)
            cur = pruned.get(key)
            if cur is None or d.idx > cur.idx:
                pruned[key] = d
        o.deps = list(pruned.values())
        if slot is not None:
            slot.count += 1
            o.slot_count = slot.count
            if gid is None:
                gid = ("u", slot.count)
            o.slot_group = gid
            g = slot.groups.get(gid, 0)
            slot.groups[gid] = max(g, slot.count)
        for (n, k) in o.reads:
            self.track.setdefault(n, {"w": {}, "r": {}})["r"].setdefault(
                k, []).append(o)
        for (n, k) in o.writes:
            t = self.track.setdefault(n, {"w": {}, "r": {}})
            if k is None:
                t["w"] = {None: o}
                t["r"] = {}
            else:
                t["w"][k] = o
                t["r"].pop(k, None)
        self.ops.append(o)
        return o

    def dma(self, queue, emit, reads=(), writes=(), slot=None, tag="",
            gid=None):
        assert slot is not None
        return self.op(queue, emit, reads=reads, writes=writes, slot=slot,
                       tag=tag, gid=gid)

    def emit_all(self, block, engine_sems):
        for o in self.ops:
            for d in o.deps:
                if d.slot is None:
                    d.need_inc = True
        counters = {e: 0 for e in ENGINES}
        for o in self.ops:
            if o.need_inc and o.slot is None:
                counters[o.engine] += 1
                o.milestone = counters[o.engine]
        per_engine = {e: [] for e in ENGINES}
        for o in self.ops:
            per_engine[o.engine].append(o)
        prog = self

        def runner(eng_name):
            def body(eng):
                waited = {}
                for o in per_engine[eng_name]:
                    for d in o.deps:
                        if d.slot is not None:
                            sem = d.slot.sem
                            val = 16 * d.slot.groups[d.slot_group]
                        else:
                            sem = engine_sems[d.engine]
                            val = d.milestone
                        key = sem.name
                        if waited.get(key, -1) >= val:
                            continue
                        eng.wait_ge(sem, val)
                        waited[key] = val
                        prog.n_waits += 1
                    inst = o.emit(eng)
                    if o.slot is not None:
                        inst.then_inc(o.slot.sem, 16)
                        prog.n_incs += 1
                    elif o.need_inc:
                        inst.then_inc(engine_sems[o.engine], 1)
                        prog.n_incs += 1
            return body

        section = {"pe": block.tensor, "act": block.scalar,
                   "dve": block.vector, "pool": block.gpsimd,
                   "sync": block.sync}
        for e in ENGINES:
            if per_engine[e]:
                section[e](runner(e))


ARENA_BYTES = 211968


class TMgr:

    def __init__(self, nc, prog, off):
        self.nc = nc
        self.p = prog
        self.off = off
        self.cms = {}
        self.logical = {}
        self.acm = nc.sbuf_tensor("arena", [128, ARENA_BYTES], mybir.dt.uint8)
        self.ah = self.acm.__enter__()
        self.abase = nc.lookup_mloc(self.ah).addr
        self.spans = {}

    def sbuf(self, name, shape, dtype):
        import functools, operator
        off = self.off[name]
        per_part = functools.reduce(operator.mul, shape[1:], 1) *             mybir.dt.size(dtype)
        assert off + per_part <= ARENA_BYTES, (name, off, per_part)
        assert off % 64 == 0, (name, off)
        h = self.nc.alloc_sbuf_tensor_at(name, shape, dtype,
                                         offset=self.abase + off)
        self.spans[name] = (off, off + per_part)
        self.logical[h.name] = name
        self.p.new_tensor(name)
        return h

    def psum(self, name, shape, dtype=F32):
        cm = self.nc.psum_tensor(name, shape, dtype)
        h = cm.__enter__()
        self.cms[name] = cm
        self.logical[h.name] = name
        self.p.psum_names.add(name)
        self.p.new_tensor(name)
        return h

    def free(self, *handles):
        for h in handles:
            name = self.logical.get(h.name, h.name)
            self.p.note_free(name)
            cm = self.cms.pop(name, None)
            if cm is not None:
                cm.__exit__(None, None, None)


def build_nc(taps=(), stop_after=None):
    nc = bass.Bass("TRN2", target_bir_lowering=False)
    taps = set(taps)

    din = lambda n, s: nc.dram_tensor(n, s, F32, kind="ExternalInput").ap()
    XE = din("xe", [TEXT, D])
    WQ = din("wq", [D, D])
    WK = din("wk", [512, D])
    WV = din("wv", [512, D])
    WO = din("wo", [D, D])
    QNW = din("qnw", [1, D])
    KNW = din("knw", [1, 512])
    GW = din("gw", [NE, D])
    WGATE = din("wgate", [NE, D, DFF])
    WUP = din("wup", [NE, D, DFF])
    WDOWN = din("wdown", [NE, DFF, D])
    SWG = din("swg", [D, DFF])
    SWU = din("swu", [D, DFF])
    SWD = din("swd", [DFF, D])
    MM = din("mmask", [128, 1152])
    POI = din("poison", [128, NS])
    IDF = din("identf", [128, 128])
    Y = nc.dram_tensor("y", [TC, D], F32, kind="ExternalOutput").ap()
    tap_outs = {}
    tap_shapes = {"KT": [512, TEXT], "QT": [D, TC], "MALL": [NH, TC],
                  "AO": [D, TC], "Y1": [TC, D], "GATES": [TC, NE],
                  "ACC2": [TC, D]}
    for t in taps:
        tap_outs[t] = nc.dram_tensor("tap_" + t, tap_shapes[t], F32,
                                     kind="ExternalOutput").ap()

    RA, RB, RC, RD, RE = 10240, 79872, 113664, 146432, 179200
    OFF = {
        "idb": 0, "idf": 256, "ones1": 768, "onesf": 1024, "small": 1536,
        "bns": 1792, "aggr": 1920, "knacc": 1984, "knbneg": 2048,
        "knscr": 2176, "epsc": 9920, "gates8": 3520, "grow8": 3776, "garow0": 5824,
        "garow1": 7872,
        "wkT": RA, "wvT": RA + 16384, "wstg0": RA + 32768, "wstg1": RA + 36864,
        **{f"wqT{i}": RA + 4096 * i for i in range(16)},
        **{f"Fr{i}": RA + 2048 * i for i in range(10)},
        "woq0": RA, "woq1": RA + 16384, "wosg0": RA + 32768,
        "wosg1": RA + 36864, "wosg2": RA + 65536, "xrs0": RA + 40960, "xrs1": RA + 49152,
        "ybf": RA + 57344, "gwbf": RA + 61440, "gwT": RA + 57344,
        **{f"wgb{i}": RA + 8192 * i for i in range(3)},
        **{f"wub{i}": RA + 24576 + 8192 * i for i in range(3)},
        **{f"wdb{i}": RA + 49152 + 5696 * i for i in range(3)},
        **{f"KT{i}": RB + 4096 * i for i in range(4)},
        **{f"VA{i}": RB + 16384 + 2112 * i for i in range(8)},
        **{f"ACC{i}": (RB + 8192 * i if i < 4 else RC + 8192 * (i - 4))
           for i in range(8)},
        "m_all": RC, "mrow0": RC + 2048, "mrow1": RC + 4096,
        "rrecb": RC + 6144,
        "Qsb": RC + 8192, "sqs": RC + 16384, "wstgq0": RC + 24576,
        "wstgq1": RC + 28672,
        "Mm": RC + 8192, "poi": RC + 12800, "Hb0": RC + 12864,
        "Hb1": RC + 15168, "Fb0": RC + 17472, "Fb1": RC + 19520,
        "rsb": RC + 21568, "rsb2": RC + 25664, "rbsb": RC + 29760,
        "xstg0": RD, "xstg1": RD + 4096, "xTc0": RD + 8192,
        "xTc1": RD + 12288, "ksqs": RD + 16384, "row1": RD + 18432,
        "kwb": RD + 22528, "qwb": RD + 24576,
        **{f"AO{i}": RD + 2048 * i for i in range(16)},
        **{f"hT{i}": RD + 2048 * i for i in range(11)},
        "gbsb": RD + 22528, "gsb": RD + 26624, "usb": RD + 28672,
        "ttmp_KT": RA + 32768, "ttmp_QT": RA + 32768, "matmp": RA + 40960,
        "dummyA": RE + 24576, "dummyB": RA + 32768,
        "gtmp": RA + 32768, "ttmp_AO": RA + 32768,
        **{f"QT{i}": RE + 2048 * i for i in range(16)},
        **{f"y1T{i}": RE + 2048 * i for i in range(16)},
    }
    p = Prog(nc, self_sync=True)
    tm = TMgr(nc, p, OFF)
    st = ExitStack()

    idb = tm.sbuf("idb", [128, 128], BF16)
    idf = tm.sbuf("idf", [128, 128], F32)
    ones1 = tm.sbuf("ones1", [1, 128], BF16)
    onesf = tm.sbuf("onesf", [1, 128], F32)
    epsc = tm.sbuf("epsc", [128, 2], F32)
    knscr = tm.sbuf("knscr", [8, 336], F32)
    m_all = tm.sbuf("m_all", [NH, TC], BF16)
    mrows = [tm.sbuf(f"mrow{i}", [1, TC], BF16) for i in range(2)]
    knacc = tm.sbuf("knacc", [128, NKV], F32)
    knbneg = tm.sbuf("knbneg", [128, NH], F32)
    gates8 = tm.sbuf("gates8", [128, NT * NE], F32)
    grow8 = tm.sbuf("grow8", [NE, TC], BF16)
    garow = [tm.sbuf(f"garow{i}", [1, TC], BF16) for i in range(2)]
    small = tm.sbuf("small", [128, 64], F32)
    bns = tm.sbuf("bns", [128, 24], F32)
    aggr = tm.sbuf("aggr", [128, 2], F32)

    sems = {}
    s_misc = p.slot("misc")
    s_miscp = p.slot("miscp")
    s_out = p.slot("out")

    def sync_dma(dst, src, reads, writes, slot, gid=None, **kw):
        return p.dma("sync", lambda e: e.dma_start(dst, src, **kw),
                     reads=reads, writes=writes, slot=slot, gid=gid)

    def cast_dma(dst, src, reads, writes, slot, gid=None):
        return p.dma("pool", lambda e: e.dma_start(dst, src),
                     reads=reads, writes=writes, slot=slot, gid=gid)

    def finish_dummy(which):
        dmy = tm.sbuf(which, [128, D], F32)
        p.op("dve", lambda e: e.memset(dmy[:], 0.5), writes=[which])
        for t in range(NT):
            sync_dma(Y[128 * t:128 * (t + 1), :], dmy[:], [which],
                     [("OUT", t)], s_out, gid="yg")
        p.op("pool", lambda e: e.nop(), reads=[("OUT", None)], tag="final")
        with nc.Block() as block:
            for e in ENGINES:
                sems[e] = st.enter_context(nc.semaphore(f"sem_{e}"))
            for sl_ in p.slots.values():
                sl_.sem = st.enter_context(nc.semaphore(f"dma_{sl_.name}"))
            p.emit_all(block, sems)
        st.close()
        nc._dbg_prog = p
        return nc

    def finish_acc(ACC):
        for t in range(NT):
            sync_dma(Y[128 * t:128 * (t + 1), :], ACC[t][:],
                     [(f"ACC{t}", "lo"), (f"ACC{t}", "hi")],
                     [("OUT", t)], s_out, gid="yg")
        p.op("pool", lambda e: e.nop(), reads=[("OUT", None)], tag="final")
        with nc.Block() as block:
            for e in ENGINES:
                sems[e] = st.enter_context(nc.semaphore(f"sem_{e}"))
            for sl_ in p.slots.values():
                sl_.sem = st.enter_context(nc.semaphore(f"dma_{sl_.name}"))
            p.emit_all(block, sems)
        st.close()
        nc._dbg_prog = p
        return nc

    sems_holder = sems

    sync_dma(idf[:], IDF, [], ["idf"], s_misc)
    cast_dma(idb[:], IDF, [], ["idb"], s_miscp)
    p.op("dve", lambda e: e.memset(ones1[:], 1.0), writes=["ones1"])
    p.op("dve", lambda e: e.memset(onesf[:], 1.0), writes=["onesf"])
    p.op("dve", lambda e: e.memset(epsc[:, 0:1], 1e-6), writes=[("epsc", 0)])
    p.op("dve", lambda e: e.memset(epsc[:, 1:2], 1e-5), writes=[("epsc", 1)])

    pt0 = tm.psum("pt0", [128, 1024], BF16)
    pt1 = tm.psum("pt1", [128, 1024], BF16)
    pts = [pt0, pt1]
    pB = tm.psum("pB", [128, 512])
    pK = tm.psum("pK", [128, 512])
    pV = tm.psum("pV", [128, 512])

    wkT = tm.sbuf("wkT", [128, ND * 512], BF16)
    wvT = tm.sbuf("wvT", [128, ND * 512], BF16)
    wstg = [tm.sbuf(f"wstg{i}", [128, D], BF16) for i in range(2)]
    s_wstg = [p.slot(f"wstg{i}") for i in range(2)]
    KT = [tm.sbuf(f"KT{i}", [128, TEXT], BF16) for i in range(4)]
    VA = [tm.sbuf(f"VA{i}", [128, NS * 65], BF16) for i in range(NKV)]
    kwb = tm.sbuf("kwb", [128, 512], F32)
    row1 = tm.sbuf("row1", [1, 2048], BF16)

    def wT_build(W, nrow_tiles, dstT, dname):
        for r in range(nrow_tiles):
            b = r % 2
            cast_dma(wstg[b][:], W[128 * r:128 * (r + 1), :],
                     [], [f"wstg{b}"], s_wstg[b])
            for dc in range(ND):
                pb = pts[dc // 8]
                p.op("pe", lambda e, b=b, dc=dc, pb=pb: e.transpose(
                    pb[:, 128 * (dc % 8):128 * (dc % 8 + 1)],
                    wstg[b][:, 128 * dc:128 * (dc + 1)], idb[:]),
                    reads=[f"wstg{b}", "idb"], writes=[(pb.name, dc % 8)])
            for half in range(2):
                pb = pts[half]
                dst3 = dstT[:].rearrange("p (dc o) -> p dc o", o=512)[
                    :, 8 * half:8 * (half + 1), 128 * r:128 * (r + 1)]
                src3 = pb[:].rearrange("p (dc o) -> p dc o", o=128)
                p.op("dve", lambda e, dst3=dst3, src3=src3: e.tensor_copy(
                    dst3, src3),
                    reads=[pb.name, (pb.name, 0), (pb.name, 1), (pb.name, 2),
                           (pb.name, 3), (pb.name, 4), (pb.name, 5),
                           (pb.name, 6), (pb.name, 7)],
                    writes=[(dname, (r, half))])

    wT_build(WK, 4, wkT, "wkT")
    wT_build(WV, 4, wvT, "wvT")

    cast_dma(row1[0:1, 0:512], KNW, [], [("row1", "k")], p.slot("knw"))
    p.op("pe", lambda e: e.matmul(pB[:], ones1[:], row1[0:1, 0:512],
                                  start=True, stop=True),
         reads=[("row1", "k"), "ones1"], writes=["pB"])
    p.op("dve", lambda e: e.tensor_copy(kwb[:], pB[:]), reads=["pB"],
         writes=["kwb"])

    xstg = [tm.sbuf(f"xstg{i}", [128, D], BF16) for i in range(2)]
    s_xstg = [p.slot(f"xstg{i}") for i in range(2)]
    xTc = [tm.sbuf(f"xTc{i}", [128, D], BF16) for i in range(2)]
    ksqs = tm.sbuf("ksqs", [128, 512], F32)

    p.op("dve", lambda e: e.memset(knacc[:], 0.0), writes=["knacc"])

    def transpose_tile(srcname, src, key=None):
        rkey = srcname if key is None else (srcname, key)
        for dc in range(ND):
            pb = pts[dc // 8]
            p.op("pe", lambda e, dc=dc, pb=pb, src=src: e.transpose(
                pb[:, 128 * (dc % 8):128 * (dc % 8 + 1)],
                src[:, 128 * dc:128 * (dc + 1)], idb[:]),
                reads=[rkey, "idb"], writes=[(pb.name, dc % 8)])

    def evac_xTc(b):
        for half in range(2):
            pb = pts[half]
            p.op("dve", lambda e, b=b, half=half, pb=pb: e.tensor_copy(
                xTc[b][:, 1024 * half:1024 * (half + 1)], pb[:]),
                reads=[pb.name] + [(pb.name, i) for i in range(8)],
                writes=[(f"xTc{b}", half)])

    for j in range(NS):
        b = j % 2
        cast_dma(xstg[b][:], XE[128 * j:128 * (j + 1), :], [], [f"xstg{b}"],
                 s_xstg[b])
        transpose_tile(f"xstg{b}", xstg[b][:])
        evac_xTc(b)
        for dc in range(ND):
            p.op("pe", lambda e, b=b, dc=dc: e.matmul(
                pK[:], xTc[b][:, 128 * dc:128 * (dc + 1)],
                wkT[:, 512 * dc:512 * (dc + 1)],
                start=(dc == 0), stop=(dc == ND - 1)),
                reads=[(f"xTc{b}", dc // 8), "wkT"], writes=["pK"])
        p.op("act", lambda e: e.activation(ksqs[:], pK[:], AF.Square),
             reads=["pK"], writes=["ksqs"])
        p.op("dve", lambda e: e.reduce_sum(small[:, 0:1], ksqs[:], AX.X),
             reads=["ksqs"], writes=[("small", 0)])
        p.op("act", lambda e: e.activation(small[:, 1:2], small[:, 0:1],
                                           AF.Sqrt, bias=epsc[:, 0:1],
                                           scale=1.0 / 512),
             reads=[("small", 0), ("epsc", 0)], writes=[("small", 1)])
        p.op("dve", lambda e: e.reciprocal(small[:, 2:3], small[:, 1:2]),
             reads=[("small", 1)], writes=[("small", 2)])
        p.op("dve", lambda e, b=b: e.scalar_tensor_tensor(
            xstg[b][:, 0:512], pK[:], small[:, 2:3], kwb[:],
            ALU.mult, ALU.mult),
            reads=["pK", ("small", 2), "kwb"], writes=[(f"xstg{b}", "ks")])
        p.op("act", lambda e, b=b: e.activation(ksqs[:], xstg[b][:, 0:512],
                                                AF.Square),
             reads=[(f"xstg{b}", "ks")], writes=["ksqs"])
        p.op("dve", lambda e: e.reduce_sum(
            small[:, 8:16], ksqs[:].rearrange("p (g x) -> p g x", g=8), AX.X),
            reads=["ksqs"], writes=[("small", 8)])
        p.op("dve", lambda e: e.tensor_tensor(knacc[:], knacc[:],
                                              small[:, 8:16], ALU.max),
             reads=[("small", 8), "knacc"], writes=["knacc"])
        for c in range(4):
            p.op("pe", lambda e, b=b, c=c: e.transpose(
                pt0[:, 128 * c:128 * (c + 1)],
                xstg[b][:, 128 * c:128 * (c + 1)], idb[:]),
                reads=[(f"xstg{b}", "ks"), "idb"], writes=[("pt0", c)])
        for c in range(4):
            p.op("dve", lambda e, c=c, j=j: e.tensor_copy(
                KT[c][:, 128 * j:128 * (j + 1)], pt0[:, 128 * c:128 * (c + 1)]),
                reads=[("pt0", c)], writes=[(f"KT{c}", j)])
        for dc in range(ND):
            p.op("pe", lambda e, b=b, dc=dc: e.matmul(
                pV[:], xTc[b][:, 128 * dc:128 * (dc + 1)],
                wvT[:, 512 * dc:512 * (dc + 1)],
                start=(dc == 0), stop=(dc == ND - 1)),
                reads=[(f"xTc{b}", dc // 8), "wvT"], writes=["pV"])
        for kv in range(NKV):
            p.op("dve", lambda e, kv=kv, j=j: e.tensor_copy(
                VA[kv][:, 65 * j:65 * j + 64], pV[:, 64 * kv:64 * (kv + 1)]),
                reads=["pV"], writes=[(f"VA{kv}", j)])
        for kv in range(NKV):
            p.op("dve", lambda e, kv=kv, j=j: e.memset(
                VA[kv][:, 65 * j + 64:65 * j + 65], 1.0),
                writes=[(f"VA{kv}", j)])

    if stop_after == "A1":
        return finish_dummy("dummyA")

    p.op("pe", lambda e: e.transpose(pB[0:8, 0:128], knacc[:], idf[:]),
         reads=["knacc", "idf"], writes=["pB"])
    p.op("dve", lambda e: e.tensor_copy(knscr[0:8, 0:128], pB[0:8, 0:128]),
         reads=["pB"], writes=["knscr"])
    p.op("dve", lambda e: e.reduce_max(small[0:8, 3:4], knscr[0:8, 0:128],
                                       AX.X),
         reads=["knscr"], writes=[("small", 3)])
    p.op("act", lambda e: e.activation(small[0:8, 4:5], small[0:8, 3:4],
                                       AF.Sqrt), reads=[("small", 3)],
         writes=[("small", 4)])
    p.op("dve", lambda e: e.tensor_scalar(small[0:8, 5:6], small[0:8, 4:5],
                                          -1.0, None, ALU.mult),
         reads=[("small", 4)], writes=[("small", 5)])
    p.op("pe", lambda e: e.transpose(pB[0:1, 128:136], small[0:8, 5:6],
                                     idf[0:8, 0:8]),
         reads=[("small", 5), "idf"], writes=["pB"])
    p.op("dve", lambda e: e.tensor_copy(knscr[0:1, 128:136],
                                        pB[0:1, 128:136]),
         reads=["pB"], writes=["knscr"])
    s_kn = p.slot("knr")
    for m_ in range(2):
        for r_ in range(4):
            sync_dma(knscr[0:1, 160 + 2 * r_ + m_:192:8],
                     knscr[0:1, 128 + m_:136:2],
                     ["knscr"], ["knscr"], s_kn,
                     allow_slow_non_contiguous=True)
    p.op("pe", lambda e: e.matmul(pB[:, 0:32], onesf[:], knscr[0:1, 160:192],
                                  start=True, stop=True),
         reads=["knscr", "onesf"], writes=["pB"])
    p.op("dve", lambda e: e.tensor_copy(knbneg[:], pB[:, 0:32]),
         reads=["pB"], writes=["knbneg"])

    tm.free(wstg[1], wstg[0], wkT, wvT)
    tm.free(pV, pK)
    pQs = [tm.psum(f"pQ{i}", [128, 512]) for i in range(4)]

    if stop_after == "KN":
        return finish_dummy("dummyA")

    wqT = [tm.sbuf(f"wqT{i}", [128, D], BF16) for i in range(ND)]
    wstgq = [tm.sbuf(f"wstgq{i}", [128, D], BF16) for i in range(2)]
    s_wstgq = [p.slot(f"wstgq{i}") for i in range(2)]
    for r in range(ND):
        b = r % 2
        cast_dma(wstgq[b][:], WQ[128 * r:128 * (r + 1), :], [], [f"wstgq{b}"],
                 s_wstgq[b])
        transpose_tile(f"wstgq{b}", wstgq[b][:])
        for dc in range(ND):
            pb = pts[dc // 8]
            p.op("dve", lambda e, r=r, dc=dc, pb=pb: e.tensor_copy(
                wqT[dc][:, 128 * r:128 * (r + 1)],
                pb[:, 128 * (dc % 8):128 * (dc % 8 + 1)]),
                reads=[(pb.name, dc % 8)], writes=[(f"wqT{dc}", r)])

    if stop_after == "WQT":
        return finish_dummy("dummyA")

    qwb = tm.sbuf("qwb", [128, D], F32)
    cast_dma(row1[0:1, 0:2048], QNW, [], [("row1", "q")], p.slot("qnw"))
    for oc in range(4):
        p.op("pe", lambda e, oc=oc: e.matmul(
            pB[:], ones1[:], row1[0:1, 512 * oc:512 * (oc + 1)],
            start=True, stop=True),
            reads=[("row1", "q"), "ones1"], writes=["pB"])
        p.op("act", lambda e, oc=oc: e.activation(
            qwb[:, 512 * oc:512 * (oc + 1)], pB[:], AF.Copy, scale=0.125),
            reads=["pB"], writes=[("qwb", oc)])

    if stop_after == "QWB":
        return finish_dummy("dummyA")

    QT = [tm.sbuf(f"QT{i}", [128, TC], BF16) for i in range(ND)]
    Qsb = tm.sbuf("Qsb", [128, D], F32)
    sqs = tm.sbuf("sqs", [128, D], F32)

    _nt = NT
    _ql = 99
    if stop_after and stop_after.startswith("Q") and len(stop_after) == 2:
        _nt = int(stop_after[1]) + 1
    if stop_after and stop_after.startswith("QL"):
        _nt, _ql = 1, int(stop_after[2:])

    def q_head(t):
        b = t % 2
        cast_dma(xstg[b][:], XE[TC + 128 * t:TC + 128 * (t + 1), :], [],
                 [f"xstg{b}"], s_xstg[b])
        transpose_tile(f"xstg{b}", xstg[b][:])
        evac_xTc(b)
        if _ql < 2:
            return
        for oc in range(4):
            pQ = pQs[oc]
            for dc in range(ND):
                p.op("pe", lambda e, b=b, dc=dc, oc=oc, pQ=pQ: e.matmul(
                    pQ[:], xTc[b][:, 128 * dc:128 * (dc + 1)],
                    wqT[dc][:, 512 * oc:512 * (oc + 1)],
                    start=(dc == 0), stop=(dc == ND - 1)),
                    reads=[(f"xTc{b}", dc // 8), f"wqT{dc}"], writes=[pQ.name])

    def q_evac(t):
        qc = 20 + 4 * (t % 2)
        if _ql < 2:
            return
        for oc in range(4):
            pQ = pQs[oc]
            p.op("dve", lambda e, oc=oc, pQ=pQ: e.tensor_copy(
                Qsb[:, 512 * oc:512 * (oc + 1)], pQ[:]),
                reads=[pQ.name], writes=[("Qsb", oc)])
            if _ql < 3:
                continue
            p.op("act", lambda e, oc=oc, pQ=pQ: e.activation(
                sqs[:, 512 * oc:512 * (oc + 1)], pQ[:], AF.Square),
                reads=[pQ.name], writes=[("sqs", oc)])
        if _ql < 4:
            return
        p.op("dve", lambda e, qc=qc: e.reduce_sum(
            small[:, qc:qc + 1], sqs[:], AX.X),
            reads=["sqs", ("sqs", 0), ("sqs", 1), ("sqs", 2), ("sqs", 3)],
            writes=[("small", qc)])
        p.op("act", lambda e, qc=qc: e.activation(
            small[:, qc + 1:qc + 2], small[:, qc:qc + 1],
            AF.Sqrt, bias=epsc[:, 0:1], scale=1.0 / D),
            reads=[("small", qc), ("epsc", 0)], writes=[("small", qc + 1)])
        p.op("dve", lambda e, qc=qc: e.reciprocal(
            small[:, qc + 2:qc + 3], small[:, qc + 1:qc + 2]),
            reads=[("small", qc + 1)], writes=[("small", qc + 2)])

    def q_tail(t):
        b = t % 2
        qc = 20 + 4 * (t % 2)
        if _ql < 5:
            return
        p.op("dve", lambda e, b=b, qc=qc: e.scalar_tensor_tensor(
            xstg[b][:], Qsb[:], small[:, qc + 2:qc + 3], qwb[:],
            ALU.mult, ALU.mult),
            reads=[("Qsb", 0), ("Qsb", 1), ("Qsb", 2), ("Qsb", 3),
                   ("small", qc + 2), "qwb"],
            writes=[(f"xstg{b}", "qh")])
        if _ql < 6:
            return
        p.op("act", lambda e, b=b: e.activation(sqs[:], xstg[b][:], AF.Square),
             reads=[(f"xstg{b}", "qh")],
             writes=["sqs", ("sqs", 0), ("sqs", 1), ("sqs", 2), ("sqs", 3)])
        p.op("dve", lambda e: e.reduce_sum(
            small[:, 32:64], sqs[:].rearrange("p (g x) -> p g x", g=32), AX.X),
            reads=["sqs"], writes=[("small", 32)])
        p.op("act", lambda e: e.activation(small[:, 32:64], small[:, 32:64],
                                           AF.Sqrt),
             reads=[("small", 32)], writes=[("small", 32)])
        if _ql < 7:
            return
        p.op("dve", lambda e: e.tensor_tensor(small[:, 32:64], small[:, 32:64],
                                              knbneg[:], ALU.mult),
             reads=[("small", 32), "knbneg"], writes=[("small", 32)])
        p.op("pe", lambda e: e.transpose(pB[0:32, 0:128], small[:, 32:64],
                                         idf[:]),
             reads=[("small", 32), "idf"], writes=["pB"])
        p.op("dve", lambda e, t=t: e.tensor_copy(
            m_all[:, 128 * t:128 * (t + 1)], pB[0:32, 0:128]),
            reads=["pB"], writes=[("m_all", t)])
        if _ql < 8:
            return
        transpose_tile(f"xstg{b}", xstg[b][:], key="qh")
        for dc in range(ND):
            pb = pts[dc // 8]
            p.op("dve", lambda e, t=t, dc=dc, pb=pb: e.tensor_copy(
                QT[dc][:, 128 * t:128 * (t + 1)],
                pb[:, 128 * (dc % 8):128 * (dc % 8 + 1)]),
                reads=[(pb.name, dc % 8)], writes=[(f"QT{dc}", t)])

    for t in range(_nt):
        q_head(t)
        if t > 0:
            q_tail(t - 1)
        q_evac(t)
    q_tail(_nt - 1)

    if stop_after and stop_after.startswith("Q"):
        return finish_dummy("dummyA")

    tm.free(Qsb, sqs, qwb, kwb, ksqs, row1, *wqT, *xstg, *xTc, *wstgq)
    tm.free(pQs[3], pQs[2], pQs[1], pQs[0], pB, pt1, pt0)

    def tap_dump(name, tiles, nrow, regions_of, width=TC):
        if name not in tap_outs:
            return
        ttmp = tm.sbuf("ttmp_" + name, [nrow, width], F32)
        s_tap = p.slot("tap_" + name)
        for c, tile in enumerate(tiles):
            p.op("act", lambda e, tile=tile: e.activation(ttmp[:], tile[:],
                                                          AF.Copy),
                 reads=regions_of(c), writes=["ttmp_" + name])
            sync_dma(tap_outs[name][nrow * c:nrow * (c + 1), :], ttmp[:],
                     ["ttmp_" + name], [("OUT", f"{name}{c}")], s_tap)
        tm.free(ttmp)

    if stop_after == "A":
        return finish_dummy("dummyA")

    tap_dump("KT", KT, 128, lambda c: [f"KT{c}"] + [(f"KT{c}", j) for j in range(NS)], width=TEXT)
    tap_dump("QT", QT, 128, lambda c: [f"QT{c}"] + [(f"QT{c}", t) for t in range(NT)])
    if "MALL" in tap_outs:
        matmp = tm.sbuf("matmp", [NH, TC], F32)
        p.op("act", lambda e: e.activation(matmp[:], m_all[:], AF.Copy),
             reads=["m_all"] + [("m_all", t) for t in range(NT)],
             writes=["matmp"])
        sync_dma(tap_outs["MALL"], matmp[:], ["matmp"], [("OUT", "matap")],
                 p.slot("tap_MALL"))
        tm.free(matmp)

    Mm = tm.sbuf("Mm", [128, 1152], F32)
    sync_dma(Mm[:], MM, [], ["Mm"], p.slot("mm"))
    poi = tm.sbuf("poi", [128, NS], F32)
    sync_dma(poi[:], POI, [], ["poi"], p.slot("poi"))
    Hb = [tm.sbuf(f"Hb{i}", [128, 1152], BF16) for i in range(2)]
    Fring = [tm.sbuf(f"Fr{i}", [128, TC], BF16) for i in range(10)]
    rsb = tm.sbuf("rsb", [32, TC], F32)
    rsb2 = tm.sbuf("rsb2", [32, TC], F32)
    
    rrecb = tm.sbuf("rrecb", [1, TC], BF16)
    rbsb = tm.sbuf("rbsb", [64, TC], BF16)
    AOsb = [tm.sbuf(f"AO{i}", [128, TC], BF16) for i in range(ND)]

    p.op("dve", lambda e: e.memset(rsb[:], 0.0), writes=[("rsb", None)])
    p.op("dve", lambda e: e.memset(rsb2[:], 0.0), writes=["rsb2"])
    Sps = [tm.psum(f"Sps{i}", [128, 1024]) for i in range(2)]
    AOps = tm.psum("AOps", [65, 1024])
    rbps = tm.psum("rbps", [64, 1024])

    s_mrow = [p.slot("mrowd0"), p.slot("mrowd1")]
    mall_regions = ["m_all"] + [("m_all", t) for t in range(NT)]

    def _av_batch(tt, kv):
        for jj in range(tt, tt + 9):
            tlo_jj = 128 * max(0, jj - 8)
            frj = Fring[jj % 10]
            p.op("pe", lambda e, tt=tt, jj=jj, kv=kv, frj=frj, tlo_jj=tlo_jj:
                 e.matmul(AOps[:, 128 * tt:128 * (tt + 1)],
                          VA[kv][:, 65 * jj:65 * (jj + 1)],
                          frj[:, 128 * tt - tlo_jj:128 * (tt + 1) - tlo_jj],
                          start=(jj == tt), stop=(jj == tt + 8)),
                 reads=[(f"VA{kv}", jj), frj.name],
                 writes=[("AOps", tt)])

    for sslot in range(NH):
        jt = sslot // 2
        bb = sslot % 2
        h = 8 * (jt // 4) + (jt % 4) + 4 * bb
        hb = sslot % 2
        kv = h // 4
        ktile = KT[kv // 2]
        kbase = 64 * (kv % 2)
        qtile = QT[jt]
        qbase = 64 * bb
        p.op("dve", lambda e, hb=hb, h=h: e.tensor_scalar(
            Hb[hb][:], Mm[:], SLOPES[h], None, ALU.mult),
            reads=["Mm"], writes=[f"Hb{hb}"])
        sync_dma(mrows[hb][:], m_all[sslot:sslot + 1, :], mall_regions,
                 [f"mrow{hb}"], s_mrow[hb])
        for j in range(NS):
            sb = Sps[j % 2]
            t_lo = 128 * max(0, j - 8)
            t_hi = 128 * (min(j, 7) + 1)
            span = t_hi - t_lo
            i_lo = t_lo + 1024 - 128 * j
            for off in range(0, span, 512):
                w = min(512, span - off)
                p.op("pe", lambda e, sb=sb, off=off, w=w, hb=hb, i_lo=i_lo:
                     e.matmul(sb[:, off:off + w], idb[:],
                              Hb[hb][:, i_lo + off:i_lo + off + w],
                              start=True, stop=False),
                     reads=[f"Hb{hb}", "idb"], writes=[sb.name])
                p.op("pe", lambda e, sb=sb, off=off, w=w, j=j, t_lo=t_lo,
                     ktile=ktile, kbase=kbase, qtile=qtile, qbase=qbase:
                     e.matmul(sb[:, off:off + w],
                              ktile[kbase:kbase + 64, 128 * j:128 * (j + 1)],
                              qtile[qbase:qbase + 64,
                                    t_lo + off:t_lo + off + w],
                              start=False, stop=False),
                     reads=[(ktile.name, j), qtile.name], writes=[sb.name])
                p.op("pe", lambda e, sb=sb, off=off, w=w, hb=hb, t_lo=t_lo:
                     e.matmul(sb[:, off:off + w], ones1[:],
                              mrows[hb][0:1, t_lo + off:t_lo + off + w],
                              start=False, stop=True),
                     reads=[f"mrow{hb}", "ones1"], writes=[sb.name])
            fr = Fring[j % 10]
            p.op("act", lambda e, sb=sb, span=span, j=j, fr=fr: e.activation(
                fr[:, 0:span], sb[:, 0:span], AF.Exp,
                bias=poi[:, j:j + 1]),
                reads=[sb.name, "poi"], writes=[fr.name])
            if j >= 9:
                _av_batch(j - 9, kv)
        for tt in (7,):
            _av_batch(tt, kv)
        p.op("dve", lambda e: e.tensor_copy(rsb[0:1, :], AOps[64:65, :]),
             reads=[("AOps", None)] + [("AOps", tt) for tt in range(8)],
             writes=[("rsb", None)])
        p.op("dve", lambda e: e.transpose(rsb2[:], rsb[:]),
             reads=[("rsb", None)], writes=["rsb2"])
        rview = rsb2[:].rearrange("p (b c) -> p b c", c=32)[:, :, 0:1]
        p.op("dve", lambda e: e.reciprocal(rview, rview),
             reads=["rsb2"], writes=["rsb2"])
        p.op("dve", lambda e: e.transpose(rsb[:], rsb2[:]),
             reads=["rsb2"], writes=[("rsb", None)])
        p.op("dve", lambda e: e.tensor_copy(rrecb[:], rsb[0:1, :]),
             reads=[("rsb", None)], writes=["rrecb"])
        for half in range(2):
            p.op("pe", lambda e, half=half: e.matmul(
                rbps[:, 512 * half:512 * (half + 1)], ones1[0:1, 0:64],
                rrecb[0:1, 512 * half:512 * (half + 1)],
                start=True, stop=True),
                reads=["rrecb", "ones1"], writes=["rbps"])
        p.op("dve", lambda e: e.tensor_copy(rbsb[:], rbps[:]),
             reads=["rbps"], writes=["rbsb"])
        p.op("dve", lambda e, jt=jt, bb=bb: e.tensor_tensor(
            AOsb[jt][64 * bb:64 * bb + 64, :], AOps[0:64, :],
            rbsb[:], ALU.mult),
            reads=[("AOps", None)] + [("AOps", tt) for tt in range(8)] +
                  ["rbsb"],
            writes=[(f"AO{jt}", bb)])

    tm.free(*QT, *KT, *VA, Mm, *Hb, *Fring, poi, rsb, rsb2, rrecb, rbsb,
            m_all, mrows[0], mrows[1])
    tm.free(rbps, AOps, Sps[1], Sps[0])

    if stop_after == "B":
        return finish_dummy("dummyB")

    tap_dump("AO", AOsb, 128, lambda c: [f"AO{c}", (f"AO{c}", 0), (f"AO{c}", 1)])

    ptc0 = tm.psum("ptc0", [128, 1024], BF16)
    ptc1 = tm.psum("ptc1", [128, 1024], BF16)
    ptcs = [ptc0, ptc1]
    pc = [tm.psum(f"pc{i}", [128, 1024]) for i in range(2)]

    ACC = [tm.sbuf(f"ACC{i}", [128, D], F32) for i in range(NT)]
    y1T = [tm.sbuf(f"y1T{i}", [128, TC], BF16) for i in range(ND)]
    woq = [tm.sbuf(f"woq{i}", [128, 4 * D], BF16) for i in range(2)]
    wosg = [tm.sbuf(f"wosg{i}", [128, 2048], BF16) for i in range(3)]
    s_wos = [p.slot(f"wos{i}") for i in range(3)]
    xrs = [tm.sbuf(f"xrs{i}", [128, D], F32) for i in range(2)]
    s_xrs = [p.slot(f"xrs{i}") for i in range(2)]
    ybf = tm.sbuf("ybf", [128, D], BF16)

    WOr = WO.rearrange("(a p) o -> p a o", p=128)
    for oq in range(4):
        qb = oq % 2
        for g in range(4):
            b2 = (4 * oq + g) % 3
            cast_dma(
                wosg[b2][:].rearrange("p (a o) -> p a o", o=512)[:, 0:4, :],
                WOr[:, 4 * g:4 * (g + 1), 512 * oq:512 * (oq + 1)],
                [], [f"wosg{b2}"], s_wos[b2])
            for ai in range(4):
                a = 4 * g + ai
                for c in range(4):
                    pb = ptcs[c % 2]
                    p.op("pe", lambda e, b2=b2, ai=ai, a=a, c=c, pb=pb:
                         e.transpose(
                             pb[:, 512 * (a % 2) + 128 * (c // 2):
                                512 * (a % 2) + 128 * (c // 2) + 128],
                             wosg[b2][:, 512 * ai + 128 * c:
                                     512 * ai + 128 * (c + 1)],
                             idb[:]),
                         reads=[f"wosg{b2}", "idb"],
                         writes=[(pb.name, (a % 2, c // 2))])
                    p.op("dve", lambda e, qb=qb, a=a, c=c, pb=pb:
                         e.tensor_copy(
                             woq[qb][:, D * c + 128 * a:D * c + 128 * (a + 1)],
                             pb[:, 512 * (a % 2) + 128 * (c // 2):
                                512 * (a % 2) + 128 * (c // 2) + 128]),
                         reads=[(pb.name, (a % 2, c // 2))],
                         writes=[(f"woq{qb}", (a, c))])
        for t in range(NT):
            tb = t % 2
            if oq == 0:
                sync_dma(xrs[tb][:], XE[TC + 128 * t:TC + 128 * (t + 1), :],
                         [], [f"xrs{tb}"], s_xrs[tb])
            for dhalf in range(2):
                for c in range(4):
                    oc = 4 * oq + c
                    for dd in range(2):
                        p.op("pe", lambda e, qb=qb, c=c, t=t, oc=oc,
                             dhalf=dhalf, dd=dd: e.matmul(
                                 pc[t % 2][:, 512 * dd:512 * (dd + 1)],
                                 AOsb[oc][:, 128 * t:128 * (t + 1)],
                                 woq[qb][:, D * c + 1024 * dhalf + 512 * dd:
                                         D * c + 1024 * dhalf + 512 * (dd + 1)],
                                 start=(c == 0), stop=(c == 3)),
                             reads=[f"AO{oc}", (f"AO{oc}", 0), (f"AO{oc}", 1),
                                    f"woq{qb}"],
                             writes=[pc[t % 2].name])
                hk = "lo" if dhalf == 0 else "hi"
                if oq == 0:
                    p.op("dve", lambda e, t=t, tb=tb, dhalf=dhalf: e.tensor_tensor(
                        ACC[t][:, 1024 * dhalf:1024 * (dhalf + 1)],
                        xrs[tb][:, 1024 * dhalf:1024 * (dhalf + 1)],
                        pc[t % 2][:], ALU.add),
                        reads=[f"xrs{tb}", pc[t % 2].name],
                        writes=[(f"ACC{t}", hk)])
                else:
                    p.op("dve", lambda e, t=t, dhalf=dhalf: e.tensor_tensor(
                        ACC[t][:, 1024 * dhalf:1024 * (dhalf + 1)],
                        ACC[t][:, 1024 * dhalf:1024 * (dhalf + 1)],
                        pc[t % 2][:], ALU.add),
                        reads=[pc[t % 2].name, (f"ACC{t}", hk)],
                        writes=[(f"ACC{t}", hk)])

    def layernorm_tile(t):
        acc_h = ACC[t]
        tname = f"ACC{t}"
        for c in range(4):
            p.op("dve", lambda e, c=c: e.bn_stats(
                bns[:, 6 * c:6 * (c + 1)], acc_h[:, 512 * c:512 * (c + 1)]),
                reads=[(tname, "lo"), (tname, "hi")], writes=[("bns", c)])
        p.op("dve", lambda e: e.bn_aggr(aggr[:], bns[:]),
             reads=[("bns", 0), ("bns", 1), ("bns", 2), ("bns", 3)],
             writes=["aggr"])
        p.op("act", lambda e: e.activation(small[:, 24:25], aggr[:, 1:2],
                                           AF.Sqrt, bias=epsc[:, 1:2]),
             reads=["aggr", ("epsc", 1)], writes=[("small", 24)])
        p.op("dve", lambda e: e.reciprocal(small[:, 25:26], small[:, 24:25]),
             reads=[("small", 24)], writes=[("small", 25)])
        p.op("dve", lambda e: e.tensor_scalar(
            acc_h[:], acc_h[:], aggr[:, 0:1], small[:, 25:26],
            ALU.subtract, ALU.mult),
            reads=[(tname, "lo"), (tname, "hi"), "aggr", ("small", 25)],
            writes=[(tname, "lo"), (tname, "hi")])

    for t in range(NT):
        layernorm_tile(t)
        p.op("act", lambda e, t=t: e.activation(ybf[:], ACC[t][:], AF.Copy),
             reads=[(f"ACC{t}", "lo"), (f"ACC{t}", "hi")], writes=["ybf"])
        for dc in range(ND):
            pb = ptcs[dc // 8]
            p.op("pe", lambda e, dc=dc, pb=pb: e.transpose(
                pb[:, 128 * (dc % 8):128 * (dc % 8 + 1)],
                ybf[:, 128 * dc:128 * (dc + 1)], idb[:]),
                reads=["ybf", "idb"], writes=[(pb.name, dc % 8)])
        for dc in range(ND):
            pb = ptcs[dc // 8]
            p.op("dve", lambda e, t=t, dc=dc, pb=pb: e.tensor_copy(
                y1T[dc][:, 128 * t:128 * (t + 1)],
                pb[:, 128 * (dc % 8):128 * (dc % 8 + 1)]),
                reads=[(pb.name, dc % 8)], writes=[(f"y1T{dc}", t)])

    tm.free(*AOsb, *woq, *wosg, *xrs, ybf)

    if "Y1" in tap_outs:
        for t in range(NT):
            sync_dma(tap_outs["Y1"][128 * t:128 * (t + 1), :], ACC[t][:],
                     [(f"ACC{t}", "lo"), (f"ACC{t}", "hi")],
                     [("OUT", f"y1tap{t}")], p.slot("tap_Y1"), gid="y1g")

    if stop_after == "C":
        return finish_acc(ACC)

    gwbf = tm.sbuf("gwbf", [NE, D], BF16)
    gwT = tm.sbuf("gwT", [128, 128], BF16)
    cast_dma(gwbf[:], GW, [], ["gwbf"], p.slot("gw"))
    pL = tm.psum("pL", [128, 512])
    for dc in range(ND):
        p.op("pe", lambda e, dc=dc: e.transpose(
            ptc0[:, 8 * dc:8 * (dc + 1)],
            gwbf[:, 128 * dc:128 * (dc + 1)], idb[0:8, 0:8]),
            reads=["gwbf", "idb"], writes=[("ptc0", ("g", dc))])
        p.op("dve", lambda e, dc=dc: e.tensor_copy(
            gwT[:, 8 * dc:8 * (dc + 1)], ptc0[:, 8 * dc:8 * (dc + 1)]),
            reads=[("ptc0", ("g", dc))], writes=[("gwT", dc)])

    for t in range(NT):
        for dc in range(ND):
            p.op("pe", lambda e, t=t, dc=dc: e.matmul(
                pL[:, 0:8], y1T[dc][:, 128 * t:128 * (t + 1)],
                gwT[:, 8 * dc:8 * (dc + 1)],
                start=(dc == 0), stop=(dc == ND - 1)),
                reads=[(f"y1T{dc}", t), ("gwT", dc)], writes=["pL"])
        g0 = 40
        gsl = gates8[:, 8 * t:8 * (t + 1)]
        p.op("dve", lambda e: e.reduce_max(small[:, g0:g0 + 1], pL[:, 0:8],
                                           AX.X, negate=True),
             reads=["pL"], writes=[("small", g0)])
        p.op("act", lambda e, gsl=gsl: e.activation(
            gsl, pL[:, 0:8], AF.Exp, bias=small[:, g0:g0 + 1]),
            reads=["pL", ("small", g0)], writes=[("gates8", t)])
        p.op("dve", lambda e, gsl=gsl: e.reduce_sum(
            small[:, g0 + 1:g0 + 2], gsl, AX.X),
            reads=[("gates8", t)], writes=[("small", g0 + 1)])
        p.op("dve", lambda e: e.reciprocal(small[:, g0 + 2:g0 + 3],
                                           small[:, g0 + 1:g0 + 2]),
             reads=[("small", g0 + 1)], writes=[("small", g0 + 2)])
        p.op("dve", lambda e, gsl=gsl: e.tensor_scalar(
            gsl, gsl, small[:, g0 + 2:g0 + 3], None, ALU.mult),
            reads=[("gates8", t), ("small", g0 + 2)], writes=[("gates8", t)])
        p.op("dve", lambda e, gsl=gsl: e.reduce_max(
            small[:, g0 + 3:g0 + 4], gsl, AX.X),
            reads=[("gates8", t)], writes=[("small", g0 + 3)])
        p.op("dve", lambda e, gsl=gsl: e.tensor_scalar(
            small[:, 48:56], gsl, small[:, g0 + 3:g0 + 4], 1e30,
            ALU.is_ge, ALU.mult),
            reads=[("gates8", t), ("small", g0 + 3)], writes=[("small", 48)])
        p.op("dve", lambda e, gsl=gsl: e.tensor_tensor(
            small[:, 48:56], gsl, small[:, 48:56], ALU.subtract),
            reads=[("gates8", t), ("small", 48)], writes=[("small", 48)])
        p.op("dve", lambda e: e.reduce_max(small[:, g0 + 4:g0 + 5],
                                           small[:, 48:56], AX.X),
             reads=[("small", 48)], writes=[("small", g0 + 4)])
        p.op("dve", lambda e: e.tensor_tensor(small[:, g0 + 5:g0 + 6],
                                              small[:, g0 + 3:g0 + 4],
                                              small[:, g0 + 4:g0 + 5],
                                              ALU.add),
             reads=[("small", g0 + 3), ("small", g0 + 4)],
             writes=[("small", g0 + 5)])
        p.op("dve", lambda e: e.reciprocal(small[:, g0 + 6:g0 + 7],
                                           small[:, g0 + 5:g0 + 6]),
             reads=[("small", g0 + 5)], writes=[("small", g0 + 6)])
        p.op("dve", lambda e, gsl=gsl: e.tensor_scalar(
            small[:, 48:56], gsl, small[:, g0 + 4:g0 + 5], None, ALU.is_ge),
            reads=[("gates8", t), ("small", g0 + 4)], writes=[("small", 48)])
        p.op("dve", lambda e, gsl=gsl: e.tensor_tensor(
            gsl, gsl, small[:, 48:56], ALU.mult),
            reads=[("gates8", t), ("small", 48)], writes=[("gates8", t)])
        p.op("dve", lambda e, gsl=gsl: e.tensor_scalar(
            gsl, gsl, small[:, g0 + 6:g0 + 7], None, ALU.mult),
            reads=[("gates8", t), ("small", g0 + 6)], writes=[("gates8", t)])
        p.op("pe", lambda e, gsl=gsl: e.transpose(
            pL[0:8, 128:256], gsl, idf[:]),
            reads=[("gates8", t), "idf"], writes=["pL"])
        p.op("dve", lambda e, t=t: e.tensor_copy(
            grow8[:, 128 * t:128 * (t + 1)], pL[0:8, 128:256]),
            reads=["pL"], writes=[("grow8", t)])

    if "GATES" in tap_outs:
        gtmp = tm.sbuf("gtmp", [128, NT * NE], F32)
        p.op("dve", lambda e: e.tensor_copy(gtmp[:], gates8[:]),
             reads=["gates8"] + [("gates8", t) for t in range(NT)],
             writes=["gtmp"])
        sync_dma(tap_outs["GATES"].rearrange("(t p) e -> p t e", p=128),
                 gtmp[:].rearrange("p (t e) -> p t e", e=NE),
                 ["gtmp"], [("OUT", "gtap")], p.slot("tap_GATES"),
                 allow_slow_non_contiguous=True)
        tm.free(gtmp)

    tm.free(gwbf, gwT)
    tm.free(pL, pc[1], pc[0], ptc1, ptc0)

    if stop_after == "D":
        return finish_acc(ACC)

    FGRP = [(0, 256), (256, 256), (512, 256), (768, 256), (1024, 256),
            (1280, 128)]
    wgb = [tm.sbuf(f"wgb{i}", [128, ND * 256], BF16) for i in range(3)]
    wub = [tm.sbuf(f"wub{i}", [128, ND * 256], BF16) for i in range(3)]
    s_wgb = [p.slot(f"wgb{i}") for i in range(3)]
    s_wub = [p.slot(f"wub{i}") for i in range(3)]
    wdb = [tm.sbuf(f"wdb{i}", [128, 11 * 256], BF16) for i in range(3)]
    s_wdb = [p.slot(f"wdb{i}") for i in range(3)]
    hT = [tm.sbuf(f"hT{i}", [128, TC], BF16) for i in range(11)]
    gbsb = tm.sbuf("gbsb", [128, TC], F32)
    gsb = tm.sbuf("gsb", [128, TC], BF16)
    usb = tm.sbuf("usb", [128, TC], BF16)
    pG = tm.psum("pG", [128, 1024])
    pU = tm.psum("pU", [128, 1024])
    pd = [tm.psum(f"pd{i}", [128, 512]) for i in range(2)]

    WGr = [WGATE[e].rearrange("(a p) f -> p a f", p=128) for e in range(NE)]
    WUr = [WUP[e].rearrange("(a p) f -> p a f", p=128) for e in range(NE)]
    WDr = [WDOWN[e].rearrange("(c p) d -> p c d", p=128) for e in range(NE)]
    WGr.append(SWG.rearrange("(a p) f -> p a f", p=128))
    WUr.append(SWU.rearrange("(a p) f -> p a f", p=128))
    WDr.append(SWD.rearrange("(c p) d -> p c d", p=128))
    grow_regions = ["grow8"] + [("grow8", t) for t in range(NT)]

    gi = 0
    di = 0
    for e_i in range(NE + 1):
        routed = e_i < NE
        for g, (f0, fw) in enumerate(FGRP):
            b = gi % 3
            gi += 1
            for blk in range(0, ND, 4):
                cast_dma(
                    wgb[b][:].rearrange("p (a f) -> p a f", f=256)
                    [:, blk:blk + 4, 0:fw],
                    WGr[e_i][:, blk:blk + 4, f0:f0 + fw],
                    [], [(f"wgb{b}", blk)], s_wgb[b], gid=("g", e_i, g))
                cast_dma(
                    wub[b][:].rearrange("p (a f) -> p a f", f=256)
                    [:, blk:blk + 4, 0:fw],
                    WUr[e_i][:, blk:blk + 4, f0:f0 + fw],
                    [], [(f"wub{b}", blk)], s_wub[b], gid=("g", e_i, g))
            for cc in range(fw // 128):
                fc = f0 // 128 + cc
                for th in range(2):
                    for a in range(ND):
                        p.op("pe", lambda e, b=b, a=a, cc=cc, th=th: e.matmul(
                            pG[:, 512 * th:512 * (th + 1)],
                            wgb[b][:, 256 * a + 128 * cc:
                                   256 * a + 128 * (cc + 1)],
                            y1T[a][:, 512 * th:512 * (th + 1)],
                            start=(a == 0), stop=(a == ND - 1)),
                            reads=[f"wgb{b}", f"y1T{a}"] +
                                  [(f"y1T{a}", t) for t in range(NT)],
                            writes=["pG"])
                    for a in range(ND):
                        p.op("pe", lambda e, b=b, a=a, cc=cc, th=th: e.matmul(
                            pU[:, 512 * th:512 * (th + 1)],
                            wub[b][:, 256 * a + 128 * cc:
                                   256 * a + 128 * (cc + 1)],
                            y1T[a][:, 512 * th:512 * (th + 1)],
                            start=(a == 0), stop=(a == ND - 1)),
                            reads=[f"wub{b}", f"y1T{a}"] +
                                  [(f"y1T{a}", t) for t in range(NT)],
                            writes=["pU"])
                p.op("act", lambda e: e.activation(gsb[:], pG[:], AF.Silu),
                     reads=["pG"], writes=["gsb"])
                p.op("dve", lambda e: e.tensor_copy(usb[:], pU[:]),
                     reads=["pU"], writes=["usb"])
                p.op("dve", lambda e, fc=fc: e.tensor_tensor(
                    hT[fc][:], gsb[:], usb[:], ALU.mult),
                    reads=["gsb", "usb"], writes=[f"hT{fc}"])
        for q in range(8):
            b = di % 3
            di += 1
            for blk, cnt in ((0, 4), (4, 4), (8, 3)):
                cast_dma(
                    wdb[b][:].rearrange("p (c d) -> p c d", d=256)
                    [:, blk:blk + cnt, :],
                    WDr[e_i][:, blk:blk + cnt, 256 * q:256 * (q + 1)],
                    [], [(f"wdb{b}", blk)], s_wdb[b], gid=("d", e_i, q))
            for t in range(NT):
                pdb = pd[t % 2]
                hk = "lo" if q < 4 else "hi"
                for fc in range(11):
                    p.op("pe", lambda e, b=b, fc=fc, t=t, pdb=pdb, q=q:
                         e.matmul(
                             pdb[:, 0:256], hT[fc][:, 128 * t:128 * (t + 1)],
                             wdb[b][:, 256 * fc:256 * (fc + 1)],
                             start=(fc == 0), stop=(fc == 10)),
                         reads=[f"hT{fc}", f"wdb{b}"],
                         writes=[pdb.name])
                if routed:
                    p.op("dve", lambda e, t=t, q=q, e_i=e_i, pdb=pdb:
                         e.scalar_tensor_tensor(
                             ACC[t][:, 256 * q:256 * (q + 1)], pdb[:, 0:256],
                             gates8[:, 8 * t + e_i:8 * t + e_i + 1],
                             ACC[t][:, 256 * q:256 * (q + 1)],
                             ALU.mult, ALU.add),
                         reads=[pdb.name, ("gates8", t), (f"ACC{t}", hk)],
                         writes=[(f"ACC{t}", hk)])
                else:
                    p.op("dve", lambda e, t=t, q=q, pdb=pdb: e.tensor_tensor(
                        ACC[t][:, 256 * q:256 * (q + 1)],
                        ACC[t][:, 256 * q:256 * (q + 1)], pdb[:, 0:256],
                        ALU.add),
                        reads=[pdb.name, (f"ACC{t}", hk)],
                        writes=[(f"ACC{t}", hk)])

    if "ACC2" in tap_outs:
        for t in range(NT):
            sync_dma(tap_outs["ACC2"][128 * t:128 * (t + 1), :], ACC[t][:],
                     [(f"ACC{t}", "lo"), (f"ACC{t}", "hi")],
                     [("OUT", f"acc2tap{t}")], p.slot("tap_ACC2"),
                     gid="a2g")

    for t in range(NT):
        layernorm_tile(t)
        sync_dma(Y[128 * t:128 * (t + 1), :], ACC[t][:],
                 [(f"ACC{t}", "lo"), (f"ACC{t}", "hi")],
                 [("OUT", t)], s_out)

    p.op("pool", lambda e: e.nop(), reads=[("OUT", None)], tag="final")

    with nc.Block() as block:
        for e in ENGINES:
            sems[e] = st.enter_context(nc.semaphore(f"sem_{e}"))
        for s in p.slots.values():
            s.sem = st.enter_context(nc.semaphore(f"dma_{s.name}"))
        p.emit_all(block, sems)
    st.close()
    nc._dbg_prog = p
    return nc


def host_consts(core):
    M = np.zeros((128, 1152), dtype=np.float32)
    s = np.arange(128)[:, None]
    i = np.arange(1152)[None, :]
    d = i - s
    M[:] = np.where((d >= 0) & (d <= WIN), (s - i).astype(np.float32),
                    np.float32(-1e9)).astype(np.float32)
    poi = np.zeros((128, NS), dtype=np.float32)
    if core == 0:
        poi[:, 0:8] = -1e5
    idf = np.eye(128, dtype=np.float32)
    return M, poi, idf


_nc_cache = {}


def _get_nc(taps=()):
    key = tuple(sorted(taps))
    if key not in _nc_cache:
        _nc_cache[key] = build_nc(key)
    return _nc_cache[key]


def kernel(L0_attn_x, L0_attn_wq, L0_attn_wk, L0_attn_wv, L0_attn_wo,
           L0_attn_q_norm_w, L0_attn_k_norm_w, L0_ffn_gate_w, L0_ffn_w_gate,
           L0_ffn_w_up, L0_ffn_w_down, L0_ffn_sw_gate, L0_ffn_sw_up,
           L0_ffn_sw_down, _taps=(), _collect=None):
    x = np.ascontiguousarray(np.asarray(L0_attn_x, dtype=np.float32)
                             .reshape(NCORE * TC, D))
    asf = lambda a: np.ascontiguousarray(np.asarray(a, dtype=np.float32))
    wq, wk, wv, wo = (asf(L0_attn_wq), asf(L0_attn_wk), asf(L0_attn_wv),
                      asf(L0_attn_wo))
    operm = np.concatenate(
        [np.arange(64 * h, 64 * h + 64)
         for j in range(16) for h in (8 * (j // 4) + (j % 4),
                                      8 * (j // 4) + (j % 4) + 4)])
    wq = np.ascontiguousarray(wq[operm, :])
    wo = np.ascontiguousarray(wo[:, operm])
    qnw = asf(L0_attn_q_norm_w).reshape(1, D)
    qnw = np.ascontiguousarray(qnw[:, operm])
    knw = asf(L0_attn_k_norm_w).reshape(1, 512)
    gw = asf(L0_ffn_gate_w)
    wgate, wup, wdown = asf(L0_ffn_w_gate), asf(L0_ffn_w_up), asf(L0_ffn_w_down)
    swg, swu, swd = asf(L0_ffn_sw_gate), asf(L0_ffn_sw_up), asf(L0_ffn_sw_down)

    nc = _get_nc(tuple(_taps))
    in_maps = []
    for c in range(NCORE):
        if c == 0:
            prev = np.zeros((TC, D), dtype=np.float32)
        else:
            prev = x[TC * (c - 1):TC * c]
        xe = np.ascontiguousarray(
            np.concatenate([prev, x[TC * c:TC * (c + 1)]], axis=0))
        M, poi, idf = host_consts(c)
        in_maps.append(dict(
            xe=xe, wq=wq, wk=wk, wv=wv, wo=wo, qnw=qnw, knw=knw, gw=gw,
            wgate=wgate, wup=wup, wdown=wdown, swg=swg, swu=swu, swd=swd,
            mmask=M, poison=poi, identf=idf))
    res = run_bass_kernel_spmd(nc, in_maps, core_ids=list(range(NCORE)))
    if _collect is not None:
        _collect.extend(res.results)
    y = np.concatenate([res.results[c]["y"] for c in range(NCORE)], axis=0)
    return y.reshape(1, NCORE * TC, D)


# version 07
# speedup vs baseline: 1.2332x; 1.0139x over previous
import numpy as np
from contextlib import ExitStack

import concourse.bass as bass
import concourse.mybir as mybir
from concourse.bass_utils import run_bass_kernel_spmd

F32 = mybir.dt.float32
BF16 = mybir.dt.bfloat16
AF = mybir.ActivationFunctionType
ALU = mybir.AluOpType
AX = mybir.AxisListType

NCORE = 8
TC = 1024
TEXT = 2048
D = 2048
NH = 32
NKV = 8
DH = 64
WIN = 1024
DFF = 1408
NE = 8
NT = TC // 128
NS = TEXT // 128
ND = D // 128

SLOPES = [float(2.0 ** (-0.25 * h)) for h in range(NH)]

ENGINES = ("pe", "act", "dve", "pool", "sync")


class DmaSlot:
    __slots__ = ("name", "sem", "count", "groups")

    def __init__(self, name):
        self.name = name
        self.sem = None
        self.count = 0
        self.groups = {}


class Op:
    __slots__ = ("idx", "engine", "emit", "reads", "writes", "deps", "slot",
                 "slot_count", "slot_group", "need_inc", "milestone", "tag")

    def __init__(self, idx, engine, emit, reads, writes, slot, tag):
        self.idx = idx
        self.engine = engine
        self.emit = emit
        self.reads = reads
        self.writes = writes
        self.deps = []
        self.slot = slot
        self.slot_count = None
        self.slot_group = None
        self.need_inc = False
        self.milestone = None
        self.tag = tag


class Prog:
    def __init__(self, nc, self_sync=True):
        self.nc = nc
        self.ops = []
        self.track = {}
        self.slots = {}
        self.self_sync = self_sync
        self.pending_free = []
        self.newdeps = {}
        self.psum_names = set()
        self.n_waits = 0
        self.n_incs = 0

    def slot(self, name):
        s = self.slots.get(name)
        if s is None:
            s = self.slots[name] = DmaSlot(name)
        return s

    def new_tensor(self, name):
        if self.pending_free:
            self.newdeps[name] = self._prune(self.pending_free)

    def note_free(self, name):
        t = self.track.get(name)
        if t is None:
            return
        ops = list(t["w"].values())
        for lst in t["r"].values():
            ops.extend(lst)
        self.pending_free.extend(ops)
        self.pending_free = self._prune(self.pending_free)

    @staticmethod
    def _prune(deps):
        best = {}
        for d in deps:
            key = ("s", d.slot.name) if d.slot is not None else ("e", d.engine)
            cur = best.get(key)
            if cur is None or d.idx > cur.idx:
                best[key] = d
        return list(best.values())

    @staticmethod
    def _regions(lst):
        out = []
        for r in lst or ():
            if isinstance(r, tuple):
                out.append(r)
            else:
                out.append((r, None))
        return out

    def _dep_read(self, name, key):
        t = self.track.get(name)
        if t is None:
            return []
        w = t["w"]
        if key is None:
            return list(w.values())
        deps = []
        if key in w:
            deps.append(w[key])
        if None in w:
            deps.append(w[None])
        return deps

    def _dep_write(self, name, key):
        deps = self._dep_read(name, key)
        t = self.track.get(name)
        if t is not None:
            r = t["r"]
            if key is None:
                for lst in r.values():
                    deps.extend(lst)
            else:
                deps.extend(r.get(key, ()))
                deps.extend(r.get(None, ()))
        return deps

    def op(self, engine, emit, reads=(), writes=(), slot=None, tag="",
           gid=None):
        wr = self._regions(writes)
        rd = self._regions(reads)
        if engine == "pe":
            wr = [(n, None) if n in self.psum_names else (n, k)
                  for (n, k) in wr]
        else:
            promote = [(n, None) for (n, k) in rd if n in self.psum_names]
            if promote:
                rd = [(n, k) for (n, k) in rd if n not in self.psum_names]
                wr = wr + promote
        o = Op(len(self.ops), engine, emit, rd, wr, slot, tag)
        deps = []
        for (n, k) in o.reads:
            deps.extend(self._dep_read(n, k))
            if n in self.newdeps:
                deps.extend(self.newdeps.pop(n))
        for (n, k) in o.writes:
            deps.extend(self._dep_write(n, k))
            if n in self.newdeps:
                deps.extend(self.newdeps.pop(n))
        pruned = {}
        for d in deps:
            if d is o:
                continue
            if d.engine == o.engine and d.slot is None:
                if o.engine == "pe" or not self.self_sync:
                    continue
            key = ("s", d.slot.name) if d.slot is not None else ("e", d.engine)
            cur = pruned.get(key)
            if cur is None or d.idx > cur.idx:
                pruned[key] = d
        o.deps = list(pruned.values())
        if slot is not None:
            slot.count += 1
            o.slot_count = slot.count
            if gid is None:
                gid = ("u", slot.count)
            o.slot_group = gid
            g = slot.groups.get(gid, 0)
            slot.groups[gid] = max(g, slot.count)
        for (n, k) in o.reads:
            self.track.setdefault(n, {"w": {}, "r": {}})["r"].setdefault(
                k, []).append(o)
        for (n, k) in o.writes:
            t = self.track.setdefault(n, {"w": {}, "r": {}})
            if k is None:
                t["w"] = {None: o}
                t["r"] = {}
            else:
                t["w"][k] = o
                t["r"].pop(k, None)
        self.ops.append(o)
        return o

    def dma(self, queue, emit, reads=(), writes=(), slot=None, tag="",
            gid=None):
        assert slot is not None
        return self.op(queue, emit, reads=reads, writes=writes, slot=slot,
                       tag=tag, gid=gid)

    def emit_all(self, block, engine_sems):
        for o in self.ops:
            for d in o.deps:
                if d.slot is None:
                    d.need_inc = True
        counters = {e: 0 for e in ENGINES}
        for o in self.ops:
            if o.need_inc and o.slot is None:
                counters[o.engine] += 1
                o.milestone = counters[o.engine]
        per_engine = {e: [] for e in ENGINES}
        for o in self.ops:
            per_engine[o.engine].append(o)
        prog = self

        def runner(eng_name):
            def body(eng):
                waited = {}
                for o in per_engine[eng_name]:
                    for d in o.deps:
                        if d.slot is not None:
                            sem = d.slot.sem
                            val = 16 * d.slot.groups[d.slot_group]
                        else:
                            sem = engine_sems[d.engine]
                            val = d.milestone
                        key = sem.name
                        if waited.get(key, -1) >= val:
                            continue
                        eng.wait_ge(sem, val)
                        waited[key] = val
                        prog.n_waits += 1
                    inst = o.emit(eng)
                    if o.slot is not None:
                        inst.then_inc(o.slot.sem, 16)
                        prog.n_incs += 1
                    elif o.need_inc:
                        inst.then_inc(engine_sems[o.engine], 1)
                        prog.n_incs += 1
            return body

        section = {"pe": block.tensor, "act": block.scalar,
                   "dve": block.vector, "pool": block.gpsimd,
                   "sync": block.sync}
        for e in ENGINES:
            if per_engine[e]:
                section[e](runner(e))


ARENA_BYTES = 211968


class TMgr:

    def __init__(self, nc, prog, off):
        self.nc = nc
        self.p = prog
        self.off = off
        self.cms = {}
        self.logical = {}
        self.acm = nc.sbuf_tensor("arena", [128, ARENA_BYTES], mybir.dt.uint8)
        self.ah = self.acm.__enter__()
        self.abase = nc.lookup_mloc(self.ah).addr
        self.spans = {}

    def sbuf(self, name, shape, dtype):
        import functools, operator
        off = self.off[name]
        per_part = functools.reduce(operator.mul, shape[1:], 1) *             mybir.dt.size(dtype)
        assert off + per_part <= ARENA_BYTES, (name, off, per_part)
        assert off % 64 == 0, (name, off)
        h = self.nc.alloc_sbuf_tensor_at(name, shape, dtype,
                                         offset=self.abase + off)
        self.spans[name] = (off, off + per_part)
        self.logical[h.name] = name
        self.p.new_tensor(name)
        return h

    def psum(self, name, shape, dtype=F32):
        cm = self.nc.psum_tensor(name, shape, dtype)
        h = cm.__enter__()
        self.cms[name] = cm
        self.logical[h.name] = name
        self.p.psum_names.add(name)
        self.p.new_tensor(name)
        return h

    def free(self, *handles):
        for h in handles:
            name = self.logical.get(h.name, h.name)
            self.p.note_free(name)
            cm = self.cms.pop(name, None)
            if cm is not None:
                cm.__exit__(None, None, None)


def build_nc(taps=(), stop_after=None):
    nc = bass.Bass("TRN2", target_bir_lowering=False)
    taps = set(taps)

    din = lambda n, s: nc.dram_tensor(n, s, F32, kind="ExternalInput").ap()
    XE = din("xe", [TEXT, D])
    WQ = din("wq", [D, D])
    WK = din("wk", [512, D])
    WV = din("wv", [512, D])
    WO = din("wo", [D, D])
    QNW = din("qnw", [1, D])
    KNW = din("knw", [1, 512])
    GW = din("gw", [NE, D])
    WGATE = din("wgate", [NE, D, DFF])
    WUP = din("wup", [NE, D, DFF])
    WDOWN = din("wdown", [NE, DFF, D])
    SWG = din("swg", [D, DFF])
    SWU = din("swu", [D, DFF])
    SWD = din("swd", [DFF, D])
    MM = din("mmask", [128, 1152])
    POI = din("poison", [128, NS])
    IDF = din("identf", [128, 128])
    Y = nc.dram_tensor("y", [TC, D], F32, kind="ExternalOutput").ap()
    tap_outs = {}
    tap_shapes = {"KT": [512, TEXT], "QT": [D, TC], "MALL": [NH, TC],
                  "AO": [D, TC], "Y1": [TC, D], "GATES": [TC, NE],
                  "ACC2": [TC, D]}
    for t in taps:
        tap_outs[t] = nc.dram_tensor("tap_" + t, tap_shapes[t], F32,
                                     kind="ExternalOutput").ap()

    RA, RB, RC, RD, RE = 10240, 79872, 113664, 146432, 179200
    OFF = {
        "idb": 0, "idf": 256, "ones1": 768, "onesf": 1024, "small": 1536,
        "bns": 1792, "aggr": 1920, "knacc": 1984, "knbneg": 2048,
        "knscr": 2176, "epsc": 9920, "gates8": 3520, "grow8": 3776, "garow0": 5824,
        "garow1": 7872,
        "wkT": RA, "wvT": RA + 16384, "wstg0": RA + 32768, "wstg1": RA + 36864,
        **{f"wqT{i}": RA + 4096 * i for i in range(16)},
        **{f"Fr{i}": RA + 2048 * i for i in range(12)},
        "woq0": RA, "woq1": RA + 16384, "wosg0": RA + 32768,
        "wosg1": RA + 36864, "wosg2": RA + 65536, "xrs0": RA + 40960, "xrs1": RA + 49152,
        "ybf": RA + 57344, "gwbf": RA + 61440, "gwT": RA + 57344,
        **{f"wgb{i}": RA + 8192 * i for i in range(3)},
        **{f"wub{i}": RA + 24576 + 8192 * i for i in range(3)},
        **{f"wdb{i}": RA + 49152 + 5696 * i for i in range(3)},
        **{f"KT{i}": RB + 4096 * i for i in range(4)},
        **{f"VA{i}": RB + 16384 + 2112 * i for i in range(8)},
        **{f"ACC{i}": (RB + 8192 * i if i < 4 else RC + 8192 * (i - 4))
           for i in range(8)},
        "m_all": RC, "mrow0": RC + 2048, "mrow1": RC + 4096,
        "rrecb": RC + 6144,
        "Qsb": RC + 8192, "sqs": RC + 16384, "wstgq0": RC + 24576,
        "wstgq1": RC + 28672,
        "Mm": RC + 8192, "poi": RC + 12800, "Hb0": RC + 12864,
        "Hb1": RC + 15168, "Fb0": RC + 17472, "Fb1": RC + 19520,
        "rsb": RC + 21568, "rsb2": RC + 25664, "rbsb": RC + 29760,
        "xstg0": RD, "xstg1": RD + 4096, "xTc0": RD + 8192,
        "xTc1": RD + 12288, "ksqs": RD + 16384, "row1": RD + 18432,
        "kwb": RD + 22528, "qwb": RD + 24576,
        **{f"AO{i}": RD + 2048 * i for i in range(16)},
        **{f"hT{i}": RD + 2048 * i for i in range(11)},
        "gbsb": RD + 22528, "gsb": RD + 26624, "usb": RD + 28672,
        "ttmp_KT": RA + 32768, "ttmp_QT": RA + 32768, "matmp": RA + 40960,
        "dummyA": RE + 24576, "dummyB": RA + 32768,
        "gtmp": RA + 32768, "ttmp_AO": RA + 32768,
        **{f"QT{i}": RE + 2048 * i for i in range(16)},
        **{f"y1T{i}": RE + 2048 * i for i in range(16)},
    }
    p = Prog(nc, self_sync=True)
    tm = TMgr(nc, p, OFF)
    st = ExitStack()

    idb = tm.sbuf("idb", [128, 128], BF16)
    idf = tm.sbuf("idf", [128, 128], F32)
    ones1 = tm.sbuf("ones1", [1, 128], BF16)
    onesf = tm.sbuf("onesf", [1, 128], F32)
    epsc = tm.sbuf("epsc", [128, 2], F32)
    knscr = tm.sbuf("knscr", [8, 336], F32)
    m_all = tm.sbuf("m_all", [NH, TC], BF16)
    mrows = [tm.sbuf(f"mrow{i}", [1, TC], BF16) for i in range(2)]
    knacc = tm.sbuf("knacc", [128, NKV], F32)
    knbneg = tm.sbuf("knbneg", [128, NH], F32)
    gates8 = tm.sbuf("gates8", [128, NT * NE], F32)
    grow8 = tm.sbuf("grow8", [NE, TC], BF16)
    garow = [tm.sbuf(f"garow{i}", [1, TC], BF16) for i in range(2)]
    small = tm.sbuf("small", [128, 64], F32)
    bns = tm.sbuf("bns", [128, 24], F32)
    aggr = tm.sbuf("aggr", [128, 2], F32)

    sems = {}
    s_misc = p.slot("misc")
    s_miscp = p.slot("miscp")
    s_out = p.slot("out")

    def sync_dma(dst, src, reads, writes, slot, gid=None, **kw):
        return p.dma("sync", lambda e: e.dma_start(dst, src, **kw),
                     reads=reads, writes=writes, slot=slot, gid=gid)

    def cast_dma(dst, src, reads, writes, slot, gid=None):
        return p.dma("pool", lambda e: e.dma_start(dst, src),
                     reads=reads, writes=writes, slot=slot, gid=gid)

    def finish_dummy(which):
        dmy = tm.sbuf(which, [128, D], F32)
        p.op("dve", lambda e: e.memset(dmy[:], 0.5), writes=[which])
        for t in range(NT):
            sync_dma(Y[128 * t:128 * (t + 1), :], dmy[:], [which],
                     [("OUT", t)], s_out, gid="yg")
        p.op("pool", lambda e: e.nop(), reads=[("OUT", None)], tag="final")
        with nc.Block() as block:
            for e in ENGINES:
                sems[e] = st.enter_context(nc.semaphore(f"sem_{e}"))
            for sl_ in p.slots.values():
                sl_.sem = st.enter_context(nc.semaphore(f"dma_{sl_.name}"))
            p.emit_all(block, sems)
        st.close()
        nc._dbg_prog = p
        return nc

    def finish_acc(ACC):
        for t in range(NT):
            sync_dma(Y[128 * t:128 * (t + 1), :], ACC[t][:],
                     [(f"ACC{t}", "lo"), (f"ACC{t}", "hi")],
                     [("OUT", t)], s_out, gid="yg")
        p.op("pool", lambda e: e.nop(), reads=[("OUT", None)], tag="final")
        with nc.Block() as block:
            for e in ENGINES:
                sems[e] = st.enter_context(nc.semaphore(f"sem_{e}"))
            for sl_ in p.slots.values():
                sl_.sem = st.enter_context(nc.semaphore(f"dma_{sl_.name}"))
            p.emit_all(block, sems)
        st.close()
        nc._dbg_prog = p
        return nc

    sems_holder = sems

    sync_dma(idf[:], IDF, [], ["idf"], s_misc)
    cast_dma(idb[:], IDF, [], ["idb"], s_miscp)
    p.op("dve", lambda e: e.memset(ones1[:], 1.0), writes=["ones1"])
    p.op("dve", lambda e: e.memset(onesf[:], 1.0), writes=["onesf"])
    p.op("dve", lambda e: e.memset(epsc[:, 0:1], 1e-6), writes=[("epsc", 0)])
    p.op("dve", lambda e: e.memset(epsc[:, 1:2], 1e-5), writes=[("epsc", 1)])

    pt0 = tm.psum("pt0", [128, 1024], BF16)
    pt1 = tm.psum("pt1", [128, 1024], BF16)
    pts = [pt0, pt1]
    pB = tm.psum("pB", [128, 512])
    pK = tm.psum("pK", [128, 512])
    pV = tm.psum("pV", [128, 512])

    wkT = tm.sbuf("wkT", [128, ND * 512], BF16)
    wvT = tm.sbuf("wvT", [128, ND * 512], BF16)
    wstg = [tm.sbuf(f"wstg{i}", [128, D], BF16) for i in range(2)]
    s_wstg = [p.slot(f"wstg{i}") for i in range(2)]
    KT = [tm.sbuf(f"KT{i}", [128, TEXT], BF16) for i in range(4)]
    VA = [tm.sbuf(f"VA{i}", [128, NS * 65], BF16) for i in range(NKV)]
    kwb = tm.sbuf("kwb", [128, 512], F32)
    row1 = tm.sbuf("row1", [1, 2048], BF16)

    def wT_build(W, nrow_tiles, dstT, dname):
        for r in range(nrow_tiles):
            b = r % 2
            cast_dma(wstg[b][:], W[128 * r:128 * (r + 1), :],
                     [], [f"wstg{b}"], s_wstg[b])
            for dc in range(ND):
                pb = pts[dc // 8]
                p.op("pe", lambda e, b=b, dc=dc, pb=pb: e.transpose(
                    pb[:, 128 * (dc % 8):128 * (dc % 8 + 1)],
                    wstg[b][:, 128 * dc:128 * (dc + 1)], idb[:]),
                    reads=[f"wstg{b}", "idb"], writes=[(pb.name, dc % 8)])
            for half in range(2):
                pb = pts[half]
                dst3 = dstT[:].rearrange("p (dc o) -> p dc o", o=512)[
                    :, 8 * half:8 * (half + 1), 128 * r:128 * (r + 1)]
                src3 = pb[:].rearrange("p (dc o) -> p dc o", o=128)
                p.op("dve", lambda e, dst3=dst3, src3=src3: e.tensor_copy(
                    dst3, src3),
                    reads=[pb.name, (pb.name, 0), (pb.name, 1), (pb.name, 2),
                           (pb.name, 3), (pb.name, 4), (pb.name, 5),
                           (pb.name, 6), (pb.name, 7)],
                    writes=[(dname, (r, half))])

    wT_build(WK, 4, wkT, "wkT")
    wT_build(WV, 4, wvT, "wvT")

    cast_dma(row1[0:1, 0:512], KNW, [], [("row1", "k")], p.slot("knw"))
    p.op("pe", lambda e: e.matmul(pB[:], ones1[:], row1[0:1, 0:512],
                                  start=True, stop=True),
         reads=[("row1", "k"), "ones1"], writes=["pB"])
    p.op("dve", lambda e: e.tensor_copy(kwb[:], pB[:]), reads=["pB"],
         writes=["kwb"])

    xstg = [tm.sbuf(f"xstg{i}", [128, D], BF16) for i in range(2)]
    s_xstg = [p.slot(f"xstg{i}") for i in range(2)]
    xTc = [tm.sbuf(f"xTc{i}", [128, D], BF16) for i in range(2)]
    ksqs = tm.sbuf("ksqs", [128, 512], F32)

    p.op("dve", lambda e: e.memset(knacc[:], 0.0), writes=["knacc"])

    def transpose_tile(srcname, src, key=None):
        rkey = srcname if key is None else (srcname, key)
        for dc in range(ND):
            pb = pts[dc // 8]
            p.op("pe", lambda e, dc=dc, pb=pb, src=src: e.transpose(
                pb[:, 128 * (dc % 8):128 * (dc % 8 + 1)],
                src[:, 128 * dc:128 * (dc + 1)], idb[:]),
                reads=[rkey, "idb"], writes=[(pb.name, dc % 8)])

    def evac_xTc(b):
        for half in range(2):
            pb = pts[half]
            p.op("dve", lambda e, b=b, half=half, pb=pb: e.tensor_copy(
                xTc[b][:, 1024 * half:1024 * (half + 1)], pb[:]),
                reads=[pb.name] + [(pb.name, i) for i in range(8)],
                writes=[(f"xTc{b}", half)])

    for j in range(NS):
        b = j % 2
        cast_dma(xstg[b][:], XE[128 * j:128 * (j + 1), :], [], [f"xstg{b}"],
                 s_xstg[b])
        transpose_tile(f"xstg{b}", xstg[b][:])
        evac_xTc(b)
        for dc in range(ND):
            p.op("pe", lambda e, b=b, dc=dc: e.matmul(
                pK[:], xTc[b][:, 128 * dc:128 * (dc + 1)],
                wkT[:, 512 * dc:512 * (dc + 1)],
                start=(dc == 0), stop=(dc == ND - 1)),
                reads=[(f"xTc{b}", dc // 8), "wkT"], writes=["pK"])
        p.op("act", lambda e: e.activation(ksqs[:], pK[:], AF.Square),
             reads=["pK"], writes=["ksqs"])
        p.op("dve", lambda e: e.reduce_sum(small[:, 0:1], ksqs[:], AX.X),
             reads=["ksqs"], writes=[("small", 0)])
        p.op("act", lambda e: e.activation(small[:, 1:2], small[:, 0:1],
                                           AF.Sqrt, bias=epsc[:, 0:1],
                                           scale=1.0 / 512),
             reads=[("small", 0), ("epsc", 0)], writes=[("small", 1)])
        p.op("dve", lambda e: e.reciprocal(small[:, 2:3], small[:, 1:2]),
             reads=[("small", 1)], writes=[("small", 2)])
        p.op("dve", lambda e, b=b: e.scalar_tensor_tensor(
            xstg[b][:, 0:512], pK[:], small[:, 2:3], kwb[:],
            ALU.mult, ALU.mult),
            reads=["pK", ("small", 2), "kwb"], writes=[(f"xstg{b}", "ks")])
        p.op("act", lambda e, b=b: e.activation(ksqs[:], xstg[b][:, 0:512],
                                                AF.Square),
             reads=[(f"xstg{b}", "ks")], writes=["ksqs"])
        p.op("dve", lambda e: e.reduce_sum(
            small[:, 8:16], ksqs[:].rearrange("p (g x) -> p g x", g=8), AX.X),
            reads=["ksqs"], writes=[("small", 8)])
        p.op("dve", lambda e: e.tensor_tensor(knacc[:], knacc[:],
                                              small[:, 8:16], ALU.max),
             reads=[("small", 8), "knacc"], writes=["knacc"])
        for c in range(4):
            p.op("pe", lambda e, b=b, c=c: e.transpose(
                pt0[:, 128 * c:128 * (c + 1)],
                xstg[b][:, 128 * c:128 * (c + 1)], idb[:]),
                reads=[(f"xstg{b}", "ks"), "idb"], writes=[("pt0", c)])
        for c in range(4):
            p.op("dve", lambda e, c=c, j=j: e.tensor_copy(
                KT[c][:, 128 * j:128 * (j + 1)], pt0[:, 128 * c:128 * (c + 1)]),
                reads=[("pt0", c)], writes=[(f"KT{c}", j)])
        for dc in range(ND):
            p.op("pe", lambda e, b=b, dc=dc: e.matmul(
                pV[:], xTc[b][:, 128 * dc:128 * (dc + 1)],
                wvT[:, 512 * dc:512 * (dc + 1)],
                start=(dc == 0), stop=(dc == ND - 1)),
                reads=[(f"xTc{b}", dc // 8), "wvT"], writes=["pV"])
        for kv in range(NKV):
            p.op("dve", lambda e, kv=kv, j=j: e.tensor_copy(
                VA[kv][:, 65 * j:65 * j + 64], pV[:, 64 * kv:64 * (kv + 1)]),
                reads=["pV"], writes=[(f"VA{kv}", j)])
        for kv in range(NKV):
            p.op("dve", lambda e, kv=kv, j=j: e.memset(
                VA[kv][:, 65 * j + 64:65 * j + 65], 1.0),
                writes=[(f"VA{kv}", j)])

    if stop_after == "A1":
        return finish_dummy("dummyA")

    p.op("pe", lambda e: e.transpose(pB[0:8, 0:128], knacc[:], idf[:]),
         reads=["knacc", "idf"], writes=["pB"])
    p.op("dve", lambda e: e.tensor_copy(knscr[0:8, 0:128], pB[0:8, 0:128]),
         reads=["pB"], writes=["knscr"])
    p.op("dve", lambda e: e.reduce_max(small[0:8, 3:4], knscr[0:8, 0:128],
                                       AX.X),
         reads=["knscr"], writes=[("small", 3)])
    p.op("act", lambda e: e.activation(small[0:8, 4:5], small[0:8, 3:4],
                                       AF.Sqrt), reads=[("small", 3)],
         writes=[("small", 4)])
    p.op("dve", lambda e: e.tensor_scalar(small[0:8, 5:6], small[0:8, 4:5],
                                          -1.0, None, ALU.mult),
         reads=[("small", 4)], writes=[("small", 5)])
    p.op("pe", lambda e: e.transpose(pB[0:1, 128:136], small[0:8, 5:6],
                                     idf[0:8, 0:8]),
         reads=[("small", 5), "idf"], writes=["pB"])
    p.op("dve", lambda e: e.tensor_copy(knscr[0:1, 128:136],
                                        pB[0:1, 128:136]),
         reads=["pB"], writes=["knscr"])
    s_kn = p.slot("knr")
    for m_ in range(2):
        for r_ in range(4):
            sync_dma(knscr[0:1, 160 + 2 * r_ + m_:192:8],
                     knscr[0:1, 128 + m_:136:2],
                     ["knscr"], ["knscr"], s_kn,
                     allow_slow_non_contiguous=True)
    p.op("pe", lambda e: e.matmul(pB[:, 0:32], onesf[:], knscr[0:1, 160:192],
                                  start=True, stop=True),
         reads=["knscr", "onesf"], writes=["pB"])
    p.op("dve", lambda e: e.tensor_copy(knbneg[:], pB[:, 0:32]),
         reads=["pB"], writes=["knbneg"])

    tm.free(wstg[1], wstg[0], wkT, wvT)
    tm.free(pV, pK)
    pQs = [tm.psum(f"pQ{i}", [128, 512]) for i in range(4)]

    if stop_after == "KN":
        return finish_dummy("dummyA")

    wqT = [tm.sbuf(f"wqT{i}", [128, D], BF16) for i in range(ND)]
    wstgq = [tm.sbuf(f"wstgq{i}", [128, D], BF16) for i in range(2)]
    s_wstgq = [p.slot(f"wstgq{i}") for i in range(2)]
    for r in range(ND):
        b = r % 2
        cast_dma(wstgq[b][:], WQ[128 * r:128 * (r + 1), :], [], [f"wstgq{b}"],
                 s_wstgq[b])
        transpose_tile(f"wstgq{b}", wstgq[b][:])
        for dc in range(ND):
            pb = pts[dc // 8]
            p.op("dve", lambda e, r=r, dc=dc, pb=pb: e.tensor_copy(
                wqT[dc][:, 128 * r:128 * (r + 1)],
                pb[:, 128 * (dc % 8):128 * (dc % 8 + 1)]),
                reads=[(pb.name, dc % 8)], writes=[(f"wqT{dc}", r)])

    if stop_after == "WQT":
        return finish_dummy("dummyA")

    qwb = tm.sbuf("qwb", [128, D], F32)
    cast_dma(row1[0:1, 0:2048], QNW, [], [("row1", "q")], p.slot("qnw"))
    for oc in range(4):
        p.op("pe", lambda e, oc=oc: e.matmul(
            pB[:], ones1[:], row1[0:1, 512 * oc:512 * (oc + 1)],
            start=True, stop=True),
            reads=[("row1", "q"), "ones1"], writes=["pB"])
        p.op("act", lambda e, oc=oc: e.activation(
            qwb[:, 512 * oc:512 * (oc + 1)], pB[:], AF.Copy, scale=0.125),
            reads=["pB"], writes=[("qwb", oc)])

    if stop_after == "QWB":
        return finish_dummy("dummyA")

    QT = [tm.sbuf(f"QT{i}", [128, TC], BF16) for i in range(ND)]
    Qsb = tm.sbuf("Qsb", [128, D], F32)
    sqs = tm.sbuf("sqs", [128, D], F32)

    _nt = NT
    _ql = 99
    if stop_after and stop_after.startswith("Q") and len(stop_after) == 2:
        _nt = int(stop_after[1]) + 1
    if stop_after and stop_after.startswith("QL"):
        _nt, _ql = 1, int(stop_after[2:])

    def q_head(t):
        b = t % 2
        cast_dma(xstg[b][:], XE[TC + 128 * t:TC + 128 * (t + 1), :], [],
                 [f"xstg{b}"], s_xstg[b])
        transpose_tile(f"xstg{b}", xstg[b][:])
        evac_xTc(b)
        if _ql < 2:
            return
        for oc in range(4):
            pQ = pQs[oc]
            for dc in range(ND):
                p.op("pe", lambda e, b=b, dc=dc, oc=oc, pQ=pQ: e.matmul(
                    pQ[:], xTc[b][:, 128 * dc:128 * (dc + 1)],
                    wqT[dc][:, 512 * oc:512 * (oc + 1)],
                    start=(dc == 0), stop=(dc == ND - 1)),
                    reads=[(f"xTc{b}", dc // 8), f"wqT{dc}"], writes=[pQ.name])

    def q_evac(t):
        qc = 20 + 4 * (t % 2)
        if _ql < 2:
            return
        for oc in range(4):
            pQ = pQs[oc]
            p.op("dve", lambda e, oc=oc, pQ=pQ: e.tensor_copy(
                Qsb[:, 512 * oc:512 * (oc + 1)], pQ[:]),
                reads=[pQ.name], writes=[("Qsb", oc)])
            if _ql < 3:
                continue
            p.op("act", lambda e, oc=oc, pQ=pQ: e.activation(
                sqs[:, 512 * oc:512 * (oc + 1)], pQ[:], AF.Square),
                reads=[pQ.name], writes=[("sqs", oc)])
        if _ql < 4:
            return
        p.op("dve", lambda e, qc=qc: e.reduce_sum(
            small[:, qc:qc + 1], sqs[:], AX.X),
            reads=["sqs", ("sqs", 0), ("sqs", 1), ("sqs", 2), ("sqs", 3)],
            writes=[("small", qc)])
        p.op("act", lambda e, qc=qc: e.activation(
            small[:, qc + 1:qc + 2], small[:, qc:qc + 1],
            AF.Sqrt, bias=epsc[:, 0:1], scale=1.0 / D),
            reads=[("small", qc), ("epsc", 0)], writes=[("small", qc + 1)])
        p.op("dve", lambda e, qc=qc: e.reciprocal(
            small[:, qc + 2:qc + 3], small[:, qc + 1:qc + 2]),
            reads=[("small", qc + 1)], writes=[("small", qc + 2)])

    def q_tail(t):
        b = t % 2
        qc = 20 + 4 * (t % 2)
        if _ql < 5:
            return
        p.op("dve", lambda e, b=b, qc=qc: e.scalar_tensor_tensor(
            xstg[b][:], Qsb[:], small[:, qc + 2:qc + 3], qwb[:],
            ALU.mult, ALU.mult),
            reads=[("Qsb", 0), ("Qsb", 1), ("Qsb", 2), ("Qsb", 3),
                   ("small", qc + 2), "qwb"],
            writes=[(f"xstg{b}", "qh")])
        if _ql < 6:
            return
        p.op("act", lambda e, b=b: e.activation(sqs[:], xstg[b][:], AF.Square),
             reads=[(f"xstg{b}", "qh")],
             writes=["sqs", ("sqs", 0), ("sqs", 1), ("sqs", 2), ("sqs", 3)])
        p.op("dve", lambda e: e.reduce_sum(
            small[:, 32:64], sqs[:].rearrange("p (g x) -> p g x", g=32), AX.X),
            reads=["sqs"], writes=[("small", 32)])
        p.op("act", lambda e: e.activation(small[:, 32:64], small[:, 32:64],
                                           AF.Sqrt),
             reads=[("small", 32)], writes=[("small", 32)])
        if _ql < 7:
            return
        p.op("dve", lambda e: e.tensor_tensor(small[:, 32:64], small[:, 32:64],
                                              knbneg[:], ALU.mult),
             reads=[("small", 32), "knbneg"], writes=[("small", 32)])
        p.op("pe", lambda e: e.transpose(pB[0:32, 0:128], small[:, 32:64],
                                         idf[:]),
             reads=[("small", 32), "idf"], writes=["pB"])
        p.op("dve", lambda e, t=t: e.tensor_copy(
            m_all[:, 128 * t:128 * (t + 1)], pB[0:32, 0:128]),
            reads=["pB"], writes=[("m_all", t)])
        if _ql < 8:
            return
        transpose_tile(f"xstg{b}", xstg[b][:], key="qh")
        for dc in range(ND):
            pb = pts[dc // 8]
            p.op("dve", lambda e, t=t, dc=dc, pb=pb: e.tensor_copy(
                QT[dc][:, 128 * t:128 * (t + 1)],
                pb[:, 128 * (dc % 8):128 * (dc % 8 + 1)]),
                reads=[(pb.name, dc % 8)], writes=[(f"QT{dc}", t)])

    for t in range(_nt):
        q_head(t)
        if t > 0:
            q_tail(t - 1)
        q_evac(t)
    q_tail(_nt - 1)

    if stop_after and stop_after.startswith("Q"):
        return finish_dummy("dummyA")

    tm.free(Qsb, sqs, qwb, kwb, ksqs, row1, *wqT, *xstg, *xTc, *wstgq)
    tm.free(pQs[3], pQs[2], pQs[1], pQs[0], pB, pt1, pt0)

    def tap_dump(name, tiles, nrow, regions_of, width=TC):
        if name not in tap_outs:
            return
        ttmp = tm.sbuf("ttmp_" + name, [nrow, width], F32)
        s_tap = p.slot("tap_" + name)
        for c, tile in enumerate(tiles):
            p.op("act", lambda e, tile=tile: e.activation(ttmp[:], tile[:],
                                                          AF.Copy),
                 reads=regions_of(c), writes=["ttmp_" + name])
            sync_dma(tap_outs[name][nrow * c:nrow * (c + 1), :], ttmp[:],
                     ["ttmp_" + name], [("OUT", f"{name}{c}")], s_tap)
        tm.free(ttmp)

    if stop_after == "A":
        return finish_dummy("dummyA")

    tap_dump("KT", KT, 128, lambda c: [f"KT{c}"] + [(f"KT{c}", j) for j in range(NS)], width=TEXT)
    tap_dump("QT", QT, 128, lambda c: [f"QT{c}"] + [(f"QT{c}", t) for t in range(NT)])
    if "MALL" in tap_outs:
        matmp = tm.sbuf("matmp", [NH, TC], F32)
        p.op("act", lambda e: e.activation(matmp[:], m_all[:], AF.Copy),
             reads=["m_all"] + [("m_all", t) for t in range(NT)],
             writes=["matmp"])
        sync_dma(tap_outs["MALL"], matmp[:], ["matmp"], [("OUT", "matap")],
                 p.slot("tap_MALL"))
        tm.free(matmp)

    Mm = tm.sbuf("Mm", [128, 1152], F32)
    sync_dma(Mm[:], MM, [], ["Mm"], p.slot("mm"))
    poi = tm.sbuf("poi", [128, NS], F32)
    sync_dma(poi[:], POI, [], ["poi"], p.slot("poi"))
    Hb = [tm.sbuf(f"Hb{i}", [128, 1152], BF16) for i in range(2)]
    Fring = [tm.sbuf(f"Fr{i}", [128, TC], BF16) for i in range(12)]
    rsb = tm.sbuf("rsb", [32, TC], F32)
    rsb2 = tm.sbuf("rsb2", [32, TC], F32)
    
    rrecb = tm.sbuf("rrecb", [1, TC], BF16)
    rbsb = tm.sbuf("rbsb", [64, TC], BF16)
    AOsb = [tm.sbuf(f"AO{i}", [128, TC], BF16) for i in range(ND)]

    p.op("dve", lambda e: e.memset(rsb[:], 0.0), writes=[("rsb", None)])
    p.op("dve", lambda e: e.memset(rsb2[:], 0.0), writes=["rsb2"])
    Sps = [tm.psum(f"Sps{i}", [128, 1024]) for i in range(2)]
    AOps = tm.psum("AOps", [65, 1024])
    rbps = tm.psum("rbps", [64, 1024])

    s_mrow = [p.slot("mrowd0"), p.slot("mrowd1")]
    mall_regions = ["m_all"] + [("m_all", t) for t in range(NT)]

    def _av_batch(tt, kv):
        for jj in range(tt, tt + 9):
            tlo_jj = 128 * max(0, jj - 8)
            frj = Fring[jj % 12]
            p.op("pe", lambda e, tt=tt, jj=jj, kv=kv, frj=frj, tlo_jj=tlo_jj:
                 e.matmul(AOps[:, 128 * tt:128 * (tt + 1)],
                          VA[kv][:, 65 * jj:65 * (jj + 1)],
                          frj[:, 128 * tt - tlo_jj:128 * (tt + 1) - tlo_jj],
                          start=(jj == tt), stop=(jj == tt + 8)),
                 reads=[(f"VA{kv}", jj), frj.name],
                 writes=[("AOps", tt)])

    for sslot in range(NH):
        jt = sslot // 2
        bb = sslot % 2
        h = 8 * (jt // 4) + (jt % 4) + 4 * bb
        hb = sslot % 2
        kv = h // 4
        ktile = KT[kv // 2]
        kbase = 64 * (kv % 2)
        qtile = QT[jt]
        qbase = 64 * bb
        p.op("dve", lambda e, hb=hb, h=h: e.tensor_scalar(
            Hb[hb][:], Mm[:], SLOPES[h], None, ALU.mult),
            reads=["Mm"], writes=[f"Hb{hb}"])
        sync_dma(mrows[hb][:], m_all[sslot:sslot + 1, :], mall_regions,
                 [f"mrow{hb}"], s_mrow[hb])
        for j in range(NS):
            sb = Sps[j % 2]
            t_lo = 128 * max(0, j - 8)
            t_hi = 128 * (min(j, 7) + 1)
            span = t_hi - t_lo
            i_lo = t_lo + 1024 - 128 * j
            for off in range(0, span, 512):
                w = min(512, span - off)
                p.op("pe", lambda e, sb=sb, off=off, w=w, hb=hb, i_lo=i_lo:
                     e.matmul(sb[:, off:off + w], idb[:],
                              Hb[hb][:, i_lo + off:i_lo + off + w],
                              start=True, stop=False),
                     reads=[f"Hb{hb}", "idb"], writes=[sb.name])
                p.op("pe", lambda e, sb=sb, off=off, w=w, j=j, t_lo=t_lo,
                     ktile=ktile, kbase=kbase, qtile=qtile, qbase=qbase:
                     e.matmul(sb[:, off:off + w],
                              ktile[kbase:kbase + 64, 128 * j:128 * (j + 1)],
                              qtile[qbase:qbase + 64,
                                    t_lo + off:t_lo + off + w],
                              start=False, stop=False),
                     reads=[(ktile.name, j), qtile.name], writes=[sb.name])
                p.op("pe", lambda e, sb=sb, off=off, w=w, hb=hb, t_lo=t_lo:
                     e.matmul(sb[:, off:off + w], ones1[:],
                              mrows[hb][0:1, t_lo + off:t_lo + off + w],
                              start=False, stop=True),
                     reads=[f"mrow{hb}", "ones1"], writes=[sb.name])
            fr = Fring[j % 12]
            p.op("act", lambda e, sb=sb, span=span, j=j, fr=fr: e.activation(
                fr[:, 0:span], sb[:, 0:span], AF.Exp,
                bias=poi[:, j:j + 1]),
                reads=[sb.name, "poi"], writes=[fr.name])
            if j >= 9:
                _av_batch(j - 9, kv)
        for tt in (7,):
            _av_batch(tt, kv)
        p.op("dve", lambda e: e.tensor_copy(rsb[0:1, :], AOps[64:65, :]),
             reads=[("AOps", None)] + [("AOps", tt) for tt in range(8)],
             writes=[("rsb", None)])
        p.op("dve", lambda e: e.transpose(rsb2[:], rsb[:]),
             reads=[("rsb", None)], writes=["rsb2"])
        rview = rsb2[:].rearrange("p (b c) -> p b c", c=32)[:, :, 0:1]
        p.op("dve", lambda e: e.reciprocal(rview, rview),
             reads=["rsb2"], writes=["rsb2"])
        p.op("dve", lambda e: e.transpose(rsb[:], rsb2[:]),
             reads=["rsb2"], writes=[("rsb", None)])
        p.op("dve", lambda e: e.tensor_copy(rrecb[:], rsb[0:1, :]),
             reads=[("rsb", None)], writes=["rrecb"])
        for half in range(2):
            p.op("pe", lambda e, half=half: e.matmul(
                rbps[:, 512 * half:512 * (half + 1)], ones1[0:1, 0:64],
                rrecb[0:1, 512 * half:512 * (half + 1)],
                start=True, stop=True),
                reads=["rrecb", "ones1"], writes=["rbps"])
        p.op("dve", lambda e: e.tensor_copy(rbsb[:], rbps[:]),
             reads=["rbps"], writes=["rbsb"])
        p.op("dve", lambda e, jt=jt, bb=bb: e.tensor_tensor(
            AOsb[jt][64 * bb:64 * bb + 64, :], AOps[0:64, :],
            rbsb[:], ALU.mult),
            reads=[("AOps", None)] + [("AOps", tt) for tt in range(8)] +
                  ["rbsb"],
            writes=[(f"AO{jt}", bb)])

    tm.free(*QT, *KT, *VA, Mm, *Hb, *Fring, poi, rsb, rsb2, rrecb, rbsb,
            m_all, mrows[0], mrows[1])
    tm.free(rbps, AOps, Sps[1], Sps[0])

    if stop_after == "B":
        return finish_dummy("dummyB")

    tap_dump("AO", AOsb, 128, lambda c: [f"AO{c}", (f"AO{c}", 0), (f"AO{c}", 1)])

    ptc0 = tm.psum("ptc0", [128, 1024], BF16)
    ptc1 = tm.psum("ptc1", [128, 1024], BF16)
    ptcs = [ptc0, ptc1]
    pc = [tm.psum(f"pc{i}", [128, 1024]) for i in range(2)]

    ACC = [tm.sbuf(f"ACC{i}", [128, D], F32) for i in range(NT)]
    y1T = [tm.sbuf(f"y1T{i}", [128, TC], BF16) for i in range(ND)]
    woq = [tm.sbuf(f"woq{i}", [128, 4 * D], BF16) for i in range(2)]
    wosg = [tm.sbuf(f"wosg{i}", [128, 2048], BF16) for i in range(3)]
    s_wos = [p.slot(f"wos{i}") for i in range(3)]
    xrs = [tm.sbuf(f"xrs{i}", [128, D], F32) for i in range(2)]
    s_xrs = [p.slot(f"xrs{i}") for i in range(2)]
    ybf = tm.sbuf("ybf", [128, D], BF16)

    WOr = WO.rearrange("(a p) o -> p a o", p=128)
    for oq in range(4):
        qb = oq % 2
        for g in range(4):
            b2 = (4 * oq + g) % 3
            cast_dma(
                wosg[b2][:].rearrange("p (a o) -> p a o", o=512)[:, 0:4, :],
                WOr[:, 4 * g:4 * (g + 1), 512 * oq:512 * (oq + 1)],
                [], [f"wosg{b2}"], s_wos[b2])
            for ai in range(4):
                a = 4 * g + ai
                for c in range(4):
                    pb = ptcs[c % 2]
                    p.op("pe", lambda e, b2=b2, ai=ai, a=a, c=c, pb=pb:
                         e.transpose(
                             pb[:, 512 * (a % 2) + 128 * (c // 2):
                                512 * (a % 2) + 128 * (c // 2) + 128],
                             wosg[b2][:, 512 * ai + 128 * c:
                                     512 * ai + 128 * (c + 1)],
                             idb[:]),
                         reads=[f"wosg{b2}", "idb"],
                         writes=[(pb.name, (a % 2, c // 2))])
                    p.op("dve", lambda e, qb=qb, a=a, c=c, pb=pb:
                         e.tensor_copy(
                             woq[qb][:, D * c + 128 * a:D * c + 128 * (a + 1)],
                             pb[:, 512 * (a % 2) + 128 * (c // 2):
                                512 * (a % 2) + 128 * (c // 2) + 128]),
                         reads=[(pb.name, (a % 2, c // 2))],
                         writes=[(f"woq{qb}", (a, c))])
        for t in range(NT):
            tb = t % 2
            if oq == 0:
                sync_dma(xrs[tb][:], XE[TC + 128 * t:TC + 128 * (t + 1), :],
                         [], [f"xrs{tb}"], s_xrs[tb])
            for dhalf in range(2):
                for c in range(4):
                    oc = 4 * oq + c
                    for dd in range(2):
                        p.op("pe", lambda e, qb=qb, c=c, t=t, oc=oc,
                             dhalf=dhalf, dd=dd: e.matmul(
                                 pc[t % 2][:, 512 * dd:512 * (dd + 1)],
                                 AOsb[oc][:, 128 * t:128 * (t + 1)],
                                 woq[qb][:, D * c + 1024 * dhalf + 512 * dd:
                                         D * c + 1024 * dhalf + 512 * (dd + 1)],
                                 start=(c == 0), stop=(c == 3)),
                             reads=[f"AO{oc}", (f"AO{oc}", 0), (f"AO{oc}", 1),
                                    f"woq{qb}"],
                             writes=[pc[t % 2].name])
                hk = "lo" if dhalf == 0 else "hi"
                if oq == 0:
                    p.op("dve", lambda e, t=t, tb=tb, dhalf=dhalf: e.tensor_tensor(
                        ACC[t][:, 1024 * dhalf:1024 * (dhalf + 1)],
                        xrs[tb][:, 1024 * dhalf:1024 * (dhalf + 1)],
                        pc[t % 2][:], ALU.add),
                        reads=[f"xrs{tb}", pc[t % 2].name],
                        writes=[(f"ACC{t}", hk)])
                else:
                    p.op("dve", lambda e, t=t, dhalf=dhalf: e.tensor_tensor(
                        ACC[t][:, 1024 * dhalf:1024 * (dhalf + 1)],
                        ACC[t][:, 1024 * dhalf:1024 * (dhalf + 1)],
                        pc[t % 2][:], ALU.add),
                        reads=[pc[t % 2].name, (f"ACC{t}", hk)],
                        writes=[(f"ACC{t}", hk)])

    def layernorm_tile(t):
        acc_h = ACC[t]
        tname = f"ACC{t}"
        for c in range(4):
            p.op("dve", lambda e, c=c: e.bn_stats(
                bns[:, 6 * c:6 * (c + 1)], acc_h[:, 512 * c:512 * (c + 1)]),
                reads=[(tname, "lo"), (tname, "hi")], writes=[("bns", c)])
        p.op("dve", lambda e: e.bn_aggr(aggr[:], bns[:]),
             reads=[("bns", 0), ("bns", 1), ("bns", 2), ("bns", 3)],
             writes=["aggr"])
        p.op("act", lambda e: e.activation(small[:, 24:25], aggr[:, 1:2],
                                           AF.Sqrt, bias=epsc[:, 1:2]),
             reads=["aggr", ("epsc", 1)], writes=[("small", 24)])
        p.op("dve", lambda e: e.reciprocal(small[:, 25:26], small[:, 24:25]),
             reads=[("small", 24)], writes=[("small", 25)])
        p.op("dve", lambda e: e.tensor_scalar(
            acc_h[:], acc_h[:], aggr[:, 0:1], small[:, 25:26],
            ALU.subtract, ALU.mult),
            reads=[(tname, "lo"), (tname, "hi"), "aggr", ("small", 25)],
            writes=[(tname, "lo"), (tname, "hi")])

    for t in range(NT):
        layernorm_tile(t)
        p.op("act", lambda e, t=t: e.activation(ybf[:], ACC[t][:], AF.Copy),
             reads=[(f"ACC{t}", "lo"), (f"ACC{t}", "hi")], writes=["ybf"])
        for dc in range(ND):
            pb = ptcs[dc // 8]
            p.op("pe", lambda e, dc=dc, pb=pb: e.transpose(
                pb[:, 128 * (dc % 8):128 * (dc % 8 + 1)],
                ybf[:, 128 * dc:128 * (dc + 1)], idb[:]),
                reads=["ybf", "idb"], writes=[(pb.name, dc % 8)])
        for dc in range(ND):
            pb = ptcs[dc // 8]
            p.op("dve", lambda e, t=t, dc=dc, pb=pb: e.tensor_copy(
                y1T[dc][:, 128 * t:128 * (t + 1)],
                pb[:, 128 * (dc % 8):128 * (dc % 8 + 1)]),
                reads=[(pb.name, dc % 8)], writes=[(f"y1T{dc}", t)])

    tm.free(*AOsb, *woq, *wosg, *xrs, ybf)

    if "Y1" in tap_outs:
        for t in range(NT):
            sync_dma(tap_outs["Y1"][128 * t:128 * (t + 1), :], ACC[t][:],
                     [(f"ACC{t}", "lo"), (f"ACC{t}", "hi")],
                     [("OUT", f"y1tap{t}")], p.slot("tap_Y1"), gid="y1g")

    if stop_after == "C":
        return finish_acc(ACC)

    gwbf = tm.sbuf("gwbf", [NE, D], BF16)
    gwT = tm.sbuf("gwT", [128, 128], BF16)
    cast_dma(gwbf[:], GW, [], ["gwbf"], p.slot("gw"))
    pL = tm.psum("pL", [128, 512])
    for dc in range(ND):
        p.op("pe", lambda e, dc=dc: e.transpose(
            ptc0[:, 8 * dc:8 * (dc + 1)],
            gwbf[:, 128 * dc:128 * (dc + 1)], idb[0:8, 0:8]),
            reads=["gwbf", "idb"], writes=[("ptc0", ("g", dc))])
        p.op("dve", lambda e, dc=dc: e.tensor_copy(
            gwT[:, 8 * dc:8 * (dc + 1)], ptc0[:, 8 * dc:8 * (dc + 1)]),
            reads=[("ptc0", ("g", dc))], writes=[("gwT", dc)])

    for t in range(NT):
        for dc in range(ND):
            p.op("pe", lambda e, t=t, dc=dc: e.matmul(
                pL[:, 0:8], y1T[dc][:, 128 * t:128 * (t + 1)],
                gwT[:, 8 * dc:8 * (dc + 1)],
                start=(dc == 0), stop=(dc == ND - 1)),
                reads=[(f"y1T{dc}", t), ("gwT", dc)], writes=["pL"])
        g0 = 40
        gsl = gates8[:, 8 * t:8 * (t + 1)]
        p.op("dve", lambda e: e.reduce_max(small[:, g0:g0 + 1], pL[:, 0:8],
                                           AX.X, negate=True),
             reads=["pL"], writes=[("small", g0)])
        p.op("act", lambda e, gsl=gsl: e.activation(
            gsl, pL[:, 0:8], AF.Exp, bias=small[:, g0:g0 + 1]),
            reads=["pL", ("small", g0)], writes=[("gates8", t)])
        p.op("dve", lambda e, gsl=gsl: e.reduce_sum(
            small[:, g0 + 1:g0 + 2], gsl, AX.X),
            reads=[("gates8", t)], writes=[("small", g0 + 1)])
        p.op("dve", lambda e: e.reciprocal(small[:, g0 + 2:g0 + 3],
                                           small[:, g0 + 1:g0 + 2]),
             reads=[("small", g0 + 1)], writes=[("small", g0 + 2)])
        p.op("dve", lambda e, gsl=gsl: e.tensor_scalar(
            gsl, gsl, small[:, g0 + 2:g0 + 3], None, ALU.mult),
            reads=[("gates8", t), ("small", g0 + 2)], writes=[("gates8", t)])
        p.op("dve", lambda e, gsl=gsl: e.reduce_max(
            small[:, g0 + 3:g0 + 4], gsl, AX.X),
            reads=[("gates8", t)], writes=[("small", g0 + 3)])
        p.op("dve", lambda e, gsl=gsl: e.tensor_scalar(
            small[:, 48:56], gsl, small[:, g0 + 3:g0 + 4], 1e30,
            ALU.is_ge, ALU.mult),
            reads=[("gates8", t), ("small", g0 + 3)], writes=[("small", 48)])
        p.op("dve", lambda e, gsl=gsl: e.tensor_tensor(
            small[:, 48:56], gsl, small[:, 48:56], ALU.subtract),
            reads=[("gates8", t), ("small", 48)], writes=[("small", 48)])
        p.op("dve", lambda e: e.reduce_max(small[:, g0 + 4:g0 + 5],
                                           small[:, 48:56], AX.X),
             reads=[("small", 48)], writes=[("small", g0 + 4)])
        p.op("dve", lambda e: e.tensor_tensor(small[:, g0 + 5:g0 + 6],
                                              small[:, g0 + 3:g0 + 4],
                                              small[:, g0 + 4:g0 + 5],
                                              ALU.add),
             reads=[("small", g0 + 3), ("small", g0 + 4)],
             writes=[("small", g0 + 5)])
        p.op("dve", lambda e: e.reciprocal(small[:, g0 + 6:g0 + 7],
                                           small[:, g0 + 5:g0 + 6]),
             reads=[("small", g0 + 5)], writes=[("small", g0 + 6)])
        p.op("dve", lambda e, gsl=gsl: e.tensor_scalar(
            small[:, 48:56], gsl, small[:, g0 + 4:g0 + 5], None, ALU.is_ge),
            reads=[("gates8", t), ("small", g0 + 4)], writes=[("small", 48)])
        p.op("dve", lambda e, gsl=gsl: e.tensor_tensor(
            gsl, gsl, small[:, 48:56], ALU.mult),
            reads=[("gates8", t), ("small", 48)], writes=[("gates8", t)])
        p.op("dve", lambda e, gsl=gsl: e.tensor_scalar(
            gsl, gsl, small[:, g0 + 6:g0 + 7], None, ALU.mult),
            reads=[("gates8", t), ("small", g0 + 6)], writes=[("gates8", t)])
        p.op("pe", lambda e, gsl=gsl: e.transpose(
            pL[0:8, 128:256], gsl, idf[:]),
            reads=[("gates8", t), "idf"], writes=["pL"])
        p.op("dve", lambda e, t=t: e.tensor_copy(
            grow8[:, 128 * t:128 * (t + 1)], pL[0:8, 128:256]),
            reads=["pL"], writes=[("grow8", t)])

    if "GATES" in tap_outs:
        gtmp = tm.sbuf("gtmp", [128, NT * NE], F32)
        p.op("dve", lambda e: e.tensor_copy(gtmp[:], gates8[:]),
             reads=["gates8"] + [("gates8", t) for t in range(NT)],
             writes=["gtmp"])
        sync_dma(tap_outs["GATES"].rearrange("(t p) e -> p t e", p=128),
                 gtmp[:].rearrange("p (t e) -> p t e", e=NE),
                 ["gtmp"], [("OUT", "gtap")], p.slot("tap_GATES"),
                 allow_slow_non_contiguous=True)
        tm.free(gtmp)

    tm.free(gwbf, gwT)
    tm.free(pL, pc[1], pc[0], ptc1, ptc0)

    if stop_after == "D":
        return finish_acc(ACC)

    FGRP = [(0, 256), (256, 256), (512, 256), (768, 256), (1024, 256),
            (1280, 128)]
    wgb = [tm.sbuf(f"wgb{i}", [128, ND * 256], BF16) for i in range(3)]
    wub = [tm.sbuf(f"wub{i}", [128, ND * 256], BF16) for i in range(3)]
    s_wgb = [p.slot(f"wgb{i}") for i in range(3)]
    s_wub = [p.slot(f"wub{i}") for i in range(3)]
    wdb = [tm.sbuf(f"wdb{i}", [128, 11 * 256], BF16) for i in range(3)]
    s_wdb = [p.slot(f"wdb{i}") for i in range(3)]
    hT = [tm.sbuf(f"hT{i}", [128, TC], BF16) for i in range(11)]
    gbsb = tm.sbuf("gbsb", [128, TC], F32)
    gsb = tm.sbuf("gsb", [128, TC], BF16)
    usb = tm.sbuf("usb", [128, TC], BF16)
    pG = tm.psum("pG", [128, 1024])
    pU = tm.psum("pU", [128, 1024])
    pd = [tm.psum(f"pd{i}", [128, 512]) for i in range(2)]

    WGr = [WGATE[e].rearrange("(a p) f -> p a f", p=128) for e in range(NE)]
    WUr = [WUP[e].rearrange("(a p) f -> p a f", p=128) for e in range(NE)]
    WDr = [WDOWN[e].rearrange("(c p) d -> p c d", p=128) for e in range(NE)]
    WGr.append(SWG.rearrange("(a p) f -> p a f", p=128))
    WUr.append(SWU.rearrange("(a p) f -> p a f", p=128))
    WDr.append(SWD.rearrange("(c p) d -> p c d", p=128))
    grow_regions = ["grow8"] + [("grow8", t) for t in range(NT)]

    gi = 0
    di = 0
    for e_i in range(NE + 1):
        routed = e_i < NE
        for g, (f0, fw) in enumerate(FGRP):
            b = gi % 3
            gi += 1
            for blk in range(0, ND, 4):
                cast_dma(
                    wgb[b][:].rearrange("p (a f) -> p a f", f=256)
                    [:, blk:blk + 4, 0:fw],
                    WGr[e_i][:, blk:blk + 4, f0:f0 + fw],
                    [], [(f"wgb{b}", blk)], s_wgb[b], gid=("g", e_i, g))
                cast_dma(
                    wub[b][:].rearrange("p (a f) -> p a f", f=256)
                    [:, blk:blk + 4, 0:fw],
                    WUr[e_i][:, blk:blk + 4, f0:f0 + fw],
                    [], [(f"wub{b}", blk)], s_wub[b], gid=("g", e_i, g))
            for cc in range(fw // 128):
                fc = f0 // 128 + cc
                for th in range(2):
                    for a in range(ND):
                        p.op("pe", lambda e, b=b, a=a, cc=cc, th=th: e.matmul(
                            pG[:, 512 * th:512 * (th + 1)],
                            wgb[b][:, 256 * a + 128 * cc:
                                   256 * a + 128 * (cc + 1)],
                            y1T[a][:, 512 * th:512 * (th + 1)],
                            start=(a == 0), stop=(a == ND - 1)),
                            reads=[f"wgb{b}", f"y1T{a}"] +
                                  [(f"y1T{a}", t) for t in range(NT)],
                            writes=["pG"])
                    for a in range(ND):
                        p.op("pe", lambda e, b=b, a=a, cc=cc, th=th: e.matmul(
                            pU[:, 512 * th:512 * (th + 1)],
                            wub[b][:, 256 * a + 128 * cc:
                                   256 * a + 128 * (cc + 1)],
                            y1T[a][:, 512 * th:512 * (th + 1)],
                            start=(a == 0), stop=(a == ND - 1)),
                            reads=[f"wub{b}", f"y1T{a}"] +
                                  [(f"y1T{a}", t) for t in range(NT)],
                            writes=["pU"])
                p.op("act", lambda e: e.activation(gsb[:], pG[:], AF.Silu),
                     reads=["pG"], writes=["gsb"])
                p.op("dve", lambda e: e.tensor_copy(usb[:], pU[:]),
                     reads=["pU"], writes=["usb"])
                p.op("dve", lambda e, fc=fc: e.tensor_tensor(
                    hT[fc][:], gsb[:], usb[:], ALU.mult),
                    reads=["gsb", "usb"], writes=[f"hT{fc}"])
        for q in range(8):
            b = di % 3
            di += 1
            for blk, cnt in ((0, 4), (4, 4), (8, 3)):
                cast_dma(
                    wdb[b][:].rearrange("p (c d) -> p c d", d=256)
                    [:, blk:blk + cnt, :],
                    WDr[e_i][:, blk:blk + cnt, 256 * q:256 * (q + 1)],
                    [], [(f"wdb{b}", blk)], s_wdb[b], gid=("d", e_i, q))
            for t in range(NT):
                pdb = pd[t % 2]
                hk = "lo" if q < 4 else "hi"
                for fc in range(11):
                    p.op("pe", lambda e, b=b, fc=fc, t=t, pdb=pdb, q=q:
                         e.matmul(
                             pdb[:, 0:256], hT[fc][:, 128 * t:128 * (t + 1)],
                             wdb[b][:, 256 * fc:256 * (fc + 1)],
                             start=(fc == 0), stop=(fc == 10)),
                         reads=[f"hT{fc}", f"wdb{b}"],
                         writes=[pdb.name])
                if routed:
                    p.op("dve", lambda e, t=t, q=q, e_i=e_i, pdb=pdb:
                         e.scalar_tensor_tensor(
                             ACC[t][:, 256 * q:256 * (q + 1)], pdb[:, 0:256],
                             gates8[:, 8 * t + e_i:8 * t + e_i + 1],
                             ACC[t][:, 256 * q:256 * (q + 1)],
                             ALU.mult, ALU.add),
                         reads=[pdb.name, ("gates8", t), (f"ACC{t}", hk)],
                         writes=[(f"ACC{t}", hk)])
                else:
                    p.op("dve", lambda e, t=t, q=q, pdb=pdb: e.tensor_tensor(
                        ACC[t][:, 256 * q:256 * (q + 1)],
                        ACC[t][:, 256 * q:256 * (q + 1)], pdb[:, 0:256],
                        ALU.add),
                        reads=[pdb.name, (f"ACC{t}", hk)],
                        writes=[(f"ACC{t}", hk)])

    if "ACC2" in tap_outs:
        for t in range(NT):
            sync_dma(tap_outs["ACC2"][128 * t:128 * (t + 1), :], ACC[t][:],
                     [(f"ACC{t}", "lo"), (f"ACC{t}", "hi")],
                     [("OUT", f"acc2tap{t}")], p.slot("tap_ACC2"),
                     gid="a2g")

    for t in range(NT):
        layernorm_tile(t)
        sync_dma(Y[128 * t:128 * (t + 1), :], ACC[t][:],
                 [(f"ACC{t}", "lo"), (f"ACC{t}", "hi")],
                 [("OUT", t)], s_out)

    p.op("pool", lambda e: e.nop(), reads=[("OUT", None)], tag="final")

    with nc.Block() as block:
        for e in ENGINES:
            sems[e] = st.enter_context(nc.semaphore(f"sem_{e}"))
        for s in p.slots.values():
            s.sem = st.enter_context(nc.semaphore(f"dma_{s.name}"))
        p.emit_all(block, sems)
    st.close()
    nc._dbg_prog = p
    return nc


def host_consts(core):
    M = np.zeros((128, 1152), dtype=np.float32)
    s = np.arange(128)[:, None]
    i = np.arange(1152)[None, :]
    d = i - s
    M[:] = np.where((d >= 0) & (d <= WIN), (s - i).astype(np.float32),
                    np.float32(-1e9)).astype(np.float32)
    poi = np.zeros((128, NS), dtype=np.float32)
    if core == 0:
        poi[:, 0:8] = -1e5
    idf = np.eye(128, dtype=np.float32)
    return M, poi, idf


_nc_cache = {}


def _get_nc(taps=()):
    key = tuple(sorted(taps))
    if key not in _nc_cache:
        _nc_cache[key] = build_nc(key)
    return _nc_cache[key]


def kernel(L0_attn_x, L0_attn_wq, L0_attn_wk, L0_attn_wv, L0_attn_wo,
           L0_attn_q_norm_w, L0_attn_k_norm_w, L0_ffn_gate_w, L0_ffn_w_gate,
           L0_ffn_w_up, L0_ffn_w_down, L0_ffn_sw_gate, L0_ffn_sw_up,
           L0_ffn_sw_down, _taps=(), _collect=None):
    x = np.ascontiguousarray(np.asarray(L0_attn_x, dtype=np.float32)
                             .reshape(NCORE * TC, D))
    asf = lambda a: np.ascontiguousarray(np.asarray(a, dtype=np.float32))
    wq, wk, wv, wo = (asf(L0_attn_wq), asf(L0_attn_wk), asf(L0_attn_wv),
                      asf(L0_attn_wo))
    operm = np.concatenate(
        [np.arange(64 * h, 64 * h + 64)
         for j in range(16) for h in (8 * (j // 4) + (j % 4),
                                      8 * (j // 4) + (j % 4) + 4)])
    wq = np.ascontiguousarray(wq[operm, :])
    wo = np.ascontiguousarray(wo[:, operm])
    qnw = asf(L0_attn_q_norm_w).reshape(1, D)
    qnw = np.ascontiguousarray(qnw[:, operm])
    knw = asf(L0_attn_k_norm_w).reshape(1, 512)
    gw = asf(L0_ffn_gate_w)
    wgate, wup, wdown = asf(L0_ffn_w_gate), asf(L0_ffn_w_up), asf(L0_ffn_w_down)
    swg, swu, swd = asf(L0_ffn_sw_gate), asf(L0_ffn_sw_up), asf(L0_ffn_sw_down)

    nc = _get_nc(tuple(_taps))
    in_maps = []
    for c in range(NCORE):
        if c == 0:
            prev = np.zeros((TC, D), dtype=np.float32)
        else:
            prev = x[TC * (c - 1):TC * c]
        xe = np.ascontiguousarray(
            np.concatenate([prev, x[TC * c:TC * (c + 1)]], axis=0))
        M, poi, idf = host_consts(c)
        in_maps.append(dict(
            xe=xe, wq=wq, wk=wk, wv=wv, wo=wo, qnw=qnw, knw=knw, gw=gw,
            wgate=wgate, wup=wup, wdown=wdown, swg=swg, swu=swu, swd=swd,
            mmask=M, poison=poi, identf=idf))
    res = run_bass_kernel_spmd(nc, in_maps, core_ids=list(range(NCORE)))
    if _collect is not None:
        _collect.extend(res.results)
    y = np.concatenate([res.results[c]["y"] for c in range(NCORE)], axis=0)
    return y.reshape(1, NCORE * TC, D)


# version 08
# speedup vs baseline: 1.4066x; 1.1406x over previous
import numpy as np
from contextlib import ExitStack

import concourse.bass as bass
import concourse.mybir as mybir
from concourse.bass_utils import run_bass_kernel_spmd

F32 = mybir.dt.float32
BF16 = mybir.dt.bfloat16
AF = mybir.ActivationFunctionType
ALU = mybir.AluOpType
AX = mybir.AxisListType

NCORE = 8
TC = 1024
TEXT = 2048
D = 2048
NH = 32
NKV = 8
DH = 64
WIN = 1024
DFF = 1408
NE = 8
NT = TC // 128
NS = TEXT // 128
ND = D // 128

SLOPES = [float(2.0 ** (-0.25 * h)) for h in range(NH)]

ENGINES = ("pe", "act", "dve", "pool", "sync")


class DmaSlot:
    __slots__ = ("name", "sem", "count", "groups")

    def __init__(self, name):
        self.name = name
        self.sem = None
        self.count = 0
        self.groups = {}


class Op:
    __slots__ = ("idx", "engine", "emit", "reads", "writes", "deps", "slot",
                 "slot_count", "slot_group", "need_inc", "milestone", "tag")

    def __init__(self, idx, engine, emit, reads, writes, slot, tag):
        self.idx = idx
        self.engine = engine
        self.emit = emit
        self.reads = reads
        self.writes = writes
        self.deps = []
        self.slot = slot
        self.slot_count = None
        self.slot_group = None
        self.need_inc = False
        self.milestone = None
        self.tag = tag


class Prog:
    def __init__(self, nc, self_sync=True):
        self.nc = nc
        self.ops = []
        self.track = {}
        self.slots = {}
        self.self_sync = self_sync
        self.pending_free = []
        self.newdeps = {}
        self.psum_names = set()
        self.n_waits = 0
        self.n_incs = 0

    def slot(self, name):
        s = self.slots.get(name)
        if s is None:
            s = self.slots[name] = DmaSlot(name)
        return s

    def new_tensor(self, name):
        if self.pending_free:
            self.newdeps[name] = self._prune(self.pending_free)

    def note_free(self, name):
        t = self.track.get(name)
        if t is None:
            return
        ops = list(t["w"].values())
        for lst in t["r"].values():
            ops.extend(lst)
        self.pending_free.extend(ops)
        self.pending_free = self._prune(self.pending_free)

    @staticmethod
    def _prune(deps):
        best = {}
        for d in deps:
            key = ("s", d.slot.name) if d.slot is not None else ("e", d.engine)
            cur = best.get(key)
            if cur is None or d.idx > cur.idx:
                best[key] = d
        return list(best.values())

    @staticmethod
    def _regions(lst):
        out = []
        for r in lst or ():
            if isinstance(r, tuple):
                out.append(r)
            else:
                out.append((r, None))
        return out

    def _dep_read(self, name, key):
        t = self.track.get(name)
        if t is None:
            return []
        w = t["w"]
        if key is None:
            return list(w.values())
        deps = []
        if key in w:
            deps.append(w[key])
        if None in w:
            deps.append(w[None])
        return deps

    def _dep_write(self, name, key):
        deps = self._dep_read(name, key)
        t = self.track.get(name)
        if t is not None:
            r = t["r"]
            if key is None:
                for lst in r.values():
                    deps.extend(lst)
            else:
                deps.extend(r.get(key, ()))
                deps.extend(r.get(None, ()))
        return deps

    def op(self, engine, emit, reads=(), writes=(), slot=None, tag="",
           gid=None):
        wr = self._regions(writes)
        rd = self._regions(reads)
        if engine == "pe":
            wr = [(n, None) if n in self.psum_names else (n, k)
                  for (n, k) in wr]
        else:
            promote = [(n, None) for (n, k) in rd if n in self.psum_names]
            if promote:
                rd = [(n, k) for (n, k) in rd if n not in self.psum_names]
                wr = wr + promote
        o = Op(len(self.ops), engine, emit, rd, wr, slot, tag)
        deps = []
        for (n, k) in o.reads:
            deps.extend(self._dep_read(n, k))
            if n in self.newdeps:
                deps.extend(self.newdeps.pop(n))
        for (n, k) in o.writes:
            deps.extend(self._dep_write(n, k))
            if n in self.newdeps:
                deps.extend(self.newdeps.pop(n))
        pruned = {}
        for d in deps:
            if d is o:
                continue
            if d.engine == o.engine and d.slot is None:
                if o.engine == "pe" or not self.self_sync:
                    continue
            key = ("s", d.slot.name) if d.slot is not None else ("e", d.engine)
            cur = pruned.get(key)
            if cur is None or d.idx > cur.idx:
                pruned[key] = d
        o.deps = list(pruned.values())
        if slot is not None:
            slot.count += 1
            o.slot_count = slot.count
            if gid is None:
                gid = ("u", slot.count)
            o.slot_group = gid
            g = slot.groups.get(gid, 0)
            slot.groups[gid] = max(g, slot.count)
        for (n, k) in o.reads:
            self.track.setdefault(n, {"w": {}, "r": {}})["r"].setdefault(
                k, []).append(o)
        for (n, k) in o.writes:
            t = self.track.setdefault(n, {"w": {}, "r": {}})
            if k is None:
                t["w"] = {None: o}
                t["r"] = {}
            else:
                t["w"][k] = o
                t["r"].pop(k, None)
        self.ops.append(o)
        return o

    def dma(self, queue, emit, reads=(), writes=(), slot=None, tag="",
            gid=None):
        assert slot is not None
        return self.op(queue, emit, reads=reads, writes=writes, slot=slot,
                       tag=tag, gid=gid)

    def emit_all(self, block, engine_sems):
        for o in self.ops:
            for d in o.deps:
                if d.slot is None:
                    d.need_inc = True
        counters = {e: 0 for e in ENGINES}
        for o in self.ops:
            if o.need_inc and o.slot is None:
                counters[o.engine] += 1
                o.milestone = counters[o.engine]
        per_engine = {e: [] for e in ENGINES}
        for o in self.ops:
            per_engine[o.engine].append(o)
        prog = self

        def runner(eng_name):
            def body(eng):
                waited = {}
                for o in per_engine[eng_name]:
                    for d in o.deps:
                        if d.slot is not None:
                            sem = d.slot.sem
                            val = 16 * d.slot.groups[d.slot_group]
                        else:
                            sem = engine_sems[d.engine]
                            val = d.milestone
                        key = sem.name
                        if waited.get(key, -1) >= val:
                            continue
                        eng.wait_ge(sem, val)
                        waited[key] = val
                        prog.n_waits += 1
                    try:
                        inst = o.emit(eng)
                    except Exception:
                        print("EMIT FAIL idx=", o.idx, "eng=", o.engine,
                              "tag=", o.tag, "reads=", o.reads,
                              "writes=", o.writes, flush=True)
                        raise
                    if o.slot is not None:
                        inst.then_inc(o.slot.sem, 16)
                        prog.n_incs += 1
                    elif o.need_inc:
                        inst.then_inc(engine_sems[o.engine], 1)
                        prog.n_incs += 1
            return body

        section = {"pe": block.tensor, "act": block.scalar,
                   "dve": block.vector, "pool": block.gpsimd,
                   "sync": block.sync}
        for e in ENGINES:
            if per_engine[e]:
                section[e](runner(e))


ARENA_BYTES = 211968


class TMgr:

    def __init__(self, nc, prog, off):
        self.nc = nc
        self.p = prog
        self.off = off
        self.cms = {}
        self.logical = {}
        self.acm = nc.sbuf_tensor("arena", [128, ARENA_BYTES], mybir.dt.uint8)
        self.ah = self.acm.__enter__()
        self.abase = nc.lookup_mloc(self.ah).addr
        self.spans = {}

    def sbuf(self, name, shape, dtype):
        import functools, operator
        off = self.off[name]
        per_part = functools.reduce(operator.mul, shape[1:], 1) *             mybir.dt.size(dtype)
        assert off + per_part <= ARENA_BYTES, (name, off, per_part)
        assert off % 64 == 0, (name, off)
        h = self.nc.alloc_sbuf_tensor_at(name, shape, dtype,
                                         offset=self.abase + off)
        self.spans[name] = (off, off + per_part)
        self.logical[h.name] = name
        self.p.new_tensor(name)
        return h

    def psum(self, name, shape, dtype=F32):
        cm = self.nc.psum_tensor(name, shape, dtype)
        h = cm.__enter__()
        self.cms[name] = cm
        self.logical[h.name] = name
        self.p.psum_names.add(name)
        self.p.new_tensor(name)
        return h

    def free(self, *handles):
        for h in handles:
            name = self.logical.get(h.name, h.name)
            self.p.note_free(name)
            cm = self.cms.pop(name, None)
            if cm is not None:
                cm.__exit__(None, None, None)


def build_nc(taps=(), stop_after=None):
    nc = bass.Bass("TRN2", target_bir_lowering=False)
    taps = set(taps)

    din = lambda n, s: nc.dram_tensor(n, s, F32, kind="ExternalInput").ap()
    XE = din("xe", [TEXT, D])
    WQ = din("wq", [D, D])
    WK = din("wk", [512, D])
    WV = din("wv", [512, D])
    WO = din("wo", [D, D])
    QNW = din("qnw", [1, D])
    KNW = din("knw", [1, 512])
    GW = din("gw", [NE, D])
    WGATE = din("wgate", [NE, D, DFF])
    WUP = din("wup", [NE, D, DFF])
    WDOWN = din("wdown", [NE, DFF, D])
    SWG = din("swg", [D, DFF])
    SWU = din("swu", [D, DFF])
    SWD = din("swd", [DFF, D])
    MM = din("mmask", [128, 1152])
    LTRI = din("ltri", [128, 128])
    TCOL = din("tcol", [128, 8])
    IOROW = din("iorow", [1, 384])
    Y1D = nc.dram_tensor("y1d", [TC, D], BF16, kind="Internal").ap()
    PK = nc.dram_tensor("pk", [NE * 384, D], BF16, kind="Internal").ap()
    POI = din("poison", [128, NS])
    IDF = din("identf", [128, 128])
    Y = nc.dram_tensor("y", [TC, D], F32, kind="ExternalOutput").ap()
    tap_outs = {}
    tap_shapes = {"KT": [512, TEXT], "QT": [D, TC], "MALL": [NH, TC],
                  "AO": [D, TC], "Y1": [TC, D], "GATES": [TC, NE],
                  "ACC2": [TC, D]}
    for t in taps:
        tap_outs[t] = nc.dram_tensor("tap_" + t, tap_shapes[t], F32,
                                     kind="ExternalOutput").ap()

    RA, RB, RC, RD, RE = 10240, 79872, 113664, 146432, 179200
    OFF = {
        "idb": 0, "idf": 256, "ones1": 768, "onesf": 1024, "small": 1536,
        "bns": 1792, "aggr": 1920, "knacc": 1984, "knbneg": 2048,
        "knscr": 2176, "epsc": 9920, "gates8": 3520, "idxall": 3776,
        "g12i": 3904, "gg12": 3968, "onescol": 4032,
        "grow8": 4800, "garow0": 5824,
        "garow1": 7872,
        "wkT": RA, "wvT": RA + 16384, "wstg0": RA + 32768, "wstg1": RA + 36864,
        **{f"wqT{i}": RA + 4096 * i for i in range(16)},
        **{f"Fr{i}": RA + 2048 * i for i in range(12)},
        "woq0": RA, "woq1": RA + 16384, "wosg0": RA + 32768,
        "wosg1": RA + 36864, "wosg2": RA + 65536, "xrs0": RA + 40960, "xrs1": RA + 49152,
        "ybf": RA + 57344, "gwbf": RA + 61440, "gwT": RA + 57344,
        **{f"wgb{i}": RA + 8192 * i for i in range(3)},
        **{f"wub{i}": RA + 24576 + 8192 * i for i in range(3)},
        **{f"wdb{i}": RA + 49152 + 5696 * i for i in range(3)},
        **{f"KT{i}": RB + 4096 * i for i in range(4)},
        **{f"VA{i}": RB + 16384 + 2112 * i for i in range(8)},
        **{f"ACC{i}": (RB + 8192 * i if i < 4 else RC + 8192 * (i - 4))
           for i in range(8)},
        "m_all": RC, "mrow0": RC + 2048, "mrow1": RC + 4096,
        "rrecb": RC + 6144,
        "Qsb": RC + 8192, "sqs": RC + 16384, "wstgq0": RC + 24576,
        "wstgq1": RC + 28672,
        "Mm": RC + 8192, "poi": RC + 12800, "Hb0": RC + 12864,
        "Hb1": RC + 15168, "Fb0": RC + 17472, "Fb1": RC + 19520,
        "rsb": RC + 21568, "rsb2": RC + 25664, "rbsb": RC + 29760,
        "xstg0": RD, "xstg1": RD + 4096, "xTc0": RD + 8192,
        "xTc1": RD + 12288, "ksqs": RD + 16384, "row1": RD + 18432,
        "kwb": RD + 22528, "qwb": RD + 24576,
        **{f"AO{i}": RD + 2048 * i for i in range(16)},
        **{f"hT{i}": RD + 2048 * i for i in range(11)},
        "gsbs": RD + 22528, "usbs": RD + 24576,
        "ygT": RD, "Ygt": RD + 12288,
        **{f"Dsb{i}": RD + 4096 * i for i in range(3)},
        **{f"hTe{i}": RD + 20480 + 768 * i for i in range(11)},
        "gsb": RD + 28928, "usb": RD + 29696,
        "PTf": RD, "iotabc": RD + 12288, "slotf": RD + 13952,
        "posf": RD + 14208, "mkb": RD + 14464, "tokids": RD + 14592,
        "LTb": RD + 14656, "bb": RD + 14912, "brow": RD + 15168,
        "crow": RD + 15424, "iorowf": RD + 15680, "g12scr": RD + 17216,
        "rload0": RA, "rload1": RA + 4096, "rload2": RA + 8192,
        "rload3": RA + 12288,
        "ttmp_KT": RA + 32768, "ttmp_QT": RA + 32768, "matmp": RA + 40960,
        "dummyA": RE + 24576, "dummyB": RA + 32768,
        "gtmp": RA + 32768, "ttmp_AO": RA + 32768,
        **{f"QT{i}": RE + 2048 * i for i in range(16)},
        **{f"y1T{i}": RE + 2048 * i for i in range(16)},
    }
    p = Prog(nc, self_sync=True)
    tm = TMgr(nc, p, OFF)
    st = ExitStack()

    idb = tm.sbuf("idb", [128, 128], BF16)
    idf = tm.sbuf("idf", [128, 128], F32)
    ones1 = tm.sbuf("ones1", [1, 128], BF16)
    onescol = tm.sbuf("onescol", [128, 1], BF16)
    onesf = tm.sbuf("onesf", [1, 128], F32)
    epsc = tm.sbuf("epsc", [128, 2], F32)
    knscr = tm.sbuf("knscr", [8, 336], F32)
    m_all = tm.sbuf("m_all", [NH, TC], BF16)
    mrows = [tm.sbuf(f"mrow{i}", [1, TC], BF16) for i in range(2)]
    knacc = tm.sbuf("knacc", [128, NKV], F32)
    knbneg = tm.sbuf("knbneg", [128, NH], F32)
    gates8 = tm.sbuf("gates8", [128, NT * NE], F32)
    grow8 = tm.sbuf("grow8", [NE, TC], BF16)
    garow = [tm.sbuf(f"garow{i}", [1, TC], BF16) for i in range(2)]
    small = tm.sbuf("small", [128, 64], F32)
    bns = tm.sbuf("bns", [128, 24], F32)
    aggr = tm.sbuf("aggr", [128, 2], F32)

    sems = {}
    s_misc = p.slot("misc")
    s_miscp = p.slot("miscp")
    s_out = p.slot("out")

    def sync_dma(dst, src, reads, writes, slot, gid=None, **kw):
        return p.dma("sync", lambda e: e.dma_start(dst, src, **kw),
                     reads=reads, writes=writes, slot=slot, gid=gid)

    def cast_dma(dst, src, reads, writes, slot, gid=None):
        return p.dma("pool", lambda e: e.dma_start(dst, src),
                     reads=reads, writes=writes, slot=slot, gid=gid)

    def finish_dummy(which):
        dmy = tm.sbuf(which, [128, D], F32)
        p.op("dve", lambda e: e.memset(dmy[:], 0.5), writes=[which])
        for t in range(NT):
            sync_dma(Y[128 * t:128 * (t + 1), :], dmy[:], [which],
                     [("OUT", t)], s_out, gid="yg")
        p.op("pool", lambda e: e.nop(), reads=[("OUT", None)], tag="final")
        with nc.Block() as block:
            for e in ENGINES:
                sems[e] = st.enter_context(nc.semaphore(f"sem_{e}"))
            for sl_ in p.slots.values():
                sl_.sem = st.enter_context(nc.semaphore(f"dma_{sl_.name}"))
            p.emit_all(block, sems)
        st.close()
        nc._dbg_prog = p
        return nc

    def finish_acc(ACC):
        for t in range(NT):
            sync_dma(Y[128 * t:128 * (t + 1), :], ACC[t][:],
                     [(f"ACC{t}", "lo"), (f"ACC{t}", "hi")],
                     [("OUT", t)], s_out, gid="yg")
        p.op("pool", lambda e: e.nop(), reads=[("OUT", None)], tag="final")
        with nc.Block() as block:
            for e in ENGINES:
                sems[e] = st.enter_context(nc.semaphore(f"sem_{e}"))
            for sl_ in p.slots.values():
                sl_.sem = st.enter_context(nc.semaphore(f"dma_{sl_.name}"))
            p.emit_all(block, sems)
        st.close()
        nc._dbg_prog = p
        return nc

    sems_holder = sems

    sync_dma(idf[:], IDF, [], ["idf"], s_misc)
    cast_dma(idb[:], IDF, [], ["idb"], s_miscp)
    p.op("dve", lambda e: e.memset(ones1[:], 1.0), writes=["ones1"])
    p.op("dve", lambda e: e.memset(onescol[:], 1.0), writes=["onescol"])
    p.op("dve", lambda e: e.memset(onesf[:], 1.0), writes=["onesf"])
    p.op("dve", lambda e: e.memset(epsc[:, 0:1], 1e-6), writes=[("epsc", 0)])
    p.op("dve", lambda e: e.memset(epsc[:, 1:2], 1e-5), writes=[("epsc", 1)])

    pt0 = tm.psum("pt0", [128, 1024], BF16)
    pt1 = tm.psum("pt1", [128, 1024], BF16)
    pts = [pt0, pt1]
    pB = tm.psum("pB", [128, 512])
    pK = tm.psum("pK", [128, 512])
    pV = tm.psum("pV", [128, 512])

    wkT = tm.sbuf("wkT", [128, ND * 512], BF16)
    wvT = tm.sbuf("wvT", [128, ND * 512], BF16)
    wstg = [tm.sbuf(f"wstg{i}", [128, D], BF16) for i in range(2)]
    s_wstg = [p.slot(f"wstg{i}") for i in range(2)]
    KT = [tm.sbuf(f"KT{i}", [128, TEXT], BF16) for i in range(4)]
    VA = [tm.sbuf(f"VA{i}", [128, NS * 65], BF16) for i in range(NKV)]
    kwb = tm.sbuf("kwb", [128, 512], F32)
    row1 = tm.sbuf("row1", [1, 2048], BF16)

    def wT_build(W, nrow_tiles, dstT, dname):
        for r in range(nrow_tiles):
            b = r % 2
            cast_dma(wstg[b][:], W[128 * r:128 * (r + 1), :],
                     [], [f"wstg{b}"], s_wstg[b])
            for dc in range(ND):
                pb = pts[dc // 8]
                p.op("pe", lambda e, b=b, dc=dc, pb=pb: e.transpose(
                    pb[:, 128 * (dc % 8):128 * (dc % 8 + 1)],
                    wstg[b][:, 128 * dc:128 * (dc + 1)], idb[:]),
                    reads=[f"wstg{b}", "idb"], writes=[(pb.name, dc % 8)])
            for half in range(2):
                pb = pts[half]
                dst3 = dstT[:].rearrange("p (dc o) -> p dc o", o=512)[
                    :, 8 * half:8 * (half + 1), 128 * r:128 * (r + 1)]
                src3 = pb[:].rearrange("p (dc o) -> p dc o", o=128)
                p.op("dve", lambda e, dst3=dst3, src3=src3: e.tensor_copy(
                    dst3, src3),
                    reads=[pb.name, (pb.name, 0), (pb.name, 1), (pb.name, 2),
                           (pb.name, 3), (pb.name, 4), (pb.name, 5),
                           (pb.name, 6), (pb.name, 7)],
                    writes=[(dname, (r, half))])

    wT_build(WK, 4, wkT, "wkT")
    wT_build(WV, 4, wvT, "wvT")

    cast_dma(row1[0:1, 0:512], KNW, [], [("row1", "k")], p.slot("knw"))
    p.op("pe", lambda e: e.matmul(pB[:], ones1[:], row1[0:1, 0:512],
                                  start=True, stop=True),
         reads=[("row1", "k"), "ones1"], writes=["pB"])
    p.op("dve", lambda e: e.tensor_copy(kwb[:], pB[:]), reads=["pB"],
         writes=["kwb"])

    xstg = [tm.sbuf(f"xstg{i}", [128, D], BF16) for i in range(2)]
    s_xstg = [p.slot(f"xstg{i}") for i in range(2)]
    xTc = [tm.sbuf(f"xTc{i}", [128, D], BF16) for i in range(2)]
    ksqs = tm.sbuf("ksqs", [128, 512], F32)

    p.op("dve", lambda e: e.memset(knacc[:], 0.0), writes=["knacc"])

    def transpose_tile(srcname, src, key=None):
        rkey = srcname if key is None else (srcname, key)
        for dc in range(ND):
            pb = pts[dc // 8]
            p.op("pe", lambda e, dc=dc, pb=pb, src=src: e.transpose(
                pb[:, 128 * (dc % 8):128 * (dc % 8 + 1)],
                src[:, 128 * dc:128 * (dc + 1)], idb[:]),
                reads=[rkey, "idb"], writes=[(pb.name, dc % 8)])

    def evac_xTc(b):
        for half in range(2):
            pb = pts[half]
            p.op("dve", lambda e, b=b, half=half, pb=pb: e.tensor_copy(
                xTc[b][:, 1024 * half:1024 * (half + 1)], pb[:]),
                reads=[pb.name] + [(pb.name, i) for i in range(8)],
                writes=[(f"xTc{b}", half)])

    for j in range(NS):
        b = j % 2
        cast_dma(xstg[b][:], XE[128 * j:128 * (j + 1), :], [], [f"xstg{b}"],
                 s_xstg[b])
        transpose_tile(f"xstg{b}", xstg[b][:])
        evac_xTc(b)
        for dc in range(ND):
            p.op("pe", lambda e, b=b, dc=dc: e.matmul(
                pK[:], xTc[b][:, 128 * dc:128 * (dc + 1)],
                wkT[:, 512 * dc:512 * (dc + 1)],
                start=(dc == 0), stop=(dc == ND - 1)),
                reads=[(f"xTc{b}", dc // 8), "wkT"], writes=["pK"])
        p.op("act", lambda e: e.activation(ksqs[:], pK[:], AF.Square),
             reads=["pK"], writes=["ksqs"])
        p.op("dve", lambda e: e.reduce_sum(small[:, 0:1], ksqs[:], AX.X),
             reads=["ksqs"], writes=[("small", 0)])
        p.op("act", lambda e: e.activation(small[:, 1:2], small[:, 0:1],
                                           AF.Sqrt, bias=epsc[:, 0:1],
                                           scale=1.0 / 512),
             reads=[("small", 0), ("epsc", 0)], writes=[("small", 1)])
        p.op("dve", lambda e: e.reciprocal(small[:, 2:3], small[:, 1:2]),
             reads=[("small", 1)], writes=[("small", 2)])
        p.op("dve", lambda e, b=b: e.scalar_tensor_tensor(
            xstg[b][:, 0:512], pK[:], small[:, 2:3], kwb[:],
            ALU.mult, ALU.mult),
            reads=["pK", ("small", 2), "kwb"], writes=[(f"xstg{b}", "ks")])
        p.op("act", lambda e, b=b: e.activation(ksqs[:], xstg[b][:, 0:512],
                                                AF.Square),
             reads=[(f"xstg{b}", "ks")], writes=["ksqs"])
        p.op("dve", lambda e: e.reduce_sum(
            small[:, 8:16], ksqs[:].rearrange("p (g x) -> p g x", g=8), AX.X),
            reads=["ksqs"], writes=[("small", 8)])
        p.op("dve", lambda e: e.tensor_tensor(knacc[:], knacc[:],
                                              small[:, 8:16], ALU.max),
             reads=[("small", 8), "knacc"], writes=["knacc"])
        for c in range(4):
            p.op("pe", lambda e, b=b, c=c: e.transpose(
                pt0[:, 128 * c:128 * (c + 1)],
                xstg[b][:, 128 * c:128 * (c + 1)], idb[:]),
                reads=[(f"xstg{b}", "ks"), "idb"], writes=[("pt0", c)])
        for c in range(4):
            p.op("dve", lambda e, c=c, j=j: e.tensor_copy(
                KT[c][:, 128 * j:128 * (j + 1)], pt0[:, 128 * c:128 * (c + 1)]),
                reads=[("pt0", c)], writes=[(f"KT{c}", j)])
        for dc in range(ND):
            p.op("pe", lambda e, b=b, dc=dc: e.matmul(
                pV[:], xTc[b][:, 128 * dc:128 * (dc + 1)],
                wvT[:, 512 * dc:512 * (dc + 1)],
                start=(dc == 0), stop=(dc == ND - 1)),
                reads=[(f"xTc{b}", dc // 8), "wvT"], writes=["pV"])
        for kv in range(NKV):
            p.op("dve", lambda e, kv=kv, j=j: e.tensor_copy(
                VA[kv][:, 65 * j:65 * j + 64], pV[:, 64 * kv:64 * (kv + 1)]),
                reads=["pV"], writes=[(f"VA{kv}", j)])
        for kv in range(NKV):
            p.op("dve", lambda e, kv=kv, j=j: e.memset(
                VA[kv][:, 65 * j + 64:65 * j + 65], 1.0),
                writes=[(f"VA{kv}", j)])

    if stop_after == "A1":
        return finish_dummy("dummyA")

    p.op("pe", lambda e: e.transpose(pB[0:8, 0:128], knacc[:], idf[:]),
         reads=["knacc", "idf"], writes=["pB"])
    p.op("dve", lambda e: e.tensor_copy(knscr[0:8, 0:128], pB[0:8, 0:128]),
         reads=["pB"], writes=["knscr"])
    p.op("dve", lambda e: e.reduce_max(small[0:8, 3:4], knscr[0:8, 0:128],
                                       AX.X),
         reads=["knscr"], writes=[("small", 3)])
    p.op("act", lambda e: e.activation(small[0:8, 4:5], small[0:8, 3:4],
                                       AF.Sqrt), reads=[("small", 3)],
         writes=[("small", 4)])
    p.op("dve", lambda e: e.tensor_scalar(small[0:8, 5:6], small[0:8, 4:5],
                                          -1.0, None, ALU.mult),
         reads=[("small", 4)], writes=[("small", 5)])
    p.op("pe", lambda e: e.transpose(pB[0:1, 128:136], small[0:8, 5:6],
                                     idf[0:8, 0:8]),
         reads=[("small", 5), "idf"], writes=["pB"])
    p.op("dve", lambda e: e.tensor_copy(knscr[0:1, 128:136],
                                        pB[0:1, 128:136]),
         reads=["pB"], writes=["knscr"])
    s_kn = p.slot("knr")
    for m_ in range(2):
        for r_ in range(4):
            sync_dma(knscr[0:1, 160 + 2 * r_ + m_:192:8],
                     knscr[0:1, 128 + m_:136:2],
                     ["knscr"], ["knscr"], s_kn,
                     allow_slow_non_contiguous=True)
    p.op("pe", lambda e: e.matmul(pB[:, 0:32], onesf[:], knscr[0:1, 160:192],
                                  start=True, stop=True),
         reads=["knscr", "onesf"], writes=["pB"])
    p.op("dve", lambda e: e.tensor_copy(knbneg[:], pB[:, 0:32]),
         reads=["pB"], writes=["knbneg"])

    tm.free(wstg[1], wstg[0], wkT, wvT)
    tm.free(pV, pK)
    pQs = [tm.psum(f"pQ{i}", [128, 512]) for i in range(4)]

    if stop_after == "KN":
        return finish_dummy("dummyA")

    wqT = [tm.sbuf(f"wqT{i}", [128, D], BF16) for i in range(ND)]
    wstgq = [tm.sbuf(f"wstgq{i}", [128, D], BF16) for i in range(2)]
    s_wstgq = [p.slot(f"wstgq{i}") for i in range(2)]
    for r in range(ND):
        b = r % 2
        cast_dma(wstgq[b][:], WQ[128 * r:128 * (r + 1), :], [], [f"wstgq{b}"],
                 s_wstgq[b])
        transpose_tile(f"wstgq{b}", wstgq[b][:])
        for dc in range(ND):
            pb = pts[dc // 8]
            p.op("dve", lambda e, r=r, dc=dc, pb=pb: e.tensor_copy(
                wqT[dc][:, 128 * r:128 * (r + 1)],
                pb[:, 128 * (dc % 8):128 * (dc % 8 + 1)]),
                reads=[(pb.name, dc % 8)], writes=[(f"wqT{dc}", r)])

    if stop_after == "WQT":
        return finish_dummy("dummyA")

    qwb = tm.sbuf("qwb", [128, D], F32)
    cast_dma(row1[0:1, 0:2048], QNW, [], [("row1", "q")], p.slot("qnw"))
    for oc in range(4):
        p.op("pe", lambda e, oc=oc: e.matmul(
            pB[:], ones1[:], row1[0:1, 512 * oc:512 * (oc + 1)],
            start=True, stop=True),
            reads=[("row1", "q"), "ones1"], writes=["pB"])
        p.op("act", lambda e, oc=oc: e.activation(
            qwb[:, 512 * oc:512 * (oc + 1)], pB[:], AF.Copy, scale=0.125),
            reads=["pB"], writes=[("qwb", oc)])

    if stop_after == "QWB":
        return finish_dummy("dummyA")

    QT = [tm.sbuf(f"QT{i}", [128, TC], BF16) for i in range(ND)]
    Qsb = tm.sbuf("Qsb", [128, D], F32)
    sqs = tm.sbuf("sqs", [128, D], F32)

    _nt = NT
    _ql = 99
    if stop_after and stop_after.startswith("Q") and len(stop_after) == 2:
        _nt = int(stop_after[1]) + 1
    if stop_after and stop_after.startswith("QL"):
        _nt, _ql = 1, int(stop_after[2:])

    def q_head(t):
        b = t % 2
        cast_dma(xstg[b][:], XE[TC + 128 * t:TC + 128 * (t + 1), :], [],
                 [f"xstg{b}"], s_xstg[b])
        transpose_tile(f"xstg{b}", xstg[b][:])
        evac_xTc(b)
        if _ql < 2:
            return
        for oc in range(4):
            pQ = pQs[oc]
            for dc in range(ND):
                p.op("pe", lambda e, b=b, dc=dc, oc=oc, pQ=pQ: e.matmul(
                    pQ[:], xTc[b][:, 128 * dc:128 * (dc + 1)],
                    wqT[dc][:, 512 * oc:512 * (oc + 1)],
                    start=(dc == 0), stop=(dc == ND - 1)),
                    reads=[(f"xTc{b}", dc // 8), f"wqT{dc}"], writes=[pQ.name])

    def q_evac(t):
        qc = 20 + 4 * (t % 2)
        if _ql < 2:
            return
        for oc in range(4):
            pQ = pQs[oc]
            p.op("dve", lambda e, oc=oc, pQ=pQ: e.tensor_copy(
                Qsb[:, 512 * oc:512 * (oc + 1)], pQ[:]),
                reads=[pQ.name], writes=[("Qsb", oc)])
            if _ql < 3:
                continue
            p.op("act", lambda e, oc=oc, pQ=pQ: e.activation(
                sqs[:, 512 * oc:512 * (oc + 1)], pQ[:], AF.Square),
                reads=[pQ.name], writes=[("sqs", oc)])
        if _ql < 4:
            return
        p.op("dve", lambda e, qc=qc: e.reduce_sum(
            small[:, qc:qc + 1], sqs[:], AX.X),
            reads=["sqs", ("sqs", 0), ("sqs", 1), ("sqs", 2), ("sqs", 3)],
            writes=[("small", qc)])
        p.op("act", lambda e, qc=qc: e.activation(
            small[:, qc + 1:qc + 2], small[:, qc:qc + 1],
            AF.Sqrt, bias=epsc[:, 0:1], scale=1.0 / D),
            reads=[("small", qc), ("epsc", 0)], writes=[("small", qc + 1)])
        p.op("dve", lambda e, qc=qc: e.reciprocal(
            small[:, qc + 2:qc + 3], small[:, qc + 1:qc + 2]),
            reads=[("small", qc + 1)], writes=[("small", qc + 2)])

    def q_tail(t):
        b = t % 2
        qc = 20 + 4 * (t % 2)
        if _ql < 5:
            return
        p.op("dve", lambda e, b=b, qc=qc: e.scalar_tensor_tensor(
            xstg[b][:], Qsb[:], small[:, qc + 2:qc + 3], qwb[:],
            ALU.mult, ALU.mult),
            reads=[("Qsb", 0), ("Qsb", 1), ("Qsb", 2), ("Qsb", 3),
                   ("small", qc + 2), "qwb"],
            writes=[(f"xstg{b}", "qh")])
        if _ql < 6:
            return
        p.op("act", lambda e, b=b: e.activation(sqs[:], xstg[b][:], AF.Square),
             reads=[(f"xstg{b}", "qh")],
             writes=["sqs", ("sqs", 0), ("sqs", 1), ("sqs", 2), ("sqs", 3)])
        p.op("dve", lambda e: e.reduce_sum(
            small[:, 32:64], sqs[:].rearrange("p (g x) -> p g x", g=32), AX.X),
            reads=["sqs"], writes=[("small", 32)])
        p.op("act", lambda e: e.activation(small[:, 32:64], small[:, 32:64],
                                           AF.Sqrt),
             reads=[("small", 32)], writes=[("small", 32)])
        if _ql < 7:
            return
        p.op("dve", lambda e: e.tensor_tensor(small[:, 32:64], small[:, 32:64],
                                              knbneg[:], ALU.mult),
             reads=[("small", 32), "knbneg"], writes=[("small", 32)])
        p.op("pe", lambda e: e.transpose(pB[0:32, 0:128], small[:, 32:64],
                                         idf[:]),
             reads=[("small", 32), "idf"], writes=["pB"])
        p.op("dve", lambda e, t=t: e.tensor_copy(
            m_all[:, 128 * t:128 * (t + 1)], pB[0:32, 0:128]),
            reads=["pB"], writes=[("m_all", t)])
        if _ql < 8:
            return
        transpose_tile(f"xstg{b}", xstg[b][:], key="qh")
        for dc in range(ND):
            pb = pts[dc // 8]
            p.op("dve", lambda e, t=t, dc=dc, pb=pb: e.tensor_copy(
                QT[dc][:, 128 * t:128 * (t + 1)],
                pb[:, 128 * (dc % 8):128 * (dc % 8 + 1)]),
                reads=[(pb.name, dc % 8)], writes=[(f"QT{dc}", t)])

    for t in range(_nt):
        q_head(t)
        if t > 0:
            q_tail(t - 1)
        q_evac(t)
    q_tail(_nt - 1)

    if stop_after and stop_after.startswith("Q"):
        return finish_dummy("dummyA")

    tm.free(Qsb, sqs, qwb, kwb, ksqs, row1, *wqT, *xstg, *xTc, *wstgq)
    tm.free(pQs[3], pQs[2], pQs[1], pQs[0], pB, pt1, pt0)

    def tap_dump(name, tiles, nrow, regions_of, width=TC):
        if name not in tap_outs:
            return
        ttmp = tm.sbuf("ttmp_" + name, [nrow, width], F32)
        s_tap = p.slot("tap_" + name)
        for c, tile in enumerate(tiles):
            p.op("act", lambda e, tile=tile: e.activation(ttmp[:], tile[:],
                                                          AF.Copy),
                 reads=regions_of(c), writes=["ttmp_" + name])
            sync_dma(tap_outs[name][nrow * c:nrow * (c + 1), :], ttmp[:],
                     ["ttmp_" + name], [("OUT", f"{name}{c}")], s_tap)
        tm.free(ttmp)

    if stop_after == "A":
        return finish_dummy("dummyA")

    tap_dump("KT", KT, 128, lambda c: [f"KT{c}"] + [(f"KT{c}", j) for j in range(NS)], width=TEXT)
    tap_dump("QT", QT, 128, lambda c: [f"QT{c}"] + [(f"QT{c}", t) for t in range(NT)])
    if "MALL" in tap_outs:
        matmp = tm.sbuf("matmp", [NH, TC], F32)
        p.op("act", lambda e: e.activation(matmp[:], m_all[:], AF.Copy),
             reads=["m_all"] + [("m_all", t) for t in range(NT)],
             writes=["matmp"])
        sync_dma(tap_outs["MALL"], matmp[:], ["matmp"], [("OUT", "matap")],
                 p.slot("tap_MALL"))
        tm.free(matmp)

    Mm = tm.sbuf("Mm", [128, 1152], F32)
    sync_dma(Mm[:], MM, [], ["Mm"], p.slot("mm"))
    poi = tm.sbuf("poi", [128, NS], F32)
    sync_dma(poi[:], POI, [], ["poi"], p.slot("poi"))
    Hb = [tm.sbuf(f"Hb{i}", [128, 1152], BF16) for i in range(2)]
    Fring = [tm.sbuf(f"Fr{i}", [128, TC], BF16) for i in range(12)]
    rsb = tm.sbuf("rsb", [32, TC], F32)
    rsb2 = tm.sbuf("rsb2", [32, TC], F32)
    
    rrecb = tm.sbuf("rrecb", [1, TC], BF16)
    rbsb = tm.sbuf("rbsb", [64, TC], BF16)
    AOsb = [tm.sbuf(f"AO{i}", [128, TC], BF16) for i in range(ND)]

    p.op("dve", lambda e: e.memset(rsb[:], 0.0), writes=[("rsb", None)])
    p.op("dve", lambda e: e.memset(rsb2[:], 0.0), writes=["rsb2"])
    Sps = [tm.psum(f"Sps{i}", [128, 1024]) for i in range(2)]
    AOps = tm.psum("AOps", [65, 1024])
    rbps = tm.psum("rbps", [64, 1024])

    s_mrow = [p.slot("mrowd0"), p.slot("mrowd1")]
    mall_regions = ["m_all"] + [("m_all", t) for t in range(NT)]

    def _av_batch(tt, kv):
        for jj in range(tt, tt + 9):
            tlo_jj = 128 * max(0, jj - 8)
            frj = Fring[jj % 12]
            p.op("pe", lambda e, tt=tt, jj=jj, kv=kv, frj=frj, tlo_jj=tlo_jj:
                 e.matmul(AOps[:, 128 * tt:128 * (tt + 1)],
                          VA[kv][:, 65 * jj:65 * (jj + 1)],
                          frj[:, 128 * tt - tlo_jj:128 * (tt + 1) - tlo_jj],
                          start=(jj == tt), stop=(jj == tt + 8)),
                 reads=[(f"VA{kv}", jj), frj.name],
                 writes=[("AOps", tt)])

    for sslot in range(NH):
        jt = sslot // 2
        bb = sslot % 2
        h = 8 * (jt // 4) + (jt % 4) + 4 * bb
        hb = sslot % 2
        kv = h // 4
        ktile = KT[kv // 2]
        kbase = 64 * (kv % 2)
        qtile = QT[jt]
        qbase = 64 * bb
        p.op("dve", lambda e, hb=hb, h=h: e.tensor_scalar(
            Hb[hb][:], Mm[:], SLOPES[h], None, ALU.mult),
            reads=["Mm"], writes=[f"Hb{hb}"])
        sync_dma(mrows[hb][:], m_all[sslot:sslot + 1, :], mall_regions,
                 [f"mrow{hb}"], s_mrow[hb])
        for j in range(NS):
            sb = Sps[j % 2]
            t_lo = 128 * max(0, j - 8)
            t_hi = 128 * (min(j, 7) + 1)
            span = t_hi - t_lo
            i_lo = t_lo + 1024 - 128 * j
            for off in range(0, span, 512):
                w = min(512, span - off)
                p.op("pe", lambda e, sb=sb, off=off, w=w, hb=hb, i_lo=i_lo:
                     e.matmul(sb[:, off:off + w], idb[:],
                              Hb[hb][:, i_lo + off:i_lo + off + w],
                              start=True, stop=False),
                     reads=[f"Hb{hb}", "idb"], writes=[sb.name])
                p.op("pe", lambda e, sb=sb, off=off, w=w, j=j, t_lo=t_lo,
                     ktile=ktile, kbase=kbase, qtile=qtile, qbase=qbase:
                     e.matmul(sb[:, off:off + w],
                              ktile[kbase:kbase + 64, 128 * j:128 * (j + 1)],
                              qtile[qbase:qbase + 64,
                                    t_lo + off:t_lo + off + w],
                              start=False, stop=False),
                     reads=[(ktile.name, j), qtile.name], writes=[sb.name])
                p.op("pe", lambda e, sb=sb, off=off, w=w, hb=hb, t_lo=t_lo:
                     e.matmul(sb[:, off:off + w], ones1[:],
                              mrows[hb][0:1, t_lo + off:t_lo + off + w],
                              start=False, stop=True),
                     reads=[f"mrow{hb}", "ones1"], writes=[sb.name])
            fr = Fring[j % 12]
            p.op("act", lambda e, sb=sb, span=span, j=j, fr=fr: e.activation(
                fr[:, 0:span], sb[:, 0:span], AF.Exp,
                bias=poi[:, j:j + 1]),
                reads=[sb.name, "poi"], writes=[fr.name])
            if j >= 9:
                _av_batch(j - 9, kv)
        for tt in (7,):
            _av_batch(tt, kv)
        p.op("dve", lambda e: e.tensor_copy(rsb[0:1, :], AOps[64:65, :]),
             reads=[("AOps", None)] + [("AOps", tt) for tt in range(8)],
             writes=[("rsb", None)])
        p.op("dve", lambda e: e.transpose(rsb2[:], rsb[:]),
             reads=[("rsb", None)], writes=["rsb2"])
        rview = rsb2[:].rearrange("p (b c) -> p b c", c=32)[:, :, 0:1]
        p.op("dve", lambda e: e.reciprocal(rview, rview),
             reads=["rsb2"], writes=["rsb2"])
        p.op("dve", lambda e: e.transpose(rsb[:], rsb2[:]),
             reads=["rsb2"], writes=[("rsb", None)])
        p.op("dve", lambda e: e.tensor_copy(rrecb[:], rsb[0:1, :]),
             reads=[("rsb", None)], writes=["rrecb"])
        for half in range(2):
            p.op("pe", lambda e, half=half: e.matmul(
                rbps[:, 512 * half:512 * (half + 1)], ones1[0:1, 0:64],
                rrecb[0:1, 512 * half:512 * (half + 1)],
                start=True, stop=True),
                reads=["rrecb", "ones1"], writes=["rbps"])
        p.op("dve", lambda e: e.tensor_copy(rbsb[:], rbps[:]),
             reads=["rbps"], writes=["rbsb"])
        p.op("dve", lambda e, jt=jt, bb=bb: e.tensor_tensor(
            AOsb[jt][64 * bb:64 * bb + 64, :], AOps[0:64, :],
            rbsb[:], ALU.mult),
            reads=[("AOps", None)] + [("AOps", tt) for tt in range(8)] +
                  ["rbsb"],
            writes=[(f"AO{jt}", bb)])

    tm.free(*QT, *KT, *VA, Mm, *Hb, *Fring, poi, rsb, rsb2, rrecb, rbsb,
            m_all, mrows[0], mrows[1])
    tm.free(rbps, AOps, Sps[1], Sps[0])

    if stop_after == "B":
        return finish_dummy("dummyB")

    tap_dump("AO", AOsb, 128, lambda c: [f"AO{c}", (f"AO{c}", 0), (f"AO{c}", 1)])

    ptc0 = tm.psum("ptc0", [128, 1024], BF16)
    ptc1 = tm.psum("ptc1", [128, 1024], BF16)
    ptcs = [ptc0, ptc1]
    pc = [tm.psum(f"pc{i}", [128, 1024]) for i in range(2)]

    ACC = [tm.sbuf(f"ACC{i}", [128, D], F32) for i in range(NT)]
    y1T = [tm.sbuf(f"y1T{i}", [128, TC], BF16) for i in range(ND)]
    woq = [tm.sbuf(f"woq{i}", [128, 4 * D], BF16) for i in range(2)]
    wosg = [tm.sbuf(f"wosg{i}", [128, 2048], BF16) for i in range(3)]
    s_wos = [p.slot(f"wos{i}") for i in range(3)]
    xrs = [tm.sbuf(f"xrs{i}", [128, D], F32) for i in range(2)]
    s_xrs = [p.slot(f"xrs{i}") for i in range(2)]
    ybf = tm.sbuf("ybf", [128, D], BF16)

    WOr = WO.rearrange("(a p) o -> p a o", p=128)
    for oq in range(4):
        qb = oq % 2
        for g in range(4):
            b2 = (4 * oq + g) % 3
            cast_dma(
                wosg[b2][:].rearrange("p (a o) -> p a o", o=512)[:, 0:4, :],
                WOr[:, 4 * g:4 * (g + 1), 512 * oq:512 * (oq + 1)],
                [], [f"wosg{b2}"], s_wos[b2])
            for ai in range(4):
                a = 4 * g + ai
                for c in range(4):
                    pb = ptcs[c % 2]
                    p.op("pe", lambda e, b2=b2, ai=ai, a=a, c=c, pb=pb:
                         e.transpose(
                             pb[:, 512 * (a % 2) + 128 * (c // 2):
                                512 * (a % 2) + 128 * (c // 2) + 128],
                             wosg[b2][:, 512 * ai + 128 * c:
                                     512 * ai + 128 * (c + 1)],
                             idb[:]),
                         reads=[f"wosg{b2}", "idb"],
                         writes=[(pb.name, (a % 2, c // 2))])
                    p.op("dve", lambda e, qb=qb, a=a, c=c, pb=pb:
                         e.tensor_copy(
                             woq[qb][:, D * c + 128 * a:D * c + 128 * (a + 1)],
                             pb[:, 512 * (a % 2) + 128 * (c // 2):
                                512 * (a % 2) + 128 * (c // 2) + 128]),
                         reads=[(pb.name, (a % 2, c // 2))],
                         writes=[(f"woq{qb}", (a, c))])
        for t in range(NT):
            tb = t % 2
            if oq == 0:
                sync_dma(xrs[tb][:], XE[TC + 128 * t:TC + 128 * (t + 1), :],
                         [], [f"xrs{tb}"], s_xrs[tb])
            for dhalf in range(2):
                for c in range(4):
                    oc = 4 * oq + c
                    for dd in range(2):
                        p.op("pe", lambda e, qb=qb, c=c, t=t, oc=oc,
                             dhalf=dhalf, dd=dd: e.matmul(
                                 pc[t % 2][:, 512 * dd:512 * (dd + 1)],
                                 AOsb[oc][:, 128 * t:128 * (t + 1)],
                                 woq[qb][:, D * c + 1024 * dhalf + 512 * dd:
                                         D * c + 1024 * dhalf + 512 * (dd + 1)],
                                 start=(c == 0), stop=(c == 3)),
                             reads=[f"AO{oc}", (f"AO{oc}", 0), (f"AO{oc}", 1),
                                    f"woq{qb}"],
                             writes=[pc[t % 2].name])
                hk = "lo" if dhalf == 0 else "hi"
                if oq == 0:
                    p.op("dve", lambda e, t=t, tb=tb, dhalf=dhalf: e.tensor_tensor(
                        ACC[t][:, 1024 * dhalf:1024 * (dhalf + 1)],
                        xrs[tb][:, 1024 * dhalf:1024 * (dhalf + 1)],
                        pc[t % 2][:], ALU.add),
                        reads=[f"xrs{tb}", pc[t % 2].name],
                        writes=[(f"ACC{t}", hk)])
                else:
                    p.op("dve", lambda e, t=t, dhalf=dhalf: e.tensor_tensor(
                        ACC[t][:, 1024 * dhalf:1024 * (dhalf + 1)],
                        ACC[t][:, 1024 * dhalf:1024 * (dhalf + 1)],
                        pc[t % 2][:], ALU.add),
                        reads=[pc[t % 2].name, (f"ACC{t}", hk)],
                        writes=[(f"ACC{t}", hk)])

    def layernorm_tile(t):
        acc_h = ACC[t]
        tname = f"ACC{t}"
        for c in range(4):
            p.op("dve", lambda e, c=c: e.bn_stats(
                bns[:, 6 * c:6 * (c + 1)], acc_h[:, 512 * c:512 * (c + 1)]),
                reads=[(tname, "lo"), (tname, "hi")], writes=[("bns", c)])
        p.op("dve", lambda e: e.bn_aggr(aggr[:], bns[:]),
             reads=[("bns", 0), ("bns", 1), ("bns", 2), ("bns", 3)],
             writes=["aggr"])
        p.op("act", lambda e: e.activation(small[:, 24:25], aggr[:, 1:2],
                                           AF.Sqrt, bias=epsc[:, 1:2]),
             reads=["aggr", ("epsc", 1)], writes=[("small", 24)])
        p.op("dve", lambda e: e.reciprocal(small[:, 25:26], small[:, 24:25]),
             reads=[("small", 24)], writes=[("small", 25)])
        p.op("dve", lambda e: e.tensor_scalar(
            acc_h[:], acc_h[:], aggr[:, 0:1], small[:, 25:26],
            ALU.subtract, ALU.mult),
            reads=[(tname, "lo"), (tname, "hi"), "aggr", ("small", 25)],
            writes=[(tname, "lo"), (tname, "hi")])

    s_y1d = p.slot("y1d")
    for t in range(NT):
        layernorm_tile(t)
        p.op("act", lambda e, t=t: e.activation(ybf[:], ACC[t][:], AF.Copy),
             reads=[(f"ACC{t}", "lo"), (f"ACC{t}", "hi")], writes=["ybf"])
        sync_dma(Y1D[128 * t:128 * (t + 1), :], ybf[:], ["ybf"],
                 [("y1dd", t)], s_y1d)
        for dc in range(ND):
            pb = ptcs[dc // 8]
            p.op("pe", lambda e, dc=dc, pb=pb: e.transpose(
                pb[:, 128 * (dc % 8):128 * (dc % 8 + 1)],
                ybf[:, 128 * dc:128 * (dc + 1)], idb[:]),
                reads=["ybf", "idb"], writes=[(pb.name, dc % 8)])
        for dc in range(ND):
            pb = ptcs[dc // 8]
            p.op("dve", lambda e, t=t, dc=dc, pb=pb: e.tensor_copy(
                y1T[dc][:, 128 * t:128 * (t + 1)],
                pb[:, 128 * (dc % 8):128 * (dc % 8 + 1)]),
                reads=[(pb.name, dc % 8)], writes=[(f"y1T{dc}", t)])

    tm.free(*AOsb, *woq, *wosg, *xrs, ybf)

    if "Y1" in tap_outs:
        for t in range(NT):
            sync_dma(tap_outs["Y1"][128 * t:128 * (t + 1), :], ACC[t][:],
                     [(f"ACC{t}", "lo"), (f"ACC{t}", "hi")],
                     [("OUT", f"y1tap{t}")], p.slot("tap_Y1"), gid="y1g")

    if stop_after == "C":
        return finish_acc(ACC)

    gwbf = tm.sbuf("gwbf", [NE, D], BF16)
    gwT = tm.sbuf("gwT", [128, 128], BF16)
    cast_dma(gwbf[:], GW, [], ["gwbf"], p.slot("gw"))
    pL = tm.psum("pL", [128, 512])
    for dc in range(ND):
        p.op("pe", lambda e, dc=dc: e.transpose(
            ptc0[:, 8 * dc:8 * (dc + 1)],
            gwbf[:, 128 * dc:128 * (dc + 1)], idb[0:8, 0:8]),
            reads=["gwbf", "idb"], writes=[("ptc0", ("g", dc))])
        p.op("dve", lambda e, dc=dc: e.tensor_copy(
            gwT[:, 8 * dc:8 * (dc + 1)], ptc0[:, 8 * dc:8 * (dc + 1)]),
            reads=[("ptc0", ("g", dc))], writes=[("gwT", dc)])

    for t in range(NT):
        for dc in range(ND):
            p.op("pe", lambda e, t=t, dc=dc: e.matmul(
                pL[:, 0:8], y1T[dc][:, 128 * t:128 * (t + 1)],
                gwT[:, 8 * dc:8 * (dc + 1)],
                start=(dc == 0), stop=(dc == ND - 1)),
                reads=[(f"y1T{dc}", t), ("gwT", dc)], writes=["pL"])
        g0 = 40
        gsl = gates8[:, 8 * t:8 * (t + 1)]
        p.op("dve", lambda e: e.reduce_max(small[:, g0:g0 + 1], pL[:, 0:8],
                                           AX.X, negate=True),
             reads=["pL"], writes=[("small", g0)])
        p.op("act", lambda e, gsl=gsl: e.activation(
            gsl, pL[:, 0:8], AF.Exp, bias=small[:, g0:g0 + 1]),
            reads=["pL", ("small", g0)], writes=[("gates8", t)])
        p.op("dve", lambda e, gsl=gsl: e.reduce_sum(
            small[:, g0 + 1:g0 + 2], gsl, AX.X),
            reads=[("gates8", t)], writes=[("small", g0 + 1)])
        p.op("dve", lambda e: e.reciprocal(small[:, g0 + 2:g0 + 3],
                                           small[:, g0 + 1:g0 + 2]),
             reads=[("small", g0 + 1)], writes=[("small", g0 + 2)])
        p.op("dve", lambda e, gsl=gsl: e.tensor_scalar(
            gsl, gsl, small[:, g0 + 2:g0 + 3], None, ALU.mult),
            reads=[("gates8", t), ("small", g0 + 2)], writes=[("gates8", t)])
        p.op("dve", lambda e, gsl=gsl: e.reduce_max(
            small[:, g0 + 3:g0 + 4], gsl, AX.X),
            reads=[("gates8", t)], writes=[("small", g0 + 3)])
        p.op("dve", lambda e, gsl=gsl: e.tensor_scalar(
            small[:, 48:56], gsl, small[:, g0 + 3:g0 + 4], 1e30,
            ALU.is_ge, ALU.mult),
            reads=[("gates8", t), ("small", g0 + 3)], writes=[("small", 48)])
        p.op("dve", lambda e, gsl=gsl: e.tensor_tensor(
            small[:, 48:56], gsl, small[:, 48:56], ALU.subtract),
            reads=[("gates8", t), ("small", 48)], writes=[("small", 48)])
        p.op("dve", lambda e: e.reduce_max(small[:, g0 + 4:g0 + 5],
                                           small[:, 48:56], AX.X),
             reads=[("small", 48)], writes=[("small", g0 + 4)])
        p.op("dve", lambda e: e.tensor_tensor(small[:, g0 + 5:g0 + 6],
                                              small[:, g0 + 3:g0 + 4],
                                              small[:, g0 + 4:g0 + 5],
                                              ALU.add),
             reads=[("small", g0 + 3), ("small", g0 + 4)],
             writes=[("small", g0 + 5)])
        p.op("dve", lambda e: e.reciprocal(small[:, g0 + 6:g0 + 7],
                                           small[:, g0 + 5:g0 + 6]),
             reads=[("small", g0 + 5)], writes=[("small", g0 + 6)])
        p.op("dve", lambda e, gsl=gsl: e.tensor_scalar(
            small[:, 48:56], gsl, small[:, g0 + 4:g0 + 5], None, ALU.is_ge),
            reads=[("gates8", t), ("small", g0 + 4)], writes=[("small", 48)])
        p.op("dve", lambda e, gsl=gsl: e.tensor_tensor(
            gsl, gsl, small[:, 48:56], ALU.mult),
            reads=[("gates8", t), ("small", 48)], writes=[("gates8", t)])
        p.op("dve", lambda e, gsl=gsl: e.tensor_scalar(
            gsl, gsl, small[:, g0 + 6:g0 + 7], None, ALU.mult),
            reads=[("gates8", t), ("small", g0 + 6)], writes=[("gates8", t)])
        p.op("pe", lambda e, gsl=gsl: e.transpose(
            pL[0:8, 128:256], gsl, idf[:]),
            reads=[("gates8", t), "idf"], writes=["pL"])
        p.op("dve", lambda e, t=t: e.tensor_copy(
            grow8[:, 128 * t:128 * (t + 1)], pL[0:8, 128:256]),
            reads=["pL"], writes=[("grow8", t)])

    from concourse.bass import IndirectOffsetOnAxis as IOA
    LTb = tm.sbuf("LTb", [128, 128], BF16)
    cast_dma(LTb[:], LTRI, [], ["LTb"], p.slot("ltri"))
    tokids = tm.sbuf("tokids", [128, 8], F32)
    sync_dma(tokids[:], TCOL, [], ["tokids"], p.slot("tcol"))
    iorowf = tm.sbuf("iorowf", [1, 384], F32)
    sync_dma(iorowf[:], IOROW, [], ["iorowf"], p.slot("iorow"))
    iotabc = tm.sbuf("iotabc", [128, 384], F32)
    slotf = tm.sbuf("slotf", [128, 64], F32)
    posf = tm.sbuf("posf", [128, 64], F32)
    mkb = tm.sbuf("mkb", [128, 64], BF16)
    bb = tm.sbuf("bb", [128, 64], F32)
    brow = tm.sbuf("brow", [1, 64], F32)
    crow = tm.sbuf("crow", [1, 64], F32)
    PTf = tm.sbuf("PTf", [128, NT * 384], F32)
    idxall = tm.sbuf("idxall", [128, 24], mybir.dt.int32)
    g12i = tm.sbuf("g12i", [128, 16], mybir.dt.int32)
    gg12 = tm.sbuf("gg12", [128, 16], F32)
    gsc = tm.sbuf("g12scr", [128, 32], F32)

    p.op("pe", lambda e: e.matmul(pL[:, 0:384], onesf[:], iorowf[:],
                                  start=True, stop=True),
         reads=["iorowf", "onesf"], writes=["pL"])
    p.op("dve", lambda e: e.tensor_copy(iotabc[:], pL[:, 0:384]),
         reads=["pL"], writes=["iotabc"])
    for t in range(NT):
        p.op("dve", lambda e, t=t: e.tensor_scalar(
            mkb[:, 8 * t:8 * (t + 1)], gates8[:, 8 * t:8 * (t + 1)],
            0.0, None, ALU.is_gt),
            reads=[("gates8", t)], writes=[("mkb", t)])
    for t in range(NT):
        p.op("pe", lambda e, t=t: e.matmul(
            pL[:, 8 * t:8 * (t + 1)], LTb[:], mkb[:, 8 * t:8 * (t + 1)],
            start=True, stop=True),
            reads=["LTb", ("mkb", t)], writes=["pL"])
        p.op("pe", lambda e, t=t: e.matmul(
            pL[0:1, 64 + 8 * t:72 + 8 * t], onescol[:],
            mkb[:, 8 * t:8 * (t + 1)], start=True, stop=True),
            reads=["onescol", ("mkb", t)], writes=["pL"])
    p.op("dve", lambda e: e.tensor_copy(posf[:], pL[:, 0:64]),
         reads=["pL"], writes=["posf"])
    p.op("dve", lambda e: e.tensor_copy(crow[:], pL[0:1, 64:128]),
         reads=["pL"], writes=["crow"])
    p.op("dve", lambda e: e.memset(brow[:, 0:8], 0.0), writes=[("brow", 0)])
    for t in range(1, NT):
        p.op("dve", lambda e, t=t: e.tensor_tensor(
            brow[0:1, 8 * t:8 * (t + 1)], brow[0:1, 8 * (t - 1):8 * t],
            crow[0:1, 8 * (t - 1):8 * t], ALU.add),
            reads=[("brow", t - 1), "crow"], writes=[("brow", t)])
    p.op("pe", lambda e: e.matmul(pL[:, 128:192], onesf[:], brow[:],
                                  start=True, stop=True),
         reads=[("brow", None)] + ["onesf"], writes=["pL"])
    p.op("dve", lambda e: e.tensor_copy(bb[:], pL[:, 128:192]),
         reads=["pL"], writes=["bb"])
    p.op("dve", lambda e: e.tensor_tensor(slotf[:], posf[:], bb[:], ALU.add),
         reads=["posf", "bb"], writes=["slotf"])
    p.op("dve", lambda e: e.tensor_scalar(slotf[:], slotf[:], 10000.0, None,
                                          ALU.add),
         reads=["slotf"], writes=["slotf"])
    p.op("dve", lambda e: e.scalar_tensor_tensor(
        slotf[:], mkb[:], -10000.0, slotf[:], ALU.mult, ALU.add),
        reads=["slotf", ("mkb", None)] + [("mkb", t) for t in range(NT)],
        writes=["slotf"])
    for e_i in range(NE):
        for t in range(NT):
            p.op("dve", lambda e, e_i=e_i, t=t: e.tensor_scalar(
                PTf[:, 384 * t:384 * (t + 1)], iotabc[:],
                slotf[:, 8 * t + e_i:8 * t + e_i + 1], None, ALU.is_equal),
                reads=["iotabc", "slotf"], writes=[("PTf", t)])
        for sti in range(3):
            for t in range(NT):
                p.op("pe", lambda e, sti=sti, t=t: e.matmul(
                    pL[:, 448 + sti:449 + sti],
                    PTf[:, 384 * t + 128 * sti:384 * t + 128 * (sti + 1)],
                    tokids[:, t:t + 1], start=(t == 0), stop=(t == NT - 1)),
                    reads=[("PTf", t), "tokids"], writes=["pL"])
        for sti in range(3):
            p.op("dve", lambda e, e_i=e_i, sti=sti: e.tensor_copy(
                idxall[:, 3 * e_i + sti:3 * e_i + sti + 1],
                pL[:, 448 + sti:449 + sti]),
                reads=["pL"], writes=[("idxall", 3 * e_i + sti)])

    for t in range(NT):
        p.op("dve", lambda e, t=t: e.reduce_max(
            gsc[:, 16:17], gates8[:, 8 * t:8 * (t + 1)], AX.X),
            reads=[("gates8", t)], writes=[("gsc", 16)])
        p.op("dve", lambda e, t=t: e.tensor_scalar(
            gsc[:, 0:8], gates8[:, 8 * t:8 * (t + 1)], gsc[:, 16:17], None,
            ALU.is_ge),
            reads=[("gates8", t), ("gsc", 16)], writes=[("gsc", 0)])
        p.op("dve", lambda e, t=t: e.tensor_scalar(
            gsc[:, 8:16], gates8[:, 8 * t:8 * (t + 1)], 0.0, None,
            ALU.is_gt),
            reads=[("gates8", t)], writes=[("gsc", 8)])
        p.op("dve", lambda e: e.tensor_tensor(
            gsc[:, 8:16], gsc[:, 8:16], gsc[:, 0:8], ALU.subtract),
            reads=[("gsc", 0), ("gsc", 8)], writes=[("gsc", 8)])
        for j in range(2):
            c0 = 8 * j
            p.op("dve", lambda e, t=t, c0=c0: e.tensor_tensor(
                gsc[:, 17:25], gsc[:, c0:c0 + 8], slotf[:, 8 * t:8 * (t + 1)],
                ALU.mult),
                reads=[("gsc", c0), "slotf"], writes=[("gsc", 17)])
            p.op("dve", lambda e: e.reduce_sum(gsc[:, 25:26], gsc[:, 17:25],
                                               AX.X),
                reads=[("gsc", 17)], writes=[("gsc", 25)])
            p.op("dve", lambda e, c0=c0: e.tensor_tensor(
                gsc[:, 17:25], gsc[:, c0:c0 + 8], iotabc[:, 0:8], ALU.mult),
                reads=[("gsc", c0), "iotabc"], writes=[("gsc", 17)])
            p.op("dve", lambda e: e.reduce_sum(gsc[:, 26:27], gsc[:, 17:25],
                                               AX.X),
                reads=[("gsc", 17)], writes=[("gsc", 26)])
            p.op("dve", lambda e: e.scalar_tensor_tensor(
                gsc[:, 27:28], gsc[:, 26:27], 384.0, gsc[:, 25:26],
                ALU.mult, ALU.add),
                reads=[("gsc", 25), ("gsc", 26)], writes=[("gsc", 27)])
            p.op("dve", lambda e, t=t, j=j: e.tensor_copy(
                g12i[:, 8 * j + t:8 * j + t + 1], gsc[:, 27:28]),
                reads=[("gsc", 27)], writes=[("g12i", 8 * j + t)])
            p.op("dve", lambda e, t=t, c0=c0: e.tensor_tensor(
                gsc[:, 17:25], gsc[:, c0:c0 + 8], gates8[:, 8 * t:8 * (t + 1)],
                ALU.mult),
                reads=[("gsc", c0), ("gates8", t)], writes=[("gsc", 17)])
            p.op("dve", lambda e, t=t, j=j: e.reduce_sum(
                gg12[:, 8 * j + t:8 * j + t + 1], gsc[:, 17:25], AX.X),
                reads=[("gsc", 17)], writes=[("gg12", 8 * j + t)])

    if "GATES" in tap_outs:
        gtmp = tm.sbuf("gtmp", [128, NT * NE], F32)
        p.op("dve", lambda e: e.tensor_copy(gtmp[:], gates8[:]),
             reads=["gates8"] + [("gates8", t) for t in range(NT)],
             writes=["gtmp"])
        sync_dma(tap_outs["GATES"].rearrange("(t p) e -> p t e", p=128),
                 gtmp[:].rearrange("p (t e) -> p t e", e=NE),
                 ["gtmp"], [("OUT", "gtap")], p.slot("tap_GATES"),
                 allow_slow_non_contiguous=True)
        tm.free(gtmp)

    tm.free(PTf, iotabc, slotf, posf, mkb, bb, brow, crow, iorowf, LTb,
            tokids, gsc)
    tm.free(gwbf, gwT)
    tm.free(pL, pc[1], pc[0], ptc1, ptc0)

    if stop_after == "D":
        return finish_acc(ACC)

    FGRP = [(0, 256), (256, 256), (512, 256), (768, 256), (1024, 256),
            (1280, 128)]
    wgb = [tm.sbuf(f"wgb{i}", [128, ND * 256], BF16) for i in range(3)]
    wub = [tm.sbuf(f"wub{i}", [128, ND * 256], BF16) for i in range(3)]
    s_wgb = [p.slot(f"wgb{i}") for i in range(3)]
    s_wub = [p.slot(f"wub{i}") for i in range(3)]
    wdb = [tm.sbuf(f"wdb{i}", [128, 11 * 256], BF16) for i in range(3)]
    s_wdb = [p.slot(f"wdb{i}") for i in range(3)]
    ygT = tm.sbuf("ygT", [128, ND * 384], BF16)
    Ygt = tm.sbuf("Ygt", [128, D], BF16)
    Dsb = [tm.sbuf(f"Dsb{i}", [128, D], BF16) for i in range(3)]
    hTe = [tm.sbuf(f"hTe{i}", [128, 384], BF16) for i in range(11)]
    gsb = tm.sbuf("gsb", [128, 384], BF16)
    usb = tm.sbuf("usb", [128, 384], BF16)
    hT = [tm.sbuf(f"hT{i}", [128, TC], BF16) for i in range(11)]
    gsbs = tm.sbuf("gsbs", [128, TC], BF16)
    usbs = tm.sbuf("usbs", [128, TC], BF16)
    pG = tm.psum("pG", [128, 1024])
    pU = tm.psum("pU", [128, 1024])
    pd = [tm.psum(f"pd{i}", [128, 512]) for i in range(2)]
    ptE = tm.psum("ptE", [128, 1024], BF16)

    WGr = [WGATE[e].rearrange("(a p) f -> p a f", p=128) for e in range(NE)]
    WUr = [WUP[e].rearrange("(a p) f -> p a f", p=128) for e in range(NE)]
    WDr = [WDOWN[e].rearrange("(c p) d -> p c d", p=128) for e in range(NE)]
    WGr.append(SWG.rearrange("(a p) f -> p a f", p=128))
    WUr.append(SWU.rearrange("(a p) f -> p a f", p=128))
    WDr.append(SWD.rearrange("(c p) d -> p c d", p=128))

    s_gath = p.slot("gath")
    s_scat = p.slot("scat")
    gi = 0
    di = 0
    for e_i in range(NE + 1):
        routed = e_i < NE
        if routed:
            for sti in range(3):
                p.dma("pool", lambda e, e_i=e_i, sti=sti: e.indirect_dma_start(
                    Ygt[:], None, Y1D,
                    IOA(idxall[:, 3 * e_i + sti:3 * e_i + sti + 1], 0),
                    bounds_check=TC - 1, oob_is_err=False),
                    reads=[("idxall", 3 * e_i + sti)] +
                          [("y1dd", t) for t in range(NT)],
                    writes=["Ygt"], slot=s_gath, gid=("g", e_i, sti))
                for dc in range(ND):
                    pb = ptE
                    p.op("pe", lambda e, dc=dc, pb=pb: e.transpose(
                        pb[:, 128 * (dc % 8):128 * (dc % 8 + 1)],
                        Ygt[:, 128 * dc:128 * (dc + 1)], idb[:]),
                        reads=["Ygt", "idb"], writes=[(pb.name, dc % 8)])
                    if dc % 8 == 7:
                        h0 = dc - 7
                        p.op("dve", lambda e, h0=h0, sti=sti, pb=pb:
                             e.tensor_copy(
                                 ygT[:].rearrange("p (a s) -> p a s", s=384)
                                 [:, h0:h0 + 8, 128 * sti:128 * (sti + 1)],
                                 pb[:].rearrange("p (a s) -> p a s", s=128)),
                             reads=[pb.name] + [(pb.name, i) for i in range(8)],
                             writes=[("ygT", (sti, h0))])
        for g, (f0, fw) in enumerate(FGRP):
            b = gi % 3
            gi += 1
            for blk in range(0, ND, 4):
                cast_dma(
                    wgb[b][:].rearrange("p (a f) -> p a f", f=256)
                    [:, blk:blk + 4, 0:fw],
                    WGr[e_i][:, blk:blk + 4, f0:f0 + fw],
                    [], [(f"wgb{b}", blk)], s_wgb[b], gid=("g", e_i, g))
                cast_dma(
                    wub[b][:].rearrange("p (a f) -> p a f", f=256)
                    [:, blk:blk + 4, 0:fw],
                    WUr[e_i][:, blk:blk + 4, f0:f0 + fw],
                    [], [(f"wub{b}", blk)], s_wub[b], gid=("g", e_i, g))
            for cc in range(fw // 128):
                fc = f0 // 128 + cc
                if routed:
                    for a in range(ND):
                        p.op("pe", lambda e, b=b, a=a, cc=cc: e.matmul(
                            pG[:, 0:384],
                            wgb[b][:, 256 * a + 128 * cc:
                                   256 * a + 128 * (cc + 1)],
                            ygT[:, 384 * a:384 * (a + 1)],
                            start=(a == 0), stop=(a == ND - 1)),
                            reads=[f"wgb{b}", ("ygT", None)], writes=["pG"])
                    for a in range(ND):
                        p.op("pe", lambda e, b=b, a=a, cc=cc: e.matmul(
                            pU[:, 0:384],
                            wub[b][:, 256 * a + 128 * cc:
                                   256 * a + 128 * (cc + 1)],
                            ygT[:, 384 * a:384 * (a + 1)],
                            start=(a == 0), stop=(a == ND - 1)),
                            reads=[f"wub{b}", ("ygT", None)], writes=["pU"])
                    p.op("act", lambda e: e.activation(gsb[:], pG[:, 0:384],
                                                       AF.Silu),
                         reads=["pG"], writes=["gsb"])
                    p.op("dve", lambda e: e.tensor_copy(usb[:], pU[:, 0:384]),
                         reads=["pU"], writes=["usb"])
                    p.op("dve", lambda e, fc=fc: e.tensor_tensor(
                        hTe[fc][:], gsb[:], usb[:], ALU.mult),
                        reads=["gsb", "usb"], writes=[f"hTe{fc}"])
                else:
                    for th in range(2):
                        for a in range(ND):
                            p.op("pe", lambda e, b=b, a=a, cc=cc, th=th:
                                 e.matmul(
                                     pG[:, 512 * th:512 * (th + 1)],
                                     wgb[b][:, 256 * a + 128 * cc:
                                            256 * a + 128 * (cc + 1)],
                                     y1T[a][:, 512 * th:512 * (th + 1)],
                                     start=(a == 0), stop=(a == ND - 1)),
                                 reads=[f"wgb{b}", f"y1T{a}"] +
                                       [(f"y1T{a}", t) for t in range(NT)],
                                 writes=["pG"])
                        for a in range(ND):
                            p.op("pe", lambda e, b=b, a=a, cc=cc, th=th:
                                 e.matmul(
                                     pU[:, 512 * th:512 * (th + 1)],
                                     wub[b][:, 256 * a + 128 * cc:
                                            256 * a + 128 * (cc + 1)],
                                     y1T[a][:, 512 * th:512 * (th + 1)],
                                     start=(a == 0), stop=(a == ND - 1)),
                                 reads=[f"wub{b}", f"y1T{a}"] +
                                       [(f"y1T{a}", t) for t in range(NT)],
                                 writes=["pU"])
                    p.op("act", lambda e: e.activation(gsbs[:], pG[:], AF.Silu),
                         reads=["pG"], writes=["gsbs"])
                    p.op("dve", lambda e: e.tensor_copy(usbs[:], pU[:]),
                         reads=["pU"], writes=["usbs"])
                    p.op("dve", lambda e, fc=fc: e.tensor_tensor(
                        hT[fc][:], gsbs[:], usbs[:], ALU.mult),
                        reads=["gsbs", "usbs"], writes=[f"hT{fc}"])
        for q in range(8):
            b = di % 3
            di += 1
            for blk, cnt in ((0, 4), (4, 4), (8, 3)):
                cast_dma(
                    wdb[b][:].rearrange("p (c d) -> p c d", d=256)
                    [:, blk:blk + cnt, :],
                    WDr[e_i][:, blk:blk + cnt, 256 * q:256 * (q + 1)],
                    [], [(f"wdb{b}", blk)], s_wdb[b], gid=("d", e_i, q))
            if routed:
                for sti in range(3):
                    pdb = pd[(q * 3 + sti) % 2]
                    for fc in range(11):
                        p.op("pe", lambda e, b=b, fc=fc, sti=sti, pdb=pdb:
                             e.matmul(
                                 pdb[:, 0:256],
                                 hTe[fc][:, 128 * sti:128 * (sti + 1)],
                                 wdb[b][:, 256 * fc:256 * (fc + 1)],
                                 start=(fc == 0), stop=(fc == 10)),
                             reads=[f"hTe{fc}", f"wdb{b}"],
                             writes=[pdb.name])
                    p.op("dve", lambda e, sti=sti, q=q, pdb=pdb:
                         e.tensor_copy(
                             Dsb[sti][:, 256 * q:256 * (q + 1)],
                             pdb[:, 0:256]),
                         reads=[pdb.name],
                         writes=[(f"Dsb{sti}", q), ("ygT", None)])
            else:
                for t in range(NT):
                    pdb = pd[t % 2]
                    hk = "lo" if q < 4 else "hi"
                    for fc in range(11):
                        p.op("pe", lambda e, b=b, fc=fc, t=t, pdb=pdb:
                             e.matmul(
                                 pdb[:, 0:256],
                                 hT[fc][:, 128 * t:128 * (t + 1)],
                                 wdb[b][:, 256 * fc:256 * (fc + 1)],
                                 start=(fc == 0), stop=(fc == 10)),
                             reads=[f"hT{fc}", f"wdb{b}"],
                             writes=[pdb.name])
                    p.op("dve", lambda e, t=t, q=q, pdb=pdb: e.tensor_tensor(
                        ACC[t][:, 256 * q:256 * (q + 1)],
                        ACC[t][:, 256 * q:256 * (q + 1)], pdb[:, 0:256],
                        ALU.add),
                        reads=[pdb.name, (f"ACC{t}", hk)],
                        writes=[(f"ACC{t}", hk)])
        if routed:
            for sti in range(3):
                sync_dma(PK[384 * e_i + 128 * sti:384 * e_i + 128 * (sti + 1),
                            :],
                         Dsb[sti][:],
                         [(f"Dsb{sti}", q) for q in range(8)],
                         [("pkd", (e_i, sti)), ("ygT", None)],
                         s_scat, gid=("s", e_i, sti))
        if e_i == NE - 1:
            p.op("dve", lambda e: e.engine_nop(),
                 reads=[("ygT", None), ("Ygt", None)] +
                       [(f"Dsb{i}", None) for i in range(3)] +
                       [(f"hTe{i}", None) for i in range(11)],
                 writes=[(f"hT{i}", None) for i in range(11)] +
                        ["gsbs", "usbs"])

    if "ACC2" in tap_outs:
        for t in range(NT):
            sync_dma(tap_outs["ACC2"][128 * t:128 * (t + 1), :], ACC[t][:],
                     [(f"ACC{t}", "lo"), (f"ACC{t}", "hi")],
                     [("OUT", f"acc2tap{t}")], p.slot("tap_ACC2"),
                     gid="a2g")

    tm.free(*wgb, *wub, *wdb, ygT, Ygt, *Dsb, *hTe, gsb, usb, *hT, gsbs, usbs)
    rload = [tm.sbuf(f"rload{i}", [128, D], BF16) for i in range(4)]
    s_rl = [p.slot("rl0"), p.slot("rl1")]
    for t in range(NT):
        for j in range(2):
            rn = (t % 2) * 2 + j
            p.dma("pool", lambda e, t=t, j=j, rn=rn: e.indirect_dma_start(
                rload[rn][:], None, PK,
                IOA(g12i[:, 8 * j + t:8 * j + t + 1], 0),
                bounds_check=NE * 384 - 1, oob_is_err=False),
                reads=[("g12i", 8 * j + t), ("pkd", None)],
                writes=[f"rload{rn}"], slot=s_rl[t % 2], gid=("r", t))
        for j in range(2):
            rn = (t % 2) * 2 + j
            p.op("dve", lambda e, t=t, j=j, rn=rn: e.scalar_tensor_tensor(
                ACC[t][:], rload[rn][:], gg12[:, 8 * j + t:8 * j + t + 1],
                ACC[t][:], ALU.mult, ALU.add),
                reads=[f"rload{rn}", ("gg12", 8 * j + t),
                       (f"ACC{t}", "lo"), (f"ACC{t}", "hi")],
                writes=[(f"ACC{t}", "lo"), (f"ACC{t}", "hi")])
        layernorm_tile(t)
        sync_dma(Y[128 * t:128 * (t + 1), :], ACC[t][:],
                 [(f"ACC{t}", "lo"), (f"ACC{t}", "hi")],
                 [("OUT", t)], s_out)

    p.op("pool", lambda e: e.nop(), reads=[("OUT", None)], tag="final")

    with nc.Block() as block:
        for e in ENGINES:
            sems[e] = st.enter_context(nc.semaphore(f"sem_{e}"))
        for s in p.slots.values():
            s.sem = st.enter_context(nc.semaphore(f"dma_{s.name}"))
        p.emit_all(block, sems)
    st.close()
    nc._dbg_prog = p
    return nc


def host_consts(core):
    M = np.zeros((128, 1152), dtype=np.float32)
    s = np.arange(128)[:, None]
    i = np.arange(1152)[None, :]
    d = i - s
    M[:] = np.where((d >= 0) & (d <= WIN), (s - i).astype(np.float32),
                    np.float32(-1e9)).astype(np.float32)
    poi = np.zeros((128, NS), dtype=np.float32)
    if core == 0:
        poi[:, 0:8] = -1e5
    idf = np.eye(128, dtype=np.float32)
    return M, poi, idf


def host_moe_consts():
    ltri = np.triu(np.ones((128, 128), dtype=np.float32), 1)
    tcol = (np.arange(128, dtype=np.float32)[:, None] +
            128.0 * np.arange(8, dtype=np.float32)[None, :])
    tcol = np.ascontiguousarray(tcol)
    iorow = np.arange(384, dtype=np.float32).reshape(1, 384)
    return ltri, tcol, iorow


_nc_cache = {}


def _get_nc(taps=()):
    key = tuple(sorted(taps))
    if key not in _nc_cache:
        _nc_cache[key] = build_nc(key)
    return _nc_cache[key]


def kernel(L0_attn_x, L0_attn_wq, L0_attn_wk, L0_attn_wv, L0_attn_wo,
           L0_attn_q_norm_w, L0_attn_k_norm_w, L0_ffn_gate_w, L0_ffn_w_gate,
           L0_ffn_w_up, L0_ffn_w_down, L0_ffn_sw_gate, L0_ffn_sw_up,
           L0_ffn_sw_down, _taps=(), _collect=None):
    x = np.ascontiguousarray(np.asarray(L0_attn_x, dtype=np.float32)
                             .reshape(NCORE * TC, D))
    asf = lambda a: np.ascontiguousarray(np.asarray(a, dtype=np.float32))
    wq, wk, wv, wo = (asf(L0_attn_wq), asf(L0_attn_wk), asf(L0_attn_wv),
                      asf(L0_attn_wo))
    operm = np.concatenate(
        [np.arange(64 * h, 64 * h + 64)
         for j in range(16) for h in (8 * (j // 4) + (j % 4),
                                      8 * (j // 4) + (j % 4) + 4)])
    wq = np.ascontiguousarray(wq[operm, :])
    wo = np.ascontiguousarray(wo[:, operm])
    qnw = asf(L0_attn_q_norm_w).reshape(1, D)
    qnw = np.ascontiguousarray(qnw[:, operm])
    knw = asf(L0_attn_k_norm_w).reshape(1, 512)
    gw = asf(L0_ffn_gate_w)
    wgate, wup, wdown = asf(L0_ffn_w_gate), asf(L0_ffn_w_up), asf(L0_ffn_w_down)
    swg, swu, swd = asf(L0_ffn_sw_gate), asf(L0_ffn_sw_up), asf(L0_ffn_sw_down)

    nc = _get_nc(tuple(_taps))
    in_maps = []
    for c in range(NCORE):
        if c == 0:
            prev = np.zeros((TC, D), dtype=np.float32)
        else:
            prev = x[TC * (c - 1):TC * c]
        xe = np.ascontiguousarray(
            np.concatenate([prev, x[TC * c:TC * (c + 1)]], axis=0))
        M, poi, idf = host_consts(c)
        ltri, tcol, iorow = host_moe_consts()
        in_maps.append(dict(
            xe=xe, wq=wq, wk=wk, wv=wv, wo=wo, qnw=qnw, knw=knw, gw=gw,
            wgate=wgate, wup=wup, wdown=wdown, swg=swg, swu=swu, swd=swd,
            mmask=M, poison=poi, identf=idf, ltri=ltri, tcol=tcol,
            iorow=iorow))
    res = run_bass_kernel_spmd(nc, in_maps, core_ids=list(range(NCORE)))
    if _collect is not None:
        _collect.extend(res.results)
    y = np.concatenate([res.results[c]["y"] for c in range(NCORE)], axis=0)
    return y.reshape(1, NCORE * TC, D)


# version 09
# speedup vs baseline: 1.4078x; 1.0009x over previous
import numpy as np
from contextlib import ExitStack

import concourse.bass as bass
import concourse.mybir as mybir
from concourse.bass_utils import run_bass_kernel_spmd

F32 = mybir.dt.float32
BF16 = mybir.dt.bfloat16
AF = mybir.ActivationFunctionType
ALU = mybir.AluOpType
AX = mybir.AxisListType

NCORE = 8
TC = 1024
TEXT = 2048
D = 2048
NH = 32
NKV = 8
DH = 64
WIN = 1024
DFF = 1408
NE = 8
NT = TC // 128
NS = TEXT // 128
ND = D // 128

SLOPES = [float(2.0 ** (-0.25 * h)) for h in range(NH)]

ENGINES = ("pe", "act", "dve", "pool", "sync")


class DmaSlot:
    __slots__ = ("name", "sem", "count", "groups")

    def __init__(self, name):
        self.name = name
        self.sem = None
        self.count = 0
        self.groups = {}


class Op:
    __slots__ = ("idx", "engine", "emit", "reads", "writes", "deps", "slot",
                 "slot_count", "slot_group", "need_inc", "milestone", "tag")

    def __init__(self, idx, engine, emit, reads, writes, slot, tag):
        self.idx = idx
        self.engine = engine
        self.emit = emit
        self.reads = reads
        self.writes = writes
        self.deps = []
        self.slot = slot
        self.slot_count = None
        self.slot_group = None
        self.need_inc = False
        self.milestone = None
        self.tag = tag


class Prog:
    def __init__(self, nc, self_sync=True):
        self.nc = nc
        self.ops = []
        self.track = {}
        self.slots = {}
        self.self_sync = self_sync
        self.pending_free = []
        self.newdeps = {}
        self.psum_names = set()
        self.n_waits = 0
        self.n_incs = 0

    def slot(self, name):
        s = self.slots.get(name)
        if s is None:
            s = self.slots[name] = DmaSlot(name)
        return s

    def new_tensor(self, name, span=None):
        deps = []
        for (sp, ops) in self.pending_free:
            if span is None or sp is None or (sp[0] < span[1] and
                                              span[0] < sp[1]):
                deps.extend(ops)
        if deps:
            self.newdeps[name] = self._prune(deps)

    def note_free(self, name, span=None):
        t = self.track.get(name)
        if t is None:
            return
        ops = list(t["w"].values())
        for lst in t["r"].values():
            ops.extend(lst)
        if ops:
            self.pending_free.append((span, self._prune(ops)))

    @staticmethod
    def _prune(deps):
        best = {}
        for d in deps:
            key = ("s", d.slot.name) if d.slot is not None else ("e", d.engine)
            cur = best.get(key)
            if cur is None or d.idx > cur.idx:
                best[key] = d
        return list(best.values())

    @staticmethod
    def _regions(lst):
        out = []
        for r in lst or ():
            if isinstance(r, tuple):
                out.append(r)
            else:
                out.append((r, None))
        return out

    def _dep_read(self, name, key):
        t = self.track.get(name)
        if t is None:
            return []
        w = t["w"]
        if key is None:
            return list(w.values())
        deps = []
        if key in w:
            deps.append(w[key])
        if None in w:
            deps.append(w[None])
        return deps

    def _dep_write(self, name, key):
        deps = self._dep_read(name, key)
        t = self.track.get(name)
        if t is not None:
            r = t["r"]
            if key is None:
                for lst in r.values():
                    deps.extend(lst)
            else:
                deps.extend(r.get(key, ()))
                deps.extend(r.get(None, ()))
        return deps

    def op(self, engine, emit, reads=(), writes=(), slot=None, tag="",
           gid=None):
        wr = self._regions(writes)
        rd = self._regions(reads)
        if engine == "pe":
            wr = [(n, None) if n in self.psum_names else (n, k)
                  for (n, k) in wr]
        else:
            promote = [(n, None) for (n, k) in rd if n in self.psum_names]
            if promote:
                rd = [(n, k) for (n, k) in rd if n not in self.psum_names]
                wr = wr + promote
        o = Op(len(self.ops), engine, emit, rd, wr, slot, tag)
        deps = []
        for (n, k) in o.reads:
            deps.extend(self._dep_read(n, k))
            if n in self.newdeps:
                deps.extend(self.newdeps.pop(n))
        for (n, k) in o.writes:
            deps.extend(self._dep_write(n, k))
            if n in self.newdeps:
                deps.extend(self.newdeps.pop(n))
        pruned = {}
        for d in deps:
            if d is o:
                continue
            if d.engine == o.engine and d.slot is None:
                if o.engine == "pe" or not self.self_sync:
                    continue
            key = ("s", d.slot.name) if d.slot is not None else ("e", d.engine)
            cur = pruned.get(key)
            if cur is None or d.idx > cur.idx:
                pruned[key] = d
        o.deps = list(pruned.values())
        if slot is not None:
            slot.count += 1
            o.slot_count = slot.count
            if gid is None:
                gid = ("u", slot.count)
            o.slot_group = gid
            g = slot.groups.get(gid, 0)
            slot.groups[gid] = max(g, slot.count)
        for (n, k) in o.reads:
            self.track.setdefault(n, {"w": {}, "r": {}})["r"].setdefault(
                k, []).append(o)
        for (n, k) in o.writes:
            t = self.track.setdefault(n, {"w": {}, "r": {}})
            if k is None:
                t["w"] = {None: o}
                t["r"] = {}
            else:
                t["w"][k] = o
                t["r"].pop(k, None)
        self.ops.append(o)
        return o

    def dma(self, queue, emit, reads=(), writes=(), slot=None, tag="",
            gid=None):
        assert slot is not None
        return self.op(queue, emit, reads=reads, writes=writes, slot=slot,
                       tag=tag, gid=gid)

    def emit_all(self, block, engine_sems):
        for o in self.ops:
            for d in o.deps:
                if d.slot is None:
                    d.need_inc = True
        counters = {e: 0 for e in ENGINES}
        for o in self.ops:
            if o.need_inc and o.slot is None:
                counters[o.engine] += 1
                o.milestone = counters[o.engine]
        per_engine = {e: [] for e in ENGINES}
        for o in self.ops:
            per_engine[o.engine].append(o)
        prog = self

        def runner(eng_name):
            def body(eng):
                waited = {}
                for o in per_engine[eng_name]:
                    for d in o.deps:
                        if d.slot is not None:
                            sem = d.slot.sem
                            val = 16 * d.slot.groups[d.slot_group]
                        else:
                            sem = engine_sems[d.engine]
                            val = d.milestone
                        key = sem.name
                        if waited.get(key, -1) >= val:
                            continue
                        eng.wait_ge(sem, val)
                        waited[key] = val
                        prog.n_waits += 1
                    try:
                        inst = o.emit(eng)
                    except Exception:
                        print("EMIT FAIL idx=", o.idx, "eng=", o.engine,
                              "tag=", o.tag, "reads=", o.reads,
                              "writes=", o.writes, flush=True)
                        raise
                    if o.slot is not None:
                        inst.then_inc(o.slot.sem, 16)
                        prog.n_incs += 1
                    elif o.need_inc:
                        inst.then_inc(engine_sems[o.engine], 1)
                        prog.n_incs += 1
            return body

        section = {"pe": block.tensor, "act": block.scalar,
                   "dve": block.vector, "pool": block.gpsimd,
                   "sync": block.sync}
        for e in ENGINES:
            if per_engine[e]:
                section[e](runner(e))


ARENA_BYTES = 211968


class TMgr:

    def __init__(self, nc, prog, off):
        self.nc = nc
        self.p = prog
        self.off = off
        self.cms = {}
        self.logical = {}
        self.acm = nc.sbuf_tensor("arena", [128, ARENA_BYTES], mybir.dt.uint8)
        self.ah = self.acm.__enter__()
        self.abase = nc.lookup_mloc(self.ah).addr
        self.spans = {}

    def sbuf(self, name, shape, dtype):
        import functools, operator
        off = self.off[name]
        per_part = functools.reduce(operator.mul, shape[1:], 1) *             mybir.dt.size(dtype)
        assert off + per_part <= ARENA_BYTES, (name, off, per_part)
        assert off % 64 == 0, (name, off)
        h = self.nc.alloc_sbuf_tensor_at(name, shape, dtype,
                                         offset=self.abase + off)
        self.spans[name] = (off, off + per_part)
        self.logical[h.name] = name
        self.p.new_tensor(name, self.spans[name])
        return h

    def psum(self, name, shape, dtype=F32):
        import functools, operator
        cm = self.nc.psum_tensor(name, shape, dtype)
        h = cm.__enter__()
        self.cms[name] = cm
        self.logical[h.name] = name
        self.p.psum_names.add(name)
        base = (1 << 24) + self.nc.lookup_mloc(h).addr
        per_part = functools.reduce(operator.mul, shape[1:], 1) *             mybir.dt.size(dtype)
        self.spans[name] = (base, base + per_part)
        self.p.new_tensor(name, self.spans[name])
        return h

    def free(self, *handles):
        for h in handles:
            name = self.logical.get(h.name, h.name)
            self.p.note_free(name, self.spans.get(name))
            cm = self.cms.pop(name, None)
            if cm is not None:
                cm.__exit__(None, None, None)


def build_nc(taps=(), stop_after=None):
    nc = bass.Bass("TRN2", target_bir_lowering=False)
    taps = set(taps)

    din = lambda n, s: nc.dram_tensor(n, s, F32, kind="ExternalInput").ap()
    XE = din("xe", [TEXT, D])
    WQ = din("wq", [D, D])
    WK = din("wk", [512, D])
    WV = din("wv", [512, D])
    WO = din("wo", [D, D])
    QNW = din("qnw", [1, D])
    KNW = din("knw", [1, 512])
    GW = din("gw", [NE, D])
    WGATE = din("wgate", [NE, D, DFF])
    WUP = din("wup", [NE, D, DFF])
    WDOWN = din("wdown", [NE, DFF, D])
    SWG = din("swg", [D, DFF])
    SWU = din("swu", [D, DFF])
    SWD = din("swd", [DFF, D])
    MM = din("mmask", [128, 1152])
    LTRI = din("ltri", [128, 128])
    TCOL = din("tcol", [128, 8])
    IOROW = din("iorow", [1, 384])
    Y1D = nc.dram_tensor("y1d", [TC, D], BF16, kind="Internal").ap()
    PK = nc.dram_tensor("pk", [NE * 384, D], BF16, kind="Internal").ap()
    POI = din("poison", [128, NS])
    IDF = din("identf", [128, 128])
    Y = nc.dram_tensor("y", [TC, D], F32, kind="ExternalOutput").ap()
    tap_outs = {}
    tap_shapes = {"KT": [512, TEXT], "QT": [D, TC], "MALL": [NH, TC],
                  "AO": [D, TC], "Y1": [TC, D], "GATES": [TC, NE],
                  "ACC2": [TC, D]}
    for t in taps:
        tap_outs[t] = nc.dram_tensor("tap_" + t, tap_shapes[t], F32,
                                     kind="ExternalOutput").ap()

    RA, RB, RC, RD, RE = 10240, 79872, 113664, 146432, 179200
    OFF = {
        "idb": 0, "idf": 256, "ones1": 768, "onesf": 1024, "small": 1536,
        "bns": 1792, "aggr": 1920, "knacc": 1984, "knbneg": 2048,
        "knscr": 2176, "epsc": 9920, "gates8": 3520, "idxall": 3776,
        "g12i": 3904, "gg12": 3968, "onescol": 4032,
        "grow8": 4800, "garow0": 5824,
        "garow1": 7872,
        "wkT": RA, "wvT": RA + 16384, "wstg0": RA + 32768, "wstg1": RA + 36864,
        **{f"wqT{i}": RA + 4096 * i for i in range(16)},
        **{f"Fr{i}": RA + 2048 * i for i in range(12)},
        "woq0": RA, "woq1": RA + 16384, "wosg0": RA + 32768,
        "wosg1": RA + 36864, "wosg2": RA + 65536, "xrs0": RA + 40960, "xrs1": RA + 49152,
        "ybf": RA + 57344, "gwbf": RA + 61440, "gwT": RA + 57344,
        **{f"wgb{i}": RA + 8192 * i for i in range(3)},
        **{f"wub{i}": RA + 24576 + 8192 * i for i in range(3)},
        **{f"wdb{i}": RA + 49152 + 5696 * i for i in range(3)},
        **{f"KT{i}": RB + 4096 * i for i in range(4)},
        **{f"VA{i}": RB + 16384 + 2112 * i for i in range(8)},
        **{f"ACC{i}": (RB + 8192 * i if i < 4 else RC + 8192 * (i - 4))
           for i in range(8)},
        "m_all": RC, "mrow0": RC + 2048, "mrow1": RC + 4096,
        "rrecb": RC + 6144,
        "Qsb": RC + 8192, "sqs": RC + 16384, "wstgq0": RC + 24576,
        "wstgq1": RC + 28672,
        "Mm": RC + 8192, "poi": RC + 12800, "Hb0": RC + 12864,
        "Hb1": RC + 15168, "Fb0": RC + 17472, "Fb1": RC + 19520,
        "rsb": RC + 21568, "rsb2": RC + 25664, "rbsb": RC + 29760,
        "xstg0": RD, "xstg1": RD + 4096, "xTc0": RD + 8192,
        "xTc1": RD + 12288, "ksqs": RD + 16384, "row1": RD + 18432,
        "kwb": RD + 22528, "qwb": RD + 24576,
        **{f"AO{i}": RD + 2048 * i for i in range(16)},
        **{f"hT{i}": RD + 2048 * i for i in range(11)},
        "gsbs": RD + 22528, "usbs": RD + 24576,
        "ygT": RD, "Ygt": RD + 12288,
        **{f"Dsb{i}": RD + 4096 * i for i in range(3)},
        **{f"hTe{i}": RD + 21312 + 768 * i for i in range(11)},
        "gsb": RD + 29760, "usb": RD + 30528,
        "PTf": RD, "iotabc": RD + 16384, "slotf": RD + 17920,
        "posf": RD + 18176, "mkb": RD + 18432, "tokids": RD + 18560,
        "LTb": RD + 18624, "bb": RD + 18880, "brow": RD + 19136,
        "crow": RD + 19392, "g12scr": RD + 19648, "iorowf": RD + 19776,
        "rload0": RA, "rload1": RA + 4096, "rload2": RA + 8192,
        "rload3": RA + 12288,
        "ttmp_KT": RA + 32768, "ttmp_QT": RA + 32768, "matmp": RA + 40960,
        "dummyA": RE + 24576, "dummyB": RA + 32768,
        "gtmp": RA + 32768, "ttmp_AO": RA + 32768,
        **{f"QT{i}": RE + 2048 * i for i in range(16)},
        **{f"y1T{i}": RE + 2048 * i for i in range(16)},
    }
    p = Prog(nc, self_sync=True)
    tm = TMgr(nc, p, OFF)
    st = ExitStack()

    idb = tm.sbuf("idb", [128, 128], BF16)
    idf = tm.sbuf("idf", [128, 128], F32)
    ones1 = tm.sbuf("ones1", [1, 128], BF16)
    onescol = tm.sbuf("onescol", [128, 1], BF16)
    onesf = tm.sbuf("onesf", [1, 128], F32)
    epsc = tm.sbuf("epsc", [128, 2], F32)
    knscr = tm.sbuf("knscr", [8, 336], F32)
    m_all = tm.sbuf("m_all", [NH, TC], BF16)
    mrows = [tm.sbuf(f"mrow{i}", [1, TC], BF16) for i in range(2)]
    knacc = tm.sbuf("knacc", [128, NKV], F32)
    knbneg = tm.sbuf("knbneg", [128, NH], F32)
    gates8 = tm.sbuf("gates8", [128, NT * NE], F32)
    grow8 = tm.sbuf("grow8", [NE, TC], BF16)
    garow = [tm.sbuf(f"garow{i}", [1, TC], BF16) for i in range(2)]
    small = tm.sbuf("small", [128, 64], F32)
    bns = tm.sbuf("bns", [128, 24], F32)
    aggr = tm.sbuf("aggr", [128, 2], F32)

    sems = {}
    s_misc = p.slot("misc")
    s_miscp = p.slot("miscp")
    s_out = p.slot("out")

    def sync_dma(dst, src, reads, writes, slot, gid=None, **kw):
        return p.dma("sync", lambda e: e.dma_start(dst, src, **kw),
                     reads=reads, writes=writes, slot=slot, gid=gid)

    def cast_dma(dst, src, reads, writes, slot, gid=None):
        return p.dma("pool", lambda e: e.dma_start(dst, src),
                     reads=reads, writes=writes, slot=slot, gid=gid)

    def finish_dummy(which):
        dmy = tm.sbuf(which, [128, D], F32)
        p.op("dve", lambda e: e.memset(dmy[:], 0.5), writes=[which])
        for t in range(NT):
            sync_dma(Y[128 * t:128 * (t + 1), :], dmy[:], [which],
                     [("OUT", t)], s_out, gid="yg")
        p.op("pool", lambda e: e.nop(), reads=[("OUT", None)], tag="final")
        with nc.Block() as block:
            for e in ENGINES:
                sems[e] = st.enter_context(nc.semaphore(f"sem_{e}"))
            for sl_ in p.slots.values():
                sl_.sem = st.enter_context(nc.semaphore(f"dma_{sl_.name}"))
            p.emit_all(block, sems)
        st.close()
        nc._dbg_prog = p
        return nc

    def finish_acc(ACC):
        for t in range(NT):
            sync_dma(Y[128 * t:128 * (t + 1), :], ACC[t][:],
                     [(f"ACC{t}", "lo"), (f"ACC{t}", "hi")],
                     [("OUT", t)], s_out, gid="yg")
        p.op("pool", lambda e: e.nop(), reads=[("OUT", None)], tag="final")
        with nc.Block() as block:
            for e in ENGINES:
                sems[e] = st.enter_context(nc.semaphore(f"sem_{e}"))
            for sl_ in p.slots.values():
                sl_.sem = st.enter_context(nc.semaphore(f"dma_{sl_.name}"))
            p.emit_all(block, sems)
        st.close()
        nc._dbg_prog = p
        return nc

    sems_holder = sems

    sync_dma(idf[:], IDF, [], ["idf"], s_misc)
    cast_dma(idb[:], IDF, [], ["idb"], s_miscp)
    p.op("dve", lambda e: e.memset(ones1[:], 1.0), writes=["ones1"])
    p.op("dve", lambda e: e.memset(onescol[:], 1.0), writes=["onescol"])
    p.op("dve", lambda e: e.memset(onesf[:], 1.0), writes=["onesf"])
    p.op("dve", lambda e: e.memset(epsc[:, 0:1], 1e-6), writes=[("epsc", 0)])
    p.op("dve", lambda e: e.memset(epsc[:, 1:2], 1e-5), writes=[("epsc", 1)])

    pt0 = tm.psum("pt0", [128, 1024], BF16)
    pt1 = tm.psum("pt1", [128, 1024], BF16)
    pts = [pt0, pt1]
    pB = tm.psum("pB", [128, 512])
    pK = tm.psum("pK", [128, 512])
    pV = tm.psum("pV", [128, 512])

    wkT = tm.sbuf("wkT", [128, ND * 512], BF16)
    wvT = tm.sbuf("wvT", [128, ND * 512], BF16)
    wstg = [tm.sbuf(f"wstg{i}", [128, D], BF16) for i in range(2)]
    s_wstg = [p.slot(f"wstg{i}") for i in range(2)]
    KT = [tm.sbuf(f"KT{i}", [128, TEXT], BF16) for i in range(4)]
    VA = [tm.sbuf(f"VA{i}", [128, NS * 65], BF16) for i in range(NKV)]
    kwb = tm.sbuf("kwb", [128, 512], F32)
    row1 = tm.sbuf("row1", [1, 2048], BF16)

    def wT_build(W, nrow_tiles, dstT, dname):
        for r in range(nrow_tiles):
            b = r % 2
            cast_dma(wstg[b][:], W[128 * r:128 * (r + 1), :],
                     [], [f"wstg{b}"], s_wstg[b])
            for dc in range(ND):
                pb = pts[dc // 8]
                p.op("pe", lambda e, b=b, dc=dc, pb=pb: e.transpose(
                    pb[:, 128 * (dc % 8):128 * (dc % 8 + 1)],
                    wstg[b][:, 128 * dc:128 * (dc + 1)], idb[:]),
                    reads=[f"wstg{b}", "idb"], writes=[(pb.name, dc % 8)])
            for half in range(2):
                pb = pts[half]
                dst3 = dstT[:].rearrange("p (dc o) -> p dc o", o=512)[
                    :, 8 * half:8 * (half + 1), 128 * r:128 * (r + 1)]
                src3 = pb[:].rearrange("p (dc o) -> p dc o", o=128)
                p.op("dve", lambda e, dst3=dst3, src3=src3: e.tensor_copy(
                    dst3, src3),
                    reads=[pb.name, (pb.name, 0), (pb.name, 1), (pb.name, 2),
                           (pb.name, 3), (pb.name, 4), (pb.name, 5),
                           (pb.name, 6), (pb.name, 7)],
                    writes=[(dname, (r, half))])

    wT_build(WK, 4, wkT, "wkT")
    wT_build(WV, 4, wvT, "wvT")

    cast_dma(row1[0:1, 0:512], KNW, [], [("row1", "k")], p.slot("knw"))
    p.op("pe", lambda e: e.matmul(pB[:], ones1[:], row1[0:1, 0:512],
                                  start=True, stop=True),
         reads=[("row1", "k"), "ones1"], writes=["pB"])
    p.op("dve", lambda e: e.tensor_copy(kwb[:], pB[:]), reads=["pB"],
         writes=["kwb"])

    xstg = [tm.sbuf(f"xstg{i}", [128, D], BF16) for i in range(2)]
    s_xstg = [p.slot(f"xstg{i}") for i in range(2)]
    xTc = [tm.sbuf(f"xTc{i}", [128, D], BF16) for i in range(2)]
    ksqs = tm.sbuf("ksqs", [128, 512], F32)

    p.op("dve", lambda e: e.memset(knacc[:], 0.0), writes=["knacc"])

    def transpose_tile(srcname, src, key=None):
        rkey = srcname if key is None else (srcname, key)
        for dc in range(ND):
            pb = pts[dc // 8]
            p.op("pe", lambda e, dc=dc, pb=pb, src=src: e.transpose(
                pb[:, 128 * (dc % 8):128 * (dc % 8 + 1)],
                src[:, 128 * dc:128 * (dc + 1)], idb[:]),
                reads=[rkey, "idb"], writes=[(pb.name, dc % 8)])

    def evac_xTc(b):
        for half in range(2):
            pb = pts[half]
            p.op("dve", lambda e, b=b, half=half, pb=pb: e.tensor_copy(
                xTc[b][:, 1024 * half:1024 * (half + 1)], pb[:]),
                reads=[pb.name] + [(pb.name, i) for i in range(8)],
                writes=[(f"xTc{b}", half)])

    for j in range(NS):
        b = j % 2
        cast_dma(xstg[b][:], XE[128 * j:128 * (j + 1), :], [], [f"xstg{b}"],
                 s_xstg[b])
        transpose_tile(f"xstg{b}", xstg[b][:])
        evac_xTc(b)
        for dc in range(ND):
            p.op("pe", lambda e, b=b, dc=dc: e.matmul(
                pK[:], xTc[b][:, 128 * dc:128 * (dc + 1)],
                wkT[:, 512 * dc:512 * (dc + 1)],
                start=(dc == 0), stop=(dc == ND - 1)),
                reads=[(f"xTc{b}", dc // 8), "wkT"], writes=["pK"])
        p.op("act", lambda e: e.activation(ksqs[:], pK[:], AF.Square),
             reads=["pK"], writes=["ksqs"])
        p.op("dve", lambda e: e.reduce_sum(small[:, 0:1], ksqs[:], AX.X),
             reads=["ksqs"], writes=[("small", 0)])
        p.op("act", lambda e: e.activation(small[:, 1:2], small[:, 0:1],
                                           AF.Sqrt, bias=epsc[:, 0:1],
                                           scale=1.0 / 512),
             reads=[("small", 0), ("epsc", 0)], writes=[("small", 1)])
        p.op("dve", lambda e: e.reciprocal(small[:, 2:3], small[:, 1:2]),
             reads=[("small", 1)], writes=[("small", 2)])
        p.op("dve", lambda e, b=b: e.scalar_tensor_tensor(
            xstg[b][:, 0:512], pK[:], small[:, 2:3], kwb[:],
            ALU.mult, ALU.mult),
            reads=["pK", ("small", 2), "kwb"], writes=[(f"xstg{b}", "ks")])
        p.op("act", lambda e, b=b: e.activation(ksqs[:], xstg[b][:, 0:512],
                                                AF.Square),
             reads=[(f"xstg{b}", "ks")], writes=["ksqs"])
        p.op("dve", lambda e: e.reduce_sum(
            small[:, 8:16], ksqs[:].rearrange("p (g x) -> p g x", g=8), AX.X),
            reads=["ksqs"], writes=[("small", 8)])
        p.op("dve", lambda e: e.tensor_tensor(knacc[:], knacc[:],
                                              small[:, 8:16], ALU.max),
             reads=[("small", 8), "knacc"], writes=["knacc"])
        for c in range(4):
            p.op("pe", lambda e, b=b, c=c: e.transpose(
                pt0[:, 128 * c:128 * (c + 1)],
                xstg[b][:, 128 * c:128 * (c + 1)], idb[:]),
                reads=[(f"xstg{b}", "ks"), "idb"], writes=[("pt0", c)])
        for c in range(4):
            p.op("dve", lambda e, c=c, j=j: e.tensor_copy(
                KT[c][:, 128 * j:128 * (j + 1)], pt0[:, 128 * c:128 * (c + 1)]),
                reads=[("pt0", c)], writes=[(f"KT{c}", j)])
        for dc in range(ND):
            p.op("pe", lambda e, b=b, dc=dc: e.matmul(
                pV[:], xTc[b][:, 128 * dc:128 * (dc + 1)],
                wvT[:, 512 * dc:512 * (dc + 1)],
                start=(dc == 0), stop=(dc == ND - 1)),
                reads=[(f"xTc{b}", dc // 8), "wvT"], writes=["pV"])
        for kv in range(NKV):
            p.op("dve", lambda e, kv=kv, j=j: e.tensor_copy(
                VA[kv][:, 65 * j:65 * j + 64], pV[:, 64 * kv:64 * (kv + 1)]),
                reads=["pV"], writes=[(f"VA{kv}", j)])
        for kv in range(NKV):
            p.op("dve", lambda e, kv=kv, j=j: e.memset(
                VA[kv][:, 65 * j + 64:65 * j + 65], 1.0),
                writes=[(f"VA{kv}", j)])

    if stop_after == "A1":
        return finish_dummy("dummyA")

    p.op("pe", lambda e: e.transpose(pB[0:8, 0:128], knacc[:], idf[:]),
         reads=["knacc", "idf"], writes=["pB"])
    p.op("dve", lambda e: e.tensor_copy(knscr[0:8, 0:128], pB[0:8, 0:128]),
         reads=["pB"], writes=["knscr"])
    p.op("dve", lambda e: e.reduce_max(small[0:8, 3:4], knscr[0:8, 0:128],
                                       AX.X),
         reads=["knscr"], writes=[("small", 3)])
    p.op("act", lambda e: e.activation(small[0:8, 4:5], small[0:8, 3:4],
                                       AF.Sqrt), reads=[("small", 3)],
         writes=[("small", 4)])
    p.op("dve", lambda e: e.tensor_scalar(small[0:8, 5:6], small[0:8, 4:5],
                                          -1.0, None, ALU.mult),
         reads=[("small", 4)], writes=[("small", 5)])
    p.op("pe", lambda e: e.transpose(pB[0:1, 128:136], small[0:8, 5:6],
                                     idf[0:8, 0:8]),
         reads=[("small", 5), "idf"], writes=["pB"])
    p.op("dve", lambda e: e.tensor_copy(knscr[0:1, 128:136],
                                        pB[0:1, 128:136]),
         reads=["pB"], writes=["knscr"])
    s_kn = p.slot("knr")
    for m_ in range(2):
        for r_ in range(4):
            sync_dma(knscr[0:1, 160 + 2 * r_ + m_:192:8],
                     knscr[0:1, 128 + m_:136:2],
                     ["knscr"], ["knscr"], s_kn,
                     allow_slow_non_contiguous=True)
    p.op("pe", lambda e: e.matmul(pB[:, 0:32], onesf[:], knscr[0:1, 160:192],
                                  start=True, stop=True),
         reads=["knscr", "onesf"], writes=["pB"])
    p.op("dve", lambda e: e.tensor_copy(knbneg[:], pB[:, 0:32]),
         reads=["pB"], writes=["knbneg"])

    tm.free(wstg[1], wstg[0], wkT, wvT)
    tm.free(pV, pK)
    pQs = [tm.psum(f"pQ{i}", [128, 512]) for i in range(4)]

    if stop_after == "KN":
        return finish_dummy("dummyA")

    wqT = [tm.sbuf(f"wqT{i}", [128, D], BF16) for i in range(ND)]
    wstgq = [tm.sbuf(f"wstgq{i}", [128, D], BF16) for i in range(2)]
    s_wstgq = [p.slot(f"wstgq{i}") for i in range(2)]
    for r in range(ND):
        b = r % 2
        cast_dma(wstgq[b][:], WQ[128 * r:128 * (r + 1), :], [], [f"wstgq{b}"],
                 s_wstgq[b])
        transpose_tile(f"wstgq{b}", wstgq[b][:])
        for dc in range(ND):
            pb = pts[dc // 8]
            p.op("dve", lambda e, r=r, dc=dc, pb=pb: e.tensor_copy(
                wqT[dc][:, 128 * r:128 * (r + 1)],
                pb[:, 128 * (dc % 8):128 * (dc % 8 + 1)]),
                reads=[(pb.name, dc % 8)], writes=[(f"wqT{dc}", r)])

    if stop_after == "WQT":
        return finish_dummy("dummyA")

    qwb = tm.sbuf("qwb", [128, D], F32)
    cast_dma(row1[0:1, 0:2048], QNW, [], [("row1", "q")], p.slot("qnw"))
    for oc in range(4):
        p.op("pe", lambda e, oc=oc: e.matmul(
            pB[:], ones1[:], row1[0:1, 512 * oc:512 * (oc + 1)],
            start=True, stop=True),
            reads=[("row1", "q"), "ones1"], writes=["pB"])
        p.op("act", lambda e, oc=oc: e.activation(
            qwb[:, 512 * oc:512 * (oc + 1)], pB[:], AF.Copy, scale=0.125),
            reads=["pB"], writes=[("qwb", oc)])

    if stop_after == "QWB":
        return finish_dummy("dummyA")

    QT = [tm.sbuf(f"QT{i}", [128, TC], BF16) for i in range(ND)]
    Qsb = tm.sbuf("Qsb", [128, D], F32)
    sqs = tm.sbuf("sqs", [128, D], F32)

    _nt = NT
    _ql = 99
    if stop_after and stop_after.startswith("Q") and len(stop_after) == 2:
        _nt = int(stop_after[1]) + 1
    if stop_after and stop_after.startswith("QL"):
        _nt, _ql = 1, int(stop_after[2:])

    def q_head(t):
        b = t % 2
        cast_dma(xstg[b][:], XE[TC + 128 * t:TC + 128 * (t + 1), :], [],
                 [f"xstg{b}"], s_xstg[b])
        transpose_tile(f"xstg{b}", xstg[b][:])
        evac_xTc(b)
        if _ql < 2:
            return
        for oc in range(4):
            pQ = pQs[oc]
            for dc in range(ND):
                p.op("pe", lambda e, b=b, dc=dc, oc=oc, pQ=pQ: e.matmul(
                    pQ[:], xTc[b][:, 128 * dc:128 * (dc + 1)],
                    wqT[dc][:, 512 * oc:512 * (oc + 1)],
                    start=(dc == 0), stop=(dc == ND - 1)),
                    reads=[(f"xTc{b}", dc // 8), f"wqT{dc}"], writes=[pQ.name])

    def q_evac(t):
        qc = 20 + 4 * (t % 2)
        if _ql < 2:
            return
        for oc in range(4):
            pQ = pQs[oc]
            p.op("dve", lambda e, oc=oc, pQ=pQ: e.tensor_copy(
                Qsb[:, 512 * oc:512 * (oc + 1)], pQ[:]),
                reads=[pQ.name], writes=[("Qsb", oc)])
            if _ql < 3:
                continue
            p.op("act", lambda e, oc=oc, pQ=pQ: e.activation(
                sqs[:, 512 * oc:512 * (oc + 1)], pQ[:], AF.Square),
                reads=[pQ.name], writes=[("sqs", oc)])
        if _ql < 4:
            return
        p.op("dve", lambda e, qc=qc: e.reduce_sum(
            small[:, qc:qc + 1], sqs[:], AX.X),
            reads=["sqs", ("sqs", 0), ("sqs", 1), ("sqs", 2), ("sqs", 3)],
            writes=[("small", qc)])
        p.op("act", lambda e, qc=qc: e.activation(
            small[:, qc + 1:qc + 2], small[:, qc:qc + 1],
            AF.Sqrt, bias=epsc[:, 0:1], scale=1.0 / D),
            reads=[("small", qc), ("epsc", 0)], writes=[("small", qc + 1)])
        p.op("dve", lambda e, qc=qc: e.reciprocal(
            small[:, qc + 2:qc + 3], small[:, qc + 1:qc + 2]),
            reads=[("small", qc + 1)], writes=[("small", qc + 2)])

    def q_tail(t):
        b = t % 2
        qc = 20 + 4 * (t % 2)
        if _ql < 5:
            return
        p.op("dve", lambda e, b=b, qc=qc: e.scalar_tensor_tensor(
            xstg[b][:], Qsb[:], small[:, qc + 2:qc + 3], qwb[:],
            ALU.mult, ALU.mult),
            reads=[("Qsb", 0), ("Qsb", 1), ("Qsb", 2), ("Qsb", 3),
                   ("small", qc + 2), "qwb"],
            writes=[(f"xstg{b}", "qh")])
        if _ql < 6:
            return
        p.op("act", lambda e, b=b: e.activation(sqs[:], xstg[b][:], AF.Square),
             reads=[(f"xstg{b}", "qh")],
             writes=["sqs", ("sqs", 0), ("sqs", 1), ("sqs", 2), ("sqs", 3)])
        p.op("dve", lambda e: e.reduce_sum(
            small[:, 32:64], sqs[:].rearrange("p (g x) -> p g x", g=32), AX.X),
            reads=["sqs"], writes=[("small", 32)])
        p.op("act", lambda e: e.activation(small[:, 32:64], small[:, 32:64],
                                           AF.Sqrt),
             reads=[("small", 32)], writes=[("small", 32)])
        if _ql < 7:
            return
        p.op("dve", lambda e: e.tensor_tensor(small[:, 32:64], small[:, 32:64],
                                              knbneg[:], ALU.mult),
             reads=[("small", 32), "knbneg"], writes=[("small", 32)])
        p.op("pe", lambda e: e.transpose(pB[0:32, 0:128], small[:, 32:64],
                                         idf[:]),
             reads=[("small", 32), "idf"], writes=["pB"])
        p.op("dve", lambda e, t=t: e.tensor_copy(
            m_all[:, 128 * t:128 * (t + 1)], pB[0:32, 0:128]),
            reads=["pB"], writes=[("m_all", t)])
        if _ql < 8:
            return
        transpose_tile(f"xstg{b}", xstg[b][:], key="qh")
        for dc in range(ND):
            pb = pts[dc // 8]
            p.op("dve", lambda e, t=t, dc=dc, pb=pb: e.tensor_copy(
                QT[dc][:, 128 * t:128 * (t + 1)],
                pb[:, 128 * (dc % 8):128 * (dc % 8 + 1)]),
                reads=[(pb.name, dc % 8)], writes=[(f"QT{dc}", t)])

    for t in range(_nt):
        q_head(t)
        if t > 0:
            q_tail(t - 1)
        q_evac(t)
    q_tail(_nt - 1)

    if stop_after and stop_after.startswith("Q"):
        return finish_dummy("dummyA")

    tm.free(Qsb, sqs, qwb, kwb, ksqs, row1, *wqT, *xstg, *xTc, *wstgq)
    tm.free(pQs[3], pQs[2], pQs[1], pQs[0], pB, pt1, pt0)

    def tap_dump(name, tiles, nrow, regions_of, width=TC):
        if name not in tap_outs:
            return
        ttmp = tm.sbuf("ttmp_" + name, [nrow, width], F32)
        s_tap = p.slot("tap_" + name)
        for c, tile in enumerate(tiles):
            p.op("act", lambda e, tile=tile: e.activation(ttmp[:], tile[:],
                                                          AF.Copy),
                 reads=regions_of(c), writes=["ttmp_" + name])
            sync_dma(tap_outs[name][nrow * c:nrow * (c + 1), :], ttmp[:],
                     ["ttmp_" + name], [("OUT", f"{name}{c}")], s_tap)
        tm.free(ttmp)

    if stop_after == "A":
        return finish_dummy("dummyA")

    tap_dump("KT", KT, 128, lambda c: [f"KT{c}"] + [(f"KT{c}", j) for j in range(NS)], width=TEXT)
    tap_dump("QT", QT, 128, lambda c: [f"QT{c}"] + [(f"QT{c}", t) for t in range(NT)])
    if "MALL" in tap_outs:
        matmp = tm.sbuf("matmp", [NH, TC], F32)
        p.op("act", lambda e: e.activation(matmp[:], m_all[:], AF.Copy),
             reads=["m_all"] + [("m_all", t) for t in range(NT)],
             writes=["matmp"])
        sync_dma(tap_outs["MALL"], matmp[:], ["matmp"], [("OUT", "matap")],
                 p.slot("tap_MALL"))
        tm.free(matmp)

    Mm = tm.sbuf("Mm", [128, 1152], F32)
    sync_dma(Mm[:], MM, [], ["Mm"], p.slot("mm"))
    poi = tm.sbuf("poi", [128, NS], F32)
    sync_dma(poi[:], POI, [], ["poi"], p.slot("poi"))
    Hb = [tm.sbuf(f"Hb{i}", [128, 1152], BF16) for i in range(2)]
    Fring = [tm.sbuf(f"Fr{i}", [128, TC], BF16) for i in range(12)]
    rsb = tm.sbuf("rsb", [32, TC], F32)
    rsb2 = tm.sbuf("rsb2", [32, TC], F32)
    
    rrecb = tm.sbuf("rrecb", [1, TC], BF16)
    rbsb = tm.sbuf("rbsb", [64, TC], BF16)
    AOsb = [tm.sbuf(f"AO{i}", [128, TC], BF16) for i in range(ND)]

    p.op("dve", lambda e: e.memset(rsb[:], 0.0), writes=[("rsb", None)])
    p.op("dve", lambda e: e.memset(rsb2[:], 0.0), writes=["rsb2"])
    Sps = [tm.psum(f"Sps{i}", [128, 1024]) for i in range(2)]
    AOps = tm.psum("AOps", [65, 1024])
    rbps = tm.psum("rbps", [64, 1024])

    s_mrow = [p.slot("mrowd0"), p.slot("mrowd1")]
    mall_regions = ["m_all"] + [("m_all", t) for t in range(NT)]

    def _av_batch(tt, kv):
        for jj in range(tt, tt + 9):
            tlo_jj = 128 * max(0, jj - 8)
            frj = Fring[jj % 12]
            p.op("pe", lambda e, tt=tt, jj=jj, kv=kv, frj=frj, tlo_jj=tlo_jj:
                 e.matmul(AOps[:, 128 * tt:128 * (tt + 1)],
                          VA[kv][:, 65 * jj:65 * (jj + 1)],
                          frj[:, 128 * tt - tlo_jj:128 * (tt + 1) - tlo_jj],
                          start=(jj == tt), stop=(jj == tt + 8)),
                 reads=[(f"VA{kv}", jj), frj.name],
                 writes=[("AOps", tt)])

    for sslot in range(NH):
        jt = sslot // 2
        bb = sslot % 2
        h = 8 * (jt // 4) + (jt % 4) + 4 * bb
        hb = sslot % 2
        kv = h // 4
        ktile = KT[kv // 2]
        kbase = 64 * (kv % 2)
        qtile = QT[jt]
        qbase = 64 * bb
        p.op("dve", lambda e, hb=hb, h=h: e.tensor_scalar(
            Hb[hb][:], Mm[:], SLOPES[h], None, ALU.mult),
            reads=["Mm"], writes=[f"Hb{hb}"])
        sync_dma(mrows[hb][:], m_all[sslot:sslot + 1, :], mall_regions,
                 [f"mrow{hb}"], s_mrow[hb])
        for j in range(NS):
            sb = Sps[j % 2]
            t_lo = 128 * max(0, j - 8)
            t_hi = 128 * (min(j, 7) + 1)
            span = t_hi - t_lo
            i_lo = t_lo + 1024 - 128 * j
            for off in range(0, span, 512):
                w = min(512, span - off)
                p.op("pe", lambda e, sb=sb, off=off, w=w, hb=hb, i_lo=i_lo:
                     e.matmul(sb[:, off:off + w], idb[:],
                              Hb[hb][:, i_lo + off:i_lo + off + w],
                              start=True, stop=False),
                     reads=[f"Hb{hb}", "idb"], writes=[sb.name])
                p.op("pe", lambda e, sb=sb, off=off, w=w, j=j, t_lo=t_lo,
                     ktile=ktile, kbase=kbase, qtile=qtile, qbase=qbase:
                     e.matmul(sb[:, off:off + w],
                              ktile[kbase:kbase + 64, 128 * j:128 * (j + 1)],
                              qtile[qbase:qbase + 64,
                                    t_lo + off:t_lo + off + w],
                              start=False, stop=False),
                     reads=[(ktile.name, j), qtile.name], writes=[sb.name])
                p.op("pe", lambda e, sb=sb, off=off, w=w, hb=hb, t_lo=t_lo:
                     e.matmul(sb[:, off:off + w], ones1[:],
                              mrows[hb][0:1, t_lo + off:t_lo + off + w],
                              start=False, stop=True),
                     reads=[f"mrow{hb}", "ones1"], writes=[sb.name])
            fr = Fring[j % 12]
            p.op("act", lambda e, sb=sb, span=span, j=j, fr=fr: e.activation(
                fr[:, 0:span], sb[:, 0:span], AF.Exp,
                bias=poi[:, j:j + 1]),
                reads=[sb.name, "poi"], writes=[fr.name])
            if j >= 9:
                _av_batch(j - 9, kv)
        for tt in (7,):
            _av_batch(tt, kv)
        p.op("dve", lambda e: e.tensor_copy(rsb[0:1, :], AOps[64:65, :]),
             reads=[("AOps", None)] + [("AOps", tt) for tt in range(8)],
             writes=[("rsb", None)])
        p.op("dve", lambda e: e.transpose(rsb2[:], rsb[:]),
             reads=[("rsb", None)], writes=["rsb2"])
        rview = rsb2[:].rearrange("p (b c) -> p b c", c=32)[:, :, 0:1]
        p.op("dve", lambda e: e.reciprocal(rview, rview),
             reads=["rsb2"], writes=["rsb2"])
        p.op("dve", lambda e: e.transpose(rsb[:], rsb2[:]),
             reads=["rsb2"], writes=[("rsb", None)])
        p.op("dve", lambda e: e.tensor_copy(rrecb[:], rsb[0:1, :]),
             reads=[("rsb", None)], writes=["rrecb"])
        for half in range(2):
            p.op("pe", lambda e, half=half: e.matmul(
                rbps[:, 512 * half:512 * (half + 1)], ones1[0:1, 0:64],
                rrecb[0:1, 512 * half:512 * (half + 1)],
                start=True, stop=True),
                reads=["rrecb", "ones1"], writes=["rbps"])
        p.op("dve", lambda e: e.tensor_copy(rbsb[:], rbps[:]),
             reads=["rbps"], writes=["rbsb"])
        p.op("dve", lambda e, jt=jt, bb=bb: e.tensor_tensor(
            AOsb[jt][64 * bb:64 * bb + 64, :], AOps[0:64, :],
            rbsb[:], ALU.mult),
            reads=[("AOps", None)] + [("AOps", tt) for tt in range(8)] +
                  ["rbsb"],
            writes=[(f"AO{jt}", bb)])

    tm.free(*QT, *KT, *VA, Mm, *Hb, *Fring, poi, rsb, rsb2, rrecb, rbsb,
            m_all, mrows[0], mrows[1])
    tm.free(rbps, AOps, Sps[1], Sps[0])

    if stop_after == "B":
        return finish_dummy("dummyB")

    tap_dump("AO", AOsb, 128, lambda c: [f"AO{c}", (f"AO{c}", 0), (f"AO{c}", 1)])

    ptc0 = tm.psum("ptc0", [128, 1024], BF16)
    ptc1 = tm.psum("ptc1", [128, 1024], BF16)
    ptcs = [ptc0, ptc1]
    pc = [tm.psum(f"pc{i}", [128, 1024]) for i in range(2)]

    ACC = [tm.sbuf(f"ACC{i}", [128, D], F32) for i in range(NT)]
    y1T = [tm.sbuf(f"y1T{i}", [128, TC], BF16) for i in range(ND)]
    woq = [tm.sbuf(f"woq{i}", [128, 4 * D], BF16) for i in range(2)]
    wosg = [tm.sbuf(f"wosg{i}", [128, 2048], BF16) for i in range(3)]
    s_wos = [p.slot(f"wos{i}") for i in range(3)]
    xrs = [tm.sbuf(f"xrs{i}", [128, D], F32) for i in range(2)]
    s_xrs = [p.slot(f"xrs{i}") for i in range(2)]
    ybf = tm.sbuf("ybf", [128, D], BF16)

    WOr = WO.rearrange("(a p) o -> p a o", p=128)
    for oq in range(4):
        qb = oq % 2
        for g in range(4):
            b2 = (4 * oq + g) % 3
            cast_dma(
                wosg[b2][:].rearrange("p (a o) -> p a o", o=512)[:, 0:4, :],
                WOr[:, 4 * g:4 * (g + 1), 512 * oq:512 * (oq + 1)],
                [], [f"wosg{b2}"], s_wos[b2])
            for ai in range(4):
                a = 4 * g + ai
                for c in range(4):
                    pb = ptcs[c % 2]
                    p.op("pe", lambda e, b2=b2, ai=ai, a=a, c=c, pb=pb:
                         e.transpose(
                             pb[:, 512 * (a % 2) + 128 * (c // 2):
                                512 * (a % 2) + 128 * (c // 2) + 128],
                             wosg[b2][:, 512 * ai + 128 * c:
                                     512 * ai + 128 * (c + 1)],
                             idb[:]),
                         reads=[f"wosg{b2}", "idb"],
                         writes=[(pb.name, (a % 2, c // 2))])
                    p.op("dve", lambda e, qb=qb, a=a, c=c, pb=pb:
                         e.tensor_copy(
                             woq[qb][:, D * c + 128 * a:D * c + 128 * (a + 1)],
                             pb[:, 512 * (a % 2) + 128 * (c // 2):
                                512 * (a % 2) + 128 * (c // 2) + 128]),
                         reads=[(pb.name, (a % 2, c // 2))],
                         writes=[(f"woq{qb}", (a, c))])
        for t in range(NT):
            tb = t % 2
            if oq == 0:
                sync_dma(xrs[tb][:], XE[TC + 128 * t:TC + 128 * (t + 1), :],
                         [], [f"xrs{tb}"], s_xrs[tb])
            for dhalf in range(2):
                for c in range(4):
                    oc = 4 * oq + c
                    for dd in range(2):
                        p.op("pe", lambda e, qb=qb, c=c, t=t, oc=oc,
                             dhalf=dhalf, dd=dd: e.matmul(
                                 pc[t % 2][:, 512 * dd:512 * (dd + 1)],
                                 AOsb[oc][:, 128 * t:128 * (t + 1)],
                                 woq[qb][:, D * c + 1024 * dhalf + 512 * dd:
                                         D * c + 1024 * dhalf + 512 * (dd + 1)],
                                 start=(c == 0), stop=(c == 3)),
                             reads=[f"AO{oc}", (f"AO{oc}", 0), (f"AO{oc}", 1),
                                    f"woq{qb}"],
                             writes=[pc[t % 2].name])
                hk = "lo" if dhalf == 0 else "hi"
                if oq == 0:
                    p.op("dve", lambda e, t=t, tb=tb, dhalf=dhalf: e.tensor_tensor(
                        ACC[t][:, 1024 * dhalf:1024 * (dhalf + 1)],
                        xrs[tb][:, 1024 * dhalf:1024 * (dhalf + 1)],
                        pc[t % 2][:], ALU.add),
                        reads=[f"xrs{tb}", pc[t % 2].name],
                        writes=[(f"ACC{t}", hk)])
                else:
                    p.op("dve", lambda e, t=t, dhalf=dhalf: e.tensor_tensor(
                        ACC[t][:, 1024 * dhalf:1024 * (dhalf + 1)],
                        ACC[t][:, 1024 * dhalf:1024 * (dhalf + 1)],
                        pc[t % 2][:], ALU.add),
                        reads=[pc[t % 2].name, (f"ACC{t}", hk)],
                        writes=[(f"ACC{t}", hk)])

    def layernorm_tile(t):
        acc_h = ACC[t]
        tname = f"ACC{t}"
        for c in range(4):
            p.op("dve", lambda e, c=c: e.bn_stats(
                bns[:, 6 * c:6 * (c + 1)], acc_h[:, 512 * c:512 * (c + 1)]),
                reads=[(tname, "lo"), (tname, "hi")], writes=[("bns", c)])
        p.op("dve", lambda e: e.bn_aggr(aggr[:], bns[:]),
             reads=[("bns", 0), ("bns", 1), ("bns", 2), ("bns", 3)],
             writes=["aggr"])
        p.op("act", lambda e: e.activation(small[:, 24:25], aggr[:, 1:2],
                                           AF.Sqrt, bias=epsc[:, 1:2]),
             reads=["aggr", ("epsc", 1)], writes=[("small", 24)])
        p.op("dve", lambda e: e.reciprocal(small[:, 25:26], small[:, 24:25]),
             reads=[("small", 24)], writes=[("small", 25)])
        p.op("dve", lambda e: e.tensor_scalar(
            acc_h[:], acc_h[:], aggr[:, 0:1], small[:, 25:26],
            ALU.subtract, ALU.mult),
            reads=[(tname, "lo"), (tname, "hi"), "aggr", ("small", 25)],
            writes=[(tname, "lo"), (tname, "hi")])

    s_y1d = p.slot("y1d")
    for t in range(NT):
        layernorm_tile(t)
        p.op("act", lambda e, t=t: e.activation(ybf[:], ACC[t][:], AF.Copy),
             reads=[(f"ACC{t}", "lo"), (f"ACC{t}", "hi")], writes=["ybf"])
        sync_dma(Y1D[128 * t:128 * (t + 1), :], ybf[:], ["ybf"],
                 [("y1dd", t)], s_y1d)
        for dc in range(ND):
            pb = ptcs[dc // 8]
            p.op("pe", lambda e, dc=dc, pb=pb: e.transpose(
                pb[:, 128 * (dc % 8):128 * (dc % 8 + 1)],
                ybf[:, 128 * dc:128 * (dc + 1)], idb[:]),
                reads=["ybf", "idb"], writes=[(pb.name, dc % 8)])
        for dc in range(ND):
            pb = ptcs[dc // 8]
            p.op("dve", lambda e, t=t, dc=dc, pb=pb: e.tensor_copy(
                y1T[dc][:, 128 * t:128 * (t + 1)],
                pb[:, 128 * (dc % 8):128 * (dc % 8 + 1)]),
                reads=[(pb.name, dc % 8)], writes=[(f"y1T{dc}", t)])

    tm.free(*AOsb, *woq, *wosg, *xrs, ybf)

    if "Y1" in tap_outs:
        for t in range(NT):
            sync_dma(tap_outs["Y1"][128 * t:128 * (t + 1), :], ACC[t][:],
                     [(f"ACC{t}", "lo"), (f"ACC{t}", "hi")],
                     [("OUT", f"y1tap{t}")], p.slot("tap_Y1"), gid="y1g")

    if stop_after == "C":
        return finish_acc(ACC)

    gwbf = tm.sbuf("gwbf", [NE, D], BF16)
    gwT = tm.sbuf("gwT", [128, 128], BF16)
    cast_dma(gwbf[:], GW, [], ["gwbf"], p.slot("gw"))
    pL = tm.psum("pL", [128, 512])
    for dc in range(ND):
        p.op("pe", lambda e, dc=dc: e.transpose(
            ptc0[:, 8 * dc:8 * (dc + 1)],
            gwbf[:, 128 * dc:128 * (dc + 1)], idb[0:8, 0:8]),
            reads=["gwbf", "idb"], writes=[("ptc0", ("g", dc))])
        p.op("dve", lambda e, dc=dc: e.tensor_copy(
            gwT[:, 8 * dc:8 * (dc + 1)], ptc0[:, 8 * dc:8 * (dc + 1)]),
            reads=[("ptc0", ("g", dc))], writes=[("gwT", dc)])

    for t in range(NT):
        for dc in range(ND):
            p.op("pe", lambda e, t=t, dc=dc: e.matmul(
                pL[:, 0:8], y1T[dc][:, 128 * t:128 * (t + 1)],
                gwT[:, 8 * dc:8 * (dc + 1)],
                start=(dc == 0), stop=(dc == ND - 1)),
                reads=[(f"y1T{dc}", t), ("gwT", dc)], writes=["pL"])
        g0 = 40
        gsl = gates8[:, 8 * t:8 * (t + 1)]
        p.op("dve", lambda e: e.reduce_max(small[:, g0:g0 + 1], pL[:, 0:8],
                                           AX.X, negate=True),
             reads=["pL"], writes=[("small", g0)])
        p.op("act", lambda e, gsl=gsl: e.activation(
            gsl, pL[:, 0:8], AF.Exp, bias=small[:, g0:g0 + 1]),
            reads=["pL", ("small", g0)], writes=[("gates8", t)])
        p.op("dve", lambda e, gsl=gsl: e.reduce_sum(
            small[:, g0 + 1:g0 + 2], gsl, AX.X),
            reads=[("gates8", t)], writes=[("small", g0 + 1)])
        p.op("dve", lambda e: e.reciprocal(small[:, g0 + 2:g0 + 3],
                                           small[:, g0 + 1:g0 + 2]),
             reads=[("small", g0 + 1)], writes=[("small", g0 + 2)])
        p.op("dve", lambda e, gsl=gsl: e.tensor_scalar(
            gsl, gsl, small[:, g0 + 2:g0 + 3], None, ALU.mult),
            reads=[("gates8", t), ("small", g0 + 2)], writes=[("gates8", t)])
        p.op("dve", lambda e, gsl=gsl: e.reduce_max(
            small[:, g0 + 3:g0 + 4], gsl, AX.X),
            reads=[("gates8", t)], writes=[("small", g0 + 3)])
        p.op("dve", lambda e, gsl=gsl: e.tensor_scalar(
            small[:, 48:56], gsl, small[:, g0 + 3:g0 + 4], 1e30,
            ALU.is_ge, ALU.mult),
            reads=[("gates8", t), ("small", g0 + 3)], writes=[("small", 48)])
        p.op("dve", lambda e, gsl=gsl: e.tensor_tensor(
            small[:, 48:56], gsl, small[:, 48:56], ALU.subtract),
            reads=[("gates8", t), ("small", 48)], writes=[("small", 48)])
        p.op("dve", lambda e: e.reduce_max(small[:, g0 + 4:g0 + 5],
                                           small[:, 48:56], AX.X),
             reads=[("small", 48)], writes=[("small", g0 + 4)])
        p.op("dve", lambda e: e.tensor_tensor(small[:, g0 + 5:g0 + 6],
                                              small[:, g0 + 3:g0 + 4],
                                              small[:, g0 + 4:g0 + 5],
                                              ALU.add),
             reads=[("small", g0 + 3), ("small", g0 + 4)],
             writes=[("small", g0 + 5)])
        p.op("dve", lambda e: e.reciprocal(small[:, g0 + 6:g0 + 7],
                                           small[:, g0 + 5:g0 + 6]),
             reads=[("small", g0 + 5)], writes=[("small", g0 + 6)])
        p.op("dve", lambda e, gsl=gsl: e.tensor_scalar(
            small[:, 48:56], gsl, small[:, g0 + 4:g0 + 5], None, ALU.is_ge),
            reads=[("gates8", t), ("small", g0 + 4)], writes=[("small", 48)])
        p.op("dve", lambda e, gsl=gsl: e.tensor_tensor(
            gsl, gsl, small[:, 48:56], ALU.mult),
            reads=[("gates8", t), ("small", 48)], writes=[("gates8", t)])
        p.op("dve", lambda e, gsl=gsl: e.tensor_scalar(
            gsl, gsl, small[:, g0 + 6:g0 + 7], None, ALU.mult),
            reads=[("gates8", t), ("small", g0 + 6)], writes=[("gates8", t)])
        p.op("pe", lambda e, gsl=gsl: e.transpose(
            pL[0:8, 128:256], gsl, idf[:]),
            reads=[("gates8", t), "idf"], writes=["pL"])
        p.op("dve", lambda e, t=t: e.tensor_copy(
            grow8[:, 128 * t:128 * (t + 1)], pL[0:8, 128:256]),
            reads=["pL"], writes=[("grow8", t)])

    from concourse.bass import IndirectOffsetOnAxis as IOA
    LTb = tm.sbuf("LTb", [128, 128], BF16)
    cast_dma(LTb[:], LTRI, [], ["LTb"], p.slot("ltri"))
    tokids = tm.sbuf("tokids", [128, 8], F32)
    sync_dma(tokids[:], TCOL, [], ["tokids"], p.slot("tcol"))
    iorowf = tm.sbuf("iorowf", [1, 384], F32)
    sync_dma(iorowf[:], IOROW, [], ["iorowf"], p.slot("iorow"))
    iotabc = tm.sbuf("iotabc", [128, 384], F32)
    slotf = tm.sbuf("slotf", [128, 64], F32)
    posf = tm.sbuf("posf", [128, 64], F32)
    mkb = tm.sbuf("mkb", [128, 64], BF16)
    bb = tm.sbuf("bb", [128, 64], F32)
    brow = tm.sbuf("brow", [1, 64], F32)
    crow = tm.sbuf("crow", [1, 64], F32)
    PTf = tm.sbuf("PTf", [128, NT * 384], F32)
    idxall = tm.sbuf("idxall", [128, 24], mybir.dt.int32)
    g12i = tm.sbuf("g12i", [128, 16], mybir.dt.int32)
    gg12 = tm.sbuf("gg12", [128, 16], F32)
    gsc = tm.sbuf("g12scr", [128, 32], F32)

    p.op("pe", lambda e: e.matmul(pL[:, 0:384], onesf[:], iorowf[:],
                                  start=True, stop=True),
         reads=["iorowf", "onesf"], writes=["pL"])
    p.op("dve", lambda e: e.tensor_copy(iotabc[:], pL[:, 0:384]),
         reads=["pL"], writes=["iotabc"])
    for t in range(NT):
        p.op("dve", lambda e, t=t: e.tensor_scalar(
            mkb[:, 8 * t:8 * (t + 1)], gates8[:, 8 * t:8 * (t + 1)],
            0.0, None, ALU.is_gt),
            reads=[("gates8", t)], writes=[("mkb", t)])
    for t in range(NT):
        p.op("pe", lambda e, t=t: e.matmul(
            pL[:, 8 * t:8 * (t + 1)], LTb[:], mkb[:, 8 * t:8 * (t + 1)],
            start=True, stop=True),
            reads=["LTb", ("mkb", t)], writes=["pL"])
        p.op("pe", lambda e, t=t: e.matmul(
            pL[0:1, 64 + 8 * t:72 + 8 * t], onescol[:],
            mkb[:, 8 * t:8 * (t + 1)], start=True, stop=True),
            reads=["onescol", ("mkb", t)], writes=["pL"])
    p.op("dve", lambda e: e.tensor_copy(posf[:], pL[:, 0:64]),
         reads=["pL"], writes=["posf"])
    p.op("dve", lambda e: e.tensor_copy(crow[:], pL[0:1, 64:128]),
         reads=["pL"], writes=["crow"])
    p.op("dve", lambda e: e.memset(brow[:, 0:8], 0.0), writes=[("brow", 0)])
    for t in range(1, NT):
        p.op("dve", lambda e, t=t: e.tensor_tensor(
            brow[0:1, 8 * t:8 * (t + 1)], brow[0:1, 8 * (t - 1):8 * t],
            crow[0:1, 8 * (t - 1):8 * t], ALU.add),
            reads=[("brow", t - 1), "crow"], writes=[("brow", t)])
    p.op("pe", lambda e: e.matmul(pL[:, 128:192], onesf[:], brow[:],
                                  start=True, stop=True),
         reads=[("brow", None)] + ["onesf"], writes=["pL"])
    p.op("dve", lambda e: e.tensor_copy(bb[:], pL[:, 128:192]),
         reads=["pL"], writes=["bb"])
    p.op("dve", lambda e: e.tensor_tensor(slotf[:], posf[:], bb[:], ALU.add),
         reads=["posf", "bb"], writes=["slotf"])
    p.op("dve", lambda e: e.tensor_scalar(slotf[:], slotf[:], 10000.0, None,
                                          ALU.add),
         reads=["slotf"], writes=["slotf"])
    p.op("dve", lambda e: e.scalar_tensor_tensor(
        slotf[:], mkb[:], -10000.0, slotf[:], ALU.mult, ALU.add),
        reads=["slotf", ("mkb", None)] + [("mkb", t) for t in range(NT)],
        writes=["slotf"])
    for e_i in range(NE):
        for t in range(NT):
            p.op("dve", lambda e, e_i=e_i, t=t: e.tensor_scalar(
                PTf[:, 384 * t:384 * (t + 1)], iotabc[:],
                slotf[:, 8 * t + e_i:8 * t + e_i + 1], None, ALU.is_equal),
                reads=["iotabc", "slotf"], writes=[("PTf", t)])
        for sti in range(3):
            for t in range(NT):
                p.op("pe", lambda e, sti=sti, t=t: e.matmul(
                    pL[:, 448 + sti:449 + sti],
                    PTf[:, 384 * t + 128 * sti:384 * t + 128 * (sti + 1)],
                    tokids[:, t:t + 1], start=(t == 0), stop=(t == NT - 1)),
                    reads=[("PTf", t), "tokids"], writes=["pL"])
        for sti in range(3):
            p.op("dve", lambda e, e_i=e_i, sti=sti: e.tensor_copy(
                idxall[:, 3 * e_i + sti:3 * e_i + sti + 1],
                pL[:, 448 + sti:449 + sti]),
                reads=["pL"], writes=[("idxall", 3 * e_i + sti)])

    for t in range(NT):
        p.op("dve", lambda e, t=t: e.reduce_max(
            gsc[:, 16:17], gates8[:, 8 * t:8 * (t + 1)], AX.X),
            reads=[("gates8", t)], writes=[("gsc", 16)])
        p.op("dve", lambda e, t=t: e.tensor_scalar(
            gsc[:, 0:8], gates8[:, 8 * t:8 * (t + 1)], gsc[:, 16:17], None,
            ALU.is_ge),
            reads=[("gates8", t), ("gsc", 16)], writes=[("gsc", 0)])
        p.op("dve", lambda e, t=t: e.tensor_scalar(
            gsc[:, 8:16], gates8[:, 8 * t:8 * (t + 1)], 0.0, None,
            ALU.is_gt),
            reads=[("gates8", t)], writes=[("gsc", 8)])
        p.op("dve", lambda e: e.tensor_tensor(
            gsc[:, 8:16], gsc[:, 8:16], gsc[:, 0:8], ALU.subtract),
            reads=[("gsc", 0), ("gsc", 8)], writes=[("gsc", 8)])
        for j in range(2):
            c0 = 8 * j
            p.op("dve", lambda e, t=t, c0=c0: e.tensor_tensor(
                gsc[:, 17:25], gsc[:, c0:c0 + 8], slotf[:, 8 * t:8 * (t + 1)],
                ALU.mult),
                reads=[("gsc", c0), "slotf"], writes=[("gsc", 17)])
            p.op("dve", lambda e: e.reduce_sum(gsc[:, 25:26], gsc[:, 17:25],
                                               AX.X),
                reads=[("gsc", 17)], writes=[("gsc", 25)])
            p.op("dve", lambda e, c0=c0: e.tensor_tensor(
                gsc[:, 17:25], gsc[:, c0:c0 + 8], iotabc[:, 0:8], ALU.mult),
                reads=[("gsc", c0), "iotabc"], writes=[("gsc", 17)])
            p.op("dve", lambda e: e.reduce_sum(gsc[:, 26:27], gsc[:, 17:25],
                                               AX.X),
                reads=[("gsc", 17)], writes=[("gsc", 26)])
            p.op("dve", lambda e: e.scalar_tensor_tensor(
                gsc[:, 27:28], gsc[:, 26:27], 384.0, gsc[:, 25:26],
                ALU.mult, ALU.add),
                reads=[("gsc", 25), ("gsc", 26)], writes=[("gsc", 27)])
            p.op("dve", lambda e, t=t, j=j: e.tensor_copy(
                g12i[:, 8 * j + t:8 * j + t + 1], gsc[:, 27:28]),
                reads=[("gsc", 27)], writes=[("g12i", 8 * j + t)])
            p.op("dve", lambda e, t=t, c0=c0: e.tensor_tensor(
                gsc[:, 17:25], gsc[:, c0:c0 + 8], gates8[:, 8 * t:8 * (t + 1)],
                ALU.mult),
                reads=[("gsc", c0), ("gates8", t)], writes=[("gsc", 17)])
            p.op("dve", lambda e, t=t, j=j: e.reduce_sum(
                gg12[:, 8 * j + t:8 * j + t + 1], gsc[:, 17:25], AX.X),
                reads=[("gsc", 17)], writes=[("gg12", 8 * j + t)])

    if "GATES" in tap_outs:
        gtmp = tm.sbuf("gtmp", [128, NT * NE], F32)
        p.op("dve", lambda e: e.tensor_copy(gtmp[:], gates8[:]),
             reads=["gates8"] + [("gates8", t) for t in range(NT)],
             writes=["gtmp"])
        sync_dma(tap_outs["GATES"].rearrange("(t p) e -> p t e", p=128),
                 gtmp[:].rearrange("p (t e) -> p t e", e=NE),
                 ["gtmp"], [("OUT", "gtap")], p.slot("tap_GATES"),
                 allow_slow_non_contiguous=True)
        tm.free(gtmp)

    tm.free(PTf, iotabc, slotf, posf, mkb, bb, brow, crow, iorowf, LTb,
            tokids, gsc)
    tm.free(gwbf, gwT)
    tm.free(pL, pc[1], pc[0], ptc1, ptc0)

    if stop_after == "D":
        return finish_acc(ACC)

    FGRP = [(0, 256), (256, 256), (512, 256), (768, 256), (1024, 256),
            (1280, 128)]
    wgb = [tm.sbuf(f"wgb{i}", [128, ND * 256], BF16) for i in range(3)]
    wub = [tm.sbuf(f"wub{i}", [128, ND * 256], BF16) for i in range(3)]
    s_wgb = [p.slot(f"wgb{i}") for i in range(3)]
    s_wub = [p.slot(f"wub{i}") for i in range(3)]
    wdb = [tm.sbuf(f"wdb{i}", [128, 11 * 256], BF16) for i in range(3)]
    s_wdb = [p.slot(f"wdb{i}") for i in range(3)]
    ygT = tm.sbuf("ygT", [128, ND * 384], BF16)
    Ygt = tm.sbuf("Ygt", [128, D], BF16)
    Dsb = [tm.sbuf(f"Dsb{i}", [128, D], BF16) for i in range(3)]
    hTe = [tm.sbuf(f"hTe{i}", [128, 384], BF16) for i in range(11)]
    gsb = tm.sbuf("gsb", [128, 384], BF16)
    usb = tm.sbuf("usb", [128, 384], BF16)
    hT = [tm.sbuf(f"hT{i}", [128, TC], BF16) for i in range(11)]
    gsbs = tm.sbuf("gsbs", [128, TC], BF16)
    usbs = tm.sbuf("usbs", [128, TC], BF16)
    pG = tm.psum("pG", [128, 1024])
    pU = tm.psum("pU", [128, 1024])
    pd = [tm.psum(f"pd{i}", [128, 512]) for i in range(2)]
    ptE = tm.psum("ptE", [128, 1024], BF16)

    WGr = [WGATE[e].rearrange("(a p) f -> p a f", p=128) for e in range(NE)]
    WUr = [WUP[e].rearrange("(a p) f -> p a f", p=128) for e in range(NE)]
    WDr = [WDOWN[e].rearrange("(c p) d -> p c d", p=128) for e in range(NE)]
    WGr.append(SWG.rearrange("(a p) f -> p a f", p=128))
    WUr.append(SWU.rearrange("(a p) f -> p a f", p=128))
    WDr.append(SWD.rearrange("(c p) d -> p c d", p=128))

    s_gath = p.slot("gath")
    s_scat = p.slot("scat")
    gi = 0
    di = 0
    for e_i in range(NE + 1):
        routed = e_i < NE
        if routed:
            for sti in range(3):
                p.dma("pool", lambda e, e_i=e_i, sti=sti: e.indirect_dma_start(
                    Ygt[:], None, Y1D,
                    IOA(idxall[:, 3 * e_i + sti:3 * e_i + sti + 1], 0),
                    bounds_check=TC - 1, oob_is_err=False),
                    reads=[("idxall", 3 * e_i + sti)] +
                          [("y1dd", t) for t in range(NT)],
                    writes=["Ygt"], slot=s_gath, gid=("g", e_i, sti))
                for dc in range(ND):
                    pb = ptE
                    p.op("pe", lambda e, dc=dc, pb=pb: e.transpose(
                        pb[:, 128 * (dc % 8):128 * (dc % 8 + 1)],
                        Ygt[:, 128 * dc:128 * (dc + 1)], idb[:]),
                        reads=["Ygt", "idb"], writes=[(pb.name, dc % 8)])
                    if dc % 8 == 7:
                        h0 = dc - 7
                        p.op("dve", lambda e, h0=h0, sti=sti, pb=pb:
                             e.tensor_copy(
                                 ygT[:].rearrange("p (a s) -> p a s", s=384)
                                 [:, h0:h0 + 8, 128 * sti:128 * (sti + 1)],
                                 pb[:].rearrange("p (a s) -> p a s", s=128)),
                             reads=[pb.name] + [(pb.name, i) for i in range(8)],
                             writes=[("ygT", (sti, h0))])
        for g, (f0, fw) in enumerate(FGRP):
            b = gi % 3
            gi += 1
            for blk in range(0, ND, 4):
                cast_dma(
                    wgb[b][:].rearrange("p (a f) -> p a f", f=256)
                    [:, blk:blk + 4, 0:fw],
                    WGr[e_i][:, blk:blk + 4, f0:f0 + fw],
                    [], [(f"wgb{b}", blk)], s_wgb[b], gid=("g", e_i, g))
                cast_dma(
                    wub[b][:].rearrange("p (a f) -> p a f", f=256)
                    [:, blk:blk + 4, 0:fw],
                    WUr[e_i][:, blk:blk + 4, f0:f0 + fw],
                    [], [(f"wub{b}", blk)], s_wub[b], gid=("g", e_i, g))
            for cc in range(fw // 128):
                fc = f0 // 128 + cc
                if routed:
                    for a in range(ND):
                        p.op("pe", lambda e, b=b, a=a, cc=cc: e.matmul(
                            pG[:, 0:384],
                            wgb[b][:, 256 * a + 128 * cc:
                                   256 * a + 128 * (cc + 1)],
                            ygT[:, 384 * a:384 * (a + 1)],
                            start=(a == 0), stop=(a == ND - 1)),
                            reads=[f"wgb{b}", ("ygT", None)], writes=["pG"])
                    for a in range(ND):
                        p.op("pe", lambda e, b=b, a=a, cc=cc: e.matmul(
                            pU[:, 0:384],
                            wub[b][:, 256 * a + 128 * cc:
                                   256 * a + 128 * (cc + 1)],
                            ygT[:, 384 * a:384 * (a + 1)],
                            start=(a == 0), stop=(a == ND - 1)),
                            reads=[f"wub{b}", ("ygT", None)], writes=["pU"])
                    p.op("act", lambda e: e.activation(gsb[:], pG[:, 0:384],
                                                       AF.Silu),
                         reads=["pG"], writes=["gsb"])
                    p.op("dve", lambda e: e.tensor_copy(usb[:], pU[:, 0:384]),
                         reads=["pU"], writes=["usb"])
                    p.op("dve", lambda e, fc=fc: e.tensor_tensor(
                        hTe[fc][:], gsb[:], usb[:], ALU.mult),
                        reads=["gsb", "usb"], writes=[f"hTe{fc}"])
                else:
                    for th in range(2):
                        for a in range(ND):
                            p.op("pe", lambda e, b=b, a=a, cc=cc, th=th:
                                 e.matmul(
                                     pG[:, 512 * th:512 * (th + 1)],
                                     wgb[b][:, 256 * a + 128 * cc:
                                            256 * a + 128 * (cc + 1)],
                                     y1T[a][:, 512 * th:512 * (th + 1)],
                                     start=(a == 0), stop=(a == ND - 1)),
                                 reads=[f"wgb{b}", f"y1T{a}"] +
                                       [(f"y1T{a}", t) for t in range(NT)],
                                 writes=["pG"])
                        for a in range(ND):
                            p.op("pe", lambda e, b=b, a=a, cc=cc, th=th:
                                 e.matmul(
                                     pU[:, 512 * th:512 * (th + 1)],
                                     wub[b][:, 256 * a + 128 * cc:
                                            256 * a + 128 * (cc + 1)],
                                     y1T[a][:, 512 * th:512 * (th + 1)],
                                     start=(a == 0), stop=(a == ND - 1)),
                                 reads=[f"wub{b}", f"y1T{a}"] +
                                       [(f"y1T{a}", t) for t in range(NT)],
                                 writes=["pU"])
                    p.op("act", lambda e: e.activation(gsbs[:], pG[:], AF.Silu),
                         reads=["pG"], writes=["gsbs"])
                    p.op("dve", lambda e: e.tensor_copy(usbs[:], pU[:]),
                         reads=["pU"], writes=["usbs"])
                    p.op("dve", lambda e, fc=fc: e.tensor_tensor(
                        hT[fc][:], gsbs[:], usbs[:], ALU.mult),
                        reads=["gsbs", "usbs"], writes=[f"hT{fc}"])
        for q in range(8):
            b = di % 3
            di += 1
            for blk, cnt in ((0, 4), (4, 4), (8, 3)):
                cast_dma(
                    wdb[b][:].rearrange("p (c d) -> p c d", d=256)
                    [:, blk:blk + cnt, :],
                    WDr[e_i][:, blk:blk + cnt, 256 * q:256 * (q + 1)],
                    [], [(f"wdb{b}", blk)], s_wdb[b], gid=("d", e_i, q))
            if routed:
                for sti in range(3):
                    pdb = pd[(q * 3 + sti) % 2]
                    for fc in range(11):
                        p.op("pe", lambda e, b=b, fc=fc, sti=sti, pdb=pdb:
                             e.matmul(
                                 pdb[:, 0:256],
                                 hTe[fc][:, 128 * sti:128 * (sti + 1)],
                                 wdb[b][:, 256 * fc:256 * (fc + 1)],
                                 start=(fc == 0), stop=(fc == 10)),
                             reads=[f"hTe{fc}", f"wdb{b}"],
                             writes=[pdb.name])
                    p.op("dve", lambda e, sti=sti, q=q, pdb=pdb:
                         e.tensor_copy(
                             Dsb[sti][:, 256 * q:256 * (q + 1)],
                             pdb[:, 0:256]),
                         reads=[pdb.name],
                         writes=[(f"Dsb{sti}", q), ("ygT", None)])
            else:
                for t in range(NT):
                    pdb = pd[t % 2]
                    hk = "lo" if q < 4 else "hi"
                    for fc in range(11):
                        p.op("pe", lambda e, b=b, fc=fc, t=t, pdb=pdb:
                             e.matmul(
                                 pdb[:, 0:256],
                                 hT[fc][:, 128 * t:128 * (t + 1)],
                                 wdb[b][:, 256 * fc:256 * (fc + 1)],
                                 start=(fc == 0), stop=(fc == 10)),
                             reads=[f"hT{fc}", f"wdb{b}"],
                             writes=[pdb.name])
                    p.op("dve", lambda e, t=t, q=q, pdb=pdb: e.tensor_tensor(
                        ACC[t][:, 256 * q:256 * (q + 1)],
                        ACC[t][:, 256 * q:256 * (q + 1)], pdb[:, 0:256],
                        ALU.add),
                        reads=[pdb.name, (f"ACC{t}", hk)],
                        writes=[(f"ACC{t}", hk)])
        if routed:
            for sti in range(3):
                sync_dma(PK[384 * e_i + 128 * sti:384 * e_i + 128 * (sti + 1),
                            :],
                         Dsb[sti][:],
                         [(f"Dsb{sti}", q) for q in range(8)],
                         [("pkd", (e_i, sti)), ("ygT", None)],
                         s_scat, gid=("s", e_i, sti))
        if e_i == NE - 1:
            p.op("dve", lambda e: e.engine_nop(),
                 reads=[("ygT", None), ("Ygt", None)] +
                       [(f"Dsb{i}", None) for i in range(3)] +
                       [(f"hTe{i}", None) for i in range(11)],
                 writes=[(f"hT{i}", None) for i in range(11)] +
                        ["gsbs", "usbs"])

    if "ACC2" in tap_outs:
        for t in range(NT):
            sync_dma(tap_outs["ACC2"][128 * t:128 * (t + 1), :], ACC[t][:],
                     [(f"ACC{t}", "lo"), (f"ACC{t}", "hi")],
                     [("OUT", f"acc2tap{t}")], p.slot("tap_ACC2"),
                     gid="a2g")

    tm.free(*wgb, *wub, *wdb, ygT, Ygt, *Dsb, *hTe, gsb, usb, *hT, gsbs, usbs)
    rload = [tm.sbuf(f"rload{i}", [128, D], BF16) for i in range(4)]
    s_rl = [p.slot("rl0"), p.slot("rl1")]
    for t in range(NT):
        for j in range(2):
            rn = (t % 2) * 2 + j
            p.dma("pool", lambda e, t=t, j=j, rn=rn: e.indirect_dma_start(
                rload[rn][:], None, PK,
                IOA(g12i[:, 8 * j + t:8 * j + t + 1], 0),
                bounds_check=NE * 384 - 1, oob_is_err=False),
                reads=[("g12i", 8 * j + t), ("pkd", None)],
                writes=[f"rload{rn}"], slot=s_rl[t % 2], gid=("r", t))
        for j in range(2):
            rn = (t % 2) * 2 + j
            p.op("dve", lambda e, t=t, j=j, rn=rn: e.scalar_tensor_tensor(
                ACC[t][:], rload[rn][:], gg12[:, 8 * j + t:8 * j + t + 1],
                ACC[t][:], ALU.mult, ALU.add),
                reads=[f"rload{rn}", ("gg12", 8 * j + t),
                       (f"ACC{t}", "lo"), (f"ACC{t}", "hi")],
                writes=[(f"ACC{t}", "lo"), (f"ACC{t}", "hi")])
        layernorm_tile(t)
        sync_dma(Y[128 * t:128 * (t + 1), :], ACC[t][:],
                 [(f"ACC{t}", "lo"), (f"ACC{t}", "hi")],
                 [("OUT", t)], s_out)

    p.op("pool", lambda e: e.nop(), reads=[("OUT", None)], tag="final")

    with nc.Block() as block:
        for e in ENGINES:
            sems[e] = st.enter_context(nc.semaphore(f"sem_{e}"))
        for s in p.slots.values():
            s.sem = st.enter_context(nc.semaphore(f"dma_{s.name}"))
        p.emit_all(block, sems)
    st.close()
    nc._dbg_prog = p
    return nc


def host_consts(core):
    M = np.zeros((128, 1152), dtype=np.float32)
    s = np.arange(128)[:, None]
    i = np.arange(1152)[None, :]
    d = i - s
    M[:] = np.where((d >= 0) & (d <= WIN), (s - i).astype(np.float32),
                    np.float32(-1e9)).astype(np.float32)
    poi = np.zeros((128, NS), dtype=np.float32)
    if core == 0:
        poi[:, 0:8] = -1e5
    idf = np.eye(128, dtype=np.float32)
    return M, poi, idf


def host_moe_consts():
    ltri = np.triu(np.ones((128, 128), dtype=np.float32), 1)
    tcol = (np.arange(128, dtype=np.float32)[:, None] +
            128.0 * np.arange(8, dtype=np.float32)[None, :])
    tcol = np.ascontiguousarray(tcol)
    iorow = np.arange(384, dtype=np.float32).reshape(1, 384)
    return ltri, tcol, iorow


_nc_cache = {}


def _get_nc(taps=()):
    key = tuple(sorted(taps))
    if key not in _nc_cache:
        _nc_cache[key] = build_nc(key)
    return _nc_cache[key]


def kernel(L0_attn_x, L0_attn_wq, L0_attn_wk, L0_attn_wv, L0_attn_wo,
           L0_attn_q_norm_w, L0_attn_k_norm_w, L0_ffn_gate_w, L0_ffn_w_gate,
           L0_ffn_w_up, L0_ffn_w_down, L0_ffn_sw_gate, L0_ffn_sw_up,
           L0_ffn_sw_down, _taps=(), _collect=None):
    x = np.ascontiguousarray(np.asarray(L0_attn_x, dtype=np.float32)
                             .reshape(NCORE * TC, D))
    asf = lambda a: np.ascontiguousarray(np.asarray(a, dtype=np.float32))
    wq, wk, wv, wo = (asf(L0_attn_wq), asf(L0_attn_wk), asf(L0_attn_wv),
                      asf(L0_attn_wo))
    operm = np.concatenate(
        [np.arange(64 * h, 64 * h + 64)
         for j in range(16) for h in (8 * (j // 4) + (j % 4),
                                      8 * (j // 4) + (j % 4) + 4)])
    wq = np.ascontiguousarray(wq[operm, :])
    wo = np.ascontiguousarray(wo[:, operm])
    qnw = asf(L0_attn_q_norm_w).reshape(1, D)
    qnw = np.ascontiguousarray(qnw[:, operm])
    knw = asf(L0_attn_k_norm_w).reshape(1, 512)
    gw = asf(L0_ffn_gate_w)
    wgate, wup, wdown = asf(L0_ffn_w_gate), asf(L0_ffn_w_up), asf(L0_ffn_w_down)
    swg, swu, swd = asf(L0_ffn_sw_gate), asf(L0_ffn_sw_up), asf(L0_ffn_sw_down)

    nc = _get_nc(tuple(_taps))
    in_maps = []
    for c in range(NCORE):
        if c == 0:
            prev = np.zeros((TC, D), dtype=np.float32)
        else:
            prev = x[TC * (c - 1):TC * c]
        xe = np.ascontiguousarray(
            np.concatenate([prev, x[TC * c:TC * (c + 1)]], axis=0))
        M, poi, idf = host_consts(c)
        ltri, tcol, iorow = host_moe_consts()
        in_maps.append(dict(
            xe=xe, wq=wq, wk=wk, wv=wv, wo=wo, qnw=qnw, knw=knw, gw=gw,
            wgate=wgate, wup=wup, wdown=wdown, swg=swg, swu=swu, swd=swd,
            mmask=M, poison=poi, identf=idf, ltri=ltri, tcol=tcol,
            iorow=iorow))
    res = run_bass_kernel_spmd(nc, in_maps, core_ids=list(range(NCORE)))
    if _collect is not None:
        _collect.extend(res.results)
    y = np.concatenate([res.results[c]["y"] for c in range(NCORE)], axis=0)
    return y.reshape(1, NCORE * TC, D)
